# version 01
# baseline (speedup 1.0000x reference)
import sys
if '/opt/trn_rl_repo' not in sys.path:
    sys.path.insert(0, '/opt/trn_rl_repo')
from contextlib import ExitStack

import numpy as np
import ml_dtypes
import concourse.bass as bass
import concourse.mybir as mybir
from concourse.bass_core_rust import IndirectOffsetOnAxis
from concourse.bass_utils import run_bass_kernel_spmd

F32 = mybir.dt.float32
BF16 = mybir.dt.bfloat16
U32 = mybir.dt.uint32
U8 = mybir.dt.uint8
I32 = mybir.dt.int32
AF = mybir.ActivationFunctionType
ALU = mybir.AluOpType
AX = mybir.AxisListType

NCORE = 8
B = 2
S = 8192
NTOK = B * S
D = 1024
BF = ml_dtypes.bfloat16

ENGINES = ("sync", "tensor", "vector", "scalar", "gpsimd")


class SemT:
    def __init__(self, handle):
        self.h = handle
        self.n = 0

    def inc(self, k=1):
        self.n += k
        return self.n


class Prog:

    def __init__(self, name="prog"):
        self.nc = bass.Bass("TRN2", target_bir_lowering=False)
        self.stack = ExitStack()
        self.streams = {e: [] for e in ENGINES}
        self.sems = {}
        self.tiles = {}
        self.name = name

    def chain_bar(self, name, eng, inst):
        s = self.sem("_chain_" + name)
        if not hasattr(s, 'rt'):
            s.rt = 0
        s.rt += 1
        inst.then_inc(s.h, 1)
        return eng.wait_ge(s.h, s.rt)

    def sem(self, name):
        if name not in self.sems:
            h = self.stack.enter_context(self.nc.semaphore(name))
            self.sems[name] = SemT(h)
        return self.sems[name]

    def sb(self, name, shape, dtype):
        t = self.stack.enter_context(self.nc.sbuf_tensor(name, list(shape), dtype))
        self.tiles[name] = t
        return t

    def ps(self, name, shape, dtype=F32):
        t = self.stack.enter_context(self.nc.psum_tensor(name, list(shape), dtype))
        self.tiles[name] = t
        return t

    def add(self, engine, fn):
        self.streams[engine].append(fn)

    def build(self):
        block = self.stack.enter_context(self.nc.Block())
        for e in ENGINES:
            fns = self.streams[e]
            if not fns:
                continue
            def runner(eng, fns=fns):
                for fn in fns:
                    fn(eng)
            getattr(block, e)(runner)
        self.stack.close()
        return self.nc


def ceil_div(a, b):
    return (a + b - 1) // b



def rope_factor_np():
    d_head = 64
    i = np.arange(d_head)
    theta = np.float32(10000.0) ** (np.float32(-2.0) * (i // 2).astype(np.float32) / np.float32(d_head))
    theta = theta.astype(np.float32)
    pos = np.arange(S, dtype=np.float32)
    ang = (pos[:, None] * theta[None, :]).astype(np.float32)
    cos = np.cos(ang.astype(np.float64))
    sin = np.sin(ang.astype(np.float64))
    sign = np.where(np.arange(S) % 2 == 0, -1.0, 1.0)[:, None]
    return (cos + sin * sign).astype(np.float32)


def causal_diag_masks_np():
    q = np.arange(512)[None, :]
    out = []
    for koff in (0, 128, 256, 384):
        k = koff + np.arange(128)[:, None]
        out.append((k <= q).astype(BF))
    return np.stack(out)



def emit_dma(p, q, out, in_, sem, wait=()):
    assert out is not None and in_ is not None, "emit_dma got None AP"
    tgt = sem.inc(16)
    waits = list(wait)
    def f(eng):
        for (s, v) in waits:
            eng.wait_ge(s.h, v)
        eng.dma_start(out=out, in_=in_).then_inc(sem.h, 16)
    p.add(q, f)
    return tgt


def emit(p, engine, fn, sem=None, wait=()):
    tgt = sem.inc(1) if sem is not None else None
    waits = list(wait)
    def f(eng):
        for (s, v) in waits:
            eng.wait_ge(s.h, v)
        last = fn(eng)
        if sem is not None:
            assert last is not None, "fn must return its last instruction"
            last.then_inc(sem.h, 1)
    p.add(engine, f)
    return tgt



def build_l0_attn_a(S=S):
    p = Prog("l0aa")
    nc = p.nc
    QT = 512
    NQT = S // QT

    x_fm = nc.dram_tensor("x_fm", [D, S], F32, kind="ExternalInput").ap()
    rf2 = nc.dram_tensor("rf2", [128, S], F32, kind="ExternalInput").ap()
    wq = nc.dram_tensor("wq", [D, 256], F32, kind="ExternalInput").ap()
    wk2 = nc.dram_tensor("wk2", [D, 128], F32, kind="ExternalInput").ap()
    wv = nc.dram_tensor("wv", [D, 64], F32, kind="ExternalInput").ap()
    wo = nc.dram_tensor("wo", [64, 4 * D], F32, kind="ExternalInput").ap()
    nwrow = nc.dram_tensor("nwrow", [1, D], F32, kind="ExternalInput").ap()
    nwcol = nc.dram_tensor("nwcol", [128, 8], F32, kind="ExternalInput").ap()
    dmask = nc.dram_tensor("dmask", [4, 128, QT], BF16, kind="ExternalInput").ap()
    hsel = nc.dram_tensor("hsel", [128, 2], F32, kind="ExternalInput").ap()
    y = nc.dram_tensor("y", [D, S], F32, kind="ExternalOutput").ap()

    xt = [p.sb(f"xt{i}", [128, 8, QT], F32) for i in range(2)]
    rft = [p.sb(f"rft{i}", [128, QT], F32) for i in range(2)]
    xsq = p.sb("xsq", [128, QT], F32)
    xq = [p.sb(f"xq{i}", [128, 8, QT], BF16) for i in range(2)]
    tmp32 = p.sb("tmp32", [128, 2, QT], F32)
    rows = p.sb("rows", [1, 7 * QT], F32)
    q01 = p.sb("q01", [128, S], BF16)
    q23 = p.sb("q23", [128, S], BF16)
    kb2 = p.sb("kb2", [128, S], BF16)
    vones = p.sb("vones", [128, 64 * 65], BF16)
    ksq = p.sb("ksq", [64, QT], F32)
    wq_bf = p.sb("wq_bf", [128, 8 * 256], BF16)
    wk_bf = p.sb("wk_bf", [128, 8 * 128], BF16)
    wv_bf = p.sb("wv_bf", [128, 8 * 64], BF16)
    wo_bf = p.sb("wo_bf", [64, 4 * 1024], BF16)
    nw_sb = p.sb("nw_sb", [1, D], F32)
    nwp = p.sb("nwp", [128, 8], F32)
    ones_cf = p.sb("ones_cf", [128, 1], F32)
    ones_f = p.sb("ones_f", [1, 128], F32)
    hsel_sb = p.sb("hsel_sb", [128, 2], F32)
    mrow = p.sb("mrow", [1, NQT], F32)
    mrow2 = p.sb("mrow2", [1, NQT], F32)
    km = p.sb("km", [1, NQT], F32)
    qm = [p.sb(f"qm{h}", [1, NQT], F32) for h in range(4)]
    negc_row = p.sb("negc_row", [1, 4 * NQT], F32)
    negc = p.sb("negc", [128, 4 * NQT], F32)
    pbf = [[p.sb(f"p_h{h}_{par}", [128, QT], BF16) for par in range(2)] for h in range(4)]
    dmask_sb = p.sb("dmask_sb", [128, 4 * QT], BF16)
    r4b = p.sb("r4b", [65, 4 * QT], F32)
    rb = p.sb("rb", [64, 4 * QT], F32)
    ones64 = p.sb("ones64", [65, 128], F32)
    aoh = [p.sb(f"aoh{h}", [64, QT], BF16) for h in range(4)]
    outf = [p.sb(f"outf{i}", [128, 8, QT], F32) for i in range(1)]

    consts = p.sb("consts", [128, 4], F32)
    pb = [p.ps(f"pb{i}", [128, QT], F32) for i in range(8)]

    LD = p.sem("ld"); LX = [p.sem("lx0"), p.sem("lx1")]; CAST = p.sem("cast"); SQ = p.sem("sq"); STM = p.sem("stm")
    ROW = p.sem("row"); ROWA = p.sem("rowa"); NRM = p.sem("nrm"); XQ = p.sem("xq")
    PRJ = p.sem("prj"); QEV = p.sem("qev"); VEV = p.sem("vev"); QN = p.sem("qn")
    CRD = p.sem("crd")
    QK = [p.sem(f"qk{h}") for h in range(4)]
    PS = [p.sem(f"ps{h}") for h in range(4)]
    PM = [p.sem(f"pm{h}") for h in range(4)]
    AV = [p.sem(f"av{h}") for h in range(4)]
    RC = p.sem("rc"); RBM = p.sem("rbm"); RBE = p.sem("rbe"); AOE = p.sem("aoe")
    WOM = p.sem("wom"); WOE = p.sem("woe"); OD = p.sem("od")

    wld = xt[0][:].rearrange("p c t -> p (c t)")
    def wprep(dram_ap, n, nchunk, dst):
        t_w = emit_dma(p, "sync",
                       wld[0:128, 0:n].rearrange("p (c q) -> p c q", c=nchunk),
                       dram_ap, LD, wait=[(CAST, CAST.n)])
        emit(p, "vector", lambda v, n=n, dst=dst: v.tensor_copy(dst[:], wld[0:128, 0:n]),
             CAST, wait=[(LD, t_w)])
    wprep(wq.rearrange("(c p) q -> p c q", p=128), 2048, 8, wq_bf)
    wprep(wk2.rearrange("(c p) q -> p c q", p=128), 1024, 8, wk_bf)
    wprep(wv.rearrange("(c p) q -> p c q", p=128), 512, 8, wv_bf)
    t_wo = emit_dma(p, "sync", wld[0:64, 0:4096].rearrange("p (c q) -> p c q", c=4),
                    wo.rearrange("p (c q) -> p c q", c=4), LD, wait=[(CAST, CAST.n)])
    emit(p, "vector", lambda v: v.tensor_copy(wo_bf[:], wld[0:64, 0:4096]),
         CAST, wait=[(LD, t_wo)])
    LDC = p.sem("ldc")
    emit_dma(p, "sync", nw_sb[:], nwrow, LDC)
    emit_dma(p, "sync", nwp[:], nwcol, LDC)
    emit_dma(p, "sync", hsel_sb[:], hsel, LDC)
    emit_dma(p, "sync", dmask_sb[:].rearrange("p (a t) -> p a t", a=4),
             dmask.rearrange("a p t -> p a t"), LDC)
    t_const = LDC.n
    def init_const(v):
        v.memset(consts[:, 0:1], 1e-5)
        v.memset(consts[:, 1:2], 1.1920929e-07)
        v.memset(consts[:, 2:3], 1.0)
        v.memset(ones_cf[:], 1.0)
        v.memset(ones64[64:65, :], 1.0)
        v.memset(ones_f[:], 1.0)
        return v.memset(vones[:].rearrange("p (a o) -> p a o", o=65)[:, :, 64:65], 1.0)
    emit(p, "vector", init_const, CAST, wait=[(LDC, t_const)])
    t_cinit = CAST.n

    r_mu = rows[:, 0:QT]; r_var = rows[:, QT:2 * QT]; r_rstd = rows[:, 2 * QT:3 * QT]
    r_s = rows[:, 3 * QT:4 * QT]; r_ms = rows[:, 4 * QT:5 * QT]
    r_t = rows[:, 5 * QT:6 * QT]; r_t2 = rows[:, 6 * QT:7 * QT]

    hist = {"prj": [0], "qev": [0], "vev": [0], "qn": [0], "stm": [0], "xq": [0], "rowe": [0]}

    for ti in range(NQT):
        par = ti % 2
        t0 = ti * QT
        waits = [(CAST, t_cinit)]
        if ti >= 2:
            waits = [(PRJ, hist["prj"][ti - 1]), (QEV, hist["qev"][ti - 1])]
        t_x = emit_dma(p, "sync", xt[par][:],
                       x_fm[:, t0:t0 + QT].rearrange("(c p) t -> p c t", p=128), LX[par],
                       wait=waits)
        t_rf = emit_dma(p, "sync", rft[par][:], rf2[:, t0:t0 + QT], LX[par])
        def stats_sum(t, par=par):
            for c in range(8):
                last = t.matmul(pb[0][0:1, :], ones_cf[:], xt[par][:, c, :],
                                start=(c == 0), stop=(c == 7))
            return last
        emit(p, "tensor", stats_sum, STM,
             wait=[(LX[par], t_rf), (VEV, hist["vev"][ti]), (ROW, hist["rowe"][ti]),
                   (QN, hist["qn"][ti])])
        for c in range(8):
            w = [(LX[par], t_rf), (QN, hist["qn"][ti])] if c == 0 else [(STM, STM.n)]
            emit(p, "scalar", lambda sc, c=c, par=par: sc.activation(
                out=xsq[:], in_=xt[par][:, c, :], func=AF.Square), SQ, wait=w)
            emit(p, "tensor", lambda t, c=c: t.matmul(
                pb[1][0:1, :], ones_cf[:], xsq[:], start=(c == 0), stop=(c == 7)),
                STM, wait=[(SQ, SQ.n)])
        t_stm = STM.n
        hist["stm"].append(t_stm)
        def row_a(v):
            v.tensor_single_scalar(out=r_mu, in_=pb[0][0:1, :], scalar=1.0 / D, op=ALU.mult)
            i = v.tensor_single_scalar(out=r_var, in_=pb[1][0:1, :], scalar=1.0 / D, op=ALU.mult)
            p.chain_bar("vector", v, i)
            i = v.tensor_mul(r_t, r_mu, r_mu)
            p.chain_bar("vector", v, i)
            return v.tensor_sub(r_var, r_var, r_t)
        emit(p, "vector", row_a, ROW, wait=[(STM, t_stm)])
        emit(p, "scalar", lambda sc: sc.activation(out=r_t, in_=r_var, func=AF.Sqrt,
                                                   bias=consts[0:1, 0:1]),
             ROWA, wait=[(ROW, ROW.n)])
        def row_c(v):
            i = v.reciprocal(r_rstd, r_t)
            p.chain_bar("vector", v, i)
            i = v.tensor_mul(r_t2, r_rstd, r_rstd)
            p.chain_bar("vector", v, i)
            return v.tensor_mul(r_t2, r_t2, r_var)
        emit(p, "vector", row_c, ROW, wait=[(ROWA, ROWA.n)])
        emit(p, "scalar", lambda sc: sc.activation(out=r_t2, in_=r_t2, func=AF.Sqrt,
                                                   bias=consts[0:1, 1:2]),
             ROWA, wait=[(ROW, ROW.n)])
        def row_e(v):
            i = v.reciprocal(r_t2, r_t2)
            p.chain_bar("vector", v, i)
            i = v.tensor_mul(r_s, r_rstd, r_t2)
            p.chain_bar("vector", v, i)
            return v.tensor_mul(r_ms, r_mu, r_s)
        emit(p, "vector", row_e, ROW, wait=[(ROWA, ROWA.n)])
        t_rowe = ROW.n
        hist["rowe"].append(t_rowe)
        emit(p, "tensor", lambda t: t.matmul(pb[2][:], ones_f[:], r_s, start=True, stop=True),
             NRM, wait=[(ROW, t_rowe), (XQ, hist["xq"][ti]), (VEV, hist["vev"][ti])])
        for c in range(8):
            bank = 3 + (c % 2)
            w = [(NRM, NRM.n)] if c < 2 else [(XQ, XQ.n - 1)]
            if c < 2:
                w.append((XQ, hist["xq"][ti]))
                w.append((VEV, hist["vev"][ti]))
            emit(p, "tensor", lambda t, c=c, bank=bank: t.matmul(
                pb[bank][:], nw_sb[:, c * 128:(c + 1) * 128], r_ms, start=True, stop=True),
                NRM, wait=w)
            t_r1 = NRM.n
            def apply_c(v, c=c, bank=bank, par=par):
                i = v.tensor_mul(tmp32[:, c % 2, :], xt[par][:, c, :], pb[2][:])
                p.chain_bar("vector", v, i)
                return v.scalar_tensor_tensor(out=xq[par][:, c, :], in0=tmp32[:, c % 2, :],
                                              scalar=nwp[:, c:c + 1], in1=pb[bank][:],
                                              op0=ALU.mult, op1=ALU.subtract)
            emit(p, "vector", apply_c, XQ, wait=[(NRM, t_r1)])
        t_xq = XQ.n
        hist["xq"].append(t_xq)
        def proj(t, par=par):
            for qc in range(2):
                for c in range(8):
                    t.matmul(pb[5 + qc][:], wq_bf[:, c * 256 + qc * 128: c * 256 + qc * 128 + 128],
                             xq[par][:, c, :], start=(c == 0), stop=(c == 7))
            for c in range(8):
                t.matmul(pb[7][:], wk_bf[:, c * 128:(c + 1) * 128], xq[par][:, c, :],
                         start=(c == 0), stop=(c == 7))
            for sj in range(4):
                for c in range(8):
                    last = t.matmul(pb[sj][:, 0:64],
                                    xq[par][:, c, sj * 128:(sj + 1) * 128],
                                    wv_bf[:, c * 64:(c + 1) * 64],
                                    start=(c == 0), stop=(c == 7))
            return last
        emit(p, "tensor", proj, PRJ,
             wait=[(XQ, t_xq), (QN, hist["qn"][ti]), (VEV, hist["vev"][ti]),
                   (ROW, t_rowe)])
        t_prj = PRJ.n
        hist["prj"].append(t_prj)
        def qev(v, par=par, t0=t0):
            v.tensor_mul(q01[:, t0:t0 + QT], pb[5][:], rft[par][:])
            v.tensor_mul(q23[:, t0:t0 + QT], pb[6][:], rft[par][:])
            return v.tensor_mul(kb2[:, t0:t0 + QT], pb[7][:], rft[par][:])
        emit(p, "vector", qev, QEV, wait=[(PRJ, t_prj)])
        t_qev = QEV.n
        hist["qev"].append(t_qev)
        def vev(sc, ti=ti):
            for sj in range(4):
                tj = ti * 4 + sj
                last = sc.activation(out=vones[:, tj * 65: tj * 65 + 64],
                                     in_=pb[sj][:, 0:64], func=AF.Copy)
            return last
        emit(p, "scalar", vev, VEV, wait=[(PRJ, t_prj)])
        t_vev = VEV.n
        hist["vev"].append(t_vev)
        def qsq_a(sc, t0=t0):
            return sc.activation(out=xsq[:], in_=q01[:, t0:t0 + QT], func=AF.Square)
        emit(p, "scalar", qsq_a, SQ, wait=[(QEV, t_qev)])
        emit(p, "tensor", lambda t: t.matmul(pb[5][0:1, :], hsel_sb[:, 0:1], xsq[:],
                                             start=True, stop=True), QN, wait=[(SQ, SQ.n)])
        emit(p, "tensor", lambda t: t.matmul(pb[6][0:1, :], hsel_sb[:, 1:2], xsq[:],
                                             start=True, stop=True), QN)
        t_qn01 = QN.n
        def qsq_b(sc, t0=t0):
            return sc.activation(out=xsq[:], in_=q23[:, t0:t0 + QT], func=AF.Square)
        emit(p, "scalar", qsq_b, SQ, wait=[(QN, t_qn01)])
        emit(p, "tensor", lambda t: t.matmul(pb[0][0:1, :], hsel_sb[:, 0:1], xsq[:],
                                             start=True, stop=True), QN,
             wait=[(SQ, SQ.n), (VEV, hist["vev"][ti + 1])])
        emit(p, "tensor", lambda t: t.matmul(pb[1][0:1, :], hsel_sb[:, 1:2], xsq[:],
                                             start=True, stop=True), QN)
        t_qn23 = QN.n
        def ksq_f(sc, t0=t0):
            return sc.activation(out=ksq[:], in_=kb2[0:64, t0:t0 + QT], func=AF.Square)
        emit(p, "scalar", ksq_f, SQ, wait=[(QN, t_qn23)])
        emit(p, "tensor", lambda t: t.matmul(pb[7][0:1, :], ones_cf[0:64, :], ksq[:],
                                             start=True, stop=True), QN, wait=[(SQ, SQ.n)])
        t_knm = QN.n
        def nrm_ev(v, ti=ti):
            v.tensor_reduce(out=qm[0][:, ti:ti + 1], in_=pb[5][0:1, :], axis=AX.X, op=ALU.max)
            v.tensor_reduce(out=qm[1][:, ti:ti + 1], in_=pb[6][0:1, :], axis=AX.X, op=ALU.max)
            v.tensor_reduce(out=qm[2][:, ti:ti + 1], in_=pb[0][0:1, :], axis=AX.X, op=ALU.max)
            v.tensor_reduce(out=qm[3][:, ti:ti + 1], in_=pb[1][0:1, :], axis=AX.X, op=ALU.max)
            return v.tensor_reduce(out=km[:, ti:ti + 1], in_=pb[7][0:1, :], axis=AX.X, op=ALU.max)
        emit(p, "vector", nrm_ev, QN, wait=[(QN, t_knm)])
        hist["qn"].append(QN.n)

    t_p1 = QN.n
    t_vev_last = VEV.n
    def cmax(v):
        i = v.tensor_copy(mrow[:], km[:])
        a, b = mrow, mrow2
        s = 1
        while s < NQT:
            p.chain_bar("vector", v, i)
            v.tensor_max(b[:, s:NQT], a[:, 0:NQT - s], a[:, s:NQT])
            i = v.tensor_copy(b[:, 0:s], a[:, 0:s])
            a, b = b, a
            s *= 2
        p.chain_bar("vector", v, i)
        for h in range(4):
            last = v.tensor_mul(negc_row[:, h * NQT:(h + 1) * NQT], qm[h][:], a[:])
        return last
    emit(p, "vector", cmax, ROW, wait=[(QN, t_p1)])
    t_cm = ROW.n
    emit(p, "scalar", lambda sc: sc.activation(out=negc_row[:], in_=negc_row[:], func=AF.Sqrt),
         ROWA, wait=[(ROW, t_cm)])
    t_cs = ROWA.n
    emit(p, "vector", lambda v: v.tensor_single_scalar(
        out=negc_row[:], in_=negc_row[:], scalar=-0.125, op=ALU.mult), ROW, wait=[(ROWA, t_cs)])
    t_cn = ROW.n
    emit(p, "tensor", lambda t: t.matmul(pb[2][:, 0:4 * NQT], ones_f[:], negc_row[:],
                                         start=True, stop=True), NRM,
         wait=[(ROW, t_cn), (VEV, t_vev_last)])
    t_cbm = NRM.n
    emit(p, "scalar", lambda sc: sc.activation(out=negc[:], in_=pb[2][:, 0:4 * NQT],
                                               func=AF.Copy),
         CRD, wait=[(NRM, t_cbm)])
    t_crd = CRD.n

    ndiag = [0] * 4
    nav = [0] * 4
    nexp = [0] * 4
    nqk = [0] * 4
    t_woe_prev = 0
    t_aoe_prev = 0
    od_hist = [0]
    for qi in range(NQT):
        nk = 4 * (qi + 1)
        q0 = qi * QT
        for kc in range(nk):
            k0 = kc * 128
            par = kc % 2
            diag = kc >= nk - 4
            dj = kc - (nk - 4)
            for h in range(4):
                qsrc = (q01 if h < 2 else q23)
                base = 0 if h % 2 == 0 else 64
                w = []
                if nexp[h] > 0:
                    w.append((PS[h], nexp[h]))
                if qi == 0 and kc == 0:
                    w.append((CRD, t_crd))
                if kc == 0 and qi > 0:
                    w.append((WOE, t_woe_prev))
                emit(p, "tensor", lambda t, h=h, k0=k0, q0=q0, base=base, qsrc=qsrc: t.matmul(
                    pb[h][:], kb2[base:base + 64, k0:k0 + 128],
                    qsrc[base:base + 64, q0:q0 + QT], start=True, stop=True), QK[h], wait=w)
                nqk[h] += 1
            for h in range(4):
                w = [(QK[h], nqk[h])]
                if nav[h] >= 2:
                    w.append((AV[h], nav[h] - 1))
                emit(p, "scalar", lambda sc, h=h, qi=qi, par=par: sc.activation(
                    out=pbf[h][par][:], in_=pb[h][:], func=AF.Exp,
                    scale=0.125, bias=negc[:, h * NQT + qi: h * NQT + qi + 1]),
                    PS[h], wait=w)
                nexp[h] += 1
            if diag:
                for h in range(4):
                    emit(p, "vector", lambda v, h=h, par=par, dj=dj: v.tensor_mul(
                        pbf[h][par][:], pbf[h][par][:], dmask_sb[:, dj * QT:(dj + 1) * QT]),
                        PM[h], wait=[(PS[h], nexp[h])])
                    ndiag[h] += 1
            for h in range(4):
                w = [(PM[h], ndiag[h])] if diag else [(PS[h], nexp[h])]
                if kc == 0 and qi > 0:
                    w.append((AOE, t_aoe_prev))
                emit(p, "tensor", lambda t, h=h, kc=kc, par=par, nk=nk: t.matmul(
                    pb[4 + h][0:65, :], vones[:, kc * 65: kc * 65 + 65],
                    pbf[h][par][:], start=(kc == 0), stop=(kc == nk - 1),
                    skip_group_check=True), AV[h], wait=w)
                nav[h] += 1
        def recip_f(v, navs=tuple(nav)):
            for h in range(4):
                v.wait_ge(AV[h].h, navs[h])
                last = v.reciprocal(r4b[64:65, h * QT:(h + 1) * QT], pb[4 + h][64:65, :])
            return last
        emit(p, "vector", recip_f, RC)
        t_rc = RC.n
        for h in range(4):
            w = [(RC, t_rc)]
            if qi > 0 and h == 0:
                w.append((WOE, t_woe_prev))
            emit(p, "tensor", lambda t, h=h: t.matmul(
                pb[h][0:64, :], ones64[64:65, 0:64], r4b[64:65, h * QT:(h + 1) * QT],
                start=True, stop=True), RBM, wait=w)
        t_rbm = RBM.n
        def rb_ev(sc):
            for h in range(4):
                last = sc.activation(out=rb[:, h * QT:(h + 1) * QT], in_=pb[h][0:64, :],
                                     func=AF.Copy)
            return last
        emit(p, "scalar", rb_ev, RBE, wait=[(RBM, t_rbm)])
        t_rbe = RBE.n
        def ao_ev(v):
            for h in range(4):
                last = v.tensor_mul(aoh[h][:], pb[4 + h][0:64, :], rb[:, h * QT:(h + 1) * QT])
            return last
        emit(p, "vector", ao_ev, AOE, wait=[(RBE, t_rbe)])
        t_aoe = AOE.n
        t_aoe_prev = t_aoe
        opar = 0
        for dc in range(8):
            bank = 2 + (dc % 2)
            w = [(AOE, t_aoe)]
            if WOE.n >= 1:
                w.append((WOE, WOE.n - 1))
            def wo_mm(t, dc=dc, bank=bank):
                for h in range(4):
                    last = t.matmul(pb[bank][:], wo_bf[:, h * 1024 + dc * 128: h * 1024 + dc * 128 + 128],
                                    aoh[h][:], start=(h == 0), stop=(h == 3),
                                    skip_group_check=True)
                return last
            emit(p, "tensor", wo_mm, WOM, wait=w)
            t_wom = WOM.n
            w2 = [(WOM, t_wom)]
            if dc == 0 and qi >= 1:
                w2.append((OD, od_hist[qi]))
            emit(p, "scalar", lambda sc, dc=dc, bank=bank, opar=opar: sc.activation(
                out=outf[opar][:, dc, :], in_=pb[bank][:], func=AF.Copy), WOE, wait=w2)
        t_woe = WOE.n
        t_woe_prev = t_woe
        t_od = emit_dma(p, "gpsimd", y[:, q0:q0 + QT].rearrange("(c p) t -> p c t", p=128),
                        outf[opar][:], OD, wait=[(WOE, t_woe)])
        od_hist.append(t_od)
    emit(p, "gpsimd", lambda g: g.nop().then_inc(p.sem("fin").h, 1), None, wait=[(OD, OD.n)])
    return p.build()



def build_l0_moe(NT=NTOK, add_ln=True):
    p = Prog("moe")
    nc = p.nc
    TOK = NT // 8
    KSEL = NT // 8
    NTI = TOK // 512
    NGR = ceil_div(KSEL, 512)
    NRD = KSEL // 128
    FJ = NT // 16
    NSEG = NT // 128
    CAP = 64
    assert KSEL % 128 == 0 and TOK % 512 == 0

    x_fm = nc.dram_tensor("x_fm", [D, TOK], F32, kind="ExternalInput").ap()
    parts = [nc.dram_tensor(f"part{i}", [D, TOK], F32, kind="ExternalInput").ap()
             for i in range(4)] if add_ln else []
    gate_wT = nc.dram_tensor("gate_wT", [D, 8], F32, kind="ExternalInput").ap()
    eselT = nc.dram_tensor("eselT", [8, 128], F32, kind="ExternalInput").ap()
    esel = nc.dram_tensor("esel", [128, 8], F32, kind="ExternalInput").ap()
    trilblk = nc.dram_tensor("trilblk", [128, 128], F32, kind="ExternalInput").ap()
    tril128 = nc.dram_tensor("tril128", [128, 128], F32, kind="ExternalInput").ap()
    identF = nc.dram_tensor("identF", [128, 128], F32, kind="ExternalInput").ap()
    identB = nc.dram_tensor("identB", [128, 128], BF16, kind="ExternalInput").ap()
    revj = nc.dram_tensor("revj", [128, 128], F32, kind="ExternalInput").ap()
    gamcol = nc.dram_tensor("gamcol", [128, 8], F32, kind="ExternalInput").ap()
    betcol = nc.dram_tensor("betcol", [128, 8], F32, kind="ExternalInput").ap()
    b1col = nc.dram_tensor("b1col", [128, 32], F32, kind="ExternalInput").ap()
    b2col = nc.dram_tensor("b2col", [128, 8], F32, kind="ExternalInput").ap()
    fc1T = nc.dram_tensor("fc1T", [D, 4096], F32, kind="ExternalInput").ap()
    fc2T = nc.dram_tensor("fc2T", [4096, D], F32, kind="ExternalInput").ap()
    y = nc.dram_tensor("y", [D, TOK], F32, kind="ExternalOutput").ap()
    wqb = nc.dram_tensor("wqb", [D, D], BF16, kind="Internal")
    wkb = nc.dram_tensor("wkb", [D, D], BF16, kind="Internal")
    wvb = nc.dram_tensor("wvb", [D, D], BF16, kind="Internal")

    x1sc = nc.dram_tensor("x1sc", [D, TOK], F32, kind="Internal")
    xfm_my = nc.dram_tensor("xfm_my", [TOK, D], BF16, kind="Internal")
    xfm_all = nc.dram_tensor("xfm_all", [NT, D], BF16, kind="Internal")
    sc_my = nc.dram_tensor("sc_my", [8, TOK], F32, kind="Internal")
    sc_all = nc.dram_tensor("sc_all", [64, TOK], F32, kind="Internal")
    sc_em = nc.dram_tensor("sc_em", [8, NT], F32, kind="Internal")
    sce_d = nc.dram_tensor("sce_d", [1, NT], F32, kind="Internal")
    idpad = nc.dram_tensor("idpad", [1, 128 * CAP], F32, kind="Internal")
    oe = nc.dram_tensor("oe", [KSEL + 1, D], BF16, kind="Internal")
    oall = nc.dram_tensor("oall", [8 * (KSEL + 1), D], BF16, kind="Internal")
    w1bf = nc.dram_tensor("w1bf", [D, 4096], BF16, kind="Internal")
    w2bf = nc.dram_tensor("w2bf", [4096, D], BF16, kind="Internal")

    xt = p.sb("xt", [128, 8, 512], F32)
    x1t = p.sb("x1t", [128, 8, 512], F32)
    tmp32 = p.sb("tmp32", [128, 2, 512], F32)
    xf32 = p.sb("xf32", [128, 2, 512], F32)
    xfb = p.sb("xfb", [128, 8, 512], BF16)
    xtm = p.sb("xtm", [128, 8, 128], BF16)
    rows = p.sb("rows", [1, 8 * 512], F32)
    consts = p.sb("consts", [128, 4], F32)
    ones_cf = p.sb("ones_cf", [128, 1], F32)
    ones_f = p.sb("ones_f", [1, 128], F32)
    ones_bf = p.sb("ones_bf", [128, 1], BF16)
    gwt = p.sb("gwt", [128, 8, 8], F32)
    scq = p.sb("scq", [8, 512], F32)
    sc4 = p.sb("sc4", [128, FJ], F32)
    eselT_s = p.sb("eselT_s", [8, 128], F32)
    esel_s = p.sb("esel_s", [128, 8], F32)
    trilb_s = p.sb("trilb_s", [128, 128], F32)
    tril_s = p.sb("tril_s", [128, 128], F32)
    identF_s = p.sb("identF_s", [128, 128], F32)
    identB_s = p.sb("identB_s", [128, 128], BF16)
    lo = p.sb("lo", [8, 1], F32)
    hi = p.sb("hi", [8, 1], F32)
    mid = p.sb("mid", [8, 1], F32)
    cnt8 = p.sb("cnt8", [8, 1], F32)
    msk8 = p.sb("msk8", [8, 1], U8)
    thr = p.sb("thr", [128, 1], F32)
    cnt128 = p.sb("cnt128", [128, 1], F32)
    mtile = p.sb("mtile", [128, FJ], F32)
    cums = p.sb("cums", [128, FJ], F32)
    gidx = p.sb("gidx", [128, FJ], F32)
    zfj = p.sb("zfj", [128, FJ], F32)
    lorow = p.sb("lorow", [1, 8], F32)
    offc = p.sb("offc", [128, 1], F32)
    gT = p.sb("gT", [128, 16 * 8], U32)
    sce = p.sb("sce", [128, 128], F32)
    mske = p.sb("mske", [128, 128], F32)
    vals = p.sb("vals", [128, 128], F32)
    mx = p.sb("mx", [128, CAP], F32)
    ids48 = p.sb("ids48", [128, CAP], F32)
    idcol = p.sb("idcol", [128, 1], I32)
    idcolf = p.sb("idcolf", [128, 1], F32)
    cnte = p.sb("cnte", [128, 1], F32)
    orow = p.sb("orow", [1, 128], F32)
    ob = p.sb("ob", [128, 128], F32)
    gcol = p.sb("gcol", [128, NRD], I32)
    gcolf = p.sb("gcolf", [128, NRD], F32)
    sig = p.sb("sig", [128, 1], F32)
    osg = p.sb("osg", [128, 1], F32)
    lam = p.sb("lam", [128, 1], F32)
    flt = p.sb("flt", [128, 1], F32)
    fltu = p.sb("fltu", [128, 1], U32)
    tid = p.sb("tid", [128, NRD], F32)
    tidu = p.sb("tidu", [128, 1], U32)
    kwv = p.sb("kwv", [128, NRD], F32)
    cmp128 = p.sb("cmp128", [128, 128], F32)
    xg = p.sb("xg", [128, 1024], BF16)
    xh = p.sb("xh", [128, 8, 512], BF16)
    pass
    w2s = [p.sb(f"w2s{i}", [128, 32, 128], BF16) for i in range(2)]
    hbf = p.sb("hbf", [128, 32, 512], BF16)
    ofm = p.sb("ofm", [128, 8, 512], BF16)
    x2c = p.sb("x2c", [128, 8, 128], F32)
    zrow = p.sb("zrow", [1, D], BF16)
    pb = [p.ps(f"pb{i}", [128, 512], F32) for i in range(6)]
    pbb = [p.ps(f"pbb{i}", [128, 128], BF16) for i in range(2)]
    ALIAS_PLACEHOLDER = 1

    LD = p.sem("ld"); LDC = p.sem("ldc"); LW = p.sem("lw"); PACC = p.sem("pacc")
    LWA = [p.sem("lwa0"), p.sem("lwa1")]; LWB = [p.sem("lwb0"), p.sem("lwb1")]
    OE = p.sem("oe_dma")
    CAST = p.sem("cast"); STM = p.sem("stm"); ROW = p.sem("row"); ROWA = p.sem("rowa")
    NRM = p.sem("nrm"); XQ = p.sem("xq"); TRP = p.sem("trp"); TRE = p.sem("tre")
    SCM = p.sem("scm"); SCE = p.sem("sce_s"); OD = p.sem("od"); CC = p.sem("cc")
    BIS = p.sem("bis"); BPE = p.sem("bpe"); BSE = p.sem("bse")
    CMP = p.sem("cmp"); CPE = p.sem("cpe"); CSE = p.sem("cse")
    GD = p.sem("gd"); FFM = p.sem("ffm"); FFE = p.sem("ffe"); GEL = p.sem("gel")
    AGD = p.sem("agd"); CMB = p.sem("cmb"); TR2 = p.sem("tr2")

    r_mu = rows[:, 0:512]; r_var = rows[:, 512:1024]; r_rstd = rows[:, 1024:1536]
    r_t = rows[:, 1536:2048]; r_ms = rows[:, 2048:2560]; r_r2 = rows[:, 2560:3072]

    xsq = p.sb("xsq", [128, 512], F32)
    xsqb = p.sb("xsqb", [128, 512], BF16)
    iot = p.sb("iot", [128, 1], I32)
    iotf = p.sb("iotf", [128, 1], F32)
    gath8 = [p.sb(f"gath8_{e}", [128, 1024], BF16) for e in range(8)]
    xfmt = p.sb("xfmt", [128, 1024], BF16)
    acct = p.sb("acct", [128, 1024], F32)
    w1t = [p.sb(f"w1t{i}", [128, 8, 128], BF16) for i in range(2)]
    vals2 = p.sb("vals2", [128, 128], F32)
    t8 = p.sb("t8", [1, 8], F32)
    myth = p.sb("myth", [128, 1], F32)

    gidxu = p.sb("gidxu", [128, 1], U32)


    gamrow = nc.dram_tensor("gamrow", [1, D], F32, kind="ExternalInput").ap()
    ehrow = nc.dram_tensor("ehrow", [1, 8], F32, kind="ExternalInput").ap()
    ecol = nc.dram_tensor("ecol", [128, 1], F32, kind="ExternalInput").ap()
    geoff = nc.dram_tensor("geoff", [128, 1], F32, kind="ExternalInput").ap()
    gam_s = p.sb("gam_s", [1, D], F32)
    gamcol_s = p.sb("gamcol_s", [128, 8], F32)
    betcol_s = p.sb("betcol_s", [128, 8], F32)
    b1col_s = p.sb("b1col_s", [128, 32], F32)
    b2col_s = p.sb("b2col_s", [128, 8], F32)
    eh_s = p.sb("eh_s", [1, 8], F32)
    ecol_s = p.sb("ecol_s", [128, 1], F32)
    geoff_s = p.sb("geoff_s", [128, 1], F32)
    kwrow = p.sb("kwrow", [1, 512], F32)
    tmpc = p.sb("tmpc", [128, 1], F32)
    sceg = hbf[:].rearrange("p a b -> p (a b)").bitcast(F32)[0:8, 0:TOK]
    gmy8 = ofm[:].rearrange("p a b -> p (a b)").bitcast(F32)[0:8, 0:TOK]

    emit_dma(p, "sync", gwt[:], gate_wT.rearrange("(c p) e -> p c e", p=128), LDC)
    emit_dma(p, "sync", eselT_s[:], eselT, LDC)
    emit_dma(p, "sync", esel_s[:], esel, LDC)
    emit_dma(p, "sync", trilb_s[:], trilblk, LDC)
    emit_dma(p, "sync", tril_s[:], tril128, LDC)
    emit_dma(p, "sync", identF_s[:], identF, LDC)
    emit_dma(p, "sync", identB_s[:], identB, LDC)
    emit_dma(p, "sync", vals2[:], revj, LDC)
    emit_dma(p, "sync", gam_s[:], gamrow, LDC)
    emit_dma(p, "sync", gamcol_s[:], gamcol, LDC)
    emit_dma(p, "sync", betcol_s[:], betcol, LDC)
    emit_dma(p, "sync", b1col_s[:], b1col, LDC)
    emit_dma(p, "sync", b2col_s[:], b2col, LDC)
    emit_dma(p, "sync", eh_s[:], ehrow, LDC)
    emit_dma(p, "sync", ecol_s[:], ecol, LDC)
    emit_dma(p, "sync", geoff_s[:], geoff, LDC)
    t_ldc = LDC.n
    def init_c(v):
        v.memset(consts[:, 0:1], 1e-5)
        v.memset(consts[:, 1:2], 1.0)
        v.memset(ones_cf[:], 1.0)
        v.memset(ones_f[:], 1.0)
        v.memset(ones_bf[:], 1.0)
        v.memset(zfj[:], 0.0)
        return v.memset(zrow[:], 0.0)
    emit(p, "vector", init_c, CAST, wait=[(LDC, t_ldc)])
    t_init = CAST.n
    GP = p.sem("gp")
    emit(p, "gpsimd", lambda g: g.iota(out=iot[:], pattern=[[1, 1]], base=0,
                                       channel_multiplier=1), GP)
    emit(p, "vector", lambda v: v.tensor_copy(iotf[:], iot[:]), CAST, wait=[(GP, GP.n)])
    t_iotf = CAST.n
    LWG = p.sem("lwg")
    t_w1c = emit_dma(p, "gpsimd", w1bf.ap(), fc1T, LWG)
    t_w2c = emit_dma(p, "gpsimd", w2bf.ap(), fc2T, LWG)
    t_oz = emit_dma(p, "gpsimd", oe.ap()[KSEL:KSEL + 1, :], zrow[:], LWG,
                    wait=[(CAST, t_init)])
    t_w1c = t_w2c = t_oz = LWG.n

    LX = p.sem("lx")
    hist = {"xq": [0], "scm": [0], "trp": [0], "tre": [0], "stm": [0], "row": [0],
            "od": [0]}
    for ti in range(NTI):
        t0 = ti * 512
        w = [(CAST, t_init)] if ti == 0 else [(XQ, hist["xq"][ti]), (TRE, hist["tre"][ti])]
        t_x = emit_dma(p, "sync", xt[:],
                       x_fm[:, t0:t0 + 512].rearrange("(c p) t -> p c t", p=128), LX, wait=w)
        if not add_ln:
            def xcopy(v, par=0):
                for c in range(8):
                    i = v.tensor_copy(x1t[:, c, :], xt[:, c, :])
                return i
            emit(p, "vector", xcopy, XQ, wait=[(LX, t_x), (STM, STM.n), (ROW, ROW.n)])
            t_x1 = XQ.n
            t_accd = PACC.n
        if add_ln:
            def ssum(t):
                for c in range(8):
                    last = t.matmul(pb[0][0:1, :], ones_cf[:], xt[:, c, :],
                                    start=(c == 0), stop=(c == 7))
                return last
            emit(p, "tensor", ssum, STM, wait=[(LX, t_x), (ROW, hist["row"][ti])])
            for c in range(8):
                w2 = [(LX, t_x)] if c == 0 else [(STM, STM.n)]
                if c == 0 and ti > 0:
                    w2.append((STM, STM.n))
                emit(p, "scalar", lambda sc, c=c: sc.activation(out=xsq[:], in_=xt[:, c, :],
                                                                func=AF.Square), SCE, wait=w2)
                emit(p, "tensor", lambda t, c=c: t.matmul(pb[1][0:1, :], ones_cf[:], xsq[:],
                                                          start=(c == 0), stop=(c == 7)),
                     STM, wait=[(SCE, SCE.n)])
            t_stm = STM.n
            def row_a(v):
                v.tensor_single_scalar(out=r_mu, in_=pb[0][0:1, :], scalar=1.0 / D, op=ALU.mult)
                i = v.tensor_single_scalar(out=r_var, in_=pb[1][0:1, :], scalar=1.0 / D, op=ALU.mult)
                p.chain_bar("vector", v, i)
                i = v.tensor_mul(r_t, r_mu, r_mu)
                p.chain_bar("vector", v, i)
                return v.tensor_sub(r_var, r_var, r_t)
            emit(p, "vector", row_a, ROW, wait=[(STM, t_stm)])
            emit(p, "scalar", lambda sc: sc.activation(out=r_t, in_=r_var, func=AF.Sqrt,
                                                       bias=consts[0:1, 0:1]),
                 ROWA, wait=[(ROW, ROW.n)])
            def row_c(v):
                i = v.reciprocal(r_rstd, r_t)
                p.chain_bar("vector", v, i)
                i = v.tensor_mul(r_ms, r_mu, r_rstd)
                p.chain_bar("vector", v, i)
                return v.tensor_single_scalar(out=r_r2, in_=r_rstd, scalar=1.0, op=ALU.add)
            emit(p, "vector", row_c, ROW, wait=[(ROWA, ROWA.n)])
            t_row = ROW.n
            emit(p, "tensor", lambda t: t.matmul(pb[2][:], ones_f[:], r_r2, start=True, stop=True),
                 NRM, wait=[(ROW, t_row)])
            emit(p, "tensor", lambda t: t.matmul(pb[3][:], ones_f[:], r_ms, start=True, stop=True),
                 NRM)
            t_nrm = NRM.n
            def x1f(v):
                for c in range(8):
                    i = v.tensor_mul(x1t[:, c, :], xt[:, c, :], pb[2][:])
                    p.chain_bar("vector", v, i)
                    i = v.tensor_sub(x1t[:, c, :], x1t[:, c, :], pb[3][:])
                    if c < 7:
                        p.chain_bar("vector", v, i)
                return i
            emit(p, "vector", x1f, XQ, wait=[(NRM, t_nrm)])
            t_x1 = XQ.n
            for k in range(4):
                wk = [(XQ, t_x1)] if k == 0 else [(PACC, PACC.n)]
                def acc_f(g, k=k, t0=t0, wk=tuple(wk)):
                    for (s, v_) in wk:
                        g.wait_ge(s.h, v_)
                    g.dma_start(out=x1t[:],
                                in_=parts[k][:, t0:t0 + 512].rearrange("(c p) t -> p c t", p=128),
                                accum_op=ALU.add).then_inc(PACC.h, 16)
                PACC.inc(16)
                p.add("gpsimd", acc_f)
            t_accd = PACC.n

        t_x1s = emit_dma(p, "sync", x1sc.ap()[:, t0:t0 + 512].rearrange("(c p) t -> p c t", p=128),
                         x1t[:], LX, wait=[(PACC, t_accd), (XQ, t_x1)])
        def ssum1(t):
            for c in range(8):
                last = t.matmul(pb[0][0:1, :], ones_cf[:], x1t[:, c, :],
                                start=(c == 0), stop=(c == 7))
            return last
        emit(p, "tensor", ssum1, STM, wait=[(PACC, t_accd), (XQ, t_x1)])
        for c in range(8):
            w2 = [(PACC, t_accd), (XQ, t_x1)] if c == 0 else [(STM, STM.n)]
            emit(p, "scalar", lambda sc, c=c: sc.activation(out=xsq[:], in_=x1t[:, c, :],
                                                            func=AF.Square), SCE, wait=w2)
            emit(p, "tensor", lambda t, c=c: t.matmul(pb[1][0:1, :], ones_cf[:], xsq[:],
                                                      start=(c == 0), stop=(c == 7)),
                 STM, wait=[(SCE, SCE.n)])
        t_stm1 = STM.n
        hist["stm"].append(t_stm1)
        def row_a1(v):
            v.tensor_single_scalar(out=r_mu, in_=pb[0][0:1, :], scalar=1.0 / D, op=ALU.mult)
            i = v.tensor_single_scalar(out=r_var, in_=pb[1][0:1, :], scalar=1.0 / D, op=ALU.mult)
            p.chain_bar("vector", v, i)
            i = v.tensor_mul(r_t, r_mu, r_mu)
            p.chain_bar("vector", v, i)
            return v.tensor_sub(r_var, r_var, r_t)
        emit(p, "vector", row_a1, ROW, wait=[(STM, t_stm1)])
        emit(p, "scalar", lambda sc: sc.activation(out=r_t, in_=r_var, func=AF.Sqrt,
                                                   bias=consts[0:1, 0:1]),
             ROWA, wait=[(ROW, ROW.n)])
        def row_c1(v):
            i = v.reciprocal(r_rstd, r_t)
            p.chain_bar("vector", v, i)
            return v.tensor_mul(r_ms, r_mu, r_rstd)
        emit(p, "vector", row_c1, ROW, wait=[(ROWA, ROWA.n)])
        t_row1 = ROW.n
        hist["row"].append(t_row1)
        emit(p, "tensor", lambda t: t.matmul(pb[2][:], ones_f[:], r_rstd, start=True, stop=True),
             NRM, wait=[(ROW, t_row1), (XQ, XQ.n)])
        emit(p, "tensor", lambda t: t.matmul(pb[3][:], ones_f[:], r_ms, start=True, stop=True),
             NRM)
        t_nrm1 = NRM.n
        for c in range(8):
            cp = c % 2
            w3 = [(NRM, t_nrm1)] if c < 2 else [(SCM, SCM.n)]
            def xff(v, c=c, cp=cp):
                i = v.tensor_mul(xf32[:, cp, :], x1t[:, c, :], pb[2][:])
                p.chain_bar("vector", v, i)
                i = v.tensor_sub(xf32[:, cp, :], xf32[:, cp, :], pb[3][:])
                p.chain_bar("vector", v, i)
                return v.tensor_copy(xfb[:, c, :], xf32[:, cp, :])
            emit(p, "vector", xff, XQ, wait=w3)
            t_xf = XQ.n
            emit(p, "tensor", lambda t, c=c, cp=cp: t.matmul(
                pb[4][0:8, :], gwt[:, c, :], xf32[:, cp, :],
                start=(c == 0), stop=(c == 7)), SCM, wait=[(XQ, t_xf)])
        hist["xq"].append(XQ.n)
        t_scm = SCM.n
        emit(p, "scalar", lambda sc: sc.activation(out=scq[:], in_=pb[4][0:8, :],
                                                   func=AF.Tanh), SCE, wait=[(SCM, t_scm)])
        emit(p, "vector", lambda v: v.tensor_scalar(out=scq[:], in0=scq[:], scalar1=1.0,
                                                    scalar2=0.5, op0=ALU.add, op1=ALU.mult),
             CAST, wait=[(SCE, SCE.n)])
        t_sca = CAST.n
        emit_dma(p, "sync", sc_my.ap()[:, t0:t0 + 512], scq[:], LX, wait=[(CAST, t_sca)])
        for sj in range(4):
            for c in range(8):
                bb = (sj * 8 + c) % 2
                w4 = [(XQ, XQ.n)] if (sj == 0 and c == 0) else [(TRE, TRE.n)]
                emit(p, "tensor", lambda t, sj=sj, c=c, bb=bb: t.transpose(
                    pbb[bb][:, :], xfb[:, c, sj * 128:(sj + 1) * 128], identB_s[:]),
                    TRP, wait=w4)
                w4b = [(TRP, TRP.n)]
                if c == 0 and LX.n >= 16:
                    w4b.append((LX, LX.n))
                emit(p, "scalar", lambda sc, sj=sj, c=c, bb=bb: sc.activation(
                    out=xtm[:, c, :], in_=pbb[bb][:, :], func=AF.Copy),
                    TRE, wait=w4b)
            t_tre = TRE.n
            emit_dma(p, "sync", xfm_my.ap()[t0 + sj * 128: t0 + (sj + 1) * 128, :]
                     .rearrange("t (c q) -> t c q", c=8),
                     xtm[:], LX, wait=[(TRE, t_tre)])
        hist["tre"].append(TRE.n)
        hist["trp"].append(TRP.n)
    t_pa_done = LX.n

    emit(p, "gpsimd", lambda g: g.collective_compute(
        "AllGather", ALU.bypass, replica_groups=[list(range(8))],
        ins=[sc_my.ap().opt()], outs=[sc_all.ap().opt()]), CC, wait=[(LX, t_pa_done)])
    emit(p, "gpsimd", lambda g: g.collective_compute(
        "AllGather", ALU.bypass, replica_groups=[list(range(8))],
        ins=[xfm_my.ap().opt()], outs=[xfm_all.ap().opt()]), CC)
    t_cc1 = CC.n

    t_em = emit_dma(p, "sync", sc_em.ap().rearrange("e (r c) -> e r c", c=TOK),
                    sc_all.ap().rearrange("(r e) c -> e r c", e=8), LX,
                    wait=[(CC, t_cc1)])
    t_sc4 = emit_dma(p, "sync", sc4[:],
                     sc_em.ap().rearrange("e (tg j) -> (e tg) j", j=FJ), LX,
                     wait=[(LX, t_em)])
    def bis_init(v):
        v.memset(lo[:], -0.1)
        v.memset(hi[:], 1.1)
        i = v.memset(mid[:], 0.5)
        return i
    emit(p, "vector", bis_init, BIS, wait=[(LX, t_sc4), (CAST, t_iotf)])
    for it in range(50):
        emit(p, "tensor", lambda t: t.matmul(pb[5][:, 0:1], eselT_s[:], mid[:],
                                             start=True, stop=True), BPE,
             wait=[(BIS, BIS.n)])
        emit(p, "scalar", lambda sc: sc.activation(out=thr[:], in_=pb[5][:, 0:1],
                                                   func=AF.Copy), BSE, wait=[(BPE, BPE.n)])
        def bis_cnt(v):
            return v.tensor_scalar(out=mtile[:], in0=sc4[:], scalar1=thr[:, 0:1],
                                   scalar2=None, op0=ALU.is_ge, op1=ALU.add,
                                   accum_out=cnt128[:, 0:1])
        emit(p, "vector", bis_cnt, BIS, wait=[(BSE, BSE.n)])
        emit(p, "tensor", lambda t: t.matmul(pb[5][0:8, 0:1], esel_s[:], cnt128[:, 0:1],
                                             start=True, stop=True), BPE, wait=[(BIS, BIS.n)])
        def bis_upd(v):
            i = v.tensor_copy(cnt8[:], pb[5][0:8, 0:1])
            p.chain_bar("vector", v, i)
            i = v.tensor_single_scalar(out=msk8[:], in_=cnt8[:], scalar=float(KSEL), op=ALU.is_ge)
            p.chain_bar("vector", v, i)
            i = v.copy_predicated(lo[:], msk8[:], mid[:])
            p.chain_bar("vector", v, i)
            i = v.tensor_single_scalar(out=msk8[:], in_=cnt8[:], scalar=float(KSEL), op=ALU.is_lt)
            p.chain_bar("vector", v, i)
            i = v.copy_predicated(hi[:], msk8[:], mid[:])
            p.chain_bar("vector", v, i)
            i = v.tensor_add(mid[:], lo[:], hi[:])
            p.chain_bar("vector", v, i)
            return v.tensor_single_scalar(out=mid[:], in_=mid[:], scalar=0.5, op=ALU.mult)
        emit(p, "vector", bis_upd, BIS, wait=[(BPE, BPE.n)])
    emit(p, "tensor", lambda t: t.matmul(pb[5][:, 0:1], eselT_s[:], lo[:],
                                         start=True, stop=True), BPE, wait=[(BIS, BIS.n)])
    emit(p, "scalar", lambda sc: sc.activation(out=thr[:], in_=pb[5][:, 0:1], func=AF.Copy),
         BSE, wait=[(BPE, BPE.n)])
    emit(p, "tensor", lambda t: t.transpose(pb[5][0:1, 0:8], lo[:], identF_s[0:8, 0:8]),
         BPE, wait=[(BSE, BSE.n)])
    emit(p, "scalar", lambda sc: sc.activation(out=lorow[:], in_=pb[5][0:1, 0:8],
                                               func=AF.Copy), BSE, wait=[(BPE, BPE.n)])
    t_bse = BSE.n

    def rk1(v):
        i = v.tensor_scalar(out=mtile[:], in0=sc4[:], scalar1=thr[:, 0:1], scalar2=None,
                            op0=ALU.is_ge, op1=ALU.add, accum_out=cnt128[:, 0:1])
        p.chain_bar("vector", v, i)
        return v.tensor_tensor_scan(out=cums[:], data0=mtile[:], data1=zfj[:],
                                    initial=0.0, op0=ALU.add, op1=ALU.add)
    emit(p, "vector", rk1, CMP, wait=[(BSE, t_bse)])
    emit(p, "tensor", lambda t: t.matmul(pb[5][:, 0:1], trilb_s[:], cnt128[:, 0:1],
                                         start=True, stop=True), CPE, wait=[(CMP, CMP.n)])
    emit(p, "scalar", lambda sc: sc.activation(out=offc[:], in_=pb[5][:, 0:1], func=AF.Copy),
         CSE, wait=[(CPE, CPE.n)])
    def rk2(v):
        i = v.scalar_tensor_tensor(out=cums[:], in0=cums[:], scalar=offc[:, 0:1],
                                   in1=mtile[:], op0=ALU.add, op1=ALU.subtract)
        p.chain_bar("vector", v, i)
        i = v.tensor_single_scalar(out=zfj[:], in_=cums[:], scalar=float(KSEL), op=ALU.is_lt)
        p.chain_bar("vector", v, i)
        i = v.tensor_mul(mtile[:], mtile[:], zfj[:])
        p.chain_bar("vector", v, i)
        i = v.memset(zfj[:], 0.0)
        p.chain_bar("vector", v, i)
        i = v.tensor_single_scalar(out=cums[:], in_=cums[:], scalar=float(KSEL),
                                   op=ALU.subtract)
        p.chain_bar("vector", v, i)
        i = v.tensor_mul(cums[:], cums[:], mtile[:])
        p.chain_bar("vector", v, i)
        return v.tensor_scalar(out=gidx[:], in0=cums[:], scalar1=geoff_s[:, 0:1],
                               scalar2=None, op0=ALU.add)
    emit(p, "vector", rk2, CMP, wait=[(CSE, CSE.n)])
    t_rk = CMP.n
    gidx_d = nc.dram_tensor("gidx_d", [128, FJ], F32, kind="Internal")

    rid = p.sb("rid", [128, 1], U32)
    t_gx = emit_dma(p, "sync", gidx_d.ap(), gidx[:], LX, wait=[(CMP, t_rk)])
    def rid_f(v):
        i = v.tensor_single_scalar(out=tmpc[:], in_=iotf[:], scalar=8.0, op=ALU.mult)
        p.chain_bar("vector", v, i)
        i = v.tensor_add(tmpc[:], tmpc[:], ecol_s[:])
        p.chain_bar("vector", v, i)
        return v.tensor_copy(rid[:], tmpc[:])
    emit(p, "vector", rid_f, CMP, wait=[(CAST, t_iotf), (LDC, t_ldc)])
    t_rid = CMP.n
    def gmy_g(g):
        g.wait_ge(CMP.h, t_rid)
        g.wait_ge(LX.h, t_gx)
        g.indirect_dma_start(
            out=gmy8, out_offset=None,
            in_=gidx_d.ap().rearrange("(a b) j -> a (b j)", b=2),
            in_offset=IndirectOffsetOnAxis(ap=rid[0:8, 0:1], axis=0)).then_inc(GD.h, 16)
    GD.inc(16)
    p.add("gpsimd", gmy_g)
    t_gmy = GD.n
    for tau in range(TOK // 128):
        bb = tau % 2
        w5 = [(GD, t_gmy)] if tau < 2 else [(CMP, CMP.n - 1)]
        emit(p, "tensor", lambda t, tau=tau, bb=bb: t.transpose(
            pb[4 + bb][:, 0:8], gmy8[0:8, tau * 128:(tau + 1) * 128],
            identF_s[0:8, 0:8]), TRP, wait=w5)
        t_tr = TRP.n
        emit(p, "vector", lambda v, tau=tau, bb=bb: v.tensor_copy(
            gT[:, tau * 8: tau * 8 + 8], pb[4 + bb][:, 0:8]), CMP,
            wait=[(TRP, t_tr)])
    t_gT = CMP.n

    def sceg_g(g):
        g.wait_ge(CMP.h, t_gT)
        g.indirect_dma_start(
            out=sceg, out_offset=None, in_=sc_all.ap(),
            in_offset=IndirectOffsetOnAxis(ap=rid[0:8, 0:1], axis=0)).then_inc(GD.h, 16)
    GD.inc(16)
    p.add("gpsimd", sceg_g)
    t_sceg = GD.n
    t_sb1 = emit_dma(p, "sync", sce_d.ap().rearrange("o (a t) -> (o a) t", a=8),
                     sceg, LX, wait=[(GD, t_sceg)])
    NSEG_ = NT // 128
    t_sb2 = emit_dma(p, "sync", sce[0:NSEG_, 0:128],
                     sce_d.ap().rearrange("o (s j) -> (o s) j", j=128), LX,
                     wait=[(LX, t_sb1)])
    def myth_f(v):
        i = v.tensor_mul(t8[:], lorow[:], eh_s[:])
        p.chain_bar("vector", v, i)
        return v.tensor_reduce(out=t8[:, 0:1], in_=t8[:], axis=AX.X, op=ALU.add)
    emit(p, "vector", myth_f, CMP, wait=[(BSE, t_bse)])
    emit(p, "tensor", lambda t: t.matmul(pb[5][:, 0:1], ones_f[:], t8[:, 0:1],
                                         start=True, stop=True), CPE, wait=[(CMP, CMP.n)])
    emit(p, "scalar", lambda sc: sc.activation(out=myth[:], in_=pb[5][:, 0:1], func=AF.Copy),
         CSE, wait=[(CPE, CPE.n)])
    t_myth = CSE.n
    def cmp1(v):
        i = v.tensor_single_scalar(out=mske[0:NSEG_, :], in_=sce[0:NSEG_, 0:128],
                                   scalar=myth[0:NSEG_, 0:1], op=ALU.is_ge)
        p.chain_bar("vector", v, i)
        i = v.tensor_mul(vals[0:NSEG_, 0:128], mske[0:NSEG_, 0:128], vals2[0:NSEG_, :])
        p.chain_bar("vector", v, i)
        return v.tensor_reduce(out=cnte[0:NSEG_, :], in_=mske[0:NSEG_, 0:128],
                               axis=AX.X, op=ALU.add)
    emit(p, "vector", cmp1, CMP, wait=[(CSE, t_myth), (LX, t_sb2)])
    ping = [vals, mtile]
    for r in range(CAP // 8):
        a = ping[r % 2]; b = ping[(r + 1) % 2]
        def mx_f(v, r=r, a=a, b=b):
            i = v.max(out=mx[0:NSEG_, r * 8:(r + 1) * 8], in_=a[0:NSEG_, 0:128])
            p.chain_bar("vector", v, i)
            i = v.match_replace(out=b[0:NSEG_, 0:128],
                                in_to_replace=mx[0:NSEG_, r * 8:(r + 1) * 8],
                                in_values=a[0:NSEG_, 0:128], imm_value=0.0)
            return i
        emit(p, "vector", mx_f, CMP, wait=[(CMP, CMP.n)])
    emit(p, "gpsimd", lambda g: g.iota(out=idcol[:], pattern=[[1, 1]], base=128,
                                       channel_multiplier=128), GP)
    t_idc = GP.n
    def ids_f(v):
        i = v.tensor_copy(idcolf[:], idcol[:])
        p.chain_bar("vector", v, i)
        i = v.tensor_scalar(out=ids48[0:NSEG_, :], in0=mx[0:NSEG_, :], scalar1=-1.0,
                            scalar2=idcolf[0:NSEG_, 0:1], op0=ALU.mult, op1=ALU.add)
        return i
    emit(p, "vector", ids_f, CMP, wait=[(GP, t_idc), (CMP, CMP.n)])
    t_ids = CMP.n
    emit(p, "tensor", lambda t: t.matmul(pb[5][0:NSEG_, 0:1], tril_s[0:NSEG_, 0:NSEG_],
                                         cnte[0:NSEG_, 0:1], start=True, stop=True),
         CPE, wait=[(CMP, t_ids)])
    emit(p, "scalar", lambda sc: sc.activation(out=offc[0:NSEG_, :],
                                               in_=pb[5][0:NSEG_, 0:1], func=AF.Copy),
         CSE, wait=[(CPE, CPE.n)])
    t_off = CSE.n
    t_idp = emit_dma(p, "sync",
                     idpad.ap().rearrange("o (s c) -> (o s) c", c=CAP)[0:NSEG_, :],
                     ids48[0:NSEG_, :], LX, wait=[(CMP, t_ids)])
    emit(p, "tensor", lambda t: t.transpose(pb[5][0:1, 0:NSEG_], offc[0:NSEG_, :],
                                            identF_s[0:NSEG_, 0:NSEG_]), CPE,
         wait=[(CSE, t_off)])
    emit(p, "scalar", lambda sc: sc.activation(out=orow[:, 0:NSEG_],
                                               in_=pb[5][0:1, 0:NSEG_], func=AF.Copy),
         CSE, wait=[(CPE, CPE.n)])
    emit(p, "tensor", lambda t: t.matmul(pb[5][:, 0:NSEG_], ones_f[:], orow[:, 0:NSEG_],
                                         start=True, stop=True), CPE, wait=[(CSE, CSE.n)])
    emit(p, "scalar", lambda sc: sc.activation(out=ob[:, 0:NSEG_], in_=pb[5][:, 0:NSEG_],
                                               func=AF.Copy), CSE, wait=[(CPE, CPE.n)])
    t_ob = CSE.n
    emit(p, "gpsimd", lambda g: g.iota(out=gcol[:], pattern=[[128, NRD]], base=0,
                                       channel_multiplier=1), GP)
    t_gc = GP.n
    emit(p, "vector", lambda v: v.tensor_copy(gcolf[:], gcol[:]), CMP,
         wait=[(GP, t_gc), (CSE, t_ob)])
    t_gcf = CMP.n

    GW = p.sem("gw")
    FF2 = p.sem("ff2")
    KWS = p.sem("kws")
    grp_rounds = 4
    for grp in range(NGR):
        for rr in range(grp_rounds):
            r = grp * grp_rounds + rr
            def sig_f(v, r=r):
                i = v.tensor_single_scalar(out=cmp128[:, 0:NSEG_], in_=ob[:, 0:NSEG_],
                                           scalar=gcolf[:, r:r + 1], op=ALU.is_le)
                p.chain_bar("vector", v, i)
                i = v.tensor_reduce(out=sig[:], in_=cmp128[:, 0:NSEG_], axis=AX.X, op=ALU.add)
                p.chain_bar("vector", v, i)
                i = v.tensor_mul(cmp128[:, 0:NSEG_], cmp128[:, 0:NSEG_], ob[:, 0:NSEG_])
                p.chain_bar("vector", v, i)
                i = v.tensor_reduce(out=osg[:], in_=cmp128[:, 0:NSEG_], axis=AX.X, op=ALU.max)
                p.chain_bar("vector", v, i)
                i = v.tensor_scalar(out=sig[:], in0=sig[:], scalar1=1.0, scalar2=float(CAP),
                                    op0=ALU.subtract, op1=ALU.mult)
                p.chain_bar("vector", v, i)
                i = v.tensor_scalar(out=lam[:], in0=osg[:], scalar1=-1.0,
                                    scalar2=gcolf[:, r:r + 1], op0=ALU.mult, op1=ALU.add)
                p.chain_bar("vector", v, i)
                i = v.tensor_add(flt[:], sig[:], lam[:])
                p.chain_bar("vector", v, i)
                return v.tensor_copy(fltu[:], flt[:])
            emit(p, "vector", sig_f, CMP, wait=[(CMP, CMP.n), (GD, GD.n)])
            t_sg = CMP.n
            def id_g(g, r=r, t_sg=t_sg):
                g.wait_ge(CMP.h, t_sg)
                g.wait_ge(LX.h, t_idp)
                g.indirect_dma_start(
                    out=tid[:, r:r + 1], out_offset=None,
                    in_=idpad.ap().rearrange("o (f x) -> (o f) x", x=1)[0:NSEG_ * CAP, :],
                    in_offset=IndirectOffsetOnAxis(ap=fltu[:, 0:1], axis=0)).then_inc(GD.h, 16)
            GD.inc(16)
            p.add("gpsimd", id_g)
            t_idg = GD.n
            def tid_f(v, r=r):
                return v.tensor_copy(tidu[:], tid[:, r:r + 1])
            emit(p, "vector", tid_f, CMP, wait=[(GD, t_idg)])
            t_tidu = CMP.n
            def kw_g(g, r=r, t_tidu=t_tidu, t_trp_prev=TRP.n):
                g.wait_ge(CMP.h, t_tidu)
                g.wait_ge(TRP.h, t_trp_prev)
                g.indirect_dma_start(
                    out=kwv[:, r:r + 1], out_offset=None,
                    in_=sce_d.ap().rearrange("o (f x) -> (o f) x", x=1),
                    in_offset=IndirectOffsetOnAxis(ap=tidu[:, 0:1], axis=0)).then_inc(GD.h, 16)
                g.indirect_dma_start(
                    out=xg[:], out_offset=None, in_=xfm_all.ap(),
                    in_offset=IndirectOffsetOnAxis(ap=tidu[:, 0:1], axis=0)).then_inc(GD.h, 16)
            GD.inc(32)
            p.add("gpsimd", kw_g)
            t_xgg = GD.n
            for c in range(8):
                bb = c % 2
                w6 = [(GD, t_xgg), (TRE, TRE.n)] if c < 2 else [(TRE, TRE.n)]
                if rr == 0 and c < 2:
                    w6 = w6 + [(FFM, FFM.n)]
                emit(p, "tensor", lambda t, c=c, bb=bb: t.transpose(
                    pbb[bb][:, :], xg[:, c * 128:(c + 1) * 128], identB_s[:]),
                    TRP, wait=w6)
                emit(p, "scalar", lambda sc, c=c, bb=bb, rr=rr: sc.activation(
                    out=xh[:, c, rr * 128:(rr + 1) * 128], in_=pbb[bb][:, :], func=AF.Copy),
                    TRE, wait=[(TRP, TRP.n)])
        t_xh = TRE.n
        def gsum(t):
            for c in range(8):
                last = t.matmul(pb[0][0:1, :], ones_bf[:], xh[:, c, :],
                                start=(c == 0), stop=(c == 7))
            return last
        emit(p, "tensor", gsum, STM, wait=[(TRE, t_xh)])
        for c in range(8):
            w7 = [(TRE, t_xh)] if c == 0 else [(STM, STM.n)]
            emit(p, "scalar", lambda sc, c=c: sc.activation(out=xsqb[:], in_=xh[:, c, :],
                                                            func=AF.Square), SCE, wait=w7)
            emit(p, "tensor", lambda t, c=c: t.matmul(pb[1][0:1, :], ones_bf[:], xsqb[:],
                                                      start=(c == 0), stop=(c == 7)),
                 STM, wait=[(SCE, SCE.n)])
        t_gstm = STM.n
        def grow_a(v):
            v.tensor_single_scalar(out=r_mu, in_=pb[0][0:1, :], scalar=1.0 / D, op=ALU.mult)
            i = v.tensor_single_scalar(out=r_var, in_=pb[1][0:1, :], scalar=1.0 / D, op=ALU.mult)
            p.chain_bar("vector", v, i)
            i = v.tensor_mul(r_t, r_mu, r_mu)
            p.chain_bar("vector", v, i)
            return v.tensor_sub(r_var, r_var, r_t)
        emit(p, "vector", grow_a, ROW, wait=[(STM, t_gstm)])
        emit(p, "scalar", lambda sc: sc.activation(out=r_t, in_=r_var, func=AF.Sqrt,
                                                   bias=consts[0:1, 0:1]),
             ROWA, wait=[(ROW, ROW.n)])
        def grow_c(v):
            i = v.reciprocal(r_rstd, r_t)
            p.chain_bar("vector", v, i)
            i = v.tensor_mul(r_ms, r_mu, r_rstd)
            p.chain_bar("vector", v, i)
            return v.tensor_single_scalar(out=r_ms, in_=r_ms, scalar=-1.0, op=ALU.mult)
        emit(p, "vector", grow_c, ROW, wait=[(ROWA, ROWA.n)])
        t_grow = ROW.n
        emit(p, "tensor", lambda t: t.matmul(pb[2][:], ones_f[:], r_rstd, start=True,
                                             stop=True), NRM, wait=[(ROW, t_grow), (XQ, XQ.n)])
        t_pb2 = NRM.n
        for c in range(8):
            bank = 3 + (c % 2)
            w8 = [(NRM, NRM.n)] if c < 2 else [(XQ, XQ.n - 1)]
            emit(p, "tensor", lambda t, c=c, bank=bank: t.matmul(
                pb[bank][:], gam_s[:, c * 128:(c + 1) * 128], r_ms, start=True, stop=True),
                NRM, wait=w8)
            t_r1 = NRM.n
            def gapply(v, c=c, bank=bank):
                i = v.tensor_mul(tmp32[:, c % 2, :], xh[:, c, :], pb[2][:])
                p.chain_bar("vector", v, i)
                i = v.scalar_tensor_tensor(out=tmp32[:, c % 2, :], in0=tmp32[:, c % 2, :],
                                           scalar=gamcol_s[:, c:c + 1], in1=pb[bank][:],
                                           op0=ALU.mult, op1=ALU.add)
                p.chain_bar("vector", v, i)
                return v.tensor_scalar(out=xh[:, c, :], in0=tmp32[:, c % 2, :],
                                       scalar1=betcol_s[:, c:c + 1], scalar2=None,
                                       op0=ALU.add)
            emit(p, "vector", gapply, XQ, wait=[(NRM, t_r1)])
        t_gap = XQ.n
        for f in range(32):
            wb = f % 2
            wld_w = [(LW, LW.n)] if True else []
            t_w1 = emit_dma(p, "sync", w1t[wb][:],
                            w1bf.ap().rearrange("(c p) q -> p c q", p=128)
                            [:, :, f * 128:(f + 1) * 128],
                            LWA[wb], wait=[(FFM, max(0, FFM.n - 1)), (LWG, t_w1c)])
            bank = 4 + (f % 2)
            def fc1_mm(t, f=f, wb=wb, bank=bank, t_w1=t_w1):
                t.wait_ge(LWA[wb].h, t_w1)
                for c in range(8):
                    last = t.matmul(pb[bank][:], w1t[wb][:, c, :], xh[:, c, :],
                                    start=(c == 0), stop=(c == 7))
                return last
            w9 = [(XQ, t_gap)]
            if f >= 2:
                w9.append((GEL, GEL.n - 1))
            emit(p, "tensor", fc1_mm, FFM, wait=w9)
            t_f1 = FFM.n
            emit(p, "scalar", lambda sc, f=f, bank=bank: sc.activation(
                out=hbf[:, f, :], in_=pb[bank][:], func=AF.Gelu_apprx_tanh,
                bias=b1col_s[:, f:f + 1]), GEL, wait=[(FFM, t_f1)])
        t_gel = GEL.n
        for dc in range(8):
            wb = dc % 2
            t_w2 = emit_dma(p, "sync", w2s[wb][:],
                            w2bf.ap().rearrange("(hc p) q -> p hc q", p=128)
                            [:, :, dc * 128:(dc + 1) * 128],
                            LWB[wb], wait=[(FF2, max(0, FF2.n - 1)), (LWG, t_w2c)])
            bank = 4 + (dc % 2)
            def fc2_mm(t, dc=dc, wb=wb, bank=bank, t_w2=t_w2):
                t.wait_ge(LWB[wb].h, t_w2)
                for hc in range(32):
                    last = t.matmul(pb[bank][:], w2s[wb][:, hc, :], hbf[:, hc, :],
                                    start=(hc == 0), stop=(hc == 31))
                return last
            w10 = [(GEL, t_gel)]
            if dc >= 2:
                w10.append((FFE, FFE.n - 1))
            emit(p, "tensor", fc2_mm, FF2, wait=w10)
            t_f2 = FF2.n
            emit(p, "scalar", lambda sc, dc=dc, bank=bank: sc.activation(
                out=ofm[:, dc, :], in_=pb[bank][:], func=AF.Identity,
                bias=b2col_s[:, dc:dc + 1]), FFE, wait=[(FF2, t_f2)])
        t_ffe = FFE.n
        for rr in range(grp_rounds):
            r = grp * grp_rounds + rr
            for c in range(8):
                bb = c % 2
                w11 = [(FFE, t_ffe), (KWS, KWS.n)] if c < 2 else [(KWS, KWS.n)]
                emit(p, "tensor", lambda t, rr=rr, c=c, bb=bb: t.transpose(
                    pbb[bb][:, :], ofm[:, c, rr * 128:(rr + 1) * 128], identB_s[:]),
                    TRP, wait=w11)
                t_tp = TRP.n
                w11b = [(TRP, t_tp)]
                if c == 0:
                    w11b.append((OE, OE.n))
                emit(p, "vector", lambda v, r=r, c=c, bb=bb: v.tensor_scalar(
                    out=xtm[:, c, :], in0=pbb[bb][:, :], scalar1=kwv[:, r:r + 1],
                    scalar2=None, op0=ALU.mult), KWS, wait=w11b)
            t_kws = KWS.n
            emit_dma(p, "sync", oe.ap()[r * 128:(r + 1) * 128, :]
                     .rearrange("t (c q) -> t c q", c=8), xtm[:], OE, wait=[(KWS, t_kws)])
    t_ffn_done = OE.n

    emit(p, "gpsimd", lambda g: g.collective_compute(
        "AllGather", ALU.bypass, replica_groups=[list(range(8))],
        ins=[oe.ap().opt()], outs=[oall.ap().opt()]), CC,
        wait=[(OE, t_ffn_done), (LWG, t_oz)])
    t_cc2 = CC.n

    for tau in range(TOK // 128):
        t0 = tau * 128
        def cg_f(g, tau=tau, t0=t0, t_cmb=CMB.n):
            g.wait_ge(CC.h, t_cc2)
            if tau > 0:
                g.wait_ge(CMB.h, t_cmb)
            for e in range(8):
                g.indirect_dma_start(
                    out=gath8[e][:], out_offset=None, in_=oall.ap(),
                    in_offset=IndirectOffsetOnAxis(ap=gT[:, tau * 8 + e: tau * 8 + e + 1],
                                                   axis=0)).then_inc(GD.h, 16)
            g.dma_start(out=xfmt[:].rearrange("t (c q) -> t c q", c=8),
                        in_=xfm_my.ap()[t0:t0 + 128, :].rearrange("t (c q) -> t c q", c=8)
                        ).then_inc(GD.h, 16)
        GD.inc(16 * 9)
        p.add("gpsimd", cg_f)
        t_cg = GD.n
        def sum_f(v):
            i = v.tensor_add(acct[:], gath8[0][:], gath8[1][:])
            p.chain_bar("vector", v, i)
            for e in range(2, 8):
                i = v.tensor_add(acct[:], acct[:], gath8[e][:])
                p.chain_bar("vector", v, i)
            i = v.tensor_add(acct[:], acct[:], xfmt[:])
            return i
        emit(p, "vector", sum_f, CMB, wait=[(GD, t_cg)])
        t_sum = CMB.n
        t_x1r = emit_dma(p, "sync", xt[:, :, 0:128],
                         x1sc.ap()[:, t0:t0 + 128].rearrange("(c p) t -> p c t", p=128),
                         LX, wait=[(TR2, TR2.n)])
        for c in range(8):
            bank = 4 + (c % 2)
            w12 = [(CMB, t_sum)]
            w12.append((TR2, TR2.n))
            emit(p, "tensor", lambda t, c=c, bank=bank: t.transpose(
                pb[bank][:, 0:128], acct[:, c * 128:(c + 1) * 128], identF_s[:]),
                TRP, wait=w12)
            t_tp2 = TRP.n
            def x2_f(v, c=c, bank=bank, t_x1r=t_x1r, t_od=OD.n):
                v.wait_ge(LX.h, t_x1r)
                if t_od > 0 and c == 0:
                    v.wait_ge(OD.h, t_od)
                return v.tensor_add(x2c[:, c, :], xt[:, c, 0:128], pb[bank][:, 0:128])
            emit(p, "vector", x2_f, TR2, wait=[(TRP, t_tp2)])
        t_x2 = TR2.n
        emit_dma(p, "gpsimd", y[:, t0:t0 + 128].rearrange("(c p) t -> p c t", p=128),
                 x2c[:], OD, wait=[(TR2, t_x2)])
    emit(p, "gpsimd", lambda g: g.nop().then_inc(p.sem("fin").h, 1), None,
         wait=[(OD, OD.n)])
    return p.build()



def build_l0_attn_b(TOK=2048):
    p = Prog("l0ab")
    nc = p.nc
    HL = 256
    TH = TOK + 2 * HL
    NTI = TH // 512
    NQT = TOK // 512
    NKC = TH // 128

    xh_fm = nc.dram_tensor("xh_fm", [D, TH], F32, kind="ExternalInput").ap()
    x2_fm = nc.dram_tensor("x2_fm", [D, TOK], F32, kind="ExternalInput").ap()
    wqT = nc.dram_tensor("wqT", [D, D], F32, kind="ExternalInput").ap()
    wkT = nc.dram_tensor("wkT", [D, D], F32, kind="ExternalInput").ap()
    wvT = nc.dram_tensor("wvT", [D, D], F32, kind="ExternalInput").ap()
    bqc = nc.dram_tensor("bqc", [128, 8], F32, kind="ExternalInput").ap()
    bkc = nc.dram_tensor("bkc", [128, 8], F32, kind="ExternalInput").ap()
    bvrow = nc.dram_tensor("bvrow", [1, D], F32, kind="ExternalInput").ap()
    bmask = nc.dram_tensor("bmask", [NQT * 8, 128, 512], BF16, kind="ExternalInput").ap()
    y = nc.dram_tensor("y", [D, TOK], F32, kind="ExternalOutput").ap()
    wqb = nc.dram_tensor("wqb", [D, D], BF16, kind="Internal")
    wkb = nc.dram_tensor("wkb", [D, D], BF16, kind="Internal")
    wvb = nc.dram_tensor("wvb", [D, D], BF16, kind="Internal")

    xt = [p.sb(f"xt{i}", [128, 8, 512], F32) for i in range(1)]
    xq = p.sb("xq", [128, 8, 512], BF16)
    xsq = p.sb("xsq", [128, 512], F32)
    ksq = p.sb("ksq", [128, 512], BF16)
    tmp32 = p.sb("tmp32", [128, 2, 512], F32)
    rows = p.sb("rows", [1, 6 * 512], F32)
    consts = p.sb("consts", [128, 4], F32)
    ones_cf = p.sb("ones_cf", [128, 1], F32)
    hselA = p.sb("hselA", [128, 2], BF16)
    ones_f = p.sb("ones_f", [1, 128], F32)
    ones64 = p.sb("ones64", [65, 128], F32)
    kbuf = [p.sb(f"kb{i}", [128, TH], BF16) for i in range(8)]
    qbuf = [p.sb(f"qb{i}", [128, TOK], BF16) for i in range(8)]
    vones = p.sb("vones", [128, 16 * NKC * 65], BF16)
    wstr = [p.sb(f"wstr{i}", [128, 8, 128], BF16) for i in range(2)]
    bvb = p.sb("bvb", [128, 1024], F32)
    negc = p.sb("negc", [128, 16], F32)
    ncrow = p.sb("ncrow", [1, 16], F32)
    nrow2 = p.sb("nrow2", [1, 16], F32)
    pbf = [p.sb(f"p_{i}", [128, 512], BF16) for i in range(4)]
    bm_sb = p.sb("bm_sb", [128, 8 * 512], BF16)
    r4b = p.sb("r4b", [65, 2 * 512], F32)
    rb = p.sb("rb", [64, 2 * 512], F32)
    aos = p.sb("aos", [128, 512], F32)
    x3c = p.sb("x3c", [128, 8, 512], F32)
    bq_s = p.sb("bq_s", [128, 8], F32)
    bk_s = p.sb("bk_s", [128, 8], F32)

    pb = [p.ps(f"pb{i}", [128, 512], F32) for i in range(8)]

    LD = p.sem("ld"); LDC = p.sem("ldc"); LX = [p.sem("lx0"), p.sem("lx1")]
    CAST = p.sem("cast"); STM = p.sem("stm"); SQ = p.sem("sq"); ROW = p.sem("row")
    ROWA = p.sem("rowa"); NRM = p.sem("nrm"); XQ = p.sem("xq"); PRJ = p.sem("prj")
    QEV = p.sem("qev"); VEV = p.sem("vev"); QN = p.sem("qn"); CRD = p.sem("crd")
    QK = [p.sem(f"qk{i}") for i in range(2)]
    PS = [p.sem(f"ps{i}") for i in range(2)]
    AV = [p.sem(f"av{i}") for i in range(2)]
    RC = p.sem("rc"); RBM = p.sem("rbm"); RBE = p.sem("rbe"); AOE = p.sem("aoe")
    X2L = p.sem("x2l"); OD = p.sem("od")

    r_mu = rows[:, 0:512]; r_var = rows[:, 512:1024]; r_rstd = rows[:, 1024:1536]
    r_t = rows[:, 1536:2048]; r_ms = rows[:, 2048:2560]

    emit_dma(p, "sync", bq_s[:], bqc, LDC)
    emit_dma(p, "sync", bk_s[:], bkc, LDC)
    t_ldc = LDC.n
    def init_c(v):
        v.memset(consts[:, 0:1], 1e-5)
        v.memset(ones_cf[:], 1.0)
        v.memset(ones_f[:], 1.0)
        v.memset(ones64[64:65, :], 1.0)
        i = v.memset(hselA[:, :], 0.0)
        p.chain_bar("vector", v, i)
        v.memset(hselA[0:64, 0:1], 1.0)
        i = v.memset(hselA[64:128, 1:2], 1.0)
        p.chain_bar("vector", v, i)
        v.memset(ncrow[:], 0.0)
        v.memset(nrow2[:], 0.0)
        return v.memset(vones[:].rearrange("p (a o) -> p a o", o=65)[:, :, 64:65], 1.0)
    emit(p, "vector", init_c, CAST, wait=[(LDC, t_ldc)])
    t_cinit = CAST.n
    t_bv = emit_dma(p, "sync", rows[0:1, 0:1024], bvrow, LD, wait=[(CAST, t_cinit)])
    emit(p, "tensor", lambda t: [t.matmul(pb[4][:], ones_f[:], rows[0:1, j * 512:(j + 1) * 512],
                                          start=True, stop=True) for j in range(1)][-1],
         NRM, wait=[(LD, t_bv)])
    emit(p, "tensor", lambda t: t.matmul(pb[5][:], ones_f[:], rows[0:1, 512:1024],
                                         start=True, stop=True), NRM)
    t_bvm = NRM.n
    def bv_ev(v):
        v.tensor_copy(bvb[:, 0:512], pb[4][:])
        return v.tensor_copy(bvb[:, 512:1024], pb[5][:])
    emit(p, "vector", bv_ev, CAST, wait=[(NRM, t_bvm)])
    t_bvb = CAST.n
    LWG = p.sem("lwg")
    emit_dma(p, "gpsimd", wqb.ap(), wqT, LWG)
    emit_dma(p, "gpsimd", wkb.ap(), wkT, LWG)
    emit_dma(p, "gpsimd", wvb.ap(), wvT, LWG)
    t_wc3 = LWG.n
    LWS = [p.sem("lws0"), p.sem("lws1")]
    t_wc = CAST.n

    hist = {"prj": [0], "qev": [0], "vev": [0], "stm": [0], "rowe": [0]}
    for ti in range(NTI):
        par = 0
        t0 = ti * 512
        w = [(CAST, t_wc)] if ti == 0 else [(PRJ, hist["prj"][ti]), (QEV, hist["qev"][ti]),
                                            (VEV, hist["vev"][ti]), (XQ, XQ.n)]
        t_x = emit_dma(p, "sync", xt[par][:],
                       xh_fm[:, t0:t0 + 512].rearrange("(c p) t -> p c t", p=128),
                       LX[0], wait=w)
        def ssum(t, par=par):
            for c in range(8):
                last = t.matmul(pb[0][0:1, :], ones_cf[:], xt[par][:, c, :],
                                start=(c == 0), stop=(c == 7))
            return last
        emit(p, "tensor", ssum, STM, wait=[(LX[0], t_x), (ROW, hist["rowe"][ti])])
        for c in range(8):
            w2 = [(LX[0], t_x)] if c == 0 else [(STM, STM.n)]
            emit(p, "scalar", lambda sc, c=c, par=par: sc.activation(
                out=xsq[:], in_=xt[par][:, c, :], func=AF.Square), SQ, wait=w2)
            emit(p, "tensor", lambda t, c=c: t.matmul(pb[1][0:1, :], ones_cf[:], xsq[:],
                                                      start=(c == 0), stop=(c == 7)),
                 STM, wait=[(SQ, SQ.n)])
        t_stm = STM.n
        hist["stm"].append(t_stm)
        def row_a(v):
            v.tensor_single_scalar(out=r_mu, in_=pb[0][0:1, :], scalar=1.0 / D, op=ALU.mult)
            i = v.tensor_single_scalar(out=r_var, in_=pb[1][0:1, :], scalar=1.0 / D, op=ALU.mult)
            p.chain_bar("vector", v, i)
            i = v.tensor_mul(r_t, r_mu, r_mu)
            p.chain_bar("vector", v, i)
            return v.tensor_sub(r_var, r_var, r_t)
        emit(p, "vector", row_a, ROW, wait=[(STM, t_stm)])
        emit(p, "scalar", lambda sc: sc.activation(out=r_t, in_=r_var, func=AF.Sqrt,
                                                   bias=consts[0:1, 0:1]),
             ROWA, wait=[(ROW, ROW.n)])
        def row_c(v):
            i = v.reciprocal(r_rstd, r_t)
            p.chain_bar("vector", v, i)
            return v.tensor_mul(r_ms, r_mu, r_rstd)
        emit(p, "vector", row_c, ROW, wait=[(ROWA, ROWA.n)])
        t_rowe = ROW.n
        hist["rowe"].append(t_rowe)
        emit(p, "tensor", lambda t: t.matmul(pb[2][:], ones_f[:], r_rstd, start=True,
                                             stop=True), NRM, wait=[(ROW, t_rowe), (XQ, XQ.n)])
        emit(p, "tensor", lambda t: t.matmul(pb[3][:], ones_f[:], r_ms, start=True,
                                             stop=True), NRM)
        t_nrm = NRM.n
        def xn_f(v, par=par):
            for c in range(8):
                i = v.tensor_mul(tmp32[:, c % 2, :], xt[par][:, c, :], pb[2][:])
                p.chain_bar("vector", v, i)
                i = v.tensor_sub(xq[:, c, :], tmp32[:, c % 2, :], pb[3][:])
                if c < 7:
                    p.chain_bar("vector", v, i)
            return i
        emit(p, "vector", xn_f, XQ, wait=[(NRM, t_nrm)])
        t_xn = XQ.n
        for hh in range(8):
            bank = 4 + hh % 2
            wbu = hh % 2
            t_ws = emit_dma(p, "sync", wstr[wbu][:],
                            wkb.ap().rearrange("(c p) q -> p c q", p=128)
                            [:, :, hh * 128:(hh + 1) * 128], LWS[wbu],
                            wait=[(LWG, t_wc3), (PRJ, max(0, PRJ.n - 1))])
            w3 = [(XQ, t_xn)]
            if QEV.n >= 1:
                w3.append((QEV, QEV.n - 1))
            def kmm(t, bank=bank, wbu=wbu, t_ws=t_ws):
                t.wait_ge(LWS[wbu].h, t_ws)
                for c in range(8):
                    last = t.matmul(pb[bank][:], wstr[wbu][:, c, :], xq[:, c, :],
                                    start=(c == 0), stop=(c == 7))
                return last
            emit(p, "tensor", kmm, PRJ, wait=w3)
            t_pk = PRJ.n
            emit(p, "vector", lambda v, hh=hh, bank=bank, t0=t0: v.tensor_scalar(
                out=kbuf[hh][:, t0:t0 + 512], in0=pb[bank][:],
                scalar1=bk_s[:, hh:hh + 1], scalar2=None, op0=ALU.add),
                QEV, wait=[(PRJ, t_pk)])
        for hh in range(8):
            wbu = hh % 2
            t_ws = emit_dma(p, "sync", wstr[wbu][:],
                            wvb.ap().rearrange("(c p) q -> p c q", p=128)
                            [:, :, hh * 128:(hh + 1) * 128], LWS[wbu],
                            wait=[(LWG, t_wc3), (PRJ, max(0, PRJ.n - 1))])
            for h2 in range(2):
                h = 2 * hh + h2
                for sj in range(4):
                    bank = 6 + (sj % 2)
                    w4 = [(XQ, t_xn)]
                    if VEV.n >= 1:
                        w4.append((VEV, VEV.n - 1))
                    def vmm(t, h2=h2, sj=sj, bank=bank, wbu=wbu, t_ws=t_ws):
                        t.wait_ge(LWS[wbu].h, t_ws)
                        for c in range(8):
                            last = t.matmul(pb[bank][:, 0:64],
                                            xq[:, c, sj * 128:(sj + 1) * 128],
                                            wstr[wbu][:, c, h2 * 64:h2 * 64 + 64],
                                            start=(c == 0), stop=(c == 7))
                        return last
                    emit(p, "tensor", vmm, PRJ, wait=w4)
                    t_pv = PRJ.n
                    def vev_f(v, h=h, sj=sj, bank=bank, ti=ti):
                        kc = ti * 4 + sj
                        return v.tensor_add(vones[:, (h * NKC + kc) * 65:(h * NKC + kc) * 65 + 64],
                                            pb[bank][:, 0:64], bvb[:, h * 64:(h + 1) * 64])
                    emit(p, "vector", vev_f, VEV, wait=[(PRJ, t_pv)])
        lo_t = max(t0, HL); hi_t = min(t0 + 512, HL + TOK)
        if lo_t < hi_t:
            for hh in range(8):
                bank = 4 + hh % 2
                wbu = hh % 2
                t_ws = emit_dma(p, "sync", wstr[wbu][:],
                                wqb.ap().rearrange("(c p) q -> p c q", p=128)
                                [:, :, hh * 128:(hh + 1) * 128], LWS[wbu],
                                wait=[(LWG, t_wc3), (PRJ, max(0, PRJ.n - 1))])
                def qmm(t, bank=bank, wbu=wbu, t_ws=t_ws):
                    t.wait_ge(LWS[wbu].h, t_ws)
                    for c in range(8):
                        last = t.matmul(pb[bank][:], wstr[wbu][:, c, :], xq[:, c, :],
                                        start=(c == 0), stop=(c == 7))
                    return last
                emit(p, "tensor", qmm, PRJ, wait=[(XQ, t_xn), (QEV, QEV.n - 1)])
                t_pq = PRJ.n
                emit(p, "vector", lambda v, hh=hh, bank=bank, t0=t0, lo_t=lo_t, hi_t=hi_t:
                     v.tensor_scalar(out=qbuf[hh][:, lo_t - HL:hi_t - HL],
                                     in0=pb[bank][:, lo_t - t0:hi_t - t0],
                                     scalar1=bq_s[:, hh:hh + 1], scalar2=None, op0=ALU.add),
                     QEV, wait=[(PRJ, t_pq)])
        hist["prj"].append(PRJ.n)
        hist["qev"].append(QEV.n)
        hist["vev"].append(VEV.n)
    t_p1_qev = QEV.n
    t_p1_vev = VEV.n

    for hh in range(8):
        for j in range(TH // 512):
            w6 = [(QEV, t_p1_qev)] if (hh == 0 and j == 0) else [(QN, QN.n)]
            emit(p, "scalar", lambda sc, hh=hh, j=j: sc.activation(
                out=ksq[:], in_=kbuf[hh][:, j * 512:(j + 1) * 512], func=AF.Square),
                SQ, wait=w6)
            emit(p, "tensor", lambda t, hh=hh: t.matmul(
                pb[0][0:1, :], hselA[:, 0:1], ksq[:], start=True, stop=True),
                QN, wait=[(SQ, SQ.n)])
            emit(p, "tensor", lambda t, hh=hh: t.matmul(
                pb[1][0:1, :], hselA[:, 1:2], ksq[:], start=True, stop=True), QN)
            t_qn = QN.n
            def nmax(v, hh=hh):
                i = v.tensor_reduce(out=r_mu[:, 0:1], in_=pb[0][0:1, :], axis=AX.X, op=ALU.max)
                p.chain_bar("vector", v, i)
                i = v.tensor_reduce(out=r_mu[:, 1:2], in_=pb[1][0:1, :], axis=AX.X, op=ALU.max)
                p.chain_bar("vector", v, i)
                i = v.tensor_max(ncrow[:, 2 * hh:2 * hh + 2], ncrow[:, 2 * hh:2 * hh + 2],
                                 r_mu[:, 0:2])
                return i
            emit(p, "vector", nmax, QN, wait=[(QN, t_qn)])
    for hh in range(8):
        for j in range(TOK // 512):
            emit(p, "scalar", lambda sc, hh=hh, j=j: sc.activation(
                out=ksq[:], in_=qbuf[hh][:, j * 512:(j + 1) * 512], func=AF.Square),
                SQ, wait=[(QN, QN.n)])
            emit(p, "tensor", lambda t: t.matmul(
                pb[0][0:1, :], hselA[:, 0:1], ksq[:], start=True, stop=True),
                QN, wait=[(SQ, SQ.n)])
            emit(p, "tensor", lambda t: t.matmul(
                pb[1][0:1, :], hselA[:, 1:2], ksq[:], start=True, stop=True), QN)
            t_qn = QN.n
            def nmax2(v, hh=hh):
                i = v.tensor_reduce(out=r_mu[:, 0:1], in_=pb[0][0:1, :], axis=AX.X, op=ALU.max)
                p.chain_bar("vector", v, i)
                i = v.tensor_reduce(out=r_mu[:, 1:2], in_=pb[1][0:1, :], axis=AX.X, op=ALU.max)
                p.chain_bar("vector", v, i)
                return v.tensor_max(nrow2[:, 2 * hh:2 * hh + 2], nrow2[:, 2 * hh:2 * hh + 2],
                                    r_mu[:, 0:2])
            emit(p, "vector", nmax2, QN, wait=[(QN, t_qn)])
    t_qnf = QN.n
    def cfin(v):
        i = v.tensor_mul(ncrow[:], ncrow[:], nrow2[:])
        return i
    emit(p, "vector", cfin, ROW, wait=[(QN, t_qnf)])
    emit(p, "scalar", lambda sc: sc.activation(out=ncrow[:], in_=ncrow[:], func=AF.Sqrt),
         ROWA, wait=[(ROW, ROW.n)])
    emit(p, "vector", lambda v: v.tensor_single_scalar(out=ncrow[:], in_=ncrow[:],
                                                       scalar=-0.125, op=ALU.mult),
         ROW, wait=[(ROWA, ROWA.n)])
    emit(p, "tensor", lambda t: t.matmul(pb[2][:, 0:16], ones_f[:], ncrow[:],
                                         start=True, stop=True), NRM, wait=[(ROW, ROW.n)])
    emit(p, "scalar", lambda sc: sc.activation(out=negc[:], in_=pb[2][:, 0:16],
                                               func=AF.Copy), CRD, wait=[(NRM, NRM.n)])
    t_crd = CRD.n

    nqk = [0, 0]; nexp = [0, 0]; nav = [0, 0]
    t_aoe_prev = 0
    t_x3_prev = [0] * 8
    od_hist = [0]
    BML = [p.sem("bml0"), p.sem("bml1")]
    for qi in range(NQT):
        q0 = qi * 512
        t_bm = emit_dma(p, "sync", bm_sb[:].rearrange("p (a t) -> p a t", a=8),
                        bmask.rearrange("a p t -> p a t")[:, qi * 8:(qi + 1) * 8, :],
                        BML[qi % 2], wait=[(PS[0], nexp[0]), (PS[1], nexp[1])])
        BMLq = BML[qi % 2]
        for hh in range(8):
            hA, hB = 2 * hh, 2 * hh + 1
            for kc in range(8):
                kcg = qi * 4 + kc
                par = kc % 2
                for sub, base in ((0, 0), (1, 64)):
                    w = []
                    if nexp[sub] > 0:
                        w.append((PS[sub], nexp[sub]))
                    if qi == 0 and hh == 0 and kc == 0:
                        w.append((CRD, t_crd))
                        w.append((VEV, t_p1_vev))
                    emit(p, "tensor", lambda t, hh=hh, kcg=kcg, q0=q0, base=base, sub=sub:
                         t.matmul(pb[sub][:], kbuf[hh][base:base + 64, kcg * 128:kcg * 128 + 128],
                                  qbuf[hh][base:base + 64, q0:q0 + 512],
                                  start=True, stop=True), QK[sub], wait=w)
                    nqk[sub] += 1
                for sub, h in ((0, hA), (1, hB)):
                    w = [(QK[sub], nqk[sub])]
                    if nav[sub] >= 2:
                        w.append((AV[sub], nav[sub] - 1))
                    emit(p, "scalar", lambda sc, sub=sub, h=h, qi=qi, par=par: sc.activation(
                        out=pbf[2 * sub + par][:], in_=pb[sub][:], func=AF.Exp, scale=0.125,
                        bias=negc[:, h:h + 1]), PS[sub], wait=w)
                    nexp[sub] += 1
                for sub in (0, 1):
                    wm = [(PS[sub], nexp[sub])]
                    if hh == 0 and kc == 0 and sub == 0:
                        wm.append((BMLq, t_bm))
                    emit(p, "vector", lambda v, sub=sub, par=par, kc=kc: v.tensor_mul(
                        pbf[2 * sub + par][:], pbf[2 * sub + par][:],
                        bm_sb[:, kc * 512:(kc + 1) * 512]),
                        PS[sub], wait=wm)
                    nexp[sub] += 1
                for sub, h in ((0, hA), (1, hB)):
                    w = [(PS[sub], nexp[sub])]
                    if kc == 0 and (qi > 0 or hh > 0):
                        w.append((AOE, AOE.n))
                    emit(p, "tensor", lambda t, sub=sub, h=h, kcg=kcg, par=par, kc=kc:
                         t.matmul(pb[4 + sub][0:65, :],
                                  vones[:, (h * NKC + kcg) * 65:(h * NKC + kcg) * 65 + 65],
                                  pbf[2 * sub + par][:], start=(kc == 0), stop=(kc == 7),
                                  skip_group_check=True), AV[sub], wait=w)
                    nav[sub] += 1
            def recip_f(v, navs=tuple(nav)):
                v.wait_ge(AV[0].h, navs[0])
                v.wait_ge(AV[1].h, navs[1])
                v.reciprocal(r4b[64:65, 0:512], pb[4][64:65, :])
                return v.reciprocal(r4b[64:65, 512:1024], pb[5][64:65, :])
            emit(p, "vector", recip_f, RC)
            t_rc = RC.n
            for sub in (0, 1):
                emit(p, "tensor", lambda t, sub=sub: t.matmul(
                    pb[2 + sub][0:64, :], ones64[64:65, 0:64],
                    r4b[64:65, sub * 512:(sub + 1) * 512], start=True, stop=True),
                    RBM, wait=[(RC, t_rc)])
            t_rbm = RBM.n
            def rb_ev(sc):
                sc.activation(out=rb[:, 0:512], in_=pb[2][0:64, :], func=AF.Copy)
                return sc.activation(out=rb[:, 512:1024], in_=pb[3][0:64, :], func=AF.Copy)
            emit(p, "scalar", rb_ev, RBE, wait=[(RBM, t_rbm)])
            t_rbe = RBE.n
            def ao_ev(v, hh=hh):
                i = v.tensor_mul(aos[0:64, :], pb[4][0:64, :], rb[:, 0:512])
                p.chain_bar("vector", v, i)
                i = v.tensor_mul(aos[64:128, :], pb[5][0:64, :], rb[:, 512:1024])
                return i
            emit(p, "vector", ao_ev, AOE, wait=[(RBE, t_rbe)])
            t_aoe = AOE.n
            t_x2l = emit_dma(p, "sync", tmp32[:, 0, :].rearrange("p (a t) -> p a t", a=1),
                             x2_fm[:, q0:q0 + 512].rearrange("(c p) t -> p c t", p=128)
                             [:, hh:hh + 1, :], X2L,
                             wait=[(OD, od_hist[qi]), (XQ, XQ.n)])
            def x3_f(v, hh=hh, t_x2l=t_x2l):
                v.wait_ge(X2L.h, t_x2l)
                return v.tensor_add(x3c[:, hh, :], tmp32[:, 0, :], aos[:])
            emit(p, "vector", x3_f, XQ, wait=[(AOE, t_aoe)])
        t_x3 = XQ.n
        t_od = emit_dma(p, "gpsimd", y[:, q0:q0 + 512].rearrange("(c p) t -> p c t", p=128),
                        x3c[:], OD, wait=[(XQ, t_x3)])
        od_hist.append(t_od)
    emit(p, "gpsimd", lambda g: g.nop().then_inc(p.sem("fin").h, 1), None, wait=[(OD, OD.n)])
    return p.build()



def build_l1_ssm(S_=S):
    p = Prog("ssm")
    nc = p.nc
    NTI = S_ // 512
    EC = 4

    x4_fm = nc.dram_tensor("x4_fm", [D, S_], F32, kind="ExternalInput").ap()
    inw = nc.dram_tensor("inw", [D, 512], F32, kind="ExternalInput").ap()
    skw = nc.dram_tensor("skw", [D, 512], F32, kind="ExternalInput").ap()
    wrow1 = nc.dram_tensor("wrow1", [512, 64], F32, kind="ExternalInput").ap()
    wrowBC = nc.dram_tensor("wrowBC", [512, 32], F32, kind="ExternalInput").ap()
    wd2sT = nc.dram_tensor("wd2sT", [64, 512], F32, kind="ExternalInput").ap()
    convw = nc.dram_tensor("convw", [128, 16], F32, kind="ExternalInput").ap()
    convb = nc.dram_tensor("convb", [128, 4], F32, kind="ExternalInput").ap()
    wd2b = nc.dram_tensor("wd2b", [128, 4], F32, kind="ExternalInput").ap()
    alog = nc.dram_tensor("alog", [128, 64], F32, kind="ExternalInput").ap()
    wdcol = nc.dram_tensor("wdcol", [128, 4], F32, kind="ExternalInput").ap()
    outw = nc.dram_tensor("outw", [512, D], F32, kind="ExternalInput").ap()
    nwcol = nc.dram_tensor("nwcol", [128, 8], F32, kind="ExternalInput").ap()
    nwrow = nc.dram_tensor("nwrow", [1, D], F32, kind="ExternalInput").ap()
    y = nc.dram_tensor("y", [D, S_], F32, kind="ExternalOutput").ap()

    xcd = nc.dram_tensor("xcd", [512, S_], BF16, kind="Internal")
    skd = nc.dram_tensor("skd", [512, S_], BF16, kind="Internal")
    dtbc_my = nc.dram_tensor("dtbc_my", [96, S_], F32, kind="Internal")
    dtbc = nc.dram_tensor("dtbc", [96, S_], F32, kind="Internal")
    po_my = nc.dram_tensor("po_my", [D, S_], F32, kind="Internal")
    po = nc.dram_tensor("po", [D, S_], F32, kind="Internal")

    xt = p.sb("xt", [128, 8, 512], F32)
    xn_bf = p.sb("xn_bf", [128, 8, 512], BF16)
    xsq = p.sb("xsq", [128, 512], F32)
    tmp32 = p.sb("tmp32", [128, 2, 512], F32)
    rows = p.sb("rows", [1, 7 * 512], F32)
    consts = p.sb("consts", [128, 4], F32)
    ones_cf = p.sb("ones_cf", [128, 1], F32)
    ones_f = p.sb("ones_f", [1, 128], F32)
    nw_sb = p.sb("nw_sb", [1, D], F32)
    inw_bf = p.sb("inw_bf", [128, 8, 512], BF16)
    skw_bf = p.sb("skw_bf", [128, 8, 512], BF16)
    w1_bf = p.sb("w1_bf", [128, 4, 64], BF16)
    wbc_bf = p.sb("wbc_bf", [128, 4, 32], BF16)
    wd2_bf = p.sb("wd2_bf", [64, 4, 128], BF16)
    outw_bf = p.sb("outw_bf", [128, 4 * 8 * 128], BF16)
    cw = p.sb("cw", [128, 16], F32)
    cb = p.sb("cb", [128, 4], F32)
    d2b = p.sb("d2b", [128, 4], F32)
    acol = p.sb("acol", [128, 64], F32)
    wdc = p.sb("wdc", [128, 4], F32)
    nwc = p.sb("nwc", [128, 8], F32)
    xit = p.sb("xit", [128, 4, 516], F32)
    xct = p.sb("xct", [128, 4, 512], BF16)
    dtbf = p.sb("dtbf", [64, S_], BF16)
    bcb = p.sb("bcb", [128, 32, 512], F32)
    delta = p.sb("delta", [128, 512], F32)
    dtx = p.sb("dtx", [128, 512], F32)
    a_t = p.sb("a_t", [128, 512], F32)
    u_t = p.sb("u_t", [128, 512], F32)
    h_t = p.sb("h_t", [128, 512], F32)
    yac = p.sb("yac", [128, 512], F32)
    sp1 = p.sb("sp1", [128, 512], F32)
    hcol = p.sb("hcol", [128, 64], F32)
    xcs = p.sb("xcs", [128, 512], BF16)
    sks = p.sb("sks", [128, 512], BF16)
    y3b = p.sb("y3b", [128, 4, 512], BF16)
    outev = p.sb("outev", [128, 8, 512], F32)
    pb = [p.ps(f"pb{i}", [128, 512], F32) for i in range(8)]

    LD = p.sem("ld"); LDC = p.sem("ldc"); LX = p.sem("lx"); CAST = p.sem("cast")
    STM = p.sem("stm"); SQ = p.sem("sq"); ROW = p.sem("row"); ROWA = p.sem("rowa")
    NRM = p.sem("nrm"); XQ = p.sem("xq"); PRJ = p.sem("prj"); PEV = p.sem("pev")
    PSK = p.sem("psk")
    CNV = p.sem("cnv"); CNE = p.sem("cne"); DBC = p.sem("dbc"); DBE = p.sem("dbe")
    CC = p.sem("cc"); DTL = p.sem("dtl")
    BCL = [p.sem("bcl0"), p.sem("bcl1")]
    XSL = [p.sem("xsl0"), p.sem("xsl1")]
    DLM = p.sem("dlm"); DLE = p.sem("dle"); SCN = p.sem("scn"); GPU = p.sem("gpu")
    OPM = p.sem("opm"); OPE = p.sem("ope"); YY = p.sem("yy")
    WR = p.sem("wr")
    POL = p.sem("pol")
    OD = p.sem("od")

    r_mu = rows[:, 0:512]; r_var = rows[:, 512:1024]; r_rstd = rows[:, 1024:1536]
    r_t = rows[:, 1536:2048]; r_ms = rows[:, 2048:2560]; r_s = rows[:, 2560:3072]
    r_t2 = rows[:, 3072:3584]

    emit_dma(p, "sync", cw[:], convw, LDC)
    emit_dma(p, "sync", cb[:], convb, LDC)
    emit_dma(p, "sync", d2b[:], wd2b, LDC)
    emit_dma(p, "sync", acol[:], alog, LDC)
    emit_dma(p, "sync", wdc[:], wdcol, LDC)
    emit_dma(p, "sync", nwc[:], nwcol, LDC)
    emit_dma(p, "sync", nw_sb[:], nwrow, LDC)
    t_ldc = LDC.n
    def init_c(v):
        v.memset(consts[:, 0:1], 1e-5)
        v.memset(consts[:, 1:2], 1.1920929e-07)
        v.memset(consts[:, 2:3], 1.0)
        v.memset(ones_cf[:], 1.0)
        v.memset(ones_f[:], 1.0)
        v.memset(hcol[:], 0.0)
        i = v.memset(xit[:], 0.0)
        return i
    emit(p, "vector", init_c, CAST, wait=[(LDC, t_ldc)])
    t_init = CAST.n
    emit(p, "scalar", lambda sc: sc.activation(out=acol[:], in_=acol[:], func=AF.Exp),
         ROWA, wait=[(CAST, t_init)])
    emit(p, "vector", lambda v: v.tensor_single_scalar(out=acol[:], in_=acol[:],
                                                       scalar=-1.0, op=ALU.mult),
         CAST, wait=[(ROWA, ROWA.n)])
    def wload(dram_ap, stage_view, dst_view):
        t_w = emit_dma(p, "sync", stage_view, dram_ap, LD, wait=[(CAST, CAST.n)])
        emit(p, "vector", lambda v: v.tensor_copy(dst_view, stage_view), CAST,
             wait=[(LD, t_w)])
    wload(inw.rearrange("(c p) q -> p c q", p=128), xt[:], inw_bf[:])
    wload(skw.rearrange("(c p) q -> p c q", p=128), xt[:], skw_bf[:])
    wload(wrow1.rearrange("(c p) q -> p c q", p=128), xt[:, 0:4, 0:64], w1_bf[:])
    wload(wrowBC.rearrange("(c p) q -> p c q", p=128), xt[:, 0:4, 0:32], wbc_bf[:])
    wload(wd2sT.rearrange("r (c q) -> r c q", c=4), xt[0:64, 0:4, 0:128], wd2_bf[:])
    wload(outw.rearrange("(ec p) f -> p ec f", p=128),
          xt[:].rearrange("p c q -> p (c q)").rearrange("p (ec f) -> p ec f", ec=4),
          outw_bf[:].rearrange("p (ec f) -> p ec f", ec=4))
    t_wc = CAST.n

    hist = {"xq": [0], "prj": [0], "pev": [0], "cnv": [0]}
    for ti in range(NTI):
        t0 = ti * 512
        w = [(CAST, t_wc)] if ti == 0 else [(XQ, hist["xq"][ti]), (CNV, hist["cnv"][ti])]
        t_x = emit_dma(p, "sync", xt[:],
                       x4_fm[:, t0:t0 + 512].rearrange("(c p) t -> p c t", p=128), LX, wait=w)
        def ssum(t):
            for c in range(8):
                last = t.matmul(pb[0][0:1, :], ones_cf[:], xt[:, c, :],
                                start=(c == 0), stop=(c == 7))
            return last
        emit(p, "tensor", ssum, STM, wait=[(LX, t_x), (ROW, ROW.n)])
        for c in range(8):
            w2 = [(LX, t_x), (WR, WR.n)] if c == 0 else [(STM, STM.n)]
            emit(p, "scalar", lambda sc, c=c: sc.activation(out=xsq[:], in_=xt[:, c, :],
                                                            func=AF.Square), SQ, wait=w2)
            emit(p, "tensor", lambda t, c=c: t.matmul(pb[1][0:1, :], ones_cf[:], xsq[:],
                                                      start=(c == 0), stop=(c == 7)),
                 STM, wait=[(SQ, SQ.n)])
        t_stm = STM.n
        def row_a(v):
            v.tensor_single_scalar(out=r_mu, in_=pb[0][0:1, :], scalar=1.0 / D, op=ALU.mult)
            i = v.tensor_single_scalar(out=r_var, in_=pb[1][0:1, :], scalar=1.0 / D, op=ALU.mult)
            p.chain_bar("vector", v, i)
            i = v.tensor_mul(r_t, r_mu, r_mu)
            p.chain_bar("vector", v, i)
            return v.tensor_sub(r_var, r_var, r_t)
        emit(p, "vector", row_a, ROW, wait=[(STM, t_stm)])
        emit(p, "scalar", lambda sc: sc.activation(out=r_t, in_=r_var, func=AF.Sqrt,
                                                   bias=consts[0:1, 0:1]),
             ROWA, wait=[(ROW, ROW.n)])
        def row_c(v):
            i = v.reciprocal(r_rstd, r_t)
            p.chain_bar("vector", v, i)
            i = v.tensor_mul(r_t2, r_rstd, r_rstd)
            p.chain_bar("vector", v, i)
            return v.tensor_mul(r_t2, r_t2, r_var)
        emit(p, "vector", row_c, ROW, wait=[(ROWA, ROWA.n)])
        emit(p, "scalar", lambda sc: sc.activation(out=r_t2, in_=r_t2, func=AF.Sqrt,
                                                   bias=consts[0:1, 1:2]),
             ROWA, wait=[(ROW, ROW.n)])
        def row_e(v):
            i = v.reciprocal(r_t2, r_t2)
            p.chain_bar("vector", v, i)
            i = v.tensor_mul(r_s, r_rstd, r_t2)
            p.chain_bar("vector", v, i)
            return v.tensor_mul(r_ms, r_mu, r_s)
        emit(p, "vector", row_e, ROW, wait=[(ROWA, ROWA.n)])
        t_rowe = ROW.n
        emit(p, "tensor", lambda t: t.matmul(pb[2][:], ones_f[:], r_s, start=True, stop=True),
             NRM, wait=[(ROW, t_rowe), (XQ, XQ.n)])
        for c in range(8):
            bank = 3 + (c % 2)
            w3 = [(NRM, NRM.n)] if c < 2 else [(XQ, XQ.n - 1)]
            emit(p, "tensor", lambda t, c=c, bank=bank: t.matmul(
                pb[bank][:], nw_sb[:, c * 128:(c + 1) * 128], r_ms, start=True, stop=True),
                NRM, wait=w3)
            t_r1 = NRM.n
            def xn_c(v, c=c, bank=bank):
                i = v.tensor_mul(tmp32[:, c % 2, :], xt[:, c, :], pb[2][:])
                p.chain_bar("vector", v, i)
                return v.scalar_tensor_tensor(out=xn_bf[:, c, :], in0=tmp32[:, c % 2, :],
                                              scalar=nwc[:, c:c + 1], in1=pb[bank][:],
                                              op0=ALU.mult, op1=ALU.subtract)
            emit(p, "vector", xn_c, XQ, wait=[(NRM, t_r1)])
        t_xn = XQ.n
        hist["xq"].append(t_xn)
        for ec in range(4):
            bank = 4 + ec % 2
            w4 = [(XQ, t_xn)]
            if PEV.n >= 2:
                w4.append((PEV, PEV.n - 1))
            def inmm(t, ec=ec, bank=bank):
                for c in range(8):
                    last = t.matmul(pb[bank][:], inw_bf[:, c, ec * 128:(ec + 1) * 128],
                                    xn_bf[:, c, :], start=(c == 0), stop=(c == 7))
                return last
            emit(p, "tensor", inmm, PRJ, wait=w4)
            t_pi = PRJ.n
            def inev(v, ec=ec, bank=bank):
                i = v.tensor_copy(xit[:, ec, 0:4], xit[:, ec, 512:516])
                p.chain_bar("vector", v, i)
                return v.tensor_copy(xit[:, ec, 4:516], pb[bank][:])
            emit(p, "vector", inev, PEV, wait=[(PRJ, t_pi), (CNV, CNV.n)])
            bank2 = 6 + ec % 2
            def skmm(t, ec=ec, bank2=bank2):
                for c in range(8):
                    last = t.matmul(pb[bank2][:], skw_bf[:, c, ec * 128:(ec + 1) * 128],
                                    xn_bf[:, c, :], start=(c == 0), stop=(c == 7))
                return last
            w5 = [(PRJ, PRJ.n)]
            if PSK.n >= 2:
                w5.append((PSK, PSK.n - 1))
            emit(p, "tensor", skmm, PRJ, wait=w5)
            t_ps = PRJ.n
            emit(p, "scalar", lambda sc, ec=ec, bank2=bank2: sc.activation(
                out=y3b[:, ec, :], in_=pb[bank2][:], func=AF.Silu), PSK,
                wait=[(PRJ, t_ps), (WR, WR.n)])
        t_pev = PEV.n
        t_psk = PSK.n
        hist["prj"].append(PRJ.n)
        emit_dma(p, "sync", skd.ap()[:, t0:t0 + 512].rearrange("(c p) t -> p c t", p=128),
                 y3b[:], WR, wait=[(PSK, t_psk)])
        for ec in range(4):
            def conv_f(v, ec=ec):
                i = v.tensor_scalar(out=tmp32[:, 0, :], in0=xit[:, ec, 4:516],
                                    scalar1=cw[:, ec * 4 + 3:ec * 4 + 4],
                                    scalar2=cb[:, ec:ec + 1], op0=ALU.mult, op1=ALU.add)
                p.chain_bar("vector", v, i)
                for k in range(1, 4):
                    i = v.scalar_tensor_tensor(out=tmp32[:, 0, :],
                                               in0=xit[:, ec, 4 - k:516 - k],
                                               scalar=cw[:, ec * 4 + 3 - k:ec * 4 + 4 - k],
                                               in1=tmp32[:, 0, :], op0=ALU.mult, op1=ALU.add)
                    if k < 3:
                        p.chain_bar("vector", v, i)
                return i
            emit(p, "vector", conv_f, CNV, wait=[(PEV, t_pev), (CNE, CNE.n)])
            t_cv = CNV.n
            emit(p, "scalar", lambda sc, ec=ec: sc.activation(
                out=xct[:, ec, :], in_=tmp32[:, 0, :], func=AF.Silu), CNE,
                wait=[(CNV, t_cv)])
        t_cne = CNE.n
        hist["cnv"].append(CNV.n)
        emit_dma(p, "sync", xcd.ap()[:, t0:t0 + 512].rearrange("(c p) t -> p c t", p=128),
                 xct[:], WR, wait=[(CNE, t_cne)])
        def dtmm(t):
            for ec in range(4):
                t.matmul(pb[0][0:64, :], w1_bf[:, ec, :], xct[:, ec, :],
                         start=(ec == 0), stop=(ec == 3))
            for ec in range(4):
                last = t.matmul(pb[1][0:32, :], wbc_bf[:, ec, :], xct[:, ec, :],
                                start=(ec == 0), stop=(ec == 3))
            return last
        emit(p, "tensor", dtmm, DBC, wait=[(CNE, t_cne), (DBE, DBE.n)])
        t_dm = DBC.n
        def dtev(v):
            i = v.tensor_copy(tmp32[:, 0, 0:512], pb[0][0:64, :].rearrange("a b -> a b"))
            return i
        def dtev2(v):
            i = v.tensor_copy(xsq[0:64, :], pb[0][0:64, :])
            p.chain_bar("vector", v, i)
            return v.tensor_copy(xsq[64:96, :] if False else tmp32[0:32, 1, :], pb[1][0:32, :])
        emit(p, "vector", dtev2, DBE, wait=[(DBC, t_dm), (WR, WR.n)])
        t_de = DBE.n
        emit_dma(p, "sync", dtbc_my.ap()[0:64, t0:t0 + 512], xsq[0:64, :], WR,
                 wait=[(DBE, t_de)])
        emit_dma(p, "sync", dtbc_my.ap()[64:96, t0:t0 + 512], tmp32[0:32, 1, :], WR,
                 wait=[(DBE, t_de)])
    t_pa = WR.n

    emit(p, "gpsimd", lambda g: g.collective_compute(
        "AllReduce", ALU.add, replica_groups=[[0, 1, 2, 3], [4, 5, 6, 7]],
        ins=[dtbc_my.ap().opt()], outs=[dtbc.ap().opt()]), CC, wait=[(WR, t_pa)])
    t_cc1 = CC.n
    PIECE = min(4096, S_)
    for j in range(ceil_div(S_, PIECE)):
        t_dt = emit_dma(p, "sync",
                        xt[0:64, :].rearrange("r c q -> r (c q)")[:, 0:PIECE]
                        .rearrange("r (c q) -> r c q", q=512),
                        dtbc.ap()[0:64, j * PIECE:(j + 1) * PIECE]
                        .rearrange("r (c q) -> r c q", q=512), DTL,
                        wait=[(CC, t_cc1), (CAST, CAST.n)])
        emit(p, "vector", lambda v, j=j: v.tensor_copy(
            dtbf[:, j * PIECE:(j + 1) * PIECE].rearrange("r (c q) -> r c q", q=512),
            xt[0:64, :].rearrange("r c q -> r (c q)")[:, 0:PIECE]
            .rearrange("r (c q) -> r c q", q=512)),
            CAST, wait=[(DTL, t_dt)])
    t_dtb = CAST.n

    ndve = 0
    for tc in range(NTI):
        t0 = tc * 512
        par = tc % 2
        t_bc = emit_dma(p, "sync", bcb[:],
                        dtbc.ap()[64:96, t0:t0 + 512].unsqueeze(0)
                        .broadcast_to([128, 32, 512]), BCL[par],
                        wait=[(GPU, GPU.n), (SCN, SCN.n)])
        for grp in range(4):
            w6 = [(CAST, t_dtb), (DLE, DLE.n)]
            emit(p, "tensor", lambda t, grp=grp, t0=t0: t.matmul(
                pb[6][:], wd2_bf[:, grp, :], dtbf[:, t0:t0 + 512], start=True, stop=True),
                DLM, wait=w6)
            t_dlm = DLM.n
            def spf(sc, grp=grp):
                i = sc.activation(out=sp1[:], in_=pb[6][:], func=AF.Abs,
                                  bias=d2b[:, grp:grp + 1])
                p.chain_bar("scalar", sc, i)
                i = sc.activation(out=sp1[:], in_=sp1[:], func=AF.Exp, scale=-1.0)
                p.chain_bar("scalar", sc, i)
                i = sc.activation(out=sp1[:], in_=sp1[:], func=AF.Ln, bias=consts[:, 2:3])
                p.chain_bar("scalar", sc, i)
                return sc.activation(out=delta[:], in_=pb[6][:], func=AF.Relu,
                                     bias=d2b[:, grp:grp + 1])
            emit(p, "scalar", spf, DLE, wait=[(DLM, t_dlm), (SCN, SCN.n)])
            t_sp = DLE.n
            t_xc = emit_dma(p, "sync", xcs[:],
                            xcd.ap()[grp * 128:(grp + 1) * 128, t0:t0 + 512], XSL[0],
                            wait=[(GPU, GPU.n), (SCN, SCN.n), (YY, YY.n)])
            t_sk = emit_dma(p, "sync", sks[:],
                            skd.ap()[grp * 128:(grp + 1) * 128, t0:t0 + 512], XSL[1],
                            wait=[(YY, YY.n)])
            def dfin(v, t_sp=t_sp, t_xc=t_xc):
                v.wait_ge(DLE.h, t_sp)
                v.wait_ge(XSL[0].h, t_xc)
                i = v.tensor_add(delta[:], delta[:], sp1[:])
                p.chain_bar("vector", v, i)
                return v.tensor_mul(dtx[:], delta[:], xcs[:])
            emit(p, "vector", dfin, SCN)
            t_df = SCN.n
            for n in range(16):
                col = grp * 16 + n
                emit(p, "scalar", lambda sc, col=col: sc.activation(
                    out=a_t[:], in_=delta[:], func=AF.Exp, scale=acol[:, col:col + 1]),
                    DLE, wait=[(SCN, SCN.n), (GPU, GPU.n)])
                t_a = DLE.n
                def umul(g, n=n, t_df=t_df, t_bc=t_bc, par=par):
                    g.wait_ge(SCN.h, t_df)
                    g.wait_ge(BCL[par].h, t_bc)
                    g.wait_ge(SCN.h, SCN.n)
                    return g.tensor_mul(u_t[:], dtx[:], bcb[:, n, :])
                emit(p, "gpsimd", lambda g, n=n, t_df=t_df, t_bc=t_bc, par=par,
                     w_scn=SCN.n: (
                    g.wait_ge(SCN.h, t_df),
                    g.wait_ge(BCL[par].h, t_bc),
                    g.wait_ge(SCN.h, w_scn),
                    g.tensor_mul(u_t[:], dtx[:], bcb[:, n, :]))[-1], GPU)
                t_u = GPU.n
                def scn_f(v, col=col, n=n, t_a=t_a, t_u=t_u):
                    v.wait_ge(DLE.h, t_a)
                    v.wait_ge(GPU.h, t_u)
                    i = v.tensor_tensor_scan(out=h_t[:], data0=a_t[:], data1=u_t[:],
                                             initial=hcol[:, col:col + 1],
                                             op0=ALU.mult, op1=ALU.add)
                    p.chain_bar("vector", v, i)
                    i = v.tensor_copy(hcol[:, col:col + 1], h_t[:, 511:512])
                    p.chain_bar("vector", v, i)
                    if n == 0:
                        i = v.tensor_mul(yac[:], h_t[:], bcb[:, 16 + n, :])
                    else:
                        i = v.tensor_mul(tmp32[:, 0, :], h_t[:], bcb[:, 16 + n, :])
                        p.chain_bar("vector", v, i)
                        i = v.tensor_add(yac[:], yac[:], tmp32[:, 0, :])
                    return i
                emit(p, "vector", scn_f, SCN)
            def yfin(v, grp=grp, t_sk=t_sk):
                v.wait_ge(XSL[1].h, t_sk)
                i = v.scalar_tensor_tensor(out=yac[:], in0=xcs[:], scalar=wdc[:, grp:grp + 1],
                                           in1=yac[:], op0=ALU.mult, op1=ALU.add)
                p.chain_bar("vector", v, i)
                return v.tensor_mul(y3b[:, grp, :], yac[:], sks[:])
            emit(p, "vector", yfin, YY, wait=[(OPM, OPM.n), (SCN, SCN.n)])
        t_yy = YY.n
        for dc in range(8):
            bank = 4 + dc % 2
            w7 = [(YY, t_yy)]
            if dc >= 2:
                w7.append((OPE, OPE.n - 1))
            def opmm(t, dc=dc, bank=bank):
                for grp in range(4):
                    last = t.matmul(pb[bank][:],
                                    outw_bf[:, grp * 1024 + dc * 128: grp * 1024 + dc * 128 + 128],
                                    y3b[:, grp, :], start=(grp == 0), stop=(grp == 3))
                return last
            emit(p, "tensor", opmm, OPM, wait=w7)
            t_om = OPM.n
            emit(p, "scalar", lambda sc, dc=dc, bank=bank: sc.activation(
                out=outev[:, dc, :], in_=pb[bank][:], func=AF.Copy), OPE,
                wait=[(OPM, t_om), (WR, WR.n)])
        t_oe = OPE.n
        emit_dma(p, "sync", po_my.ap()[:, t0:t0 + 512].rearrange("(c p) t -> p c t", p=128),
                 outev[:], WR, wait=[(OPE, t_oe)])
    t_scan_done = WR.n

    emit(p, "gpsimd", lambda g: g.collective_compute(
        "AllReduce", ALU.add, replica_groups=[[0, 1, 2, 3], [4, 5, 6, 7]],
        ins=[po_my.ap().opt()], outs=[po.ap().opt()]), CC, wait=[(WR, t_scan_done)])
    t_cc2 = CC.n

    for ti in range(NTI):
        t0 = ti * 512
        t_x = emit_dma(p, "sync", xt[:],
                       x4_fm[:, t0:t0 + 512].rearrange("(c p) t -> p c t", p=128), LX,
                       wait=[(CC, t_cc2), (XQ, XQ.n), (STM, STM.n), (OD, OD.n)])
        t_p = emit_dma(p, "sync", inw_bf[:].rearrange("p c q -> p (c q)").bitcast(F32).rearrange("p (c q) -> p c q", q=512),
                       po.ap()[:, t0:t0 + 512].rearrange("(c p) t -> p c t", p=128)[:, 0:4, :],
                       POL, wait=[(XQ, XQ.n)])
        t_p2 = emit_dma(p, "sync", skw_bf[:].rearrange("p c q -> p (c q)").bitcast(F32).rearrange("p (c q) -> p c q", q=512),
                        po.ap()[:, t0:t0 + 512].rearrange("(c p) t -> p c t", p=128)[:, 4:8, :],
                        POL)
        def ssum2(t):
            for c in range(8):
                last = t.matmul(pb[0][0:1, :], ones_cf[:], xt[:, c, :],
                                start=(c == 0), stop=(c == 7))
            return last
        emit(p, "tensor", ssum2, STM, wait=[(LX, LX.n), (ROW, ROW.n)])
        for c in range(8):
            w2 = [(LX, LX.n)] if c == 0 else [(STM, STM.n)]
            emit(p, "scalar", lambda sc, c=c: sc.activation(out=xsq[:], in_=xt[:, c, :],
                                                            func=AF.Square), SQ, wait=w2)
            emit(p, "tensor", lambda t, c=c: t.matmul(pb[1][0:1, :], ones_cf[:], xsq[:],
                                                      start=(c == 0), stop=(c == 7)),
                 STM, wait=[(SQ, SQ.n)])
        t_stm = STM.n
        def row_a2(v):
            v.tensor_single_scalar(out=r_mu, in_=pb[0][0:1, :], scalar=1.0 / D, op=ALU.mult)
            i = v.tensor_single_scalar(out=r_var, in_=pb[1][0:1, :], scalar=1.0 / D, op=ALU.mult)
            p.chain_bar("vector", v, i)
            i = v.tensor_mul(r_t, r_mu, r_mu)
            p.chain_bar("vector", v, i)
            return v.tensor_sub(r_var, r_var, r_t)
        emit(p, "vector", row_a2, ROW, wait=[(STM, t_stm)])
        emit(p, "scalar", lambda sc: sc.activation(out=r_t, in_=r_var, func=AF.Sqrt,
                                                   bias=consts[0:1, 0:1]),
             ROWA, wait=[(ROW, ROW.n)])
        def row_c2(v):
            i = v.reciprocal(r_rstd, r_t)
            p.chain_bar("vector", v, i)
            i = v.tensor_mul(r_ms, r_mu, r_rstd)
            p.chain_bar("vector", v, i)
            return v.tensor_single_scalar(out=r_s, in_=r_rstd, scalar=1.0, op=ALU.add)
        emit(p, "vector", row_c2, ROW, wait=[(ROWA, ROWA.n)])
        t_row2 = ROW.n
        emit(p, "tensor", lambda t: t.matmul(pb[2][:], ones_f[:], r_s, start=True, stop=True),
             NRM, wait=[(ROW, t_row2), (XQ, XQ.n)])
        emit(p, "tensor", lambda t: t.matmul(pb[3][:], ones_f[:], r_ms, start=True, stop=True),
             NRM)
        t_nrm = NRM.n
        def x5f(v, t_p=t_p, t_p2=t_p2):
            v.wait_ge(POL.h, t_p2)
            for c in range(8):
                i = v.tensor_mul(tmp32[:, c % 2, :], xt[:, c, :], pb[2][:])
                p.chain_bar("vector", v, i)
                i = v.tensor_sub(tmp32[:, c % 2, :], tmp32[:, c % 2, :], pb[3][:])
                p.chain_bar("vector", v, i)
                if c < 4:
                    src_po = inw_bf[:].rearrange("p c q -> p (c q)").bitcast(F32).rearrange("p (cc q) -> p cc q", q=512)[:, c, :]
                else:
                    src_po = skw_bf[:].rearrange("p c q -> p (c q)").bitcast(F32).rearrange("p (cc q) -> p cc q", q=512)[:, c - 4, :]
                i = v.tensor_add(xt[:, c, :], tmp32[:, c % 2, :], src_po)
                if c < 7:
                    p.chain_bar("vector", v, i)
            return i
        emit(p, "vector", x5f, XQ, wait=[(NRM, t_nrm)])
        t_x5 = XQ.n
        emit_dma(p, "gpsimd", y[:, t0:t0 + 512].rearrange("(c p) t -> p c t", p=128),
                 xt[:], OD, wait=[(XQ, t_x5)])
    emit(p, "gpsimd", lambda g: g.nop().then_inc(p.sem("fin").h, 1), None, wait=[(OD, OD.n)])
    return p.build()



def build_l1_ffn(TOK=2048):
    p = Prog("l1ffn")
    nc = p.nc
    NTI = TOK // 512

    x_fm = nc.dram_tensor("x_fm", [D, TOK], F32, kind="ExternalInput").ap()
    w1T = nc.dram_tensor("w1T", [D, 4096], F32, kind="ExternalInput").ap()
    w2T = nc.dram_tensor("w2T", [4096, D], F32, kind="ExternalInput").ap()
    b1col = nc.dram_tensor("b1col", [128, 32], F32, kind="ExternalInput").ap()
    b2col = nc.dram_tensor("b2col", [128, 8], F32, kind="ExternalInput").ap()
    y = nc.dram_tensor("y", [D, TOK], F32, kind="ExternalOutput").ap()
    w1b = nc.dram_tensor("w1b", [D, 4096], BF16, kind="Internal")
    w2b = nc.dram_tensor("w2b", [4096, D], BF16, kind="Internal")

    xt = p.sb("xt", [128, 8, 512], F32)
    x5t = p.sb("x5t", [128, 8, 512], F32)
    xfb = p.sb("xfb", [128, 8, 512], BF16)
    xsq = p.sb("xsq", [128, 512], F32)
    tmp32 = p.sb("tmp32", [128, 2, 512], F32)
    rows = p.sb("rows", [1, 6 * 512], F32)
    consts = p.sb("consts", [128, 4], F32)
    ones_cf = p.sb("ones_cf", [128, 1], F32)
    ones_f = p.sb("ones_f", [1, 128], F32)
    wst = p.sb("wst", [128, 4096], F32)
    wtb = p.sb("wtb", [128, 4096], BF16)
    absr = p.sb("absr", [128, 1], F32)
    acc1 = p.sb("acc1", [128, 1], F32)
    thr = p.sb("thr", [128, 1], F32)
    scl = p.sb("scl", [1, 2], F32)
    w1s = [p.sb(f"w1s{i}", [128, 8, 128], BF16) for i in range(2)]
    w2s = [p.sb(f"w2s{i}", [128, 32, 128], BF16) for i in range(2)]
    hbf = p.sb("hbf", [128, 32, 512], BF16)
    b1c = p.sb("b1c", [128, 32], F32)
    b2c = p.sb("b2c", [128, 8], F32)
    pb = [p.ps(f"pb{i}", [128, 512], F32) for i in range(8)]

    LD = p.sem("ld"); LDC = p.sem("ldc"); LX = p.sem("lx"); CAST = p.sem("cast")
    STM = p.sem("stm"); SQ = p.sem("sq"); ROW = p.sem("row"); ROWA = p.sem("rowa")
    NRM = p.sem("nrm"); XQ = p.sem("xq"); TRN = p.sem("trn"); TPE = p.sem("tpe")
    WW = p.sem("ww"); FFM = p.sem("ffm"); GEL = p.sem("gel"); FF2 = p.sem("ff2")
    FFE = p.sem("ffe"); OD = p.sem("od")
    LWA = [p.sem("lwa0"), p.sem("lwa1")]; LWB = [p.sem("lwb0"), p.sem("lwb1")]
    WT = [p.sem("wt0"), p.sem("wt1")]

    r_mu = rows[:, 0:512]; r_var = rows[:, 512:1024]; r_rstd = rows[:, 1024:1536]
    r_t = rows[:, 1536:2048]; r_ms = rows[:, 2048:2560]

    emit_dma(p, "sync", b1c[:], b1col, LDC)
    emit_dma(p, "sync", b2c[:], b2col, LDC)
    t_ldc = LDC.n
    def init_c(v):
        v.memset(consts[:, 0:1], 1e-5)
        v.memset(ones_cf[:], 1.0)
        i = v.memset(ones_f[:], 1.0)
        p.chain_bar("vector", v, i)
        return v.memset(acc1[:], 0.0)
    emit(p, "vector", init_c, CAST, wait=[(LDC, t_ldc)])
    t_init = CAST.n

    def ternarize(wT, wout, nrows, ncols):
        ntile = nrows // 128
        emit(p, "vector", lambda v: v.memset(acc1[:], 0.0), TRN,
             wait=[(TRN, TRN.n), (CAST, CAST.n), (TPE, TPE.n)])
        for j in range(ntile):
            t_w = emit_dma(p, "sync", wst[:, 0:ncols], wT[j * 128:(j + 1) * 128, :],
                           WT[j % 2], wait=[(TRN, TRN.n)])
            def absf(v, t_w=t_w, jp=j % 2, ncols=ncols):
                v.wait_ge(WT[jp].h, t_w)
                i = v.tensor_reduce(out=absr[:], in_=wst[:, 0:ncols], axis=AX.X, op=ALU.add,
                                    apply_absolute_value=True)
                p.chain_bar("vector", v, i)
                return v.tensor_add(acc1[:], acc1[:], absr[:])
            emit(p, "vector", absf, TRN)
        t_abs = TRN.n
        emit(p, "tensor", lambda t: t.matmul(pb[0][0:1, 0:1], acc1[:], ones_cf[:],
                                             start=True, stop=True), TPE,
             wait=[(TRN, t_abs)])
        t_tm = TPE.n
        def sclf(v, nrows=nrows, ncols=ncols):
            i = v.tensor_single_scalar(out=scl[:, 0:1], in_=pb[0][0:1, 0:1],
                                       scalar=1.0 / (nrows * ncols), op=ALU.mult)
            p.chain_bar("vector", v, i)
            i = v.tensor_scalar_max(scl[:, 0:1], scl[:, 0:1], 1e-8)
            p.chain_bar("vector", v, i)
            return v.tensor_single_scalar(out=scl[:, 1:2], in_=scl[:, 0:1], scalar=0.5,
                                          op=ALU.mult)
        emit(p, "vector", sclf, TRN, wait=[(TPE, t_tm)])
        emit(p, "tensor", lambda t: t.matmul(pb[0][:, 0:1], ones_f[:], scl[:, 1:2],
                                             start=True, stop=True), TPE,
             wait=[(TRN, TRN.n)])
        emit(p, "scalar", lambda sc: sc.activation(out=thr[:], in_=pb[0][:, 0:1],
                                                   func=AF.Copy), TRN, wait=[(TPE, TPE.n)])
        t_thr = TRN.n
        for j in range(ntile):
            t_w = emit_dma(p, "sync", wst[:, 0:ncols], wT[j * 128:(j + 1) * 128, :],
                           WT[j % 2], wait=[(TRN, TRN.n), (WW, WW.n)])
            def trnf3(v, t_w=t_w, jp=j % 2, ncols=ncols):
                v.wait_ge(WT[jp].h, t_w)
                i = v.tensor_single_scalar(out=tmp32[:, 0, 0:1], in_=thr[:, 0:1],
                                           scalar=-1.0, op=ALU.mult)
                p.chain_bar("vector", v, i)
                i = v.tensor_single_scalar(out=wtb[:, 0:ncols], in_=wst[:, 0:ncols],
                                           scalar=thr[:, 0:1], op=ALU.is_gt)
                p.chain_bar("vector", v, i)
                i = v.tensor_single_scalar(out=wst[:, 0:ncols], in_=wst[:, 0:ncols],
                                           scalar=tmp32[:, 0, 0:1], op=ALU.is_lt)
                p.chain_bar("vector", v, i)
                return v.tensor_sub(wtb[:, 0:ncols], wtb[:, 0:ncols], wst[:, 0:ncols])
            emit(p, "vector", trnf3, TRN)
            t_t = TRN.n
            emit_dma(p, "sync", wout[j * 128:(j + 1) * 128, :], wtb[:, 0:ncols], WW,
                     wait=[(TRN, t_t)])
    ternarize(w1T, w1b.ap(), D, 4096)
    ternarize(w2T, w2b.ap(), 4096, D)
    t_tern = WW.n

    for g in range(NTI):
        t0 = g * 512
        w = [(CAST, t_init)] if g == 0 else [(XQ, XQ.n), (FFE, FFE.n)]
        t_x = emit_dma(p, "sync", xt[:],
                       x_fm[:, t0:t0 + 512].rearrange("(c p) t -> p c t", p=128), LX, wait=w)
        def ssum(t):
            for c in range(8):
                last = t.matmul(pb[0][0:1, :], ones_cf[:], xt[:, c, :],
                                start=(c == 0), stop=(c == 7))
            return last
        emit(p, "tensor", ssum, STM, wait=[(LX, t_x), (ROW, ROW.n)])
        for c in range(8):
            w2_ = [(LX, t_x)] if c == 0 else [(STM, STM.n)]
            emit(p, "scalar", lambda sc, c=c: sc.activation(out=xsq[:], in_=xt[:, c, :],
                                                            func=AF.Square), SQ, wait=w2_)
            emit(p, "tensor", lambda t, c=c: t.matmul(pb[1][0:1, :], ones_cf[:], xsq[:],
                                                      start=(c == 0), stop=(c == 7)),
                 STM, wait=[(SQ, SQ.n)])
        t_stm = STM.n
        def row_a(v):
            v.tensor_single_scalar(out=r_mu, in_=pb[0][0:1, :], scalar=1.0 / D, op=ALU.mult)
            i = v.tensor_single_scalar(out=r_var, in_=pb[1][0:1, :], scalar=1.0 / D, op=ALU.mult)
            p.chain_bar("vector", v, i)
            i = v.tensor_mul(r_t, r_mu, r_mu)
            p.chain_bar("vector", v, i)
            return v.tensor_sub(r_var, r_var, r_t)
        emit(p, "vector", row_a, ROW, wait=[(STM, t_stm)])
        emit(p, "scalar", lambda sc: sc.activation(out=r_t, in_=r_var, func=AF.Sqrt,
                                                   bias=consts[0:1, 0:1]),
             ROWA, wait=[(ROW, ROW.n)])
        def row_c(v):
            i = v.reciprocal(r_rstd, r_t)
            p.chain_bar("vector", v, i)
            return v.tensor_mul(r_ms, r_mu, r_rstd)
        emit(p, "vector", row_c, ROW, wait=[(ROWA, ROWA.n)])
        t_rowe = ROW.n
        emit(p, "tensor", lambda t: t.matmul(pb[2][:], ones_f[:], r_rstd, start=True,
                                             stop=True), NRM, wait=[(ROW, t_rowe), (XQ, XQ.n)])
        emit(p, "tensor", lambda t: t.matmul(pb[3][:], ones_f[:], r_ms, start=True,
                                             stop=True), NRM)
        t_nrm = NRM.n
        def xn_f(v):
            for c in range(8):
                i = v.tensor_mul(tmp32[:, c % 2, :], xt[:, c, :], pb[2][:])
                p.chain_bar("vector", v, i)
                i = v.tensor_sub(xfb[:, c, :], tmp32[:, c % 2, :], pb[3][:])
                if c < 7:
                    p.chain_bar("vector", v, i)
            return i
        emit(p, "vector", xn_f, XQ, wait=[(NRM, t_nrm)])
        t_xn = XQ.n
        def rescopy(v):
            for c in range(8):
                i = v.tensor_copy(x5t[:, c, :], xt[:, c, :])
            return i
        emit(p, "vector", rescopy, XQ)
        t_res = XQ.n
        for f in range(32):
            wb = f % 2
            t_w1 = emit_dma(p, "sync", w1s[wb][:],
                            w1b.ap().rearrange("(c p) q -> p c q", p=128)
                            [:, :, f * 128:(f + 1) * 128], LWA[wb],
                            wait=[(WW, t_tern), (FFM, max(0, FFM.n - 1))])
            bank = 4 + (f % 2)
            def fc1_mm(t, f=f, wb=wb, bank=bank, t_w1=t_w1):
                t.wait_ge(LWA[wb].h, t_w1)
                for c in range(8):
                    last = t.matmul(pb[bank][:], w1s[wb][:, c, :], xfb[:, c, :],
                                    start=(c == 0), stop=(c == 7))
                return last
            w9 = [(XQ, t_xn)]
            if f >= 2:
                w9.append((GEL, GEL.n - 1))
            emit(p, "tensor", fc1_mm, FFM, wait=w9)
            t_f1 = FFM.n
            emit(p, "scalar", lambda sc, f=f, bank=bank: sc.activation(
                out=hbf[:, f, :], in_=pb[bank][:], func=AF.Gelu,
                bias=b1c[:, f:f + 1]), GEL, wait=[(FFM, t_f1)])
        t_gel = GEL.n
        for dc in range(8):
            wb = dc % 2
            t_w2 = emit_dma(p, "sync", w2s[wb][:],
                            w2b.ap().rearrange("(hc p) q -> p hc q", p=128)
                            [:, :, dc * 128:(dc + 1) * 128], LWB[wb],
                            wait=[(WW, t_tern), (FF2, max(0, FF2.n - 1))])
            bank = 6 + (dc % 2)
            def fc2_mm(t, dc=dc, wb=wb, bank=bank, t_w2=t_w2):
                t.wait_ge(LWB[wb].h, t_w2)
                for hc in range(32):
                    last = t.matmul(pb[bank][:], w2s[wb][:, hc, :], hbf[:, hc, :],
                                    start=(hc == 0), stop=(hc == 31))
                return last
            w10 = [(GEL, t_gel)]
            if dc >= 2:
                w10.append((FFE, FFE.n - 1))
            emit(p, "tensor", fc2_mm, FF2, wait=w10)
            t_f2 = FF2.n
            def yev(v, dc=dc, bank=bank):
                i = v.tensor_scalar(out=tmp32[:, dc % 2, :], in0=pb[bank][:],
                                    scalar1=b2c[:, dc:dc + 1], scalar2=None, op0=ALU.add)
                p.chain_bar("vector", v, i)
                return v.tensor_add(x5t[:, dc, :], x5t[:, dc, :], tmp32[:, dc % 2, :])
            wy = [(FF2, t_f2), (XQ, t_res)]
            if FFE.n >= 1:
                wy.append((FFE, FFE.n - 1))
            emit(p, "vector", yev, FFE, wait=wy)
        t_ffe = FFE.n
        emit_dma(p, "gpsimd", y[:, t0:t0 + 512].rearrange("(c p) t -> p c t", p=128),
                 x5t[:], OD, wait=[(FFE, t_ffe)])
    emit(p, "gpsimd", lambda g_: g_.nop().then_inc(p.sem("fin").h, 1), None,
         wait=[(OD, OD.n)])
    return p.build()



def build_l1_dn(TOK=2048, S_=S, dbg_nch=None, dbg_stage=99, dbg_cb=4):
    p = Prog("l1dn")
    nc = p.nc
    HB = 64
    TH = TOK + HB
    CH = 128
    NCH = S_ // CH
    CPR = TOK // CH
    NRK = S_ // TOK
    FT = 256
    NFT = TOK // FT
    FIROFF = {3: 1, 7: 4, 15: 11, 31: 26}

    xh_fm = nc.dram_tensor("xh_fm", [D, TH], F32, kind="ExternalInput").ap()
    x6_fm = nc.dram_tensor("x6_fm", [D, TOK], F32, kind="ExternalInput").ap()
    wq = nc.dram_tensor("wq", [D, D], F32, kind="ExternalInput").ap()
    wk = nc.dram_tensor("wk", [D, D], F32, kind="ExternalInput").ap()
    wv = nc.dram_tensor("wv", [D, D], F32, kind="ExternalInput").ap()
    wb4 = nc.dram_tensor("wb4", [D, 4], F32, kind="ExternalInput").ap()
    cw3 = nc.dram_tensor("cw3", [128, 3 * 32], F32, kind="ExternalInput").ap()
    firw = nc.dram_tensor("firw", [128, 8 * 57], F32, kind="ExternalInput").ap()
    gw1h = nc.dram_tensor("gw1h", [D, 2048], F32, kind="ExternalInput").ap()
    gw1s = nc.dram_tensor("gw1s", [28, 2048], F32, kind="ExternalInput").ap()
    gb1c = nc.dram_tensor("gb1c", [128, 16], F32, kind="ExternalInput").ap()
    gw2 = nc.dram_tensor("gw2", [2048, 28], F32, kind="ExternalInput").ap()
    gb2r = nc.dram_tensor("gb2r", [28, 1], F32, kind="ExternalInput").ap()
    nwc2 = nc.dram_tensor("nwc2", [128, 2], F32, kind="ExternalInput").ap()
    wo = nc.dram_tensor("wo", [D, D], F32, kind="ExternalInput").ap()
    masks4 = nc.dram_tensor("masks4", [128, 5 * 128], BF16, kind="ExternalInput").ap()
    scol28 = nc.dram_tensor("scol28", [128, 28 * 28], BF16, kind="ExternalInput").ap()
    gselc = nc.dram_tensor("gselc", [28, 4], F32, kind="ExternalInput").ap()
    ridx_d = nc.dram_tensor("ridx", [128, (S_ // CH) * 7], U32,
                            kind="ExternalInput").ap()
    didx_d = nc.dram_tensor("didx", [128, (TOK // FT) * 8], U32,
                            kind="ExternalInput").ap()
    y = nc.dram_tensor("y", [D, TOK], F32, kind="ExternalOutput").ap()

    qkv_h = [nc.dram_tensor(f"qkvh{j}", [769, TOK], BF16, kind="Internal")
             for j in range(4)]
    qkvh_all = nc.dram_tensor("qkvh_all", [4 * NRK * 769, TOK], BF16, kind="Internal")
    do_my = nc.dram_tensor("do_my", [256, S_], BF16, kind="Internal")
    do_all = nc.dram_tensor("do_all", [NRK * 256, S_], BF16, kind="Internal")
    wob = nc.dram_tensor("wob", [D, D], BF16, kind="Internal")
    wts_d = nc.dram_tensor("wts_d", [32, 2 * FT], F32, kind="Internal")
    sol3_d = nc.dram_tensor("sol3_d", [32, 512], BF16, kind="Internal")

    xt = p.sb("xt", [128, 8, 512], F32)
    xn_bf = p.sb("xn_bf", [128, 8, TH], BF16)
    vc = p.sb("vc", [128, 8, TH], BF16)
    xsq = p.sb("xsq", [128, 512], F32)
    tmp32 = p.sb("tmp32", [128, 2, 512], F32)
    rows = p.sb("rows", [1, 2560], F32)
    consts = p.sb("consts", [128, 4], F32)
    ones_cf = p.sb("ones_cf", [128, 1], F32)
    ones_cb = p.sb("ones_cb", [128, 1], BF16)
    ones_f = p.sb("ones_f", [1, 128], F32)
    wqk_bf = p.sb("wqk_bf", [128, 8, 2048], BF16)
    wv_bf = p.sb("wv_bf", [128, 8, 1024], BF16)
    wb_bf = p.sb("wb_bf", [128, 8, 4], BF16)
    cw_s = p.sb("cw_s", [128, 3, 32], F32)
    firw_s = p.sb("firw_s", [128, 8 * 57], F32)
    msk_s = p.sb("msk_s", [128, 5, 128], BF16)
    scol_s = p.sb("scol_s", [128, 28 * 28], BF16)
    gsel_s = p.sb("gsel_s", [28, 4], F32)
    gb1c_s = p.sb("gb1c_s", [128, 16], F32)
    gb2_s = p.sb("gb2_s", [28, 1], F32)
    nw_s = p.sb("nw_s", [128, 2], F32)
    gw1sb = p.sb("gw1sb", [28, 2048], BF16)
    gw2b = p.sb("gw2b", [128, 16, 28], BF16)
    ridx = p.sb("ridx_s", [128, NCH * 7], U32)
    didx = p.sb("didx_s", [128, NFT * 8], U32)
    pjt = p.sb("pjt", [128, 8, 516], F32)
    cvt = p.sb("cvt", [128, 8, 512], BF16)
    tails = p.sb("tails", [128, 3, 8, 4], F32)
    bt4 = p.sb("bt4", [4, 512], BF16)
    wbc = p.sb("wbc", [128, 4, FT], F32)

    pjtF = pjt[:].rearrange("p a b -> p (a b)")
    pjtB = pjtF.bitcast(BF16)
    qn = pjtB[:, 0:512].rearrange("p (a g t) -> p a g t", a=2, g=2)
    kn = pjtB[:, 512:1024].rearrange("p (a g t) -> p a g t", a=2, g=2)
    kbb = pjtB[:, 1024:1536].rearrange("p (a g t) -> p a g t", a=2, g=2)
    vbb = pjtB[:, 1536:2048].rearrange("p (a g t) -> p a g t", a=2, g=2)
    xyb = pjtB[:, 2048:3840].rearrange("p (a t) -> p a t", a=14)
    S_ab = [pjtB[:, 3840:4352], pjtB[:, 4352:4864]]
    uadj = pjtB[:, 4864:5120]
    wfm = pjtB[:, 5120:5376].rearrange("p (g t) -> p g t", g=2)
    ktm = pjtB[:, 5376:5632].rearrange("p (g t) -> p g t", g=2)
    sbfm = pjtB[:, 5632:6144].rearrange("p (g t) -> p g t", g=2)
    ofm_o = pjtB[:, 6144:6400].rearrange("p (g t) -> p g t", g=2)
    otm = pjtB[:, 6400:6656]
    btb = pjtB[:, 6656:6912].rearrange("p (a t) -> p a t", a=2)
    s_f32 = pjtF[:, 3456:3968]
    firb = pjtB[:, 0:8192].rearrange("p (s c t) -> p s c t", s=4, c=8)
    cvtF = cvt[:].rearrange("p a b -> p (a b)")
    cvtF32 = cvtF.bitcast(F32)
    scrq = cvtF[:, 0:512].rearrange("p (a t) -> p a t", a=4)
    scrA = cvtF[:, 0:256]
    scrB = cvtF[:, 256:512]
    zt = cvtF32[0:28, 256:512]
    wtile = cvtF32[0:28, 512:768]
    sw4 = cvtF32[0:4, 768:1024]
    st28b = cvtF[0:28, 1664:1920]
    onb = cvtF[:, 2048:4096].rearrange("p (c t) -> p c t", c=8)
    wvF = wv_bf[:].rearrange("p a b -> p (a b)")
    hm_bf = wvF[:, 0:4096].rearrange("p (c t) -> p c t", c=16)
    dt_bf = wvF[:, 4096:6144].rearrange("p (c t) -> p c t", c=8)
    wos = wvF[:, 6144:8192].rearrange("p (a c t) -> p a c t", a=2, c=8)

    pb = [p.ps(f"pb{i}", [128, 512], F32) for i in range(6)]
    pbb = [p.ps(f"pbb{i}", [128, 1024], BF16) for i in range(2)]

    LDC = p.sem("ldc"); LD = p.sem("ld"); CAST = p.sem("cast"); LX = p.sem("lx")
    STM = p.sem("stm"); SQ = p.sem("sq"); ROW = p.sem("row"); ROWA = p.sem("rowa")
    NRM = p.sem("nrm"); XQ = p.sem("xq"); PRJ = p.sem("prj"); PEV = p.sem("pev")
    CNV = p.sem("cnv"); CNE = p.sem("cne"); WRQ = p.sem("wrq"); CC = p.sem("cc")
    GQ = p.sem("gq"); GC = p.sem("gc"); GF = p.sem("gf"); BQ = p.sem("bq")
    DPE = p.sem("dpe"); DAC = p.sem("dac"); DVD = p.sem("dvd"); OD = p.sem("od")
    DS3 = p.sem("ds3")
    FPE = p.sem("fpe"); FAC = p.sem("fac"); FVD = p.sem("fvd"); FGP = p.sem("fgp")
    WBC = p.sem("wbc"); WOS = p.sem("wos")

    r_mu = rows[:, 0:512]; r_var = rows[:, 512:1024]; r_rstd = rows[:, 1024:1536]
    r_t = rows[:, 1536:2048]; r_ms = rows[:, 2048:2560]

    for (dst, src) in ((cw_s[:].rearrange("p a b -> p (a b)"), cw3),
                       (firw_s[:], firw), (msk_s[:].rearrange("p a b -> p (a b)"), masks4),
                       (scol_s[:], scol28), (gsel_s[:], gselc), (gb1c_s[:], gb1c),
                       (gb2_s[:], gb2r), (nw_s[:], nwc2), (ridx[:], ridx_d),
                       (didx[:], didx_d)):
        emit_dma(p, "sync", dst, src, LDC)
    t_ldc = LDC.n
    def init_c(v):
        v.memset(consts[:, 0:1], 1e-5)
        v.memset(consts[:, 1:2], 1.0)
        v.memset(consts[:, 2:3], 1e-6)
        v.memset(ones_cf[:], 1.0)
        v.memset(ones_cb[:], 1.0)
        v.memset(ones_f[:], 1.0)
        i = v.memset(tails[:].rearrange("p a b c -> p (a b c)"), 0.0)
        p.chain_bar("vector", v, i)
        return v.memset(pjtF, 0.0)
    emit(p, "vector", init_c, CAST, wait=[(LDC, t_ldc)])
    def wload(dram, dstsl, ncol):
        for jh in range(ceil_div(ncol, 512)):
            w0 = [(CAST, CAST.n)]
            t_w = emit_dma(p, "sync", xt[:, :, 0:512],
                           dram.rearrange("(c p) q -> p c q", p=128)
                           [:, :, jh * 512:(jh + 1) * 512], LD, wait=w0)
            emit(p, "vector", lambda v, dstsl=dstsl, jh=jh: v.tensor_copy(
                dstsl[:, :, jh * 512:(jh + 1) * 512], xt[:]), CAST, wait=[(LD, t_w)])
    wload(wq, wqk_bf[:, :, 0:1024], 1024)
    wload(wk, wqk_bf[:, :, 1024:2048], 1024)
    wload(wv, wv_bf[:], 1024)
    t_w4 = emit_dma(p, "sync", xt[:, :, 0:4], wb4.rearrange("(c p) q -> p c q", p=128),
                    LD, wait=[(CAST, CAST.n)])
    emit(p, "vector", lambda v: v.tensor_copy(wb_bf[:], xt[:, :, 0:4]), CAST,
         wait=[(LD, t_w4)])
    t_wc = CAST.n

    NTI = TOK // 512
    tile_ranges = [(0, HB)] + [(HB + i * 512, HB + (i + 1) * 512) for i in range(NTI)]
    KINDS = ("q", "k", "v")
    t_conv_prev = 0
    t_pev_all_prev = 0
    t_wrq_beta_prev = 0
    t_xn_prev = 0
    t_row_prev = 0
    for ti, (lo_, hi_) in enumerate(tile_ranges):
        TW = hi_ - lo_
        w0 = [(CAST, t_wc)] if ti == 0 else [(XQ, t_xn_prev), (STM, STM.n)]
        t_x = emit_dma(p, "sync", xt[:, :, 0:TW],
                       xh_fm[:, lo_:hi_].rearrange("(c p) t -> p c t", p=128), LX, wait=w0)
        def ssum(t, TW=TW):
            for c in range(8):
                last = t.matmul(pb[0][0:1, 0:TW], ones_cf[:], xt[:, c, 0:TW],
                                start=(c == 0), stop=(c == 7))
            return last
        emit(p, "tensor", ssum, STM, wait=[(LX, t_x), (ROW, t_row_prev)])
        for c in range(8):
            w2 = [(LX, t_x)] if c == 0 else []
            if c >= 1:
                w2.append((STM, STM.n))
            emit(p, "scalar", lambda sc, c=c, TW=TW: sc.activation(
                out=xsq[:, 0:TW], in_=xt[:, c, 0:TW], func=AF.Square), SQ, wait=w2)
            emit(p, "tensor", lambda t, c=c, TW=TW: t.matmul(
                pb[1][0:1, 0:TW], ones_cf[:], xsq[:, 0:TW], start=(c == 0), stop=(c == 7)),
                 STM, wait=[(SQ, SQ.n)])
        t_stm = STM.n
        def row_a(v, TW=TW):
            v.tensor_single_scalar(out=r_mu[:, 0:TW], in_=pb[0][0:1, 0:TW],
                                   scalar=1.0 / D, op=ALU.mult)
            i = v.tensor_single_scalar(out=r_var[:, 0:TW], in_=pb[1][0:1, 0:TW],
                                       scalar=1.0 / D, op=ALU.mult)
            p.chain_bar("vector", v, i)
            i = v.tensor_mul(r_t[:, 0:TW], r_mu[:, 0:TW], r_mu[:, 0:TW])
            p.chain_bar("vector", v, i)
            return v.tensor_sub(r_var[:, 0:TW], r_var[:, 0:TW], r_t[:, 0:TW])
        emit(p, "vector", row_a, ROW, wait=[(STM, t_stm)])
        emit(p, "scalar", lambda sc, TW=TW: sc.activation(
            out=r_t[:, 0:TW], in_=r_var[:, 0:TW], func=AF.Sqrt,
            bias=consts[0:1, 0:1]), ROWA, wait=[(ROW, ROW.n)])
        def row_c(v, TW=TW):
            i = v.reciprocal(r_rstd[:, 0:TW], r_t[:, 0:TW])
            p.chain_bar("vector", v, i)
            return v.tensor_mul(r_ms[:, 0:TW], r_mu[:, 0:TW], r_rstd[:, 0:TW])
        emit(p, "vector", row_c, ROW, wait=[(ROWA, ROWA.n)])
        t_rowe = ROW.n
        t_row_prev = t_rowe
        wnrm = [(ROW, t_rowe)]
        if ti > 0:
            wnrm.append((PEV, t_pev_all_prev))
            wnrm.append((XQ, t_xn_prev))
        emit(p, "tensor", lambda t, TW=TW: t.matmul(pb[2][:, 0:TW], ones_f[:],
                                                    r_rstd[:, 0:TW], start=True, stop=True),
             NRM, wait=wnrm)
        emit(p, "tensor", lambda t, TW=TW: t.matmul(pb[3][:, 0:TW], ones_f[:],
                                                    r_ms[:, 0:TW], start=True, stop=True),
             NRM)
        t_nrm = NRM.n
        def xn_f(v, lo_=lo_, TW=TW):
            for c in range(8):
                i = v.tensor_mul(tmp32[:, c % 2, 0:TW], xt[:, c, 0:TW], pb[2][:, 0:TW])
                p.chain_bar("vector", v, i)
                i = v.tensor_sub(xn_bf[:, c, lo_:lo_ + TW], tmp32[:, c % 2, 0:TW],
                                 pb[3][:, 0:TW])
                if c < 7:
                    p.chain_bar("vector", v, i)
            return i
        emit(p, "vector", xn_f, XQ, wait=[(NRM, t_nrm)])
        t_xn = XQ.n
        t_xn_prev = t_xn
        TWprev = tile_ranges[ti - 1][1] - tile_ranges[ti - 1][0] if ti > 0 else 0
        for kx, kind in enumerate(KINDS):
            t_pev_k = []
            for dc in range(8):
                bank = 3 + dc % 2
                w3 = [(XQ, t_xn)]
                if PEV.n >= 2:
                    w3.append((PEV, PEV.n - 1))
                if dc == 0 and kx == 0:
                    w3.append((XQ, t_xn))
                def pmm(t, dc=dc, bank=bank, kx=kx, lo_=lo_, TW=TW):
                    for c in range(8):
                        if kx < 2:
                            lh = wqk_bf[:, c, kx * 1024 + dc * 128:kx * 1024 + (dc + 1) * 128]
                        else:
                            lh = wv_bf[:, c, dc * 128:(dc + 1) * 128]
                        last = t.matmul(pb[bank][:, 0:TW], lh, xn_bf[:, c, lo_:lo_ + TW],
                                        start=(c == 0), stop=(c == 7))
                    return last
                emit(p, "tensor", pmm, PRJ, wait=w3)
                t_pm = PRJ.n
                def pev_f(v, dc=dc, bank=bank, kx=kx, TW=TW, ti=ti, TWprev=TWprev):
                    if ti > 0:
                        i = v.tensor_copy(pjt[:, dc, 0:4], tails[:, kx, dc, :])
                        p.chain_bar("vector", v, i)
                    i = v.tensor_copy(pjt[:, dc, 4:4 + TW], pb[bank][:, 0:TW])
                    p.chain_bar("vector", v, i)
                    return v.tensor_copy(tails[:, kx, dc, :], pjt[:, dc, TW:TW + 4])
                emit(p, "vector", pev_f, PEV, wait=[(PRJ, t_pm), (CNV, t_conv_prev)])
                t_pev_k.append(PEV.n)
            for dc in range(8):
                def conv_f(v, dc=dc, kx=kx, TW=TW):
                    i = v.tensor_scalar(out=tmp32[:, 0, 0:TW], in0=pjt[:, dc, 4:4 + TW],
                                        scalar1=cw_s[:, kx, dc * 4 + 3:dc * 4 + 4],
                                        scalar2=None, op0=ALU.mult)
                    for k in range(1, 4):
                        p.chain_bar("vector", v, i)
                        i = v.scalar_tensor_tensor(
                            out=tmp32[:, 0, 0:TW], in0=pjt[:, dc, 4 - k:4 + TW - k],
                            scalar=cw_s[:, kx, dc * 4 + 3 - k:dc * 4 + 4 - k],
                            in1=tmp32[:, 0, 0:TW], op0=ALU.mult, op1=ALU.add)
                    return i
                wcv = [(PEV, t_pev_k[dc]), (CNE, CNE.n)]
                emit(p, "vector", conv_f, CNV, wait=wcv)
                t_cv = CNV.n
                if kind == "v":
                    emit(p, "scalar", lambda sc, dc=dc, lo_=lo_, TW=TW: sc.activation(
                        out=vc[:, dc, lo_:lo_ + TW], in_=tmp32[:, 0, 0:TW], func=AF.Silu),
                        CNE, wait=[(CNV, t_cv)])
                else:
                    emit(p, "scalar", lambda sc, dc=dc, TW=TW: sc.activation(
                        out=cvt[:, dc, 0:TW], in_=tmp32[:, 0, 0:TW], func=AF.Silu),
                        CNE, wait=[(CNV, t_cv), (WRQ, WRQ.n)])
            t_conv_prev = CNV.n
            t_cne = CNE.n
            if kind != "v" and ti > 0:
                for g in range(2):
                    for hh in range(4):
                        emit_dma(p, "sync",
                                 qkv_h[hh].ap()[(kx * 2 + g) * 128:(kx * 2 + g + 1) * 128,
                                                lo_ - HB:hi_ - HB],
                                 cvt[:, hh * 2 + g, 0:TW], WRQ, wait=[(CNE, t_cne)])
        t_pev_all_prev = PEV.n
        if ti > 0:
            def bmm(t, lo_=lo_, TW=TW):
                for c in range(8):
                    last = t.matmul(pb[5][0:4, 0:TW], wb_bf[:, c, :],
                                    xn_bf[:, c, lo_:lo_ + TW], start=(c == 0),
                                    stop=(c == 7))
                return last
            emit(p, "tensor", bmm, PRJ, wait=[(XQ, t_xn), (BQ, t_wrq_beta_prev)])
            t_bm = PRJ.n
            emit(p, "scalar", lambda sc, TW=TW: sc.activation(
                out=bt4[:, 0:TW], in_=pb[5][0:4, 0:TW], func=AF.Sigmoid),
                 CNE, wait=[(PRJ, t_bm)])
            t_be = CNE.n
            for hh in range(4):
                emit_dma(p, "sync", qkv_h[hh].ap()[768:769, lo_ - HB:hi_ - HB],
                         bt4[hh:hh + 1, 0:TW], BQ, wait=[(CNE, t_be)])
            t_wrq_beta_prev = BQ.n
    t_cne_all = CNE.n
    for g in range(2):
        for hh in range(4):
            for jp in range(ceil_div(TOK, 1024)):
                emit_dma(p, "sync",
                         qkv_h[hh].ap()[(4 + g) * 128:(5 + g) * 128,
                                        jp * 1024:min((jp + 1) * 1024, TOK)],
                         vc[:, hh * 2 + g, HB + jp * 1024:HB + min((jp + 1) * 1024, TOK)],
                         WRQ, wait=[(CNE, t_cne_all)])
    t_pa = WRQ.n
    for j in range(4):
        emit(p, "gpsimd", lambda g, j=j: g.collective_compute(
            "AllGather", ALU.bypass,
            replica_groups=[[0, 1, 2, 3], [4, 5, 6, 7]],
            ins=[qkv_h[j].ap().opt()],
            outs=[qkvh_all.ap()[j * NRK * 769:(j + 1) * NRK * 769, :].opt()]),
             CC, wait=([(WRQ, t_pa), (BQ, BQ.n)] if j == 0 else []))
    t_cc1 = CC.n

    t_prjA = PRJ.n
    t_xqA = XQ.n
    def wob_cast(g):
        g.dma_start(out=wob.ap(), in_=wo).then_inc(GC.h, 16)
    GC.inc(16)
    p.add("gpsimd", wob_cast)
    t_wob = GC.n
    for jh in range(4):
        t_l = emit_dma(p, "sync", xt[:, :, 0:512],
                       gw1h.rearrange("(c p) q -> p c q", p=128)
                       [:, :, jh * 512:(jh + 1) * 512], LD,
                       wait=[(XQ, t_xqA), (PRJ, t_prjA)] if jh == 0 else
                            [(CAST, CAST.n)])
        emit(p, "vector", lambda v, jh=jh: v.tensor_copy(
            wqk_bf[:, :, jh * 512:(jh + 1) * 512], xt[:]), CAST, wait=[(LD, t_l)])
    xt28 = xt[:].rearrange("p c q -> p (c q)")[0:28, 0:2048]
    for jp in range(4):
        t_l = emit_dma(p, "sync", xt28[:, jp * 512:(jp + 1) * 512],
                       gw1s[:, jp * 512:(jp + 1) * 512], LD,
                       wait=[(CAST, CAST.n)] if jp == 0 else [(LD, LD.n)])
    emit(p, "vector", lambda v: v.tensor_copy(gw1sb[:], xt28), CAST, wait=[(LD, t_l)])
    tmpg = tmp32[:].rearrange("p c q -> p (c q)")[:, 0:448].rearrange(
        "p (c j) -> p c j", c=16)
    t_l = emit_dma(p, "sync", tmpg, gw2.rearrange("(c p) j -> p c j", p=128), LD,
                   wait=[(CNV, CNV.n), (XQ, t_xqA), (LD, LD.n)])
    emit(p, "vector", lambda v: v.tensor_copy(gw2b[:], tmpg), CAST, wait=[(LD, t_l)])
    t_fwc = CAST.n

    def dinit(v):
        i1 = v.memset(s_f32, 0.0)
        p.chain_bar("vector", v, i1)
        return v.memset(sbfm.rearrange("p g t -> p (g t)"), 0.0)
    emit(p, "vector", dinit, CAST, wait=[(CNV, CNV.n), (PEV, PEV.n)])
    t_dinit = CAST.n
    rq = rows[:, 0:128]; rk = rows[:, 128:256]
    rqi = rows[:, 256:384]; rki = rows[:, 384:512]
    Z = dict(prep=0, mask=0, rhsev=0, sqev=0, applylast=0, solev=0, wkev=0,
             wsmm=0, uadj=0, atb=0, outmm=0, dsmm=0, sadd=0, sbfmcp=0, outev=0,
             ofmtr=0, ofmev=0, od=0, colsum=0, sqrt=0, bcast=0, sq4=0, gath=0)
    prev = dict(Z)
    prev2 = dict(Z)
    for i in range(NCH if dbg_nch is None else dbg_nch):
        cur = dict(Z)
        par = i % 2
        rl = i // CPR
        c0 = (i % CPR) * CH
        tau = i * CH
        def gath(g, i=i, par=par, p2pe=prev2["outmm"],
                 p2vd=prev2["prep"], gq0=GQ.n):
            if i == 0:
                g.wait_ge(CC.h, t_cc1)
            else:
                g.wait_ge(GQ.h, gq0)
            if i >= 2:
                g.wait_ge(DPE.h, p2pe)
                g.wait_ge(DVD.h, p2vd)
            dsts = [qn, qn, kn, kn, vbb, vbb]
            qv = qkvh_all.ap().rearrange("(r) (cb t) -> (r cb) t", cb=CPR)
            for j6 in range(6):
                g.indirect_dma_start(
                    out=dsts[j6][:, par, j6 % 2, :], out_offset=None,
                    in_=qv,
                    in_offset=IndirectOffsetOnAxis(
                        ap=ridx[:, i * 7 + j6:i * 7 + j6 + 1], axis=0)
                ).then_inc(GQ.h, 16)
            g.indirect_dma_start(
                out=btb[:, par, :], out_offset=None,
                in_=qv,
                in_offset=IndirectOffsetOnAxis(
                    ap=ridx[:, i * 7 + 6:i * 7 + 7], axis=0)).then_inc(GQ.h, 16)
        GQ.inc(7 * 16)
        p.add("gpsimd", gath)
        cur["gath"] = GQ.n
        srcs = [qn[:, par, 0, :], qn[:, par, 1, :], kn[:, par, 0, :], kn[:, par, 1, :]]
        for j in range(4):
            w = [(GQ, cur["gath"])] if j == 0 else []
            if i > 0 and j == 0:
                w.append((DPE, prev["colsum"]))
            emit(p, "scalar", lambda sc, j=j, src=srcs[j]: sc.activation(
                out=scrq[:, j, :], in_=src, func=AF.Square), DAC, wait=w)
        cur["sq4"] = DAC.n
        def csum(t):
            t.matmul(pb[4][0:1, 256:384], ones_cb[:], scrq[:, 0, :], start=True,
                     stop=False)
            t.matmul(pb[4][0:1, 256:384], ones_cb[:], scrq[:, 1, :], start=False,
                     stop=True)
            t.matmul(pb[4][0:1, 384:512], ones_cb[:], scrq[:, 2, :], start=True,
                     stop=False)
            return t.matmul(pb[4][0:1, 384:512], ones_cb[:], scrq[:, 3, :],
                            start=False, stop=True)
        emit(p, "tensor", csum, DPE,
             wait=[(DAC, cur["sq4"])] + ([(DAC, prev["sqrt"])] if i > 0 else []))
        cur["colsum"] = DPE.n
        def sqr2(sc):
            sc.activation(out=rq, in_=pb[4][0:1, 256:384], func=AF.Sqrt,
                          bias=consts[0:1, 2:3])
            return sc.activation(out=rk, in_=pb[4][0:1, 384:512], func=AF.Sqrt,
                                 bias=consts[0:1, 2:3])
        emit(p, "scalar", sqr2, DAC, wait=[(DPE, cur["colsum"])])
        cur["sqrt"] = DAC.n
        def rcp2(v):
            i1 = v.reciprocal(rqi, rq)
            p.chain_bar("vector", v, i1)
            return v.reciprocal(rki, rk)
        emit(p, "vector", rcp2, DVD,
             wait=[(DAC, cur["sqrt"])] + ([(DPE, prev["bcast"])] if i > 0 else []))
        t_rcp = DVD.n
        def bc2(t):
            t.matmul(pb[1][:, 256:384], ones_f[:], rqi[0:1, :], start=True, stop=True)
            return t.matmul(pb[1][:, 384:512], ones_f[:], rki[0:1, :], start=True,
                            stop=True)
        emit(p, "tensor", bc2, DPE,
             wait=[(DVD, t_rcp)] + ([(DVD, prev["prep"])] if i > 0 else []))
        cur["bcast"] = DPE.n
        def preptt(v, par=par):
            for g in range(2):
                i1 = v.tensor_mul(qn[:, par, g, :], qn[:, par, g, :], pb[1][:, 256:384])
                p.chain_bar("vector", v, i1)
            for g in range(2):
                i1 = v.tensor_mul(kn[:, par, g, :], kn[:, par, g, :], pb[1][:, 384:512])
                p.chain_bar("vector", v, i1)
            for g in range(2):
                i1 = v.tensor_mul(kbb[:, par, g, :], kn[:, par, g, :], btb[:, par, :])
                p.chain_bar("vector", v, i1)
            i1 = v.tensor_mul(vbb[:, par, 0, :], vbb[:, par, 0, :], btb[:, par, :])
            p.chain_bar("vector", v, i1)
            return v.tensor_mul(vbb[:, par, 1, :], vbb[:, par, 1, :], btb[:, par, :])
        emit(p, "vector", preptt, DVD, wait=[(DPE, cur["bcast"])])
        cur["prep"] = DVD.n
        def amm(t, par=par):
            t.matmul(pb[0][:, 0:128], kbb[:, par, 0, :], kn[:, par, 0, :],
                     start=True, stop=False)
            t.matmul(pb[0][:, 0:128], kbb[:, par, 1, :], kn[:, par, 1, :],
                     start=False, stop=True)
            t.matmul(pb[0][:, 128:256], kn[:, par, 0, :], kbb[:, par, 0, :],
                     start=True, stop=False)
            return t.matmul(pb[0][:, 128:256], kn[:, par, 1, :], kbb[:, par, 1, :],
                            start=False, stop=True)
        emit(p, "tensor", amm, DPE, wait=[(DVD, cur["prep"])])
        cur["amm"] = DPE.n
        if dbg_stage < 2:
            prev2 = prev; prev = cur
            continue
        def mks(v):
            i1 = v.tensor_mul(xyb[:, 0, :], pb[0][:, 0:128], msk_s[:, 3, :])
            p.chain_bar("vector", v, i1)
            i1 = v.tensor_mul(xyb[:, 1, :], pb[0][:, 128:256], msk_s[:, 4, :])
            p.chain_bar("vector", v, i1)
            i1 = v.tensor_mul(xyb[:, 2, :], pb[0][:, 128:256], msk_s[:, 1, :])
            p.chain_bar("vector", v, i1)
            return v.tensor_sub(xyb[:, 3, :], msk_s[:, 0, :], xyb[:, 1, :])
        emit(p, "vector", mks, DVD, wait=[(DPE, cur["amm"])])
        cur["mask"] = DVD.n
        def rtr(t, par=par):
            for g in range(2):
                t.transpose(pbb[0][:, g * 128:(g + 1) * 128], vbb[:, par, g, :],
                            msk_s[:, 0, :])
            for g in range(2):
                i1 = t.transpose(pbb[0][:, 256 + g * 128:256 + (g + 1) * 128],
                                 kbb[:, par, g, :], msk_s[:, 0, :])
            return i1
        wrt = [(DVD, cur["prep"])]
        if i > 0:
            wrt += [(DAC, prev["rhsev"]), (DPE, prev["applylast"]), (DS3, 32 * i)]
        emit(p, "tensor", rtr, DPE, wait=wrt)
        cur["rhstr"] = DPE.n
        emit(p, "scalar", lambda sc: sc.activation(
            out=S_ab[0], in_=pbb[0][:, 0:512], func=AF.Copy, bias=0.0), DAC,
             wait=[(DPE, cur["rhstr"])])
        cur["rhsev"] = DAC.n
        rs_, rd_ = 3, 6
        xs_, xd_ = 0, 4
        ys_, yd_ = 1, 5
        for lv in range(4):
            wsq = [(DVD, cur["mask"])] if lv == 0 else [(DAC, DAC.n)]
            if lv == 0 and i > 0:
                wsq.append((DAC, prev["sqev"]))
            def sqmm(t, lv=lv, xs_=xs_, ys_=ys_):
                i1 = t.matmul(pb[1][:, 0:128], xyb[:, ys_, :], xyb[:, xs_, :],
                              start=True, stop=True)
                if lv < 3:
                    i1 = t.matmul(pb[1][:, 128:256], xyb[:, xs_, :], xyb[:, ys_, :],
                                  start=True, stop=True)
                return i1
            emit(p, "tensor", sqmm, DPE, wait=wsq)
            t_sq = DPE.n
            def sqev(sc, lv=lv, xd_=xd_, yd_=yd_):
                i1 = sc.activation(out=xyb[:, xd_, :], in_=pb[1][:, 0:128],
                                   func=AF.Copy, bias=0.0)
                if lv < 3:
                    i1 = sc.activation(out=xyb[:, yd_, :], in_=pb[1][:, 128:256],
                                       func=AF.Copy, bias=0.0)
                return i1
            emit(p, "scalar", sqev, DAC, wait=[(DPE, t_sq)])
            t_se = DAC.n
            def rmmf(t, rs_=rs_, xd_=xd_):
                t.matmul(pb[1][:, 256:384], msk_s[:, 0, :], xyb[:, rs_, :],
                         start=True, stop=False)
                return t.matmul(pb[1][:, 256:384], xyb[:, xd_, :], xyb[:, rs_, :],
                                start=False, stop=True)
            emit(p, "tensor", rmmf, DPE, wait=[(DAC, t_se)])
            t_rm2 = DPE.n
            emit(p, "scalar", lambda sc, rd_=rd_: sc.activation(
                out=xyb[:, rd_, :], in_=pb[1][:, 256:384], func=AF.Copy, bias=0.0),
                 DAC, wait=[(DPE, t_rm2)])
            rs_, rd_ = rd_, rs_
            xs_, xd_ = xd_, xs_
            ys_, yd_ = yd_, ys_
        cur["sqev"] = DAC.n
        wap = [(DAC, cur["rhsev"]), (DAC, cur["sqev"])]
        if i > 0:
            wap.append((DAC, prev["solev"]))
        if i == 0:
            wap.append((CAST, t_wc))
        if dbg_stage < 3:
            prev2 = prev; prev = cur
            continue
        def rel1(t):
            return t.transpose(pbb[0][0:64, 512:576], xyb[64:128, 3, 64:128],
                               msk_s[64:128, 0, 64:128])
        emit(p, "tensor", rel1, DPE, wait=wap)
        t_r1 = DPE.n
        emit(p, "scalar", lambda sc: sc.activation(
            out=xyb[0:64, 12, 0:64], in_=pbb[0][0:64, 512:576], func=AF.Copy,
            bias=0.0), DAC, wait=[(DPE, t_r1)])
        t_r1e = DAC.n
        def rel2(t):
            return t.transpose(pbb[0][0:32, 576:608], xyb[32:64, 12, 32:64],
                               msk_s[32:64, 0, 32:64])
        emit(p, "tensor", rel2, DPE, wait=[(DAC, t_r1e)])
        t_r2 = DPE.n
        emit(p, "scalar", lambda sc: sc.activation(
            out=xyb[0:32, 13, 0:32], in_=pbb[0][0:32, 576:608], func=AF.Copy,
            bias=0.0), DAC, wait=[(DPE, t_r2)])
        t_r2e = DAC.n
        if dbg_stage < 3.7:
            prev2 = prev; prev = cur
            continue
        t_sv = 0
        for cb in range(min(int(round(dbg_cb)), 4)):
            pbase = 0 if cb == 3 else cb * 32
            if cb >= 1:
                stg_lo = pbase
                t_ae = 0
                for gi in range(cb):
                    def accg2(t, cb=cb, pbase=pbase, gi=gi, stg_lo=stg_lo):
                        if gi == 0:
                            lh1 = msk_s[:, 0, cb * 32:(cb + 1) * 32]
                        else:
                            lh1 = msk_s[:, 0, stg_lo:stg_lo + 32]
                        t.matmul(pb[2][pbase:pbase + 32, 0:512], lh1, S_ab[0],
                                 start=True, stop=False)
                        c2 = gi
                        return t.matmul(pb[2][pbase:pbase + 32, 0:512],
                                        xyb[c2 * 32:(c2 + 1) * 32, 2,
                                            cb * 32:(cb + 1) * 32],
                                        S_ab[1][c2 * 32:(c2 + 1) * 32, :],
                                        start=False, stop=True)
                    wag = [(DAC, t_sv)] if gi == 0 else [(DAC, t_ae)]
                    if cb == 1 and gi == 0:
                        wag += [(DAC, cur["rhsev"]), (DAC, cur["sqev"])]
                        if i > 0:
                            wag.append((DAC, prev["solev"]))
                    emit(p, "tensor", accg2, DPE, wait=wag)
                    t_ag = DPE.n
                    wae = [(DPE, t_ag)]
                    if cb == 1 and i > 0:
                        wae.append((DVD, prev["uadj"]))
                    last_g = (gi == cb - 1)
                    if last_g and cb < 3:
                        dst_ae = S_ab[1][cb * 32:(cb + 1) * 32, :]
                    else:
                        dst_ae = S_ab[0][stg_lo:stg_lo + 32, :]
                    def accev(sc, pbase=pbase, dst_ae=dst_ae):
                        return sc.activation(out=dst_ae,
                                             in_=pb[2][pbase:pbase + 32, :],
                                             func=AF.Copy, bias=0.0)
                    emit(p, "scalar", accev, DAC, wait=wae)
                    t_ae = DAC.n
            if cb == 0:
                wta = [(DAC, cur["rhsev"]), (DAC, cur["sqev"])]
                if i > 0:
                    wta.append((DVD, prev["uadj"]))
            else:
                wta = [(DAC, t_ae)]
            if cb == 3:
                wta.append((DAC, t_r2e))
            def tap(t, cb=cb, pbase=pbase):
                if cb == 0:
                    rhsap = S_ab[0][0:32, :]
                    lh = xyb[0:32, 3, 0:32]
                elif cb == 3:
                    rhsap = S_ab[0][0:32, :]
                    lh = xyb[0:32, 13, 0:32]
                else:
                    rhsap = S_ab[1][cb * 32:(cb + 1) * 32, :]
                    lh = xyb[cb * 32:(cb + 1) * 32, 3, cb * 32:(cb + 1) * 32]
                return t.matmul(pb[3][pbase:pbase + 32, 0:512], lh, rhsap,
                                start=True, stop=True)
            emit(p, "tensor", tap, DPE, wait=wta)
            t_ta = DPE.n
            wse = [(DPE, t_ta)]
            if cb == 0 and i > 0:
                wse.append((DVD, prev["uadj"]))
            dst_se = S_ab[0][0:32, :] if cb == 3 else \
                S_ab[1][cb * 32:(cb + 1) * 32, :]
            def solevf(sc, pbase=pbase, dst_se=dst_se):
                return sc.activation(out=dst_se,
                                     in_=pb[3][pbase:pbase + 32, :],
                                     func=AF.Copy, bias=0.0)
            emit(p, "scalar", solevf, DAC, wait=wse)
            t_sv = DAC.n
            if cb == 3:
                t_d1 = emit_dma(p, "sync", sol3_d.ap(), S_ab[0][0:32, :], DS3,
                                wait=[(DAC, t_sv), (DS3, 32 * i)])
                emit_dma(p, "sync", S_ab[1][96:128, :], sol3_d.ap(), DS3,
                         wait=[(DS3, t_d1)])
        cur["applylast"] = DPE.n
        cur["solev"] = DAC.n
        if dbg_stage < 4:
            prev2 = prev; prev = cur
            continue
        def wktr(t, par=par):
            for g in range(2):
                t.transpose(pbb[1][:, g * 128:(g + 1) * 128],
                            S_ab[1][:, 256 + g * 128:256 + (g + 1) * 128],
                            msk_s[:, 0, :])
            for g in range(2):
                i1 = t.transpose(pbb[1][:, 256 + g * 128:256 + (g + 1) * 128],
                                 kn[:, par, g, :], msk_s[:, 0, :])
            return i1
        emit(p, "tensor", wktr, DPE,
             wait=[(DAC, cur["solev"])] + ([(DS3, 32 * (i + 1))] if dbg_cb >= 4 else []))
        cur["wktr"] = DPE.n
        def wkev(sc):
            for g in range(2):
                sc.activation(out=wfm[:, g, :], in_=pbb[1][:, g * 128:(g + 1) * 128],
                              func=AF.Copy, bias=0.0)
            for g in range(2):
                i1 = sc.activation(out=ktm[:, g, :],
                                   in_=pbb[1][:, 256 + g * 128:256 + (g + 1) * 128],
                                   func=AF.Copy, bias=0.0)
            return i1
        emit(p, "scalar", wkev, DAC, wait=[(DPE, cur["wktr"])])
        cur["wkev"] = DAC.n
        def wsm(t):
            t.matmul(pb[3][:, 0:256], wfm[:, 0, :], sbfm[:, 0, :], start=True,
                     stop=False)
            return t.matmul(pb[3][:, 0:256], wfm[:, 1, :], sbfm[:, 1, :],
                            start=False, stop=True)
        wws = [(DAC, cur["wkev"])]
        if i > 0:
            wws += [(DVD, prev["uadj"]), (DAC, prev["sbfmcp"])]
        emit(p, "tensor", wsm, DPE, wait=wws)
        cur["wsmm"] = DPE.n
        emit(p, "vector", lambda v: v.tensor_sub(uadj, S_ab[1][:, 0:256],
                                                 pb[3][:, 0:256]), DVD,
             wait=[(DPE, cur["wsmm"])])
        cur["uadj"] = DVD.n
        def atm(t, par=par):
            t.matmul(pb[0][:, 256:384], kn[:, par, 0, :], qn[:, par, 0, :],
                     start=True, stop=False)
            return t.matmul(pb[0][:, 256:384], kn[:, par, 1, :], qn[:, par, 1, :],
                            start=False, stop=True)
        wat = [(DVD, prev["atb"])] if i > 0 else []
        emit(p, "tensor", atm, DPE, wait=wat)
        cur["atmm"] = DPE.n
        emit(p, "vector", lambda v: v.tensor_mul(xyb[:, 7, :], pb[0][:, 256:384],
                                                 msk_s[:, 2, :]), DVD,
             wait=[(DPE, cur["atmm"])])
        cur["atb"] = DVD.n
        def omm(t, par=par):
            t.matmul(pb[4][:, 0:256], qn[:, par, 0, :], sbfm[:, 0, :], start=True,
                     stop=False)
            t.matmul(pb[4][:, 0:256], qn[:, par, 1, :], sbfm[:, 1, :], start=False,
                     stop=False)
            return t.matmul(pb[4][:, 0:256], xyb[:, 7, :], uadj, start=False,
                            stop=True)
        wom = [(DVD, cur["atb"]), (DVD, cur["uadj"])]
        if i > 0:
            wom.append((DAC, prev["outev"]))
        emit(p, "tensor", omm, DPE, wait=wom)
        cur["outmm"] = DPE.n
        def dsm(t):
            t.matmul(pb[5][:, 0:256], ktm[:, 0, :], uadj, start=True, stop=True)
            return t.matmul(pb[5][:, 256:512], ktm[:, 1, :], uadj, start=True,
                            stop=True)
        wds = [(DVD, cur["uadj"]), (DAC, cur["wkev"])]
        if i > 0:
            wds.append((DVD, prev["sadd"]))
        emit(p, "tensor", dsm, DPE, wait=wds)
        cur["dsmm"] = DPE.n
        def sadd(v):
            i1 = v.tensor_add(s_f32[:, 0:256], s_f32[:, 0:256], pb[5][:, 0:256])
            p.chain_bar("vector", v, i1)
            return v.tensor_add(s_f32[:, 256:512], s_f32[:, 256:512],
                                pb[5][:, 256:512])
        wsa = [(DPE, cur["dsmm"])]
        if i > 0:
            wsa.append((DAC, prev["sbfmcp"]))
        emit(p, "vector", sadd, DVD, wait=wsa)
        cur["sadd"] = DVD.n
        def sbcp(sc):
            sc.activation(out=sbfm[:, 0, :], in_=s_f32[:, 0:256], func=AF.Copy,
                          bias=0.0)
            return sc.activation(out=sbfm[:, 1, :], in_=s_f32[:, 256:512],
                                 func=AF.Copy, bias=0.0)
        emit(p, "scalar", sbcp, DAC, wait=[(DVD, cur["sadd"]), (DPE, cur["outmm"])])
        cur["sbfmcp"] = DAC.n
        emit(p, "scalar", lambda sc: sc.activation(out=otm, in_=pb[4][:, 0:256],
                                                   func=AF.Copy, bias=0.0), DAC,
             wait=[(DPE, cur["outmm"])])
        cur["outev"] = DAC.n
        def otr(t):
            t.transpose(pbb[1][:, 512 + 0 * 128:512 + 1 * 128], otm[:, 0:128],
                        msk_s[:, 0, :])
            return t.transpose(pbb[1][:, 512 + 1 * 128:512 + 2 * 128], otm[:, 128:256],
                               msk_s[:, 0, :])
        emit(p, "tensor", otr, DPE, wait=[(DAC, cur["outev"])])
        cur["ofmtr"] = DPE.n
        def ofev(sc):
            sc.activation(out=ofm_o[:, 0, :], in_=pbb[1][:, 512:640], func=AF.Copy,
                          bias=0.0)
            return sc.activation(out=ofm_o[:, 1, :], in_=pbb[1][:, 640:768],
                                 func=AF.Copy, bias=0.0)
        wof = [(DPE, cur["ofmtr"])]
        if i > 0:
            wof.append((OD, prev["od"]))
        emit(p, "scalar", ofev, DAC, wait=wof)
        cur["ofmev"] = DAC.n
        emit_dma(p, "sync",
                 do_my.ap().rearrange("(g p) t -> p g t", p=128)[:, :, tau:tau + CH],
                 ofm_o, OD, wait=[(DAC, cur["ofmev"])])
        cur["od"] = OD.n
        prev2 = prev
        prev = cur
    t_dexp = OD.n
    if dbg_nch is not None:
        lpar = (dbg_nch - 1) % 2
        def dmp(v, lpar=lpar):
            for g in range(2):
                i1 = v.tensor_copy(xt[:, g, 0:128], qn[:, lpar, g, :])
                p.chain_bar("vector", v, i1)
                i1 = v.tensor_copy(xt[:, 2 + g, 0:128], kn[:, lpar, g, :])
                p.chain_bar("vector", v, i1)
            i1 = v.tensor_copy(xt[:, 4, 0:128], xyb[:, 7, :])
            p.chain_bar("vector", v, i1)
            i1 = v.tensor_copy(xt[:, 5, 0:256], uadj)
            p.chain_bar("vector", v, i1)
            i1 = v.tensor_copy(xt[:, 7, 0:256], otm)
            p.chain_bar("vector", v, i1)
            i1 = v.tensor_copy(xt[:, 7, 256:512], sbfm.rearrange("p g t -> p (g t)")[:, 0:256])
            p.chain_bar("vector", v, i1)
            return v.tensor_copy(xt[:, 6, 0:512].rearrange("p (a b) -> p a b", a=2)
                                 .rearrange("p a b -> p (a b)"), S_ab[1])
        emit(p, "vector", dmp, CAST, wait=[(DAC, DAC.n), (DPE, DPE.n),
                                           (OD, OD.n), (DS3, DS3.n)])
        emit_dma(p, "sync", y[0:128, :].rearrange("p (a t) -> p a t", a=TOK // 512)
                 if False else y.rearrange("(c p) t -> p c t", p=128)[:, 0:8, 0:512],
                 xt[:, :, 0:512], OD, wait=[(CAST, CAST.n)])
        emit(p, "gpsimd", lambda g_: g_.nop().then_inc(p.sem("fin").h, 1), None,
             wait=[(OD, OD.n)])
        return p.build()
    def agdo(g):
        g.wait_ge(OD.h, t_dexp)
        g.collective_compute(
            "AllGather", ALU.bypass, replica_groups=[[0, 1, 2, 3], [4, 5, 6, 7]],
            ins=[do_my.ap().opt()], outs=[do_all.ap().opt()]).then_inc(CC.h, 1)
    CC.inc(1)
    p.add("gpsimd", agdo)
    t_cc2 = CC.n
    t_dDAC = DAC.n
    t_dDPE = DPE.n
    t_dDVD = DVD.n

    do_v = do_all.ap().rearrange("r (fb t) -> (r fb) t", fb=S_ // FT)
    FZ = dict(gf=0, stev=0, ztev=0, sw4ev=0, rmsev=0, cmbD=0, cmbG=0, onbD=0,
              onbG=0, yd=0, wbcn=0)
    fprev = dict(FZ)
    slot_used = [(None, 0)] * 4
    FSMAP = {3: 0, 7: 1, 15: 2, 31: 3}
    for ti in range(NFT):
        t0 = ti * FT
        sl = ti % 2
        fcur = dict(FZ)
        wx6 = [(CAST, t_fwc)] if ti == 0 else [(OD, fprev["yd"])]
        t_x6 = emit_dma(p, "sync", xt[:, :, 0:FT],
                        x6_fm[:, t0:t0 + FT].rearrange("(c p) t -> p c t", p=128),
                        LX, wait=wx6)
        def dga(g, ti=ti, t0=t0, pD=fprev["cmbD"], pG=fprev["cmbG"],
                pA=fprev["stev"]):
            if ti == 0:
                g.wait_ge(CC.h, t_cc2)
                g.wait_ge(PRJ.h, t_prjA)
            else:
                g.wait_ge(FVD.h, pD)
                g.wait_ge(FGP.h, pG)
                g.wait_ge(FAC.h, pA)
            for dc in range(8):
                g.indirect_dma_start(
                    out=dt_bf[:, dc, :], out_offset=None,
                    in_=do_v,
                    in_offset=IndirectOffsetOnAxis(
                        ap=didx[:, ti * 8 + dc:ti * 8 + dc + 1], axis=0)
                ).then_inc(GF.h, 16)
        GF.inc(8 * 16)
        p.add("gpsimd", dga)
        fcur["gf"] = GF.n
        t_fir = {}
        for s_ in (31, 15, 7, 3):
            fs = FSMAP[s_]
            eng = "vector"
            sem = FVD
            scr = tmp32[:, 0, 0:FT]
            wf = []
            if ti == 0:
                wf = [(CNE, t_cne_all), (CAST, t_fwc), (DAC, t_dDAC),
                      (DPE, t_dDPE), (DVD, t_dDVD), (OD, t_dexp)]
            else:
                wf = [(FGP, fprev["cmbG"])]
            for dc in range(8):
                def firf(v, s_=s_, fs=fs, dc=dc, t0=t0, scr=scr, eng=eng):
                    off = FIROFF[s_]
                    i1 = v.tensor_scalar(
                        out=scr, in0=vc[:, dc, HB + t0 - (s_ - 1):HB + t0 - (s_ - 1) + FT],
                        scalar1=firw_s[:, dc * 57 + off:dc * 57 + off + 1],
                        scalar2=None, op0=ALU.mult)
                    for j in range(1, s_):
                        p.chain_bar(eng, v, i1)
                        lag = s_ - 1 - j
                        i1 = v.scalar_tensor_tensor(
                            out=scr, in0=vc[:, dc, HB + t0 - lag:HB + t0 - lag + FT],
                            scalar=firw_s[:, dc * 57 + off + j:dc * 57 + off + j + 1],
                            in1=scr, op0=ALU.mult, op1=ALU.add)
                    p.chain_bar(eng, v, i1)
                    return v.tensor_copy(firb[:, fs, dc, :], scr)
                emit(p, eng, firf, sem,
                     wait=(wf + [(sem, sem.n)] if dc == 0 else [(sem, sem.n)]))
            t_fir[fs] = sem.n
        nmm = 0
        t_abs = {}
        for s_ in (0, 6, 5, 1, 2, 3, 4):
            for dc in range(8):
                scr = scrA if nmm % 2 == 0 else scrB
                wab = []
                if nmm >= 2:
                    wab.append((FPE, fprev_mm2[nmm - 2]))
                if s_ == 0 or s_ == 6:
                    src = vc[:, dc, HB + t0:HB + t0 + FT]
                elif s_ == 5:
                    src = dt_bf[:, dc, :]
                    if dc == 0:
                        wab.append((GF, fcur["gf"]))
                else:
                    src = firb[:, s_ - 1, dc, :]
                    if dc == 0:
                        wab.append((FVD, t_fir[s_ - 1]))
                if nmm == 0 and ti > 0:
                    wab += [(FAC, fprev["stev"]), (FAC, fprev["rmsev"])]
                if nmm == 0 and ti == 0:
                    wab.append((DPE, t_dDPE))
                def absf(sc, s_=s_, dc=dc, scr=scr, src=src):
                    if s_ == 0:
                        return sc.activation(out=scr, in_=src, func=AF.Abs,
                                             scale=firw_s[:, dc * 57:dc * 57 + 1])
                    return sc.activation(out=scr, in_=src, func=AF.Abs)
                emit(p, "scalar", absf, FAC, wait=wab)
                t_ab = FAC.n
                sidx = s_ * 4 + dc // 2
                def stmm(t, sidx=sidx, scr=scr, nmm=nmm):
                    return t.matmul(pb[0][0:28, 0:FT], scol_s[:, sidx * 28:(sidx + 1) * 28],
                                    scr, start=(nmm == 0), stop=(nmm == 55))
                emit(p, "tensor", stmm, FPE, wait=[(FAC, t_ab)])
                if nmm == 0:
                    fprev_mm2 = {}
                fprev_mm2[nmm] = FPE.n
                nmm += 1
        t_stmm = FPE.n
        emit(p, "scalar", lambda sc: sc.activation(
            out=st28b, in_=pb[0][0:28, 0:FT], func=AF.Copy, bias=0.0,
            scale=1.0 / 256.0), FAC, wait=[(FPE, t_stmm)])
        fcur["stev"] = FAC.n
        t_gel = {}
        for oc in range(16):
            bank = 3 + oc % 2
            wg = [(FAC, fcur["stev"])]
            if oc >= 2:
                wg.append((FAC, t_gel[oc - 2]))
            elif ti > 0:
                wg.append((FVD, fprev["yd_evD"]))
            else:
                wg.append((PRJ, t_prjA))
                wg.append((CAST, t_fwc))
            def gmm(t, oc=oc, bank=bank, t0=t0):
                for c in range(8):
                    t.matmul(pb[bank][:, 0:FT], wqk_bf[:, c, oc * 128:(oc + 1) * 128],
                             xn_bf[:, c, HB + t0:HB + t0 + FT],
                             start=(c == 0), stop=False)
                return t.matmul(pb[bank][:, 0:FT], gw1sb[:, oc * 128:(oc + 1) * 128],
                                st28b, start=False, stop=True)
            emit(p, "tensor", gmm, FPE, wait=wg)
            t_g = FPE.n
            emit(p, "scalar", lambda sc, oc=oc, bank=bank: sc.activation(
                out=hm_bf[:, oc, :], in_=pb[bank][:, 0:FT], func=AF.Gelu,
                bias=gb1c_s[:, oc:oc + 1]), FAC, wait=[(FPE, t_g)])
            t_gel[oc] = FAC.n
        wf2 = [(FAC, t_gel[15])]
        if ti > 0:
            wf2.append((FAC, fprev["ztev"]))
        def f2mm(t):
            for oc in range(16):
                last = t.matmul(pb[1][0:28, 0:FT], gw2b[:, oc, :], hm_bf[:, oc, :],
                                start=(oc == 0), stop=(oc == 15))
            return last
        emit(p, "tensor", f2mm, FPE, wait=wf2)
        t_f2 = FPE.n
        emit(p, "scalar", lambda sc: sc.activation(
            out=zt, in_=pb[1][0:28, 0:FT], func=AF.Identity, bias=gb2_s[0:28, 0:1]),
             FAC, wait=[(FPE, t_f2)])
        fcur["ztev"] = FAC.n
        zet = xsq[0:28, 256:512]
        def splus(sc):
            i1 = sc.activation(out=wtile, in_=zt, func=AF.Abs)
            p.chain_bar("scalar", sc, i1)
            i1 = sc.activation(out=zet, in_=wtile, func=AF.Exp, scale=-1.0)
            p.chain_bar("scalar", sc, i1)
            i1 = sc.activation(out=wtile, in_=zet, func=AF.Ln,
                               bias=consts[0:28, 1:2])
            p.chain_bar("scalar", sc, i1)
            return sc.activation(out=zet, in_=zt, func=AF.Relu)
        wsp = [(FAC, fcur["ztev"])]
        if ti > 0:
            wsp.append((WRQ, fprev["wtsd"]))
        emit(p, "scalar", splus, FAC, wait=wsp)
        t_sp = FAC.n
        def wtsf(v):
            i1 = v.tensor_add(wtile, wtile, zet)
            p.chain_bar("vector", v, i1)
            return v.tensor_single_scalar(out=wtile, in_=wtile, scalar=0.02,
                                          op=ALU.add)
        wwt = [(FAC, t_sp)]
        if ti > 0:
            wwt.append((WRQ, fprev["wtsd"]))
        emit(p, "vector", wtsf, FVD, wait=wwt)
        t_wts = FVD.n
        wsw = [(FVD, t_wts)]
        if ti > 0:
            wsw.append((FAC, fprev["sw4ev"]))
        emit(p, "tensor", lambda t: t.matmul(pb[2][0:4, 0:FT], gsel_s[:], wtile,
                                             start=True, stop=True), FPE, wait=wsw)
        t_swm = FPE.n
        emit(p, "scalar", lambda sc: sc.activation(
            out=sw4, in_=pb[2][0:4, 0:FT], func=AF.Copy, bias=0.0), FAC,
             wait=[(FPE, t_swm)])
        fcur["sw4ev"] = FAC.n
        wwd = [(FVD, t_wts), (FAC, fcur["sw4ev"]), (WBC, WBC.n)]
        emit_dma(p, "sync", wts_d.ap()[0:28, sl * FT:(sl + 1) * FT], wtile, WRQ,
                 wait=wwd)
        t_wtsd = emit_dma(p, "sync", wts_d.ap()[28:32, sl * FT:(sl + 1) * FT], sw4,
                          WRQ)
        fcur["wtsd"] = t_wtsd
        for h in range(4):
            scr2 = tmp32[:, 1, 256:512]
            t_s0 = 0
            for s_ in range(7):
                eng = "vector" if s_ == 0 else "gpsimd"
                sem = FVD if s_ == 0 else FGP
                slot = (h * 7 + s_) % 4
                wsl = [(WRQ, t_wtsd), (WBC, WBC.n)]
                if slot_used[slot][0] is not None:
                    wsl.append(slot_used[slot])
                emit_dma(p, "sync", wbc[:, slot, :],
                         wts_d.ap()[h * 7 + s_:h * 7 + s_ + 1,
                                    sl * FT:(sl + 1) * FT].broadcast_to([128, FT]),
                         WBC, wait=wsl)
                t_bc = WBC.n
                for dci in range(2):
                    dc = h * 2 + dci
                    wcm = ([(WBC, t_bc)] if dci == 0 else []) + [(sem, sem.n)]
                    if s_ == 0 and dci == 0:
                        wcm.append((LX, t_x6))
                        if ti > 0:
                            wcm.append((FVD, fprev["onbD"]))
                        if ti > 0:
                            wcm.append((FGP, fprev["onbG"]))
                    if s_ == 1 and dci == 0:
                        wcm.append((FVD, t_s0))
                    if s_ == 5 and dci == 0:
                        wcm.append((GF, fcur["gf"]))
                    if 1 <= s_ <= 4 and dci == 0:
                        wcm.append((FVD, t_fir[s_ - 1]))
                    oad = xt[:, dc, 256:512]
                    def cmbf(v, s_=s_, dc=dc, t0=t0, slot=slot, scr2=scr2, oad=oad,
                             eng=eng):
                        if s_ == 0:
                            return v.scalar_tensor_tensor(
                                out=oad, in0=vc[:, dc, HB + t0:HB + t0 + FT],
                                scalar=firw_s[:, dc * 57:dc * 57 + 1],
                                in1=wbc[:, slot, :], op0=ALU.mult, op1=ALU.mult)
                        if s_ == 6:
                            src = vc[:, dc, HB + t0:HB + t0 + FT]
                        elif s_ == 5:
                            src = dt_bf[:, dc, :]
                        else:
                            src = firb[:, s_ - 1, dc, :]
                        i1 = v.tensor_mul(scr2, src, wbc[:, slot, :])
                        p.chain_bar(eng, v, i1)
                        return v.tensor_add(oad, oad, scr2)
                    emit(p, eng, cmbf, sem, wait=wcm)
                slot_used[slot] = (sem, sem.n)
                if s_ == 0:
                    t_s0 = FVD.n
        fcur["cmbD"] = FVD.n
        fcur["cmbG"] = FGP.n
        for h in range(4):
            eng = "gpsimd"
            sem = FGP
            slot = (28 + h) % 4
            wsl = [(WRQ, t_wtsd), (WBC, WBC.n)]
            if slot_used[slot][0] is not None:
                wsl.append(slot_used[slot])
            emit_dma(p, "sync", wbc[:, slot, :],
                     wts_d.ap()[28 + h:29 + h,
                                sl * FT:(sl + 1) * FT].broadcast_to([128, FT]),
                     WBC, wait=wsl)
            t_bc = WBC.n
            emit(p, "vector", lambda v, slot=slot: v.reciprocal(
                wbc[:, slot, :], wbc[:, slot, :]), FVD, wait=[(WBC, t_bc)])
            t_rcp2 = FVD.n
            def dvf(v, h=h, slot=slot, eng=eng):
                i1 = None
                for dci in range(2):
                    if i1 is not None:
                        p.chain_bar(eng, v, i1)
                    dc = h * 2 + dci
                    i1 = v.tensor_mul(xt[:, dc, 256:512], xt[:, dc, 256:512],
                                      wbc[:, slot, :])
                return i1
            emit(p, eng, dvf, sem, wait=[(FVD, t_rcp2)])
            slot_used[slot] = (sem, sem.n)
        t_divD = FVD.n
        t_divG = FGP.n
        rbank = [(pb[5], 0), (pb[5], 256), (pb[0], 0), (pb[0], 256)]
        t_onb = {}
        for h in range(4):
            bnk, cof = rbank[h]
            wq1 = [(FGP, t_divG)]
            if h == 2:
                wq1.append((FAC, fcur["stev"]))
            for dci in range(2):
                dc = h * 2 + dci
                xss = xsq[:, 0:256] if dci == 0 else xsq[:, 256:512]
                wsq_ = list(wq1) if dci == 0 else []
                if h > 0 or dci > 0:
                    wsq_.append((FPE, FPE.n))
                def sqf(sc, dc=dc, xss=xss):
                    return sc.activation(out=xss, in_=xt[:, dc, 256:512],
                                         func=AF.Square)
                emit(p, "scalar", sqf, FAC, wait=wsq_)
                t_s2 = FAC.n
                def rmm(t, bnk=bnk, cof=cof, xss=xss, dci=dci):
                    return t.matmul(bnk[0:1, cof:cof + FT], ones_cf[:], xss,
                                    start=(dci == 0), stop=(dci == 1))
                emit(p, "tensor", rmm, FPE, wait=[(FAC, t_s2)])
            t_rm = FPE.n
            rr = rows[:, h * 256:(h + 1) * 256]
            rri = rows[:, 1024 + h * 256:1024 + (h + 1) * 256]
            def rsq(sc, bnk=bnk, cof=cof, rr=rr):
                return sc.activation(out=rr, in_=bnk[0:1, cof:cof + FT],
                                     func=AF.Sqrt, scale=1.0 / 256.0,
                                     bias=consts[0:1, 0:1])
            emit(p, "scalar", rsq, FAC, wait=[(FPE, t_rm)])
            fcur["rmsev"] = FAC.n
            t_rs = FAC.n
            emit(p, "vector", lambda v, rr=rr, rri=rri: v.reciprocal(rri, rr), FVD,
                 wait=[(FAC, t_rs)])
            t_rc = FVD.n
            wbcm = [(FVD, t_rc)]
            if h > 0:
                wbcm.append((FVD, t_onb[h - 1]))
            elif ti > 0:
                wbcm.append((FVD, fprev["onbD"]))
            def bcm(t, rri=rri):
                return t.matmul(pb[2][:, 256:512], ones_f[:], rri[0:1, :],
                                start=True, stop=True)
            emit(p, "tensor", bcm, FPE, wait=wbcm)
            t_bcm = FPE.n
            def onf(v, h=h):
                for dci in range(2):
                    dc = h * 2 + dci
                    i1 = v.scalar_tensor_tensor(
                        out=onb[:, dc, :], in0=xt[:, dc, 256:512],
                        scalar=nw_s[:, dc % 2:dc % 2 + 1], in1=pb[2][:, 256:512],
                        op0=ALU.mult, op1=ALU.mult)
                return i1
            won = [(FPE, t_bcm), (FGP, t_divG)]
            emit(p, "vector", onf, FVD, wait=won)
            t_onb[h] = FVD.n
        fcur["onbD"] = FVD.n
        fcur["onbG"] = FGP.n
        t_yev = {}
        for oc in range(8):
            wwo = [(GC, t_wob), (WOS, WOS.n)]
            if ti * 8 + oc >= 2:
                wwo.append((FPE, t_womm_hist[(ti * 8 + oc) - 2]))
            t_wl = emit_dma(
                p, "sync", wos[:, oc % 2, :, :],
                wob.ap().rearrange("(c p) o -> p c o", p=128)
                [:, :, oc * 128:(oc + 1) * 128], WOS, wait=wwo)
            bank = 3 + oc % 2
            wmo = [(WOS, t_wl), (FVD, fcur["onbD"])]
            if oc == 0:
                wmo.append((FGP, fcur["onbG"]))
            if ti * 8 + oc >= 2:
                wmo.append((FVD, t_yev_hist[(ti * 8 + oc) - 2]))
            def womm(t, oc=oc, bank=bank):
                for dc in range(8):
                    last = t.matmul(pb[bank][:, 0:FT], wos[:, oc % 2, dc, :],
                                    onb[:, dc, :], start=(dc == 0), stop=(dc == 7))
                return last
            emit(p, "tensor", womm, FPE, wait=wmo)
            if ti == 0 and oc == 0:
                t_womm_hist = {}
                t_yev_hist = {}
            t_womm_hist[ti * 8 + oc] = FPE.n
            t_mo = FPE.n
            def yev(v, oc=oc, bank=bank):
                return v.tensor_add(xt[:, oc, 0:FT], xt[:, oc, 0:FT],
                                    pb[bank][:, 0:FT])
            emit(p, "vector", yev, FVD, wait=[(FPE, t_mo), (LX, t_x6)])
            t_yev_hist[ti * 8 + oc] = FVD.n
        fcur["yd_evD"] = FVD.n
        t_yd = emit_dma(p, "sync", y[:, t0:t0 + FT].rearrange("(c p) t -> p c t", p=128),
                        xt[:, :, 0:FT], OD,
                        wait=[(FVD, fcur["yd_evD"])])
        fcur["yd"] = t_yd
        fprev = fcur
    emit(p, "gpsimd", lambda g_: g_.nop().then_inc(p.sem("fin").h, 1), None,
         wait=[(OD, OD.n)])
    return p.build()


def _host_tables_moe(KSEL=2048):
    esel = np.zeros((128, 8), np.float32)
    for e in range(8):
        esel[16 * e:16 * e + 16, e] = 1
    trilblk = np.zeros((128, 128), np.float32)
    for q in range(128):
        for pp in range(16 * (q // 16), q):
            trilblk[pp, q] = 1
    tril128 = np.zeros((128, 128), np.float32)
    for q in range(128):
        tril128[:q, q] = 1
    identF = np.eye(128, dtype=np.float32)
    identB = np.eye(128).astype(BF)
    revj = np.tile((128 - np.arange(128))[None, :], (128, 1)).astype(np.float32)
    geoff = (KSEL + (KSEL + 1) * (np.arange(128) // 16)).astype(np.float32).reshape(128, 1)
    return esel, trilblk, tril128, identF, identB, revj, geoff


def _moe_maps(xf_full, inp, parts=None):
    TOK = 2048
    esel, trilblk, tril128, identF, identB, revj, geoff = _host_tables_moe()
    gate_w = inp["L0_ffn_gate_w"]; ln_g = inp["L0_ffn_ln_gamma"]
    ln_b = inp["L0_ffn_ln_beta"]
    fc1s = inp["L0_ffn_fc1s"]; b1s = inp["L0_ffn_b1s"]
    fc2s = inp["L0_ffn_fc2s"]; b2s = inp["L0_ffn_b2s"]
    maps = []
    for cid in range(8):
        e = cid
        sl = slice(cid * TOK, (cid + 1) * TOK)
        m = {
            "x_fm": np.ascontiguousarray(xf_full[:, sl]),
            "gate_wT": np.ascontiguousarray(gate_w.T),
            "eselT": np.ascontiguousarray(esel.T),
            "esel": esel, "trilblk": trilblk, "tril128": tril128,
            "identF": identF, "identB": identB, "revj": revj,
            "gamcol": np.ascontiguousarray(ln_g.reshape(8, 128).T),
            "betcol": np.ascontiguousarray(ln_b.reshape(8, 128).T),
            "b1col": np.ascontiguousarray(b1s[e].reshape(32, 128).T),
            "b2col": np.ascontiguousarray(b2s[e].reshape(8, 128).T),
            "fc1T": np.ascontiguousarray(fc1s[e].T),
            "fc2T": np.ascontiguousarray(fc2s[e].T),
            "gamrow": ln_g[None, :],
            "ehrow": np.eye(8, dtype=np.float32)[e][None, :],
            "ecol": np.full((128, 1), cid, np.float32), "geoff": geoff,
        }
        if parts is not None:
            b = cid // 4
            psl = slice((cid % 4) * TOK, (cid % 4 + 1) * TOK)
            for i in range(4):
                m[f"part{i}"] = np.ascontiguousarray(parts[b * 4 + i][:, psl])
        maps.append(m)
    return maps


_BUILD_CACHE = {}


def _get(name, fn):
    if name not in _BUILD_CACHE:
        _BUILD_CACHE[name] = fn()
    return _BUILD_CACHE[name]


def kernel(**inp):
    x = np.asarray(inp["L0_attn_a_x"], np.float32)
    S_ = 8192
    TOK = 2048

    wq = inp["L0_attn_a_w_q"]; wk = inp["L0_attn_a_w_k"]
    wv = inp["L0_attn_a_w_v"]; wo = inp["L0_attn_a_w_o"]
    nw0 = inp["L0_attn_a_norm_w"]
    rf = rope_factor_np()
    rf2 = np.ascontiguousarray(np.concatenate([rf, rf], 1).T)
    dmask = causal_diag_masks_np()
    hsel = np.zeros((128, 2), np.float32); hsel[:64, 0] = 1; hsel[64:, 1] = 1
    nc = _get("l1", build_l0_attn_a)
    maps = []
    for cid in range(8):
        b, g = cid // 4, cid % 4
        hs = [g, g + 4, g + 8, g + 12]
        wk_g = wk[:, g * 64:(g + 1) * 64]
        maps.append({
            "x_fm": np.ascontiguousarray(x[b].T), "rf2": rf2,
            "wq": np.ascontiguousarray(np.concatenate(
                [wq[:, 64 * h:64 * h + 64] for h in hs], 1)),
            "wk2": np.ascontiguousarray(np.concatenate([wk_g, wk_g], 1)),
            "wv": np.ascontiguousarray(wv[:, g * 64:(g + 1) * 64]),
            "wo": np.ascontiguousarray(np.concatenate(
                [wo[64 * h:64 * h + 64, :] for h in hs], 1)),
            "nwrow": nw0[None, :],
            "nwcol": np.ascontiguousarray(nw0.reshape(8, 128).T),
            "dmask": dmask, "hsel": hsel,
        })
    res = run_bass_kernel_spmd(nc, maps, core_ids=list(range(8)))
    parts = [np.asarray(res.results[c]["y"]) for c in range(8)]

    xf_full = np.ascontiguousarray(x.reshape(NTOK, 1024).T)
    nc = _get("l2", lambda: build_l0_moe(NT=NTOK))
    maps = _moe_maps(xf_full, inp, parts=parts)
    res = run_bass_kernel_spmd(nc, maps, core_ids=list(range(8)))
    x2f = np.concatenate([np.asarray(res.results[c]["y"]) for c in range(8)], 1)

    Wq = inp["L0_attn_b_Wq"]; Wk = inp["L0_attn_b_Wk"]; Wv = inp["L0_attn_b_Wv"]
    bq = inp["L0_attn_b_bq"]; bk = inp["L0_attn_b_bk"]; bv = inp["L0_attn_b_bv"]
    HL = 256
    nc = _get("l3", lambda: build_l0_attn_b(TOK=TOK))
    maps = []
    for cid in range(8):
        base = cid * TOK
        b = cid // 4
        bs, be = b * S_, (b + 1) * S_
        xh = np.zeros((1024, TOK + 2 * HL), np.float32)
        lo = max(bs, base - HL); hi = min(be, base + TOK + HL)
        xh[:, lo - (base - HL):hi - (base - HL)] = x2f[:, lo:hi]
        NQT = TOK // 512
        bm = np.zeros((NQT * 8, 128, 512), BF)
        for qi in range(NQT):
            for kc in range(8):
                qg = base + qi * 512 + np.arange(512)[None, :]
                kg = base - HL + (qi * 4 + kc) * 128 + np.arange(128)[:, None]
                m = (np.abs(qg - kg) <= 256) & (kg >= bs) & (kg < be)
                bm[qi * 8 + kc] = m.astype(BF)
        maps.append({
            "xh_fm": xh,
            "x2_fm": np.ascontiguousarray(x2f[:, base:base + TOK]),
            "wqT": np.ascontiguousarray(Wq.T), "wkT": np.ascontiguousarray(Wk.T),
            "wvT": np.ascontiguousarray(Wv.T),
            "bqc": np.ascontiguousarray(bq.reshape(8, 128).T),
            "bkc": np.ascontiguousarray(bk.reshape(8, 128).T),
            "bvrow": bv[None, :], "bmask": bm,
        })
    res = run_bass_kernel_spmd(nc, maps, core_ids=list(range(8)))
    x3f = np.concatenate([np.asarray(res.results[c]["y"]) for c in range(8)], 1)

    nc = _get("l4", lambda: build_l0_moe(NT=NTOK, add_ln=False))
    maps = _moe_maps(x3f, inp)
    res = run_bass_kernel_spmd(nc, maps, core_ids=list(range(8)))
    x4 = np.stack([np.ascontiguousarray(np.concatenate(
        [np.asarray(res.results[b * 4 + i]["y"]) for i in range(4)], 1).T)
        for b in range(2)])

    nwa = inp["L1_attn_a_norm_w"]; skip_w = inp["L1_attn_a_skip_w"]
    in_w = inp["L1_attn_a_in_w"]
    conv_w = inp["L1_attn_a_conv_w"]; conv_b = inp["L1_attn_a_conv_b"]
    wd1 = inp["L1_attn_a_wd1"]; wd2 = inp["L1_attn_a_wd2"]
    wd2_b = inp["L1_attn_a_wd2_b"]
    wB = inp["L1_attn_a_wB"]; wC = inp["L1_attn_a_wC"]
    A_log = inp["L1_attn_a_A_log"]; W_D = inp["L1_attn_a_W_D"]
    out_w = inp["L1_attn_a_out_w"]
    nc = _get("l5", lambda: build_l1_ssm(S_=S_))
    maps = []
    for cid in range(8):
        b, qtr = cid // 4, cid % 4
        sh = slice(qtr * 512, qtr * 512 + 512)
        def col(v512):
            return np.ascontiguousarray(v512.reshape(4, 128).T)
        maps.append({
            "x4_fm": np.ascontiguousarray(x4[b].T),
            "inw": np.ascontiguousarray(in_w[sh].T),
            "skw": np.ascontiguousarray(skip_w[sh].T),
            "wrow1": np.ascontiguousarray(wd1[:, sh].T),
            "wrowBC": np.ascontiguousarray(np.concatenate([wB, wC], 0)[:, sh].T),
            "wd2sT": np.ascontiguousarray(wd2[sh].T),
            "convw": np.ascontiguousarray(
                conv_w[sh].reshape(4, 128, 4).transpose(1, 0, 2).reshape(128, 16)),
            "convb": col(conv_b[sh]), "wd2b": col(wd2_b[sh]),
            "alog": np.ascontiguousarray(
                A_log[sh].reshape(4, 128, 16).transpose(1, 0, 2).reshape(128, 64)),
            "wdcol": col(W_D[sh]),
            "outw": np.ascontiguousarray(out_w[:, sh].T),
            "nwcol": np.ascontiguousarray(nwa.reshape(8, 128).T),
            "nwrow": nwa[None, :],
        })
    res = run_bass_kernel_spmd(nc, maps, core_ids=list(range(8)))
    x5f = np.concatenate([np.asarray(res.results[0]["y"]),
                          np.asarray(res.results[4]["y"])], 1)

    W1 = inp["L1_ffn_W1"]; b1 = inp["L1_ffn_b1"]
    W2 = inp["L1_ffn_W2"]; b2 = inp["L1_ffn_b2"]
    nc = _get("l6", lambda: build_l1_ffn(TOK=TOK))
    ffn_maps = lambda xf: [{
        "x_fm": np.ascontiguousarray(xf[:, cid * TOK:(cid + 1) * TOK]),
        "w1T": np.ascontiguousarray(W1.T), "w2T": np.ascontiguousarray(W2.T),
        "b1col": np.ascontiguousarray(b1.reshape(32, 128).T),
        "b2col": np.ascontiguousarray(b2.reshape(8, 128).T),
    } for cid in range(8)]
    res = run_bass_kernel_spmd(nc, ffn_maps(x5f), core_ids=list(range(8)))
    x6f = np.concatenate([np.asarray(res.results[c]["y"]) for c in range(8)], 1)

    g7 = lambda n: inp["L1_attn_b_" + n]
    w_q7, w_k7, w_v7, w_b7 = g7("w_q"), g7("w_k"), g7("w_v"), g7("w_b")
    firs_w = {k: g7("fir%d" % k) for k in (1, 3, 7, 15, 31)}
    NRK = S_ // TOK
    CPR = TOK // 128
    NCH7 = S_ // 128
    FT = 256
    NFT = TOK // FT
    FIROFF = {1: 0, 3: 1, 7: 4, 15: 11, 31: 26}
    firw_np = np.zeros((128, 8 * 57), np.float32)
    for dc in range(8):
        for ks, off in FIROFF.items():
            for j in range(ks):
                firw_np[:, dc * 57 + off + j] = firs_w[ks][dc * 128:(dc + 1) * 128, j]
    cw3_np = np.zeros((128, 96), np.float32)
    for kx, cw in enumerate((g7("conv_q_w"), g7("conv_k_w"), g7("conv_v_w"))):
        for dc in range(8):
            cw3_np[:, kx * 32 + dc * 4:kx * 32 + dc * 4 + 4] = cw[dc * 128:(dc + 1) * 128]
    r_ = np.arange(128)
    ident = np.eye(128)
    bdm = np.zeros((128, 128))
    for bb in range(4):
        bdm[bb * 32:(bb + 1) * 32, bb * 32:(bb + 1) * 32] = 1
    masks4 = np.concatenate([ident, -np.triu(np.ones((128, 128)), 1),
                             np.triu(np.ones((128, 128)), 0),
                             np.tril(np.ones((128, 128)), -1) * bdm,
                             np.triu(np.ones((128, 128)), 1) * bdm], 1).astype(BF)
    scol = np.zeros((128, 28 * 28), np.float32)
    for vv in range(28):
        scol[:, vv * 28 + vv] = 1.0
    scol = scol.astype(BF)
    gsel = np.zeros((28, 4), np.float32)
    for rr in range(28):
        gsel[rr, rr // 7] = 1.0
    gate_w1 = g7("gate_w1")
    nc = _get("l7", lambda: build_l1_dn(TOK=TOK, S_=S_))
    maps = []
    for cid in range(8):
        b, tb = cid // 4, cid % 4
        base = b * S_ + tb * TOK
        xh = np.zeros((D, TOK + 64), np.float32)
        lo = tb * TOK - 64
        src = x6f[:, max(b * S_ + lo, b * S_):base + TOK]
        xh[:, (TOK + 64) - src.shape[1]:] = src
        h = cid % 4
        ridx = np.zeros((128, NCH7 * 7), np.uint32)
        for i in range(NCH7):
            rl, cb = i // CPR, i % CPR
            for j in range(6):
                ridx[:, i * 7 + j] = (((h * NRK + rl) * 769 + j * 128 + r_) * CPR + cb)
            ridx[:, i * 7 + 6] = (((h * NRK + rl) * 769 + 768) * CPR + cb)
        didx = np.zeros((128, NFT * 8), np.uint32)
        for ti in range(NFT):
            for dc in range(8):
                didx[:, ti * 8 + dc] = (((dc // 2) * 256 + (dc % 2) * 128 + r_)
                                        * (S_ // FT) + tb * NFT + ti)
        maps.append({
            "xh_fm": xh,
            "x6_fm": np.ascontiguousarray(x6f[:, base:base + TOK]),
            "wq": w_q7, "wk": w_k7, "wv": w_v7, "wb4": w_b7, "cw3": cw3_np,
            "firw": firw_np, "gw1h": np.ascontiguousarray(gate_w1[0:D]),
            "gw1s": np.ascontiguousarray(gate_w1[D:]),
            "gb1c": np.ascontiguousarray(g7("gate_b1").reshape(16, 128).T),
            "gw2": g7("gate_w2"),
            "gb2r": np.ascontiguousarray(g7("gate_b2")[:, None]),
            "nwc2": np.ascontiguousarray(g7("norm_w").reshape(2, 128).T),
            "wo": g7("w_o"), "masks4": masks4, "scol28": scol, "gselc": gsel,
            "ridx": ridx, "didx": didx,
        })
    res = run_bass_kernel_spmd(nc, maps, core_ids=list(range(8)))
    x7f = np.concatenate([np.asarray(res.results[c]["y"]) for c in range(8)], 1)

    nc = _get("l6", lambda: build_l1_ffn(TOK=TOK))
    res = run_bass_kernel_spmd(nc, ffn_maps(x7f), core_ids=list(range(8)))
    outf = np.concatenate([np.asarray(res.results[c]["y"]) for c in range(8)], 1)
    return np.ascontiguousarray(outf.T).reshape(2, S_, 1024).astype(np.float32)


# version 02 (final)
# speedup vs baseline: 1.0074x; 1.0074x over previous
import sys
if '/opt/trn_rl_repo' not in sys.path:
    sys.path.insert(0, '/opt/trn_rl_repo')
from contextlib import ExitStack

import numpy as np
import ml_dtypes
import concourse.bass as bass
import concourse.mybir as mybir
from concourse.bass_core_rust import IndirectOffsetOnAxis
from concourse.bass_utils import run_bass_kernel_spmd

F32 = mybir.dt.float32
BF16 = mybir.dt.bfloat16
U32 = mybir.dt.uint32
U8 = mybir.dt.uint8
I32 = mybir.dt.int32
AF = mybir.ActivationFunctionType
ALU = mybir.AluOpType
AX = mybir.AxisListType

NCORE = 8
B = 2
S = 8192
NTOK = B * S
D = 1024
BF = ml_dtypes.bfloat16

ENGINES = ("sync", "tensor", "vector", "scalar", "gpsimd")


class SemT:
    def __init__(self, handle):
        self.h = handle
        self.n = 0

    def inc(self, k=1):
        self.n += k
        return self.n


class Prog:

    def __init__(self, name="prog"):
        self.nc = bass.Bass("TRN2", target_bir_lowering=False)
        self.stack = ExitStack()
        self.streams = {e: [] for e in ENGINES}
        self.sems = {}
        self.tiles = {}
        self.name = name

    def chain_bar(self, name, eng, inst):
        s = self.sem("_chain_" + name)
        if not hasattr(s, 'rt'):
            s.rt = 0
        s.rt += 1
        inst.then_inc(s.h, 1)
        return eng.wait_ge(s.h, s.rt)

    def sem(self, name):
        if name not in self.sems:
            h = self.stack.enter_context(self.nc.semaphore(name))
            self.sems[name] = SemT(h)
        return self.sems[name]

    def sb(self, name, shape, dtype):
        t = self.stack.enter_context(self.nc.sbuf_tensor(name, list(shape), dtype))
        self.tiles[name] = t
        return t

    def ps(self, name, shape, dtype=F32):
        t = self.stack.enter_context(self.nc.psum_tensor(name, list(shape), dtype))
        self.tiles[name] = t
        return t

    def add(self, engine, fn):
        self.streams[engine].append(fn)

    def build(self):
        block = self.stack.enter_context(self.nc.Block())
        for e in ENGINES:
            fns = self.streams[e]
            if not fns:
                continue
            def runner(eng, fns=fns):
                for fn in fns:
                    fn(eng)
            getattr(block, e)(runner)
        self.stack.close()
        return self.nc


def ceil_div(a, b):
    return (a + b - 1) // b



def rope_factor_np():
    d_head = 64
    i = np.arange(d_head)
    theta = np.float32(10000.0) ** (np.float32(-2.0) * (i // 2).astype(np.float32) / np.float32(d_head))
    theta = theta.astype(np.float32)
    pos = np.arange(S, dtype=np.float32)
    ang = (pos[:, None] * theta[None, :]).astype(np.float32)
    cos = np.cos(ang.astype(np.float64))
    sin = np.sin(ang.astype(np.float64))
    sign = np.where(np.arange(S) % 2 == 0, -1.0, 1.0)[:, None]
    return (cos + sin * sign).astype(np.float32)


def causal_diag_masks_np():
    q = np.arange(512)[None, :]
    out = []
    for koff in (0, 128, 256, 384):
        k = koff + np.arange(128)[:, None]
        out.append((k <= q).astype(BF))
    return np.stack(out)



def emit_dma(p, q, out, in_, sem, wait=()):
    assert out is not None and in_ is not None, "emit_dma got None AP"
    tgt = sem.inc(16)
    waits = list(wait)
    def f(eng):
        for (s, v) in waits:
            eng.wait_ge(s.h, v)
        eng.dma_start(out=out, in_=in_).then_inc(sem.h, 16)
    p.add(q, f)
    return tgt


def emit(p, engine, fn, sem=None, wait=()):
    tgt = sem.inc(1) if sem is not None else None
    waits = list(wait)
    def f(eng):
        for (s, v) in waits:
            eng.wait_ge(s.h, v)
        last = fn(eng)
        if sem is not None:
            assert last is not None, "fn must return its last instruction"
            last.then_inc(sem.h, 1)
    p.add(engine, f)
    return tgt



def build_l0_attn_a(S=S):
    p = Prog("l0aa")
    nc = p.nc
    QT = 512
    NQT = S // QT

    x_fm = nc.dram_tensor("x_fm", [D, S], F32, kind="ExternalInput").ap()
    rf2 = nc.dram_tensor("rf2", [128, S], F32, kind="ExternalInput").ap()
    wq = nc.dram_tensor("wq", [D, 256], F32, kind="ExternalInput").ap()
    wk2 = nc.dram_tensor("wk2", [D, 128], F32, kind="ExternalInput").ap()
    wv = nc.dram_tensor("wv", [D, 64], F32, kind="ExternalInput").ap()
    wo = nc.dram_tensor("wo", [64, 4 * D], F32, kind="ExternalInput").ap()
    nwrow = nc.dram_tensor("nwrow", [1, D], F32, kind="ExternalInput").ap()
    nwcol = nc.dram_tensor("nwcol", [128, 8], F32, kind="ExternalInput").ap()
    dmask = nc.dram_tensor("dmask", [4, 128, QT], BF16, kind="ExternalInput").ap()
    hsel = nc.dram_tensor("hsel", [128, 2], F32, kind="ExternalInput").ap()
    y = nc.dram_tensor("y", [D, S], F32, kind="ExternalOutput").ap()

    xt = [p.sb(f"xt{i}", [128, 8, QT], F32) for i in range(2)]
    rft = [p.sb(f"rft{i}", [128, QT], F32) for i in range(2)]
    xsq = p.sb("xsq", [128, QT], F32)
    xq = [p.sb(f"xq{i}", [128, 8, QT], BF16) for i in range(2)]
    tmp32 = p.sb("tmp32", [128, 2, QT], F32)
    rows = p.sb("rows", [1, 7 * QT], F32)
    q01 = p.sb("q01", [128, S], BF16)
    q23 = p.sb("q23", [128, S], BF16)
    kb2 = p.sb("kb2", [128, S], BF16)
    vones = p.sb("vones", [128, 64 * 65], BF16)
    ksq = p.sb("ksq", [64, QT], F32)
    wq_bf = p.sb("wq_bf", [128, 8 * 256], BF16)
    wk_bf = p.sb("wk_bf", [128, 8 * 128], BF16)
    wv_bf = p.sb("wv_bf", [128, 8 * 64], BF16)
    wo_bf = p.sb("wo_bf", [64, 4 * 1024], BF16)
    nw_sb = p.sb("nw_sb", [1, D], F32)
    nwp = p.sb("nwp", [128, 8], F32)
    ones_cf = p.sb("ones_cf", [128, 1], F32)
    ones_f = p.sb("ones_f", [1, 128], F32)
    hsel_sb = p.sb("hsel_sb", [128, 2], F32)
    mrow = p.sb("mrow", [1, NQT], F32)
    mrow2 = p.sb("mrow2", [1, NQT], F32)
    km = p.sb("km", [1, NQT], F32)
    qm = [p.sb(f"qm{h}", [1, NQT], F32) for h in range(4)]
    negc_row = p.sb("negc_row", [1, 4 * NQT], F32)
    negc = p.sb("negc", [128, 4 * NQT], F32)
    pbf = [[p.sb(f"p_h{h}_{par}", [128, QT], BF16) for par in range(2)] for h in range(4)]
    dmask_sb = p.sb("dmask_sb", [128, 4 * QT], BF16)
    r4b = p.sb("r4b", [65, 4 * QT], F32)
    rb = p.sb("rb", [64, 4 * QT], F32)
    ones64 = p.sb("ones64", [65, 128], F32)
    aoh = [p.sb(f"aoh{h}", [64, QT], BF16) for h in range(4)]
    outf = [p.sb(f"outf{i}", [128, 8, QT], F32) for i in range(1)]

    consts = p.sb("consts", [128, 4], F32)
    pb = [p.ps(f"pb{i}", [128, QT], F32) for i in range(8)]

    LD = p.sem("ld"); LX = [p.sem("lx0"), p.sem("lx1")]; CAST = p.sem("cast"); SQ = p.sem("sq"); STM = p.sem("stm")
    ROW = p.sem("row"); ROWA = p.sem("rowa"); NRM = p.sem("nrm"); XQ = p.sem("xq")
    PRJ = p.sem("prj"); QEV = p.sem("qev"); VEV = p.sem("vev"); QN = p.sem("qn")
    CRD = p.sem("crd")
    QK = [p.sem(f"qk{h}") for h in range(4)]
    PS = [p.sem(f"ps{h}") for h in range(4)]
    PM = [p.sem(f"pm{h}") for h in range(4)]
    AV = [p.sem(f"av{h}") for h in range(4)]
    RC = p.sem("rc"); RBM = p.sem("rbm"); RBE = p.sem("rbe"); AOE = p.sem("aoe")
    WOM = p.sem("wom"); WOE = p.sem("woe"); OD = p.sem("od")

    wld = xt[0][:].rearrange("p c t -> p (c t)")
    def wprep(dram_ap, n, nchunk, dst):
        t_w = emit_dma(p, "sync",
                       wld[0:128, 0:n].rearrange("p (c q) -> p c q", c=nchunk),
                       dram_ap, LD, wait=[(CAST, CAST.n)])
        emit(p, "vector", lambda v, n=n, dst=dst: v.tensor_copy(dst[:], wld[0:128, 0:n]),
             CAST, wait=[(LD, t_w)])
    wprep(wq.rearrange("(c p) q -> p c q", p=128), 2048, 8, wq_bf)
    wprep(wk2.rearrange("(c p) q -> p c q", p=128), 1024, 8, wk_bf)
    wprep(wv.rearrange("(c p) q -> p c q", p=128), 512, 8, wv_bf)
    t_wo = emit_dma(p, "sync", wld[0:64, 0:4096].rearrange("p (c q) -> p c q", c=4),
                    wo.rearrange("p (c q) -> p c q", c=4), LD, wait=[(CAST, CAST.n)])
    emit(p, "vector", lambda v: v.tensor_copy(wo_bf[:], wld[0:64, 0:4096]),
         CAST, wait=[(LD, t_wo)])
    LDC = p.sem("ldc")
    emit_dma(p, "sync", nw_sb[:], nwrow, LDC)
    emit_dma(p, "sync", nwp[:], nwcol, LDC)
    emit_dma(p, "sync", hsel_sb[:], hsel, LDC)
    emit_dma(p, "sync", dmask_sb[:].rearrange("p (a t) -> p a t", a=4),
             dmask.rearrange("a p t -> p a t"), LDC)
    t_const = LDC.n
    def init_const(v):
        v.memset(consts[:, 0:1], 1e-5)
        v.memset(consts[:, 1:2], 1.1920929e-07)
        v.memset(consts[:, 2:3], 1.0)
        v.memset(ones_cf[:], 1.0)
        v.memset(ones64[64:65, :], 1.0)
        v.memset(ones_f[:], 1.0)
        return v.memset(vones[:].rearrange("p (a o) -> p a o", o=65)[:, :, 64:65], 1.0)
    emit(p, "vector", init_const, CAST, wait=[(LDC, t_const)])
    t_cinit = CAST.n

    r_mu = rows[:, 0:QT]; r_var = rows[:, QT:2 * QT]; r_rstd = rows[:, 2 * QT:3 * QT]
    r_s = rows[:, 3 * QT:4 * QT]; r_ms = rows[:, 4 * QT:5 * QT]
    r_t = rows[:, 5 * QT:6 * QT]; r_t2 = rows[:, 6 * QT:7 * QT]

    hist = {"prj": [0], "qev": [0], "vev": [0], "qn": [0], "stm": [0], "xq": [0], "rowe": [0]}

    for ti in range(NQT):
        par = ti % 2
        t0 = ti * QT
        waits = [(CAST, t_cinit)]
        if ti >= 2:
            waits = [(PRJ, hist["prj"][ti - 1]), (QEV, hist["qev"][ti - 1])]
        t_x = emit_dma(p, "sync", xt[par][:],
                       x_fm[:, t0:t0 + QT].rearrange("(c p) t -> p c t", p=128), LX[par],
                       wait=waits)
        t_rf = emit_dma(p, "sync", rft[par][:], rf2[:, t0:t0 + QT], LX[par])
        def stats_sum(t, par=par):
            for c in range(8):
                last = t.matmul(pb[0][0:1, :], ones_cf[:], xt[par][:, c, :],
                                start=(c == 0), stop=(c == 7))
            return last
        emit(p, "tensor", stats_sum, STM,
             wait=[(LX[par], t_rf), (VEV, hist["vev"][ti]), (ROW, hist["rowe"][ti]),
                   (QN, hist["qn"][ti])])
        for c in range(8):
            w = [(LX[par], t_rf), (QN, hist["qn"][ti])] if c == 0 else [(STM, STM.n)]
            emit(p, "scalar", lambda sc, c=c, par=par: sc.activation(
                out=xsq[:], in_=xt[par][:, c, :], func=AF.Square), SQ, wait=w)
            emit(p, "tensor", lambda t, c=c: t.matmul(
                pb[1][0:1, :], ones_cf[:], xsq[:], start=(c == 0), stop=(c == 7)),
                STM, wait=[(SQ, SQ.n)])
        t_stm = STM.n
        hist["stm"].append(t_stm)
        def row_a(v):
            v.tensor_single_scalar(out=r_mu, in_=pb[0][0:1, :], scalar=1.0 / D, op=ALU.mult)
            i = v.tensor_single_scalar(out=r_var, in_=pb[1][0:1, :], scalar=1.0 / D, op=ALU.mult)
            p.chain_bar("vector", v, i)
            i = v.tensor_mul(r_t, r_mu, r_mu)
            p.chain_bar("vector", v, i)
            return v.tensor_sub(r_var, r_var, r_t)
        emit(p, "vector", row_a, ROW, wait=[(STM, t_stm)])
        emit(p, "scalar", lambda sc: sc.activation(out=r_t, in_=r_var, func=AF.Sqrt,
                                                   bias=consts[0:1, 0:1]),
             ROWA, wait=[(ROW, ROW.n)])
        def row_c(v):
            i = v.reciprocal(r_rstd, r_t)
            p.chain_bar("vector", v, i)
            i = v.tensor_mul(r_t2, r_rstd, r_rstd)
            p.chain_bar("vector", v, i)
            return v.tensor_mul(r_t2, r_t2, r_var)
        emit(p, "vector", row_c, ROW, wait=[(ROWA, ROWA.n)])
        emit(p, "scalar", lambda sc: sc.activation(out=r_t2, in_=r_t2, func=AF.Sqrt,
                                                   bias=consts[0:1, 1:2]),
             ROWA, wait=[(ROW, ROW.n)])
        def row_e(v):
            i = v.reciprocal(r_t2, r_t2)
            p.chain_bar("vector", v, i)
            i = v.tensor_mul(r_s, r_rstd, r_t2)
            p.chain_bar("vector", v, i)
            return v.tensor_mul(r_ms, r_mu, r_s)
        emit(p, "vector", row_e, ROW, wait=[(ROWA, ROWA.n)])
        t_rowe = ROW.n
        hist["rowe"].append(t_rowe)
        emit(p, "tensor", lambda t: t.matmul(pb[2][:], ones_f[:], r_s, start=True, stop=True),
             NRM, wait=[(ROW, t_rowe), (XQ, hist["xq"][ti]), (VEV, hist["vev"][ti])])
        for c in range(8):
            bank = 3 + (c % 2)
            w = [(NRM, NRM.n)] if c < 2 else [(XQ, XQ.n - 1)]
            if c < 2:
                w.append((XQ, hist["xq"][ti]))
                w.append((VEV, hist["vev"][ti]))
            emit(p, "tensor", lambda t, c=c, bank=bank: t.matmul(
                pb[bank][:], nw_sb[:, c * 128:(c + 1) * 128], r_ms, start=True, stop=True),
                NRM, wait=w)
            t_r1 = NRM.n
            def apply_c(v, c=c, bank=bank, par=par):
                i = v.tensor_mul(tmp32[:, c % 2, :], xt[par][:, c, :], pb[2][:])
                p.chain_bar("vector", v, i)
                return v.scalar_tensor_tensor(out=xq[par][:, c, :], in0=tmp32[:, c % 2, :],
                                              scalar=nwp[:, c:c + 1], in1=pb[bank][:],
                                              op0=ALU.mult, op1=ALU.subtract)
            emit(p, "vector", apply_c, XQ, wait=[(NRM, t_r1)])
        t_xq = XQ.n
        hist["xq"].append(t_xq)
        def proj(t, par=par):
            for qc in range(2):
                for c in range(8):
                    t.matmul(pb[5 + qc][:], wq_bf[:, c * 256 + qc * 128: c * 256 + qc * 128 + 128],
                             xq[par][:, c, :], start=(c == 0), stop=(c == 7))
            for c in range(8):
                t.matmul(pb[7][:], wk_bf[:, c * 128:(c + 1) * 128], xq[par][:, c, :],
                         start=(c == 0), stop=(c == 7))
            for sj in range(4):
                for c in range(8):
                    last = t.matmul(pb[sj][:, 0:64],
                                    xq[par][:, c, sj * 128:(sj + 1) * 128],
                                    wv_bf[:, c * 64:(c + 1) * 64],
                                    start=(c == 0), stop=(c == 7))
            return last
        emit(p, "tensor", proj, PRJ,
             wait=[(XQ, t_xq), (QN, hist["qn"][ti]), (VEV, hist["vev"][ti]),
                   (ROW, t_rowe)])
        t_prj = PRJ.n
        hist["prj"].append(t_prj)
        def qev(v, par=par, t0=t0):
            v.tensor_mul(q01[:, t0:t0 + QT], pb[5][:], rft[par][:])
            v.tensor_mul(q23[:, t0:t0 + QT], pb[6][:], rft[par][:])
            return v.tensor_mul(kb2[:, t0:t0 + QT], pb[7][:], rft[par][:])
        emit(p, "vector", qev, QEV, wait=[(PRJ, t_prj)])
        t_qev = QEV.n
        hist["qev"].append(t_qev)
        def vev(sc, ti=ti):
            for sj in range(4):
                tj = ti * 4 + sj
                last = sc.activation(out=vones[:, tj * 65: tj * 65 + 64],
                                     in_=pb[sj][:, 0:64], func=AF.Copy)
            return last
        emit(p, "scalar", vev, VEV, wait=[(PRJ, t_prj)])
        t_vev = VEV.n
        hist["vev"].append(t_vev)
        def qsq_a(sc, t0=t0):
            return sc.activation(out=xsq[:], in_=q01[:, t0:t0 + QT], func=AF.Square)
        emit(p, "scalar", qsq_a, SQ, wait=[(QEV, t_qev)])
        emit(p, "tensor", lambda t: t.matmul(pb[5][0:1, :], hsel_sb[:, 0:1], xsq[:],
                                             start=True, stop=True), QN, wait=[(SQ, SQ.n)])
        emit(p, "tensor", lambda t: t.matmul(pb[6][0:1, :], hsel_sb[:, 1:2], xsq[:],
                                             start=True, stop=True), QN)
        t_qn01 = QN.n
        def qsq_b(sc, t0=t0):
            return sc.activation(out=xsq[:], in_=q23[:, t0:t0 + QT], func=AF.Square)
        emit(p, "scalar", qsq_b, SQ, wait=[(QN, t_qn01)])
        emit(p, "tensor", lambda t: t.matmul(pb[0][0:1, :], hsel_sb[:, 0:1], xsq[:],
                                             start=True, stop=True), QN,
             wait=[(SQ, SQ.n), (VEV, hist["vev"][ti + 1])])
        emit(p, "tensor", lambda t: t.matmul(pb[1][0:1, :], hsel_sb[:, 1:2], xsq[:],
                                             start=True, stop=True), QN)
        t_qn23 = QN.n
        def ksq_f(sc, t0=t0):
            return sc.activation(out=ksq[:], in_=kb2[0:64, t0:t0 + QT], func=AF.Square)
        emit(p, "scalar", ksq_f, SQ, wait=[(QN, t_qn23)])
        emit(p, "tensor", lambda t: t.matmul(pb[7][0:1, :], ones_cf[0:64, :], ksq[:],
                                             start=True, stop=True), QN, wait=[(SQ, SQ.n)])
        t_knm = QN.n
        def nrm_ev(v, ti=ti):
            v.tensor_reduce(out=qm[0][:, ti:ti + 1], in_=pb[5][0:1, :], axis=AX.X, op=ALU.max)
            v.tensor_reduce(out=qm[1][:, ti:ti + 1], in_=pb[6][0:1, :], axis=AX.X, op=ALU.max)
            v.tensor_reduce(out=qm[2][:, ti:ti + 1], in_=pb[0][0:1, :], axis=AX.X, op=ALU.max)
            v.tensor_reduce(out=qm[3][:, ti:ti + 1], in_=pb[1][0:1, :], axis=AX.X, op=ALU.max)
            return v.tensor_reduce(out=km[:, ti:ti + 1], in_=pb[7][0:1, :], axis=AX.X, op=ALU.max)
        emit(p, "vector", nrm_ev, QN, wait=[(QN, t_knm)])
        hist["qn"].append(QN.n)

    t_p1 = QN.n
    t_vev_last = VEV.n
    def cmax(v):
        i = v.tensor_copy(mrow[:], km[:])
        a, b = mrow, mrow2
        s = 1
        while s < NQT:
            p.chain_bar("vector", v, i)
            v.tensor_max(b[:, s:NQT], a[:, 0:NQT - s], a[:, s:NQT])
            i = v.tensor_copy(b[:, 0:s], a[:, 0:s])
            a, b = b, a
            s *= 2
        p.chain_bar("vector", v, i)
        for h in range(4):
            last = v.tensor_mul(negc_row[:, h * NQT:(h + 1) * NQT], qm[h][:], a[:])
        return last
    emit(p, "vector", cmax, ROW, wait=[(QN, t_p1)])
    t_cm = ROW.n
    emit(p, "scalar", lambda sc: sc.activation(out=negc_row[:], in_=negc_row[:], func=AF.Sqrt),
         ROWA, wait=[(ROW, t_cm)])
    t_cs = ROWA.n
    emit(p, "vector", lambda v: v.tensor_single_scalar(
        out=negc_row[:], in_=negc_row[:], scalar=-0.125, op=ALU.mult), ROW, wait=[(ROWA, t_cs)])
    t_cn = ROW.n
    emit(p, "tensor", lambda t: t.matmul(pb[2][:, 0:4 * NQT], ones_f[:], negc_row[:],
                                         start=True, stop=True), NRM,
         wait=[(ROW, t_cn), (VEV, t_vev_last)])
    t_cbm = NRM.n
    emit(p, "scalar", lambda sc: sc.activation(out=negc[:], in_=pb[2][:, 0:4 * NQT],
                                               func=AF.Copy),
         CRD, wait=[(NRM, t_cbm)])
    t_crd = CRD.n

    ndiag = [0] * 4
    nav = [0] * 4
    nexp = [0] * 4
    nqk = [0] * 4
    t_woe_prev = 0
    t_aoe_prev = 0
    od_hist = [0]
    for qi in range(NQT):
        nk = 4 * (qi + 1)
        q0 = qi * QT
        for kc in range(nk):
            k0 = kc * 128
            par = kc % 2
            diag = kc >= nk - 4
            dj = kc - (nk - 4)
            for h in range(4):
                qsrc = (q01 if h < 2 else q23)
                base = 0 if h % 2 == 0 else 64
                w = []
                if nexp[h] > 0:
                    w.append((PS[h], nexp[h]))
                if qi == 0 and kc == 0:
                    w.append((CRD, t_crd))
                if kc == 0 and qi > 0:
                    w.append((WOE, t_woe_prev))
                emit(p, "tensor", lambda t, h=h, k0=k0, q0=q0, base=base, qsrc=qsrc: t.matmul(
                    pb[h][:], kb2[base:base + 64, k0:k0 + 128],
                    qsrc[base:base + 64, q0:q0 + QT], start=True, stop=True), QK[h], wait=w)
                nqk[h] += 1
            for h in range(4):
                w = [(QK[h], nqk[h])]
                if nav[h] >= 2:
                    w.append((AV[h], nav[h] - 1))
                emit(p, "scalar", lambda sc, h=h, qi=qi, par=par: sc.activation(
                    out=pbf[h][par][:], in_=pb[h][:], func=AF.Exp,
                    scale=0.125, bias=negc[:, h * NQT + qi: h * NQT + qi + 1]),
                    PS[h], wait=w)
                nexp[h] += 1
            if diag:
                for h in range(4):
                    emit(p, "vector", lambda v, h=h, par=par, dj=dj: v.tensor_mul(
                        pbf[h][par][:], pbf[h][par][:], dmask_sb[:, dj * QT:(dj + 1) * QT]),
                        PM[h], wait=[(PS[h], nexp[h])])
                    ndiag[h] += 1
            for h in range(4):
                w = [(PM[h], ndiag[h])] if diag else [(PS[h], nexp[h])]
                if kc == 0 and qi > 0:
                    w.append((AOE, t_aoe_prev))
                emit(p, "tensor", lambda t, h=h, kc=kc, par=par, nk=nk: t.matmul(
                    pb[4 + h][0:65, :], vones[:, kc * 65: kc * 65 + 65],
                    pbf[h][par][:], start=(kc == 0), stop=(kc == nk - 1),
                    skip_group_check=True), AV[h], wait=w)
                nav[h] += 1
        def recip_f(v, navs=tuple(nav)):
            for h in range(4):
                v.wait_ge(AV[h].h, navs[h])
                last = v.reciprocal(r4b[64:65, h * QT:(h + 1) * QT], pb[4 + h][64:65, :])
            return last
        emit(p, "vector", recip_f, RC)
        t_rc = RC.n
        for h in range(4):
            w = [(RC, t_rc)]
            if qi > 0 and h == 0:
                w.append((WOE, t_woe_prev))
            emit(p, "tensor", lambda t, h=h: t.matmul(
                pb[h][0:64, :], ones64[64:65, 0:64], r4b[64:65, h * QT:(h + 1) * QT],
                start=True, stop=True), RBM, wait=w)
        t_rbm = RBM.n
        def rb_ev(sc):
            for h in range(4):
                last = sc.activation(out=rb[:, h * QT:(h + 1) * QT], in_=pb[h][0:64, :],
                                     func=AF.Copy)
            return last
        emit(p, "scalar", rb_ev, RBE, wait=[(RBM, t_rbm)])
        t_rbe = RBE.n
        def ao_ev(v):
            for h in range(4):
                last = v.tensor_mul(aoh[h][:], pb[4 + h][0:64, :], rb[:, h * QT:(h + 1) * QT])
            return last
        emit(p, "vector", ao_ev, AOE, wait=[(RBE, t_rbe)])
        t_aoe = AOE.n
        t_aoe_prev = t_aoe
        opar = 0
        for dc in range(8):
            bank = 2 + (dc % 2)
            w = [(AOE, t_aoe)]
            if WOE.n >= 1:
                w.append((WOE, WOE.n - 1))
            def wo_mm(t, dc=dc, bank=bank):
                for h in range(4):
                    last = t.matmul(pb[bank][:], wo_bf[:, h * 1024 + dc * 128: h * 1024 + dc * 128 + 128],
                                    aoh[h][:], start=(h == 0), stop=(h == 3),
                                    skip_group_check=True)
                return last
            emit(p, "tensor", wo_mm, WOM, wait=w)
            t_wom = WOM.n
            w2 = [(WOM, t_wom)]
            if dc == 0 and qi >= 1:
                w2.append((OD, od_hist[qi]))
            emit(p, "scalar", lambda sc, dc=dc, bank=bank, opar=opar: sc.activation(
                out=outf[opar][:, dc, :], in_=pb[bank][:], func=AF.Copy), WOE, wait=w2)
        t_woe = WOE.n
        t_woe_prev = t_woe
        t_od = emit_dma(p, "gpsimd", y[:, q0:q0 + QT].rearrange("(c p) t -> p c t", p=128),
                        outf[opar][:], OD, wait=[(WOE, t_woe)])
        od_hist.append(t_od)
    emit(p, "gpsimd", lambda g: g.nop().then_inc(p.sem("fin").h, 1), None, wait=[(OD, OD.n)])
    return p.build()



def build_l0_moe(NT=NTOK, add_ln=True):
    p = Prog("moe")
    nc = p.nc
    TOK = NT // 8
    KSEL = NT // 8
    NTI = TOK // 512
    NGR = ceil_div(KSEL, 512)
    NRD = KSEL // 128
    FJ = NT // 16
    NSEG = NT // 128
    CAP = 64
    assert KSEL % 128 == 0 and TOK % 512 == 0

    x_fm = nc.dram_tensor("x_fm", [D, TOK], F32, kind="ExternalInput").ap()
    parts = [nc.dram_tensor(f"part{i}", [D, TOK], F32, kind="ExternalInput").ap()
             for i in range(4)] if add_ln else []
    gate_wT = nc.dram_tensor("gate_wT", [D, 8], F32, kind="ExternalInput").ap()
    eselT = nc.dram_tensor("eselT", [8, 128], F32, kind="ExternalInput").ap()
    esel = nc.dram_tensor("esel", [128, 8], F32, kind="ExternalInput").ap()
    trilblk = nc.dram_tensor("trilblk", [128, 128], F32, kind="ExternalInput").ap()
    tril128 = nc.dram_tensor("tril128", [128, 128], F32, kind="ExternalInput").ap()
    identF = nc.dram_tensor("identF", [128, 128], F32, kind="ExternalInput").ap()
    identB = nc.dram_tensor("identB", [128, 128], BF16, kind="ExternalInput").ap()
    revj = nc.dram_tensor("revj", [128, 128], F32, kind="ExternalInput").ap()
    gamcol = nc.dram_tensor("gamcol", [128, 8], F32, kind="ExternalInput").ap()
    betcol = nc.dram_tensor("betcol", [128, 8], F32, kind="ExternalInput").ap()
    b1col = nc.dram_tensor("b1col", [128, 32], F32, kind="ExternalInput").ap()
    b2col = nc.dram_tensor("b2col", [128, 8], F32, kind="ExternalInput").ap()
    fc1T = nc.dram_tensor("fc1T", [D, 4096], F32, kind="ExternalInput").ap()
    fc2T = nc.dram_tensor("fc2T", [4096, D], F32, kind="ExternalInput").ap()
    y = nc.dram_tensor("y", [D, TOK], F32, kind="ExternalOutput").ap()
    wqb = nc.dram_tensor("wqb", [D, D], BF16, kind="Internal")
    wkb = nc.dram_tensor("wkb", [D, D], BF16, kind="Internal")
    wvb = nc.dram_tensor("wvb", [D, D], BF16, kind="Internal")

    x1sc = nc.dram_tensor("x1sc", [D, TOK], F32, kind="Internal")
    xfm_my = nc.dram_tensor("xfm_my", [TOK, D], BF16, kind="Internal")
    xfm_all = nc.dram_tensor("xfm_all", [NT, D], BF16, kind="Internal")
    sc_my = nc.dram_tensor("sc_my", [8, TOK], F32, kind="Internal")
    sc_all = nc.dram_tensor("sc_all", [64, TOK], F32, kind="Internal")
    sc_em = nc.dram_tensor("sc_em", [8, NT], F32, kind="Internal")
    sce_d = nc.dram_tensor("sce_d", [1, NT], F32, kind="Internal")
    idpad = nc.dram_tensor("idpad", [1, 128 * CAP], F32, kind="Internal")
    oe = nc.dram_tensor("oe", [KSEL + 1, D], BF16, kind="Internal")
    oall = nc.dram_tensor("oall", [8 * (KSEL + 1), D], BF16, kind="Internal")
    w1bf = nc.dram_tensor("w1bf", [D, 4096], BF16, kind="Internal")
    w2bf = nc.dram_tensor("w2bf", [4096, D], BF16, kind="Internal")

    xt = p.sb("xt", [128, 8, 512], F32)
    x1t = p.sb("x1t", [128, 8, 512], F32)
    tmp32 = p.sb("tmp32", [128, 2, 512], F32)
    xf32 = p.sb("xf32", [128, 2, 512], F32)
    xfb = p.sb("xfb", [128, 8, 512], BF16)
    xtm = p.sb("xtm", [128, 8, 128], BF16)
    rows = p.sb("rows", [1, 8 * 512], F32)
    consts = p.sb("consts", [128, 4], F32)
    ones_cf = p.sb("ones_cf", [128, 1], F32)
    ones_f = p.sb("ones_f", [1, 128], F32)
    ones_bf = p.sb("ones_bf", [128, 1], BF16)
    gwt = p.sb("gwt", [128, 8, 8], F32)
    scq = p.sb("scq", [8, 512], F32)
    sc4 = p.sb("sc4", [128, FJ], F32)
    eselT_s = p.sb("eselT_s", [8, 128], F32)
    esel_s = p.sb("esel_s", [128, 8], F32)
    trilb_s = p.sb("trilb_s", [128, 128], F32)
    tril_s = p.sb("tril_s", [128, 128], F32)
    identF_s = p.sb("identF_s", [128, 128], F32)
    identB_s = p.sb("identB_s", [128, 128], BF16)
    lo = p.sb("lo", [8, 1], F32)
    hi = p.sb("hi", [8, 1], F32)
    mid = p.sb("mid", [8, 1], F32)
    cnt8 = p.sb("cnt8", [8, 1], F32)
    msk8 = p.sb("msk8", [8, 1], U8)
    thr = p.sb("thr", [128, 1], F32)
    cnt128 = p.sb("cnt128", [128, 1], F32)
    mtile = p.sb("mtile", [128, FJ], F32)
    cums = p.sb("cums", [128, FJ], F32)
    gidx = p.sb("gidx", [128, FJ], F32)
    zfj = p.sb("zfj", [128, FJ], F32)
    lorow = p.sb("lorow", [1, 8], F32)
    offc = p.sb("offc", [128, 1], F32)
    gT = p.sb("gT", [128, 16 * 8], U32)
    sce = p.sb("sce", [128, 128], F32)
    mske = p.sb("mske", [128, 128], F32)
    vals = p.sb("vals", [128, 128], F32)
    mx = p.sb("mx", [128, CAP], F32)
    ids48 = p.sb("ids48", [128, CAP], F32)
    idcol = p.sb("idcol", [128, 1], I32)
    idcolf = p.sb("idcolf", [128, 1], F32)
    cnte = p.sb("cnte", [128, 1], F32)
    orow = p.sb("orow", [1, 128], F32)
    ob = p.sb("ob", [128, 128], F32)
    gcol = p.sb("gcol", [128, NRD], I32)
    gcolf = p.sb("gcolf", [128, NRD], F32)
    sig = p.sb("sig", [128, 1], F32)
    osg = p.sb("osg", [128, 1], F32)
    lam = p.sb("lam", [128, 1], F32)
    flt = p.sb("flt", [128, 1], F32)
    fltu = p.sb("fltu", [128, 1], U32)
    tid = p.sb("tid", [128, NRD], F32)
    tidu = p.sb("tidu", [128, 1], U32)
    kwv = p.sb("kwv", [128, NRD], F32)
    cmp128 = p.sb("cmp128", [128, 128], F32)
    xg = p.sb("xg", [128, 1024], BF16)
    xh = p.sb("xh", [128, 8, 512], BF16)
    pass
    w2s = [p.sb(f"w2s{i}", [128, 32, 128], BF16) for i in range(2)]
    hbf = p.sb("hbf", [128, 32, 512], BF16)
    ofm = p.sb("ofm", [128, 8, 512], BF16)
    x2c = p.sb("x2c", [128, 8, 128], F32)
    zrow = p.sb("zrow", [1, D], BF16)
    pb = [p.ps(f"pb{i}", [128, 512], F32) for i in range(6)]
    pbb = [p.ps(f"pbb{i}", [128, 128], BF16) for i in range(2)]
    ALIAS_PLACEHOLDER = 1

    LD = p.sem("ld"); LDC = p.sem("ldc"); LW = p.sem("lw"); PACC = p.sem("pacc")
    LWA = [p.sem("lwa0"), p.sem("lwa1")]; LWB = [p.sem("lwb0"), p.sem("lwb1")]
    OE = p.sem("oe_dma")
    CAST = p.sem("cast"); STM = p.sem("stm"); ROW = p.sem("row"); ROWA = p.sem("rowa")
    NRM = p.sem("nrm"); XQ = p.sem("xq"); TRP = p.sem("trp"); TRE = p.sem("tre")
    SCM = p.sem("scm"); SCE = p.sem("sce_s"); OD = p.sem("od"); CC = p.sem("cc")
    BIS = p.sem("bis"); BPE = p.sem("bpe"); BSE = p.sem("bse")
    CMP = p.sem("cmp"); CPE = p.sem("cpe"); CSE = p.sem("cse")
    GD = p.sem("gd"); FFM = p.sem("ffm"); FFE = p.sem("ffe"); GEL = p.sem("gel")
    AGD = p.sem("agd"); CMB = p.sem("cmb"); TR2 = p.sem("tr2")

    r_mu = rows[:, 0:512]; r_var = rows[:, 512:1024]; r_rstd = rows[:, 1024:1536]
    r_t = rows[:, 1536:2048]; r_ms = rows[:, 2048:2560]; r_r2 = rows[:, 2560:3072]

    xsq = p.sb("xsq", [128, 512], F32)
    xsqb = p.sb("xsqb", [128, 512], BF16)
    iot = p.sb("iot", [128, 1], I32)
    iotf = p.sb("iotf", [128, 1], F32)
    gath8 = [p.sb(f"gath8_{e}", [128, 1024], BF16) for e in range(8)]
    xfmt = p.sb("xfmt", [128, 1024], BF16)
    acct = p.sb("acct", [128, 1024], F32)
    w1t = [p.sb(f"w1t{i}", [128, 8, 128], BF16) for i in range(2)]
    vals2 = p.sb("vals2", [128, 128], F32)
    t8 = p.sb("t8", [1, 8], F32)
    myth = p.sb("myth", [128, 1], F32)

    gidxu = p.sb("gidxu", [128, 1], U32)


    gamrow = nc.dram_tensor("gamrow", [1, D], F32, kind="ExternalInput").ap()
    ehrow = nc.dram_tensor("ehrow", [1, 8], F32, kind="ExternalInput").ap()
    ecol = nc.dram_tensor("ecol", [128, 1], F32, kind="ExternalInput").ap()
    geoff = nc.dram_tensor("geoff", [128, 1], F32, kind="ExternalInput").ap()
    gam_s = p.sb("gam_s", [1, D], F32)
    gamcol_s = p.sb("gamcol_s", [128, 8], F32)
    betcol_s = p.sb("betcol_s", [128, 8], F32)
    b1col_s = p.sb("b1col_s", [128, 32], F32)
    b2col_s = p.sb("b2col_s", [128, 8], F32)
    eh_s = p.sb("eh_s", [1, 8], F32)
    ecol_s = p.sb("ecol_s", [128, 1], F32)
    geoff_s = p.sb("geoff_s", [128, 1], F32)
    kwrow = p.sb("kwrow", [1, 512], F32)
    tmpc = p.sb("tmpc", [128, 1], F32)
    sceg = hbf[:].rearrange("p a b -> p (a b)").bitcast(F32)[0:8, 0:TOK]
    gmy8 = ofm[:].rearrange("p a b -> p (a b)").bitcast(F32)[0:8, 0:TOK]

    emit_dma(p, "sync", gwt[:], gate_wT.rearrange("(c p) e -> p c e", p=128), LDC)
    emit_dma(p, "sync", eselT_s[:], eselT, LDC)
    emit_dma(p, "sync", esel_s[:], esel, LDC)
    emit_dma(p, "sync", trilb_s[:], trilblk, LDC)
    emit_dma(p, "sync", tril_s[:], tril128, LDC)
    emit_dma(p, "sync", identF_s[:], identF, LDC)
    emit_dma(p, "sync", identB_s[:], identB, LDC)
    emit_dma(p, "sync", vals2[:], revj, LDC)
    emit_dma(p, "sync", gam_s[:], gamrow, LDC)
    emit_dma(p, "sync", gamcol_s[:], gamcol, LDC)
    emit_dma(p, "sync", betcol_s[:], betcol, LDC)
    emit_dma(p, "sync", b1col_s[:], b1col, LDC)
    emit_dma(p, "sync", b2col_s[:], b2col, LDC)
    emit_dma(p, "sync", eh_s[:], ehrow, LDC)
    emit_dma(p, "sync", ecol_s[:], ecol, LDC)
    emit_dma(p, "sync", geoff_s[:], geoff, LDC)
    t_ldc = LDC.n
    def init_c(v):
        v.memset(consts[:, 0:1], 1e-5)
        v.memset(consts[:, 1:2], 1.0)
        v.memset(ones_cf[:], 1.0)
        v.memset(ones_f[:], 1.0)
        v.memset(ones_bf[:], 1.0)
        v.memset(zfj[:], 0.0)
        return v.memset(zrow[:], 0.0)
    emit(p, "vector", init_c, CAST, wait=[(LDC, t_ldc)])
    t_init = CAST.n
    GP = p.sem("gp")
    emit(p, "gpsimd", lambda g: g.iota(out=iot[:], pattern=[[1, 1]], base=0,
                                       channel_multiplier=1), GP)
    emit(p, "vector", lambda v: v.tensor_copy(iotf[:], iot[:]), CAST, wait=[(GP, GP.n)])
    t_iotf = CAST.n
    LWG = p.sem("lwg")
    t_w1c = emit_dma(p, "gpsimd", w1bf.ap(), fc1T, LWG)
    t_w2c = emit_dma(p, "gpsimd", w2bf.ap(), fc2T, LWG)
    t_oz = emit_dma(p, "gpsimd", oe.ap()[KSEL:KSEL + 1, :], zrow[:], LWG,
                    wait=[(CAST, t_init)])
    t_w1c = t_w2c = t_oz = LWG.n

    LX = p.sem("lx")
    hist = {"xq": [0], "scm": [0], "trp": [0], "tre": [0], "stm": [0], "row": [0],
            "od": [0]}
    for ti in range(NTI):
        t0 = ti * 512
        w = [(CAST, t_init)] if ti == 0 else [(XQ, hist["xq"][ti]), (TRE, hist["tre"][ti])]
        t_x = emit_dma(p, "sync", xt[:],
                       x_fm[:, t0:t0 + 512].rearrange("(c p) t -> p c t", p=128), LX, wait=w)
        if not add_ln:
            def xcopy(v, par=0):
                for c in range(8):
                    i = v.tensor_copy(x1t[:, c, :], xt[:, c, :])
                return i
            emit(p, "vector", xcopy, XQ, wait=[(LX, t_x), (STM, STM.n), (ROW, ROW.n)])
            t_x1 = XQ.n
            t_accd = PACC.n
        if add_ln:
            def ssum(t):
                for c in range(8):
                    last = t.matmul(pb[0][0:1, :], ones_cf[:], xt[:, c, :],
                                    start=(c == 0), stop=(c == 7))
                return last
            emit(p, "tensor", ssum, STM, wait=[(LX, t_x), (ROW, hist["row"][ti])])
            for c in range(8):
                w2 = [(LX, t_x)] if c == 0 else [(STM, STM.n)]
                if c == 0 and ti > 0:
                    w2.append((STM, STM.n))
                emit(p, "scalar", lambda sc, c=c: sc.activation(out=xsq[:], in_=xt[:, c, :],
                                                                func=AF.Square), SCE, wait=w2)
                emit(p, "tensor", lambda t, c=c: t.matmul(pb[1][0:1, :], ones_cf[:], xsq[:],
                                                          start=(c == 0), stop=(c == 7)),
                     STM, wait=[(SCE, SCE.n)])
            t_stm = STM.n
            def row_a(v):
                v.tensor_single_scalar(out=r_mu, in_=pb[0][0:1, :], scalar=1.0 / D, op=ALU.mult)
                i = v.tensor_single_scalar(out=r_var, in_=pb[1][0:1, :], scalar=1.0 / D, op=ALU.mult)
                p.chain_bar("vector", v, i)
                i = v.tensor_mul(r_t, r_mu, r_mu)
                p.chain_bar("vector", v, i)
                return v.tensor_sub(r_var, r_var, r_t)
            emit(p, "vector", row_a, ROW, wait=[(STM, t_stm)])
            emit(p, "scalar", lambda sc: sc.activation(out=r_t, in_=r_var, func=AF.Sqrt,
                                                       bias=consts[0:1, 0:1]),
                 ROWA, wait=[(ROW, ROW.n)])
            def row_c(v):
                i = v.reciprocal(r_rstd, r_t)
                p.chain_bar("vector", v, i)
                i = v.tensor_mul(r_ms, r_mu, r_rstd)
                p.chain_bar("vector", v, i)
                return v.tensor_single_scalar(out=r_r2, in_=r_rstd, scalar=1.0, op=ALU.add)
            emit(p, "vector", row_c, ROW, wait=[(ROWA, ROWA.n)])
            t_row = ROW.n
            emit(p, "tensor", lambda t: t.matmul(pb[2][:], ones_f[:], r_r2, start=True, stop=True),
                 NRM, wait=[(ROW, t_row)])
            emit(p, "tensor", lambda t: t.matmul(pb[3][:], ones_f[:], r_ms, start=True, stop=True),
                 NRM)
            t_nrm = NRM.n
            def x1f(v):
                for c in range(8):
                    i = v.tensor_mul(x1t[:, c, :], xt[:, c, :], pb[2][:])
                    p.chain_bar("vector", v, i)
                    i = v.tensor_sub(x1t[:, c, :], x1t[:, c, :], pb[3][:])
                    if c < 7:
                        p.chain_bar("vector", v, i)
                return i
            emit(p, "vector", x1f, XQ, wait=[(NRM, t_nrm)])
            t_x1 = XQ.n
            for k in range(4):
                wk = [(XQ, t_x1)] if k == 0 else [(PACC, PACC.n)]
                def acc_f(g, k=k, t0=t0, wk=tuple(wk)):
                    for (s, v_) in wk:
                        g.wait_ge(s.h, v_)
                    g.dma_start(out=x1t[:],
                                in_=parts[k][:, t0:t0 + 512].rearrange("(c p) t -> p c t", p=128),
                                accum_op=ALU.add).then_inc(PACC.h, 16)
                PACC.inc(16)
                p.add("gpsimd", acc_f)
            t_accd = PACC.n

        t_x1s = emit_dma(p, "sync", x1sc.ap()[:, t0:t0 + 512].rearrange("(c p) t -> p c t", p=128),
                         x1t[:], LX, wait=[(PACC, t_accd), (XQ, t_x1)])
        def ssum1(t):
            for c in range(8):
                last = t.matmul(pb[0][0:1, :], ones_cf[:], x1t[:, c, :],
                                start=(c == 0), stop=(c == 7))
            return last
        emit(p, "tensor", ssum1, STM, wait=[(PACC, t_accd), (XQ, t_x1)])
        for c in range(8):
            w2 = [(PACC, t_accd), (XQ, t_x1)] if c == 0 else [(STM, STM.n)]
            emit(p, "scalar", lambda sc, c=c: sc.activation(out=xsq[:], in_=x1t[:, c, :],
                                                            func=AF.Square), SCE, wait=w2)
            emit(p, "tensor", lambda t, c=c: t.matmul(pb[1][0:1, :], ones_cf[:], xsq[:],
                                                      start=(c == 0), stop=(c == 7)),
                 STM, wait=[(SCE, SCE.n)])
        t_stm1 = STM.n
        hist["stm"].append(t_stm1)
        def row_a1(v):
            v.tensor_single_scalar(out=r_mu, in_=pb[0][0:1, :], scalar=1.0 / D, op=ALU.mult)
            i = v.tensor_single_scalar(out=r_var, in_=pb[1][0:1, :], scalar=1.0 / D, op=ALU.mult)
            p.chain_bar("vector", v, i)
            i = v.tensor_mul(r_t, r_mu, r_mu)
            p.chain_bar("vector", v, i)
            return v.tensor_sub(r_var, r_var, r_t)
        emit(p, "vector", row_a1, ROW, wait=[(STM, t_stm1)])
        emit(p, "scalar", lambda sc: sc.activation(out=r_t, in_=r_var, func=AF.Sqrt,
                                                   bias=consts[0:1, 0:1]),
             ROWA, wait=[(ROW, ROW.n)])
        def row_c1(v):
            i = v.reciprocal(r_rstd, r_t)
            p.chain_bar("vector", v, i)
            return v.tensor_mul(r_ms, r_mu, r_rstd)
        emit(p, "vector", row_c1, ROW, wait=[(ROWA, ROWA.n)])
        t_row1 = ROW.n
        hist["row"].append(t_row1)
        emit(p, "tensor", lambda t: t.matmul(pb[2][:], ones_f[:], r_rstd, start=True, stop=True),
             NRM, wait=[(ROW, t_row1), (XQ, XQ.n)])
        emit(p, "tensor", lambda t: t.matmul(pb[3][:], ones_f[:], r_ms, start=True, stop=True),
             NRM)
        t_nrm1 = NRM.n
        for c in range(8):
            cp = c % 2
            w3 = [(NRM, t_nrm1)] if c < 2 else [(SCM, SCM.n)]
            def xff(v, c=c, cp=cp):
                i = v.tensor_mul(xf32[:, cp, :], x1t[:, c, :], pb[2][:])
                p.chain_bar("vector", v, i)
                i = v.tensor_sub(xf32[:, cp, :], xf32[:, cp, :], pb[3][:])
                p.chain_bar("vector", v, i)
                return v.tensor_copy(xfb[:, c, :], xf32[:, cp, :])
            emit(p, "vector", xff, XQ, wait=w3)
            t_xf = XQ.n
            emit(p, "tensor", lambda t, c=c, cp=cp: t.matmul(
                pb[4][0:8, :], gwt[:, c, :], xf32[:, cp, :],
                start=(c == 0), stop=(c == 7)), SCM, wait=[(XQ, t_xf)])
        hist["xq"].append(XQ.n)
        t_scm = SCM.n
        emit(p, "scalar", lambda sc: sc.activation(out=scq[:], in_=pb[4][0:8, :],
                                                   func=AF.Tanh), SCE, wait=[(SCM, t_scm)])
        emit(p, "vector", lambda v: v.tensor_scalar(out=scq[:], in0=scq[:], scalar1=1.0,
                                                    scalar2=0.5, op0=ALU.add, op1=ALU.mult),
             CAST, wait=[(SCE, SCE.n)])
        t_sca = CAST.n
        emit_dma(p, "sync", sc_my.ap()[:, t0:t0 + 512], scq[:], LX, wait=[(CAST, t_sca)])
        for sj in range(4):
            for c in range(8):
                bb = (sj * 8 + c) % 2
                w4 = [(XQ, XQ.n)] if (sj == 0 and c == 0) else [(TRE, TRE.n)]
                emit(p, "tensor", lambda t, sj=sj, c=c, bb=bb: t.transpose(
                    pbb[bb][:, :], xfb[:, c, sj * 128:(sj + 1) * 128], identB_s[:]),
                    TRP, wait=w4)
                w4b = [(TRP, TRP.n)]
                if c == 0 and LX.n >= 16:
                    w4b.append((LX, LX.n))
                emit(p, "scalar", lambda sc, sj=sj, c=c, bb=bb: sc.activation(
                    out=xtm[:, c, :], in_=pbb[bb][:, :], func=AF.Copy),
                    TRE, wait=w4b)
            t_tre = TRE.n
            emit_dma(p, "sync", xfm_my.ap()[t0 + sj * 128: t0 + (sj + 1) * 128, :]
                     .rearrange("t (c q) -> t c q", c=8),
                     xtm[:], LX, wait=[(TRE, t_tre)])
        hist["tre"].append(TRE.n)
        hist["trp"].append(TRP.n)
    t_pa_done = LX.n

    emit(p, "gpsimd", lambda g: g.collective_compute(
        "AllGather", ALU.bypass, replica_groups=[list(range(8))],
        ins=[sc_my.ap().opt()], outs=[sc_all.ap().opt()]), CC, wait=[(LX, t_pa_done)])
    emit(p, "gpsimd", lambda g: g.collective_compute(
        "AllGather", ALU.bypass, replica_groups=[list(range(8))],
        ins=[xfm_my.ap().opt()], outs=[xfm_all.ap().opt()]), CC)
    t_cc1 = CC.n

    t_em = emit_dma(p, "sync", sc_em.ap().rearrange("e (r c) -> e r c", c=TOK),
                    sc_all.ap().rearrange("(r e) c -> e r c", e=8), LX,
                    wait=[(CC, t_cc1)])
    t_sc4 = emit_dma(p, "sync", sc4[:],
                     sc_em.ap().rearrange("e (tg j) -> (e tg) j", j=FJ), LX,
                     wait=[(LX, t_em)])
    def bis_init(v):
        v.memset(lo[:], -0.1)
        v.memset(hi[:], 1.1)
        i = v.memset(mid[:], 0.5)
        return i
    emit(p, "vector", bis_init, BIS, wait=[(LX, t_sc4), (CAST, t_iotf)])
    for it in range(50):
        emit(p, "tensor", lambda t: t.matmul(pb[5][:, 0:1], eselT_s[:], mid[:],
                                             start=True, stop=True), BPE,
             wait=[(BIS, BIS.n)])
        emit(p, "scalar", lambda sc: sc.activation(out=thr[:], in_=pb[5][:, 0:1],
                                                   func=AF.Copy), BSE, wait=[(BPE, BPE.n)])
        def bis_cnt(v):
            return v.tensor_scalar(out=mtile[:], in0=sc4[:], scalar1=thr[:, 0:1],
                                   scalar2=None, op0=ALU.is_ge, op1=ALU.add,
                                   accum_out=cnt128[:, 0:1])
        emit(p, "vector", bis_cnt, BIS, wait=[(BSE, BSE.n)])
        emit(p, "tensor", lambda t: t.matmul(pb[5][0:8, 0:1], esel_s[:], cnt128[:, 0:1],
                                             start=True, stop=True), BPE, wait=[(BIS, BIS.n)])
        def bis_upd(v):
            i = v.tensor_copy(cnt8[:], pb[5][0:8, 0:1])
            p.chain_bar("vector", v, i)
            i = v.tensor_single_scalar(out=msk8[:], in_=cnt8[:], scalar=float(KSEL), op=ALU.is_ge)
            p.chain_bar("vector", v, i)
            i = v.copy_predicated(lo[:], msk8[:], mid[:])
            p.chain_bar("vector", v, i)
            i = v.tensor_single_scalar(out=msk8[:], in_=cnt8[:], scalar=float(KSEL), op=ALU.is_lt)
            p.chain_bar("vector", v, i)
            i = v.copy_predicated(hi[:], msk8[:], mid[:])
            p.chain_bar("vector", v, i)
            i = v.tensor_add(mid[:], lo[:], hi[:])
            p.chain_bar("vector", v, i)
            return v.tensor_single_scalar(out=mid[:], in_=mid[:], scalar=0.5, op=ALU.mult)
        emit(p, "vector", bis_upd, BIS, wait=[(BPE, BPE.n)])
    emit(p, "tensor", lambda t: t.matmul(pb[5][:, 0:1], eselT_s[:], lo[:],
                                         start=True, stop=True), BPE, wait=[(BIS, BIS.n)])
    emit(p, "scalar", lambda sc: sc.activation(out=thr[:], in_=pb[5][:, 0:1], func=AF.Copy),
         BSE, wait=[(BPE, BPE.n)])
    emit(p, "tensor", lambda t: t.transpose(pb[5][0:1, 0:8], lo[:], identF_s[0:8, 0:8]),
         BPE, wait=[(BSE, BSE.n)])
    emit(p, "scalar", lambda sc: sc.activation(out=lorow[:], in_=pb[5][0:1, 0:8],
                                               func=AF.Copy), BSE, wait=[(BPE, BPE.n)])
    t_bse = BSE.n

    def rk1(v):
        i = v.tensor_scalar(out=mtile[:], in0=sc4[:], scalar1=thr[:, 0:1], scalar2=None,
                            op0=ALU.is_ge, op1=ALU.add, accum_out=cnt128[:, 0:1])
        p.chain_bar("vector", v, i)
        return v.tensor_tensor_scan(out=cums[:], data0=mtile[:], data1=zfj[:],
                                    initial=0.0, op0=ALU.add, op1=ALU.add)
    emit(p, "vector", rk1, CMP, wait=[(BSE, t_bse)])
    emit(p, "tensor", lambda t: t.matmul(pb[5][:, 0:1], trilb_s[:], cnt128[:, 0:1],
                                         start=True, stop=True), CPE, wait=[(CMP, CMP.n)])
    emit(p, "scalar", lambda sc: sc.activation(out=offc[:], in_=pb[5][:, 0:1], func=AF.Copy),
         CSE, wait=[(CPE, CPE.n)])
    def rk2(v):
        i = v.scalar_tensor_tensor(out=cums[:], in0=cums[:], scalar=offc[:, 0:1],
                                   in1=mtile[:], op0=ALU.add, op1=ALU.subtract)
        p.chain_bar("vector", v, i)
        i = v.tensor_single_scalar(out=zfj[:], in_=cums[:], scalar=float(KSEL), op=ALU.is_lt)
        p.chain_bar("vector", v, i)
        i = v.tensor_mul(mtile[:], mtile[:], zfj[:])
        p.chain_bar("vector", v, i)
        i = v.memset(zfj[:], 0.0)
        p.chain_bar("vector", v, i)
        i = v.tensor_single_scalar(out=cums[:], in_=cums[:], scalar=float(KSEL),
                                   op=ALU.subtract)
        p.chain_bar("vector", v, i)
        i = v.tensor_mul(cums[:], cums[:], mtile[:])
        p.chain_bar("vector", v, i)
        return v.tensor_scalar(out=gidx[:], in0=cums[:], scalar1=geoff_s[:, 0:1],
                               scalar2=None, op0=ALU.add)
    emit(p, "vector", rk2, CMP, wait=[(CSE, CSE.n)])
    t_rk = CMP.n
    gidx_d = nc.dram_tensor("gidx_d", [128, FJ], F32, kind="Internal")

    rid = p.sb("rid", [128, 1], U32)
    t_gx = emit_dma(p, "sync", gidx_d.ap(), gidx[:], LX, wait=[(CMP, t_rk)])
    def rid_f(v):
        i = v.tensor_single_scalar(out=tmpc[:], in_=iotf[:], scalar=8.0, op=ALU.mult)
        p.chain_bar("vector", v, i)
        i = v.tensor_add(tmpc[:], tmpc[:], ecol_s[:])
        p.chain_bar("vector", v, i)
        return v.tensor_copy(rid[:], tmpc[:])
    emit(p, "vector", rid_f, CMP, wait=[(CAST, t_iotf), (LDC, t_ldc)])
    t_rid = CMP.n
    def gmy_g(g):
        g.wait_ge(CMP.h, t_rid)
        g.wait_ge(LX.h, t_gx)
        g.indirect_dma_start(
            out=gmy8, out_offset=None,
            in_=gidx_d.ap().rearrange("(a b) j -> a (b j)", b=2),
            in_offset=IndirectOffsetOnAxis(ap=rid[0:8, 0:1], axis=0)).then_inc(GD.h, 16)
    GD.inc(16)
    p.add("gpsimd", gmy_g)
    t_gmy = GD.n
    for tau in range(TOK // 128):
        bb = tau % 2
        w5 = [(GD, t_gmy)] if tau < 2 else [(CMP, CMP.n - 1)]
        emit(p, "tensor", lambda t, tau=tau, bb=bb: t.transpose(
            pb[4 + bb][:, 0:8], gmy8[0:8, tau * 128:(tau + 1) * 128],
            identF_s[0:8, 0:8]), TRP, wait=w5)
        t_tr = TRP.n
        emit(p, "vector", lambda v, tau=tau, bb=bb: v.tensor_copy(
            gT[:, tau * 8: tau * 8 + 8], pb[4 + bb][:, 0:8]), CMP,
            wait=[(TRP, t_tr)])
    t_gT = CMP.n

    def sceg_g(g):
        g.wait_ge(CMP.h, t_gT)
        g.indirect_dma_start(
            out=sceg, out_offset=None, in_=sc_all.ap(),
            in_offset=IndirectOffsetOnAxis(ap=rid[0:8, 0:1], axis=0)).then_inc(GD.h, 16)
    GD.inc(16)
    p.add("gpsimd", sceg_g)
    t_sceg = GD.n
    t_sb1 = emit_dma(p, "sync", sce_d.ap().rearrange("o (a t) -> (o a) t", a=8),
                     sceg, LX, wait=[(GD, t_sceg)])
    NSEG_ = NT // 128
    t_sb2 = emit_dma(p, "sync", sce[0:NSEG_, 0:128],
                     sce_d.ap().rearrange("o (s j) -> (o s) j", j=128), LX,
                     wait=[(LX, t_sb1)])
    def myth_f(v):
        i = v.tensor_mul(t8[:], lorow[:], eh_s[:])
        p.chain_bar("vector", v, i)
        return v.tensor_reduce(out=t8[:, 0:1], in_=t8[:], axis=AX.X, op=ALU.add)
    emit(p, "vector", myth_f, CMP, wait=[(BSE, t_bse)])
    emit(p, "tensor", lambda t: t.matmul(pb[5][:, 0:1], ones_f[:], t8[:, 0:1],
                                         start=True, stop=True), CPE, wait=[(CMP, CMP.n)])
    emit(p, "scalar", lambda sc: sc.activation(out=myth[:], in_=pb[5][:, 0:1], func=AF.Copy),
         CSE, wait=[(CPE, CPE.n)])
    t_myth = CSE.n
    def cmp1(v):
        i = v.tensor_single_scalar(out=mske[0:NSEG_, :], in_=sce[0:NSEG_, 0:128],
                                   scalar=myth[0:NSEG_, 0:1], op=ALU.is_ge)
        p.chain_bar("vector", v, i)
        i = v.tensor_mul(vals[0:NSEG_, 0:128], mske[0:NSEG_, 0:128], vals2[0:NSEG_, :])
        p.chain_bar("vector", v, i)
        return v.tensor_reduce(out=cnte[0:NSEG_, :], in_=mske[0:NSEG_, 0:128],
                               axis=AX.X, op=ALU.add)
    emit(p, "vector", cmp1, CMP, wait=[(CSE, t_myth), (LX, t_sb2)])
    ping = [vals, mtile]
    for r in range(CAP // 8):
        a = ping[r % 2]; b = ping[(r + 1) % 2]
        def mx_f(v, r=r, a=a, b=b):
            i = v.max(out=mx[0:NSEG_, r * 8:(r + 1) * 8], in_=a[0:NSEG_, 0:128])
            p.chain_bar("vector", v, i)
            i = v.match_replace(out=b[0:NSEG_, 0:128],
                                in_to_replace=mx[0:NSEG_, r * 8:(r + 1) * 8],
                                in_values=a[0:NSEG_, 0:128], imm_value=0.0)
            return i
        emit(p, "vector", mx_f, CMP, wait=[(CMP, CMP.n)])
    emit(p, "gpsimd", lambda g: g.iota(out=idcol[:], pattern=[[1, 1]], base=128,
                                       channel_multiplier=128), GP)
    t_idc = GP.n
    def ids_f(v):
        i = v.tensor_copy(idcolf[:], idcol[:])
        p.chain_bar("vector", v, i)
        i = v.tensor_scalar(out=ids48[0:NSEG_, :], in0=mx[0:NSEG_, :], scalar1=-1.0,
                            scalar2=idcolf[0:NSEG_, 0:1], op0=ALU.mult, op1=ALU.add)
        return i
    emit(p, "vector", ids_f, CMP, wait=[(GP, t_idc), (CMP, CMP.n)])
    t_ids = CMP.n
    emit(p, "tensor", lambda t: t.matmul(pb[5][0:NSEG_, 0:1], tril_s[0:NSEG_, 0:NSEG_],
                                         cnte[0:NSEG_, 0:1], start=True, stop=True),
         CPE, wait=[(CMP, t_ids)])
    emit(p, "scalar", lambda sc: sc.activation(out=offc[0:NSEG_, :],
                                               in_=pb[5][0:NSEG_, 0:1], func=AF.Copy),
         CSE, wait=[(CPE, CPE.n)])
    t_off = CSE.n
    t_idp = emit_dma(p, "sync",
                     idpad.ap().rearrange("o (s c) -> (o s) c", c=CAP)[0:NSEG_, :],
                     ids48[0:NSEG_, :], LX, wait=[(CMP, t_ids)])
    emit(p, "tensor", lambda t: t.transpose(pb[5][0:1, 0:NSEG_], offc[0:NSEG_, :],
                                            identF_s[0:NSEG_, 0:NSEG_]), CPE,
         wait=[(CSE, t_off)])
    emit(p, "scalar", lambda sc: sc.activation(out=orow[:, 0:NSEG_],
                                               in_=pb[5][0:1, 0:NSEG_], func=AF.Copy),
         CSE, wait=[(CPE, CPE.n)])
    emit(p, "tensor", lambda t: t.matmul(pb[5][:, 0:NSEG_], ones_f[:], orow[:, 0:NSEG_],
                                         start=True, stop=True), CPE, wait=[(CSE, CSE.n)])
    emit(p, "scalar", lambda sc: sc.activation(out=ob[:, 0:NSEG_], in_=pb[5][:, 0:NSEG_],
                                               func=AF.Copy), CSE, wait=[(CPE, CPE.n)])
    t_ob = CSE.n
    emit(p, "gpsimd", lambda g: g.iota(out=gcol[:], pattern=[[128, NRD]], base=0,
                                       channel_multiplier=1), GP)
    t_gc = GP.n
    emit(p, "vector", lambda v: v.tensor_copy(gcolf[:], gcol[:]), CMP,
         wait=[(GP, t_gc), (CSE, t_ob)])
    t_gcf = CMP.n

    GW = p.sem("gw")
    FF2 = p.sem("ff2")
    KWS = p.sem("kws")
    grp_rounds = 4
    for grp in range(NGR):
        for rr in range(grp_rounds):
            r = grp * grp_rounds + rr
            def sig_f(v, r=r):
                i = v.tensor_single_scalar(out=cmp128[:, 0:NSEG_], in_=ob[:, 0:NSEG_],
                                           scalar=gcolf[:, r:r + 1], op=ALU.is_le)
                p.chain_bar("vector", v, i)
                i = v.tensor_reduce(out=sig[:], in_=cmp128[:, 0:NSEG_], axis=AX.X, op=ALU.add)
                p.chain_bar("vector", v, i)
                i = v.tensor_mul(cmp128[:, 0:NSEG_], cmp128[:, 0:NSEG_], ob[:, 0:NSEG_])
                p.chain_bar("vector", v, i)
                i = v.tensor_reduce(out=osg[:], in_=cmp128[:, 0:NSEG_], axis=AX.X, op=ALU.max)
                p.chain_bar("vector", v, i)
                i = v.tensor_scalar(out=sig[:], in0=sig[:], scalar1=1.0, scalar2=float(CAP),
                                    op0=ALU.subtract, op1=ALU.mult)
                p.chain_bar("vector", v, i)
                i = v.tensor_scalar(out=lam[:], in0=osg[:], scalar1=-1.0,
                                    scalar2=gcolf[:, r:r + 1], op0=ALU.mult, op1=ALU.add)
                p.chain_bar("vector", v, i)
                i = v.tensor_add(flt[:], sig[:], lam[:])
                p.chain_bar("vector", v, i)
                return v.tensor_copy(fltu[:], flt[:])
            emit(p, "vector", sig_f, CMP, wait=[(CMP, CMP.n), (GD, GD.n)])
            t_sg = CMP.n
            def id_g(g, r=r, t_sg=t_sg):
                g.wait_ge(CMP.h, t_sg)
                g.wait_ge(LX.h, t_idp)
                g.indirect_dma_start(
                    out=tid[:, r:r + 1], out_offset=None,
                    in_=idpad.ap().rearrange("o (f x) -> (o f) x", x=1)[0:NSEG_ * CAP, :],
                    in_offset=IndirectOffsetOnAxis(ap=fltu[:, 0:1], axis=0)).then_inc(GD.h, 16)
            GD.inc(16)
            p.add("gpsimd", id_g)
            t_idg = GD.n
            def tid_f(v, r=r):
                return v.tensor_copy(tidu[:], tid[:, r:r + 1])
            emit(p, "vector", tid_f, CMP, wait=[(GD, t_idg)])
            t_tidu = CMP.n
            def kw_g(g, r=r, t_tidu=t_tidu, t_trp_prev=TRP.n):
                g.wait_ge(CMP.h, t_tidu)
                g.wait_ge(TRP.h, t_trp_prev)
                g.indirect_dma_start(
                    out=kwv[:, r:r + 1], out_offset=None,
                    in_=sce_d.ap().rearrange("o (f x) -> (o f) x", x=1),
                    in_offset=IndirectOffsetOnAxis(ap=tidu[:, 0:1], axis=0)).then_inc(GD.h, 16)
                g.indirect_dma_start(
                    out=xg[:], out_offset=None, in_=xfm_all.ap(),
                    in_offset=IndirectOffsetOnAxis(ap=tidu[:, 0:1], axis=0)).then_inc(GD.h, 16)
            GD.inc(32)
            p.add("gpsimd", kw_g)
            t_xgg = GD.n
            for c in range(8):
                bb = c % 2
                w6 = [(GD, t_xgg), (TRE, TRE.n)] if c < 2 else [(TRE, TRE.n)]
                if rr == 0 and c < 2:
                    w6 = w6 + [(FFM, FFM.n)]
                emit(p, "tensor", lambda t, c=c, bb=bb: t.transpose(
                    pbb[bb][:, :], xg[:, c * 128:(c + 1) * 128], identB_s[:]),
                    TRP, wait=w6)
                emit(p, "scalar", lambda sc, c=c, bb=bb, rr=rr: sc.activation(
                    out=xh[:, c, rr * 128:(rr + 1) * 128], in_=pbb[bb][:, :], func=AF.Copy),
                    TRE, wait=[(TRP, TRP.n)])
        t_xh = TRE.n
        def gsum(t):
            for c in range(8):
                last = t.matmul(pb[0][0:1, :], ones_bf[:], xh[:, c, :],
                                start=(c == 0), stop=(c == 7))
            return last
        emit(p, "tensor", gsum, STM, wait=[(TRE, t_xh)])
        for c in range(8):
            w7 = [(TRE, t_xh)] if c == 0 else [(STM, STM.n)]
            emit(p, "scalar", lambda sc, c=c: sc.activation(out=xsqb[:], in_=xh[:, c, :],
                                                            func=AF.Square), SCE, wait=w7)
            emit(p, "tensor", lambda t, c=c: t.matmul(pb[1][0:1, :], ones_bf[:], xsqb[:],
                                                      start=(c == 0), stop=(c == 7)),
                 STM, wait=[(SCE, SCE.n)])
        t_gstm = STM.n
        def grow_a(v):
            v.tensor_single_scalar(out=r_mu, in_=pb[0][0:1, :], scalar=1.0 / D, op=ALU.mult)
            i = v.tensor_single_scalar(out=r_var, in_=pb[1][0:1, :], scalar=1.0 / D, op=ALU.mult)
            p.chain_bar("vector", v, i)
            i = v.tensor_mul(r_t, r_mu, r_mu)
            p.chain_bar("vector", v, i)
            return v.tensor_sub(r_var, r_var, r_t)
        emit(p, "vector", grow_a, ROW, wait=[(STM, t_gstm)])
        emit(p, "scalar", lambda sc: sc.activation(out=r_t, in_=r_var, func=AF.Sqrt,
                                                   bias=consts[0:1, 0:1]),
             ROWA, wait=[(ROW, ROW.n)])
        def grow_c(v):
            i = v.reciprocal(r_rstd, r_t)
            p.chain_bar("vector", v, i)
            i = v.tensor_mul(r_ms, r_mu, r_rstd)
            p.chain_bar("vector", v, i)
            return v.tensor_single_scalar(out=r_ms, in_=r_ms, scalar=-1.0, op=ALU.mult)
        emit(p, "vector", grow_c, ROW, wait=[(ROWA, ROWA.n)])
        t_grow = ROW.n
        emit(p, "tensor", lambda t: t.matmul(pb[2][:], ones_f[:], r_rstd, start=True,
                                             stop=True), NRM, wait=[(ROW, t_grow), (XQ, XQ.n)])
        t_pb2 = NRM.n
        for c in range(8):
            bank = 3 + (c % 2)
            w8 = [(NRM, NRM.n)] if c < 2 else [(XQ, XQ.n - 1)]
            emit(p, "tensor", lambda t, c=c, bank=bank: t.matmul(
                pb[bank][:], gam_s[:, c * 128:(c + 1) * 128], r_ms, start=True, stop=True),
                NRM, wait=w8)
            t_r1 = NRM.n
            def gapply(v, c=c, bank=bank):
                i = v.tensor_mul(tmp32[:, c % 2, :], xh[:, c, :], pb[2][:])
                p.chain_bar("vector", v, i)
                i = v.scalar_tensor_tensor(out=tmp32[:, c % 2, :], in0=tmp32[:, c % 2, :],
                                           scalar=gamcol_s[:, c:c + 1], in1=pb[bank][:],
                                           op0=ALU.mult, op1=ALU.add)
                p.chain_bar("vector", v, i)
                return v.tensor_scalar(out=xh[:, c, :], in0=tmp32[:, c % 2, :],
                                       scalar1=betcol_s[:, c:c + 1], scalar2=None,
                                       op0=ALU.add)
            emit(p, "vector", gapply, XQ, wait=[(NRM, t_r1)])
        t_gap = XQ.n
        for f in range(32):
            wb = f % 2
            wld_w = [(LW, LW.n)] if True else []
            t_w1 = emit_dma(p, "sync", w1t[wb][:],
                            w1bf.ap().rearrange("(c p) q -> p c q", p=128)
                            [:, :, f * 128:(f + 1) * 128],
                            LWA[wb], wait=[(FFM, max(0, FFM.n - 1)), (LWG, t_w1c)])
            bank = 4 + (f % 2)
            def fc1_mm(t, f=f, wb=wb, bank=bank, t_w1=t_w1):
                t.wait_ge(LWA[wb].h, t_w1)
                for c in range(8):
                    last = t.matmul(pb[bank][:], w1t[wb][:, c, :], xh[:, c, :],
                                    start=(c == 0), stop=(c == 7))
                return last
            w9 = [(XQ, t_gap)]
            if f >= 2:
                w9.append((GEL, GEL.n - 1))
            emit(p, "tensor", fc1_mm, FFM, wait=w9)
            t_f1 = FFM.n
            emit(p, "scalar", lambda sc, f=f, bank=bank: sc.activation(
                out=hbf[:, f, :], in_=pb[bank][:], func=AF.Gelu_apprx_tanh,
                bias=b1col_s[:, f:f + 1]), GEL, wait=[(FFM, t_f1)])
        t_gel = GEL.n
        for dc in range(8):
            wb = dc % 2
            t_w2 = emit_dma(p, "sync", w2s[wb][:],
                            w2bf.ap().rearrange("(hc p) q -> p hc q", p=128)
                            [:, :, dc * 128:(dc + 1) * 128],
                            LWB[wb], wait=[(FF2, max(0, FF2.n - 1)), (LWG, t_w2c)])
            bank = 4 + (dc % 2)
            def fc2_mm(t, dc=dc, wb=wb, bank=bank, t_w2=t_w2):
                t.wait_ge(LWB[wb].h, t_w2)
                for hc in range(32):
                    last = t.matmul(pb[bank][:], w2s[wb][:, hc, :], hbf[:, hc, :],
                                    start=(hc == 0), stop=(hc == 31))
                return last
            w10 = [(GEL, t_gel)]
            if dc >= 2:
                w10.append((FFE, FFE.n - 1))
            emit(p, "tensor", fc2_mm, FF2, wait=w10)
            t_f2 = FF2.n
            emit(p, "scalar", lambda sc, dc=dc, bank=bank: sc.activation(
                out=ofm[:, dc, :], in_=pb[bank][:], func=AF.Identity,
                bias=b2col_s[:, dc:dc + 1]), FFE, wait=[(FF2, t_f2)])
        t_ffe = FFE.n
        for rr in range(grp_rounds):
            r = grp * grp_rounds + rr
            for c in range(8):
                bb = c % 2
                w11 = [(FFE, t_ffe), (KWS, KWS.n)] if c < 2 else [(KWS, KWS.n)]
                emit(p, "tensor", lambda t, rr=rr, c=c, bb=bb: t.transpose(
                    pbb[bb][:, :], ofm[:, c, rr * 128:(rr + 1) * 128], identB_s[:]),
                    TRP, wait=w11)
                t_tp = TRP.n
                w11b = [(TRP, t_tp)]
                if c == 0:
                    w11b.append((OE, OE.n))
                emit(p, "vector", lambda v, r=r, c=c, bb=bb: v.tensor_scalar(
                    out=xtm[:, c, :], in0=pbb[bb][:, :], scalar1=kwv[:, r:r + 1],
                    scalar2=None, op0=ALU.mult), KWS, wait=w11b)
            t_kws = KWS.n
            emit_dma(p, "sync", oe.ap()[r * 128:(r + 1) * 128, :]
                     .rearrange("t (c q) -> t c q", c=8), xtm[:], OE, wait=[(KWS, t_kws)])
    t_ffn_done = OE.n

    emit(p, "gpsimd", lambda g: g.collective_compute(
        "AllGather", ALU.bypass, replica_groups=[list(range(8))],
        ins=[oe.ap().opt()], outs=[oall.ap().opt()]), CC,
        wait=[(OE, t_ffn_done), (LWG, t_oz)])
    t_cc2 = CC.n

    for tau in range(TOK // 128):
        t0 = tau * 128
        def cg_f(g, tau=tau, t0=t0, t_cmb=CMB.n):
            g.wait_ge(CC.h, t_cc2)
            if tau > 0:
                g.wait_ge(CMB.h, t_cmb)
            for e in range(8):
                g.indirect_dma_start(
                    out=gath8[e][:], out_offset=None, in_=oall.ap(),
                    in_offset=IndirectOffsetOnAxis(ap=gT[:, tau * 8 + e: tau * 8 + e + 1],
                                                   axis=0)).then_inc(GD.h, 16)
            g.dma_start(out=xfmt[:].rearrange("t (c q) -> t c q", c=8),
                        in_=xfm_my.ap()[t0:t0 + 128, :].rearrange("t (c q) -> t c q", c=8)
                        ).then_inc(GD.h, 16)
        GD.inc(16 * 9)
        p.add("gpsimd", cg_f)
        t_cg = GD.n
        def sum_f(v):
            i = v.tensor_add(acct[:], gath8[0][:], gath8[1][:])
            p.chain_bar("vector", v, i)
            for e in range(2, 8):
                i = v.tensor_add(acct[:], acct[:], gath8[e][:])
                p.chain_bar("vector", v, i)
            i = v.tensor_add(acct[:], acct[:], xfmt[:])
            return i
        emit(p, "vector", sum_f, CMB, wait=[(GD, t_cg)])
        t_sum = CMB.n
        t_x1r = emit_dma(p, "sync", xt[:, :, 0:128],
                         x1sc.ap()[:, t0:t0 + 128].rearrange("(c p) t -> p c t", p=128),
                         LX, wait=[(TR2, TR2.n)])
        for c in range(8):
            bank = 4 + (c % 2)
            w12 = [(CMB, t_sum)]
            w12.append((TR2, TR2.n))
            emit(p, "tensor", lambda t, c=c, bank=bank: t.transpose(
                pb[bank][:, 0:128], acct[:, c * 128:(c + 1) * 128], identF_s[:]),
                TRP, wait=w12)
            t_tp2 = TRP.n
            def x2_f(v, c=c, bank=bank, t_x1r=t_x1r, t_od=OD.n):
                v.wait_ge(LX.h, t_x1r)
                if t_od > 0 and c == 0:
                    v.wait_ge(OD.h, t_od)
                return v.tensor_add(x2c[:, c, :], xt[:, c, 0:128], pb[bank][:, 0:128])
            emit(p, "vector", x2_f, TR2, wait=[(TRP, t_tp2)])
        t_x2 = TR2.n
        emit_dma(p, "gpsimd", y[:, t0:t0 + 128].rearrange("(c p) t -> p c t", p=128),
                 x2c[:], OD, wait=[(TR2, t_x2)])
    emit(p, "gpsimd", lambda g: g.nop().then_inc(p.sem("fin").h, 1), None,
         wait=[(OD, OD.n)])
    return p.build()



def build_l0_attn_b(TOK=2048):
    p = Prog("l0ab")
    nc = p.nc
    HL = 256
    TH = TOK + 2 * HL
    NTI = TH // 512
    NQT = TOK // 512
    NKC = TH // 128

    xh_fm = nc.dram_tensor("xh_fm", [D, TH], F32, kind="ExternalInput").ap()
    x2_fm = nc.dram_tensor("x2_fm", [D, TOK], F32, kind="ExternalInput").ap()
    wqT = nc.dram_tensor("wqT", [D, D], F32, kind="ExternalInput").ap()
    wkT = nc.dram_tensor("wkT", [D, D], F32, kind="ExternalInput").ap()
    wvT = nc.dram_tensor("wvT", [D, D], F32, kind="ExternalInput").ap()
    bqc = nc.dram_tensor("bqc", [128, 8], F32, kind="ExternalInput").ap()
    bkc = nc.dram_tensor("bkc", [128, 8], F32, kind="ExternalInput").ap()
    bvrow = nc.dram_tensor("bvrow", [1, D], F32, kind="ExternalInput").ap()
    bmask = nc.dram_tensor("bmask", [NQT * 8, 128, 512], BF16, kind="ExternalInput").ap()
    y = nc.dram_tensor("y", [D, TOK], F32, kind="ExternalOutput").ap()
    wqb = nc.dram_tensor("wqb", [D, D], BF16, kind="Internal")
    wkb = nc.dram_tensor("wkb", [D, D], BF16, kind="Internal")
    wvb = nc.dram_tensor("wvb", [D, D], BF16, kind="Internal")

    xt = [p.sb(f"xt{i}", [128, 8, 512], F32) for i in range(1)]
    xq = p.sb("xq", [128, 8, 512], BF16)
    xsq = p.sb("xsq", [128, 512], F32)
    ksq = p.sb("ksq", [128, 512], BF16)
    tmp32 = p.sb("tmp32", [128, 2, 512], F32)
    rows = p.sb("rows", [1, 6 * 512], F32)
    consts = p.sb("consts", [128, 4], F32)
    ones_cf = p.sb("ones_cf", [128, 1], F32)
    hselA = p.sb("hselA", [128, 2], BF16)
    ones_f = p.sb("ones_f", [1, 128], F32)
    ones64 = p.sb("ones64", [65, 128], F32)
    kbuf = [p.sb(f"kb{i}", [128, TH], BF16) for i in range(8)]
    qbuf = [p.sb(f"qb{i}", [128, TOK], BF16) for i in range(8)]
    vones = p.sb("vones", [128, 16 * NKC * 65], BF16)
    wstr = [p.sb(f"wstr{i}", [128, 8, 128], BF16) for i in range(2)]
    bvb = p.sb("bvb", [128, 1024], F32)
    negc = p.sb("negc", [128, 16], F32)
    ncrow = p.sb("ncrow", [1, 16], F32)
    nrow2 = p.sb("nrow2", [1, 16], F32)
    pbf = [p.sb(f"p_{i}", [128, 512], BF16) for i in range(4)]
    bm_sb = p.sb("bm_sb", [128, 8 * 512], BF16)
    r4b = p.sb("r4b", [65, 2 * 512], F32)
    rb = p.sb("rb", [64, 2 * 512], F32)
    aos = p.sb("aos", [128, 512], F32)
    x3c = p.sb("x3c", [128, 8, 512], F32)
    bq_s = p.sb("bq_s", [128, 8], F32)
    bk_s = p.sb("bk_s", [128, 8], F32)

    pb = [p.ps(f"pb{i}", [128, 512], F32) for i in range(8)]

    LD = p.sem("ld"); LDC = p.sem("ldc"); LX = [p.sem("lx0"), p.sem("lx1")]
    CAST = p.sem("cast"); STM = p.sem("stm"); SQ = p.sem("sq"); ROW = p.sem("row")
    ROWA = p.sem("rowa"); NRM = p.sem("nrm"); XQ = p.sem("xq"); PRJ = p.sem("prj")
    QEV = p.sem("qev"); VEV = p.sem("vev"); QN = p.sem("qn"); CRD = p.sem("crd")
    QK = [p.sem(f"qk{i}") for i in range(2)]
    PS = [p.sem(f"ps{i}") for i in range(2)]
    AV = [p.sem(f"av{i}") for i in range(2)]
    RC = p.sem("rc"); RBM = p.sem("rbm"); RBE = p.sem("rbe"); AOE = p.sem("aoe")
    X2L = p.sem("x2l"); OD = p.sem("od")

    r_mu = rows[:, 0:512]; r_var = rows[:, 512:1024]; r_rstd = rows[:, 1024:1536]
    r_t = rows[:, 1536:2048]; r_ms = rows[:, 2048:2560]

    emit_dma(p, "sync", bq_s[:], bqc, LDC)
    emit_dma(p, "sync", bk_s[:], bkc, LDC)
    t_ldc = LDC.n
    def init_c(v):
        v.memset(consts[:, 0:1], 1e-5)
        v.memset(ones_cf[:], 1.0)
        v.memset(ones_f[:], 1.0)
        v.memset(ones64[64:65, :], 1.0)
        i = v.memset(hselA[:, :], 0.0)
        p.chain_bar("vector", v, i)
        v.memset(hselA[0:64, 0:1], 1.0)
        i = v.memset(hselA[64:128, 1:2], 1.0)
        p.chain_bar("vector", v, i)
        v.memset(ncrow[:], 0.0)
        v.memset(nrow2[:], 0.0)
        return v.memset(vones[:].rearrange("p (a o) -> p a o", o=65)[:, :, 64:65], 1.0)
    emit(p, "vector", init_c, CAST, wait=[(LDC, t_ldc)])
    t_cinit = CAST.n
    t_bv = emit_dma(p, "sync", rows[0:1, 0:1024], bvrow, LD, wait=[(CAST, t_cinit)])
    emit(p, "tensor", lambda t: [t.matmul(pb[4][:], ones_f[:], rows[0:1, j * 512:(j + 1) * 512],
                                          start=True, stop=True) for j in range(1)][-1],
         NRM, wait=[(LD, t_bv)])
    emit(p, "tensor", lambda t: t.matmul(pb[5][:], ones_f[:], rows[0:1, 512:1024],
                                         start=True, stop=True), NRM)
    t_bvm = NRM.n
    def bv_ev(v):
        v.tensor_copy(bvb[:, 0:512], pb[4][:])
        return v.tensor_copy(bvb[:, 512:1024], pb[5][:])
    emit(p, "vector", bv_ev, CAST, wait=[(NRM, t_bvm)])
    t_bvb = CAST.n
    LWG = p.sem("lwg")
    emit_dma(p, "gpsimd", wqb.ap(), wqT, LWG)
    emit_dma(p, "gpsimd", wkb.ap(), wkT, LWG)
    emit_dma(p, "gpsimd", wvb.ap(), wvT, LWG)
    t_wc3 = LWG.n
    LWS = [p.sem("lws0"), p.sem("lws1")]
    t_wc = CAST.n

    hist = {"prj": [0], "qev": [0], "vev": [0], "stm": [0], "rowe": [0]}
    for ti in range(NTI):
        par = 0
        t0 = ti * 512
        w = [(CAST, t_wc)] if ti == 0 else [(PRJ, hist["prj"][ti]), (QEV, hist["qev"][ti]),
                                            (VEV, hist["vev"][ti]), (XQ, XQ.n)]
        t_x = emit_dma(p, "sync", xt[par][:],
                       xh_fm[:, t0:t0 + 512].rearrange("(c p) t -> p c t", p=128),
                       LX[0], wait=w)
        def ssum(t, par=par):
            for c in range(8):
                last = t.matmul(pb[0][0:1, :], ones_cf[:], xt[par][:, c, :],
                                start=(c == 0), stop=(c == 7))
            return last
        emit(p, "tensor", ssum, STM, wait=[(LX[0], t_x), (ROW, hist["rowe"][ti])])
        for c in range(8):
            w2 = [(LX[0], t_x)] if c == 0 else [(STM, STM.n)]
            emit(p, "scalar", lambda sc, c=c, par=par: sc.activation(
                out=xsq[:], in_=xt[par][:, c, :], func=AF.Square), SQ, wait=w2)
            emit(p, "tensor", lambda t, c=c: t.matmul(pb[1][0:1, :], ones_cf[:], xsq[:],
                                                      start=(c == 0), stop=(c == 7)),
                 STM, wait=[(SQ, SQ.n)])
        t_stm = STM.n
        hist["stm"].append(t_stm)
        def row_a(v):
            v.tensor_single_scalar(out=r_mu, in_=pb[0][0:1, :], scalar=1.0 / D, op=ALU.mult)
            i = v.tensor_single_scalar(out=r_var, in_=pb[1][0:1, :], scalar=1.0 / D, op=ALU.mult)
            p.chain_bar("vector", v, i)
            i = v.tensor_mul(r_t, r_mu, r_mu)
            p.chain_bar("vector", v, i)
            return v.tensor_sub(r_var, r_var, r_t)
        emit(p, "vector", row_a, ROW, wait=[(STM, t_stm)])
        emit(p, "scalar", lambda sc: sc.activation(out=r_t, in_=r_var, func=AF.Sqrt,
                                                   bias=consts[0:1, 0:1]),
             ROWA, wait=[(ROW, ROW.n)])
        def row_c(v):
            i = v.reciprocal(r_rstd, r_t)
            p.chain_bar("vector", v, i)
            return v.tensor_mul(r_ms, r_mu, r_rstd)
        emit(p, "vector", row_c, ROW, wait=[(ROWA, ROWA.n)])
        t_rowe = ROW.n
        hist["rowe"].append(t_rowe)
        emit(p, "tensor", lambda t: t.matmul(pb[2][:], ones_f[:], r_rstd, start=True,
                                             stop=True), NRM, wait=[(ROW, t_rowe), (XQ, XQ.n)])
        emit(p, "tensor", lambda t: t.matmul(pb[3][:], ones_f[:], r_ms, start=True,
                                             stop=True), NRM)
        t_nrm = NRM.n
        def xn_f(v, par=par):
            for c in range(8):
                i = v.tensor_mul(tmp32[:, c % 2, :], xt[par][:, c, :], pb[2][:])
                p.chain_bar("vector", v, i)
                i = v.tensor_sub(xq[:, c, :], tmp32[:, c % 2, :], pb[3][:])
                if c < 7:
                    p.chain_bar("vector", v, i)
            return i
        emit(p, "vector", xn_f, XQ, wait=[(NRM, t_nrm)])
        t_xn = XQ.n
        for hh in range(8):
            bank = 4 + hh % 2
            wbu = hh % 2
            t_ws = emit_dma(p, "sync", wstr[wbu][:],
                            wkb.ap().rearrange("(c p) q -> p c q", p=128)
                            [:, :, hh * 128:(hh + 1) * 128], LWS[wbu],
                            wait=[(LWG, t_wc3), (PRJ, max(0, PRJ.n - 1))])
            w3 = [(XQ, t_xn)]
            if QEV.n >= 1:
                w3.append((QEV, QEV.n - 1))
            def kmm(t, bank=bank, wbu=wbu, t_ws=t_ws):
                t.wait_ge(LWS[wbu].h, t_ws)
                for c in range(8):
                    last = t.matmul(pb[bank][:], wstr[wbu][:, c, :], xq[:, c, :],
                                    start=(c == 0), stop=(c == 7))
                return last
            emit(p, "tensor", kmm, PRJ, wait=w3)
            t_pk = PRJ.n
            emit(p, "vector", lambda v, hh=hh, bank=bank, t0=t0: v.tensor_scalar(
                out=kbuf[hh][:, t0:t0 + 512], in0=pb[bank][:],
                scalar1=bk_s[:, hh:hh + 1], scalar2=None, op0=ALU.add),
                QEV, wait=[(PRJ, t_pk)])
        for hh in range(8):
            wbu = hh % 2
            t_ws = emit_dma(p, "sync", wstr[wbu][:],
                            wvb.ap().rearrange("(c p) q -> p c q", p=128)
                            [:, :, hh * 128:(hh + 1) * 128], LWS[wbu],
                            wait=[(LWG, t_wc3), (PRJ, max(0, PRJ.n - 1))])
            for h2 in range(2):
                h = 2 * hh + h2
                for sj in range(4):
                    bank = 6 + (sj % 2)
                    w4 = [(XQ, t_xn)]
                    if VEV.n >= 1:
                        w4.append((VEV, VEV.n - 1))
                    def vmm(t, h2=h2, sj=sj, bank=bank, wbu=wbu, t_ws=t_ws):
                        t.wait_ge(LWS[wbu].h, t_ws)
                        for c in range(8):
                            last = t.matmul(pb[bank][:, 0:64],
                                            xq[:, c, sj * 128:(sj + 1) * 128],
                                            wstr[wbu][:, c, h2 * 64:h2 * 64 + 64],
                                            start=(c == 0), stop=(c == 7))
                        return last
                    emit(p, "tensor", vmm, PRJ, wait=w4)
                    t_pv = PRJ.n
                    def vev_f(v, h=h, sj=sj, bank=bank, ti=ti):
                        kc = ti * 4 + sj
                        return v.tensor_add(vones[:, (h * NKC + kc) * 65:(h * NKC + kc) * 65 + 64],
                                            pb[bank][:, 0:64], bvb[:, h * 64:(h + 1) * 64])
                    emit(p, "vector", vev_f, VEV, wait=[(PRJ, t_pv)])
        lo_t = max(t0, HL); hi_t = min(t0 + 512, HL + TOK)
        if lo_t < hi_t:
            for hh in range(8):
                bank = 4 + hh % 2
                wbu = hh % 2
                t_ws = emit_dma(p, "sync", wstr[wbu][:],
                                wqb.ap().rearrange("(c p) q -> p c q", p=128)
                                [:, :, hh * 128:(hh + 1) * 128], LWS[wbu],
                                wait=[(LWG, t_wc3), (PRJ, max(0, PRJ.n - 1))])
                def qmm(t, bank=bank, wbu=wbu, t_ws=t_ws):
                    t.wait_ge(LWS[wbu].h, t_ws)
                    for c in range(8):
                        last = t.matmul(pb[bank][:], wstr[wbu][:, c, :], xq[:, c, :],
                                        start=(c == 0), stop=(c == 7))
                    return last
                emit(p, "tensor", qmm, PRJ, wait=[(XQ, t_xn), (QEV, QEV.n - 1)])
                t_pq = PRJ.n
                emit(p, "vector", lambda v, hh=hh, bank=bank, t0=t0, lo_t=lo_t, hi_t=hi_t:
                     v.tensor_scalar(out=qbuf[hh][:, lo_t - HL:hi_t - HL],
                                     in0=pb[bank][:, lo_t - t0:hi_t - t0],
                                     scalar1=bq_s[:, hh:hh + 1], scalar2=None, op0=ALU.add),
                     QEV, wait=[(PRJ, t_pq)])
        hist["prj"].append(PRJ.n)
        hist["qev"].append(QEV.n)
        hist["vev"].append(VEV.n)
    t_p1_qev = QEV.n
    t_p1_vev = VEV.n

    for hh in range(8):
        for j in range(TH // 512):
            w6 = [(QEV, t_p1_qev)] if (hh == 0 and j == 0) else [(QN, QN.n)]
            emit(p, "scalar", lambda sc, hh=hh, j=j: sc.activation(
                out=ksq[:], in_=kbuf[hh][:, j * 512:(j + 1) * 512], func=AF.Square),
                SQ, wait=w6)
            emit(p, "tensor", lambda t, hh=hh: t.matmul(
                pb[0][0:1, :], hselA[:, 0:1], ksq[:], start=True, stop=True),
                QN, wait=[(SQ, SQ.n)])
            emit(p, "tensor", lambda t, hh=hh: t.matmul(
                pb[1][0:1, :], hselA[:, 1:2], ksq[:], start=True, stop=True), QN)
            t_qn = QN.n
            def nmax(v, hh=hh):
                i = v.tensor_reduce(out=r_mu[:, 0:1], in_=pb[0][0:1, :], axis=AX.X, op=ALU.max)
                p.chain_bar("vector", v, i)
                i = v.tensor_reduce(out=r_mu[:, 1:2], in_=pb[1][0:1, :], axis=AX.X, op=ALU.max)
                p.chain_bar("vector", v, i)
                i = v.tensor_max(ncrow[:, 2 * hh:2 * hh + 2], ncrow[:, 2 * hh:2 * hh + 2],
                                 r_mu[:, 0:2])
                return i
            emit(p, "vector", nmax, QN, wait=[(QN, t_qn)])
    for hh in range(8):
        for j in range(TOK // 512):
            emit(p, "scalar", lambda sc, hh=hh, j=j: sc.activation(
                out=ksq[:], in_=qbuf[hh][:, j * 512:(j + 1) * 512], func=AF.Square),
                SQ, wait=[(QN, QN.n)])
            emit(p, "tensor", lambda t: t.matmul(
                pb[0][0:1, :], hselA[:, 0:1], ksq[:], start=True, stop=True),
                QN, wait=[(SQ, SQ.n)])
            emit(p, "tensor", lambda t: t.matmul(
                pb[1][0:1, :], hselA[:, 1:2], ksq[:], start=True, stop=True), QN)
            t_qn = QN.n
            def nmax2(v, hh=hh):
                i = v.tensor_reduce(out=r_mu[:, 0:1], in_=pb[0][0:1, :], axis=AX.X, op=ALU.max)
                p.chain_bar("vector", v, i)
                i = v.tensor_reduce(out=r_mu[:, 1:2], in_=pb[1][0:1, :], axis=AX.X, op=ALU.max)
                p.chain_bar("vector", v, i)
                return v.tensor_max(nrow2[:, 2 * hh:2 * hh + 2], nrow2[:, 2 * hh:2 * hh + 2],
                                    r_mu[:, 0:2])
            emit(p, "vector", nmax2, QN, wait=[(QN, t_qn)])
    t_qnf = QN.n
    def cfin(v):
        i = v.tensor_mul(ncrow[:], ncrow[:], nrow2[:])
        return i
    emit(p, "vector", cfin, ROW, wait=[(QN, t_qnf)])
    emit(p, "scalar", lambda sc: sc.activation(out=ncrow[:], in_=ncrow[:], func=AF.Sqrt),
         ROWA, wait=[(ROW, ROW.n)])
    emit(p, "vector", lambda v: v.tensor_single_scalar(out=ncrow[:], in_=ncrow[:],
                                                       scalar=-0.125, op=ALU.mult),
         ROW, wait=[(ROWA, ROWA.n)])
    emit(p, "tensor", lambda t: t.matmul(pb[2][:, 0:16], ones_f[:], ncrow[:],
                                         start=True, stop=True), NRM, wait=[(ROW, ROW.n)])
    emit(p, "scalar", lambda sc: sc.activation(out=negc[:], in_=pb[2][:, 0:16],
                                               func=AF.Copy), CRD, wait=[(NRM, NRM.n)])
    t_crd = CRD.n

    nqk = [0, 0]; nexp = [0, 0]; nav = [0, 0]
    t_aoe_prev = 0
    t_x3_prev = [0] * 8
    od_hist = [0]
    BML = [p.sem("bml0"), p.sem("bml1")]
    for qi in range(NQT):
        q0 = qi * 512
        t_bm = emit_dma(p, "sync", bm_sb[:].rearrange("p (a t) -> p a t", a=8),
                        bmask.rearrange("a p t -> p a t")[:, qi * 8:(qi + 1) * 8, :],
                        BML[qi % 2], wait=[(PS[0], nexp[0]), (PS[1], nexp[1])])
        BMLq = BML[qi % 2]
        for hh in range(8):
            hA, hB = 2 * hh, 2 * hh + 1
            for kc in range(8):
                kcg = qi * 4 + kc
                par = kc % 2
                for sub, base in ((0, 0), (1, 64)):
                    w = []
                    if nexp[sub] > 0:
                        w.append((PS[sub], nexp[sub]))
                    if qi == 0 and hh == 0 and kc == 0:
                        w.append((CRD, t_crd))
                        w.append((VEV, t_p1_vev))
                    emit(p, "tensor", lambda t, hh=hh, kcg=kcg, q0=q0, base=base, sub=sub:
                         t.matmul(pb[sub][:], kbuf[hh][base:base + 64, kcg * 128:kcg * 128 + 128],
                                  qbuf[hh][base:base + 64, q0:q0 + 512],
                                  start=True, stop=True), QK[sub], wait=w)
                    nqk[sub] += 1
                for sub, h in ((0, hA), (1, hB)):
                    w = [(QK[sub], nqk[sub])]
                    if nav[sub] >= 2:
                        w.append((AV[sub], nav[sub] - 1))
                    emit(p, "scalar", lambda sc, sub=sub, h=h, qi=qi, par=par: sc.activation(
                        out=pbf[2 * sub + par][:], in_=pb[sub][:], func=AF.Exp, scale=0.125,
                        bias=negc[:, h:h + 1]), PS[sub], wait=w)
                    nexp[sub] += 1
                for sub in (0, 1):
                    wm = [(PS[sub], nexp[sub])]
                    if hh == 0 and kc == 0 and sub == 0:
                        wm.append((BMLq, t_bm))
                    emit(p, "vector", lambda v, sub=sub, par=par, kc=kc: v.tensor_mul(
                        pbf[2 * sub + par][:], pbf[2 * sub + par][:],
                        bm_sb[:, kc * 512:(kc + 1) * 512]),
                        PS[sub], wait=wm)
                    nexp[sub] += 1
                for sub, h in ((0, hA), (1, hB)):
                    w = [(PS[sub], nexp[sub])]
                    if kc == 0 and (qi > 0 or hh > 0):
                        w.append((AOE, AOE.n))
                    emit(p, "tensor", lambda t, sub=sub, h=h, kcg=kcg, par=par, kc=kc:
                         t.matmul(pb[4 + sub][0:65, :],
                                  vones[:, (h * NKC + kcg) * 65:(h * NKC + kcg) * 65 + 65],
                                  pbf[2 * sub + par][:], start=(kc == 0), stop=(kc == 7),
                                  skip_group_check=True), AV[sub], wait=w)
                    nav[sub] += 1
            def recip_f(v, navs=tuple(nav)):
                v.wait_ge(AV[0].h, navs[0])
                v.wait_ge(AV[1].h, navs[1])
                v.reciprocal(r4b[64:65, 0:512], pb[4][64:65, :])
                return v.reciprocal(r4b[64:65, 512:1024], pb[5][64:65, :])
            emit(p, "vector", recip_f, RC)
            t_rc = RC.n
            for sub in (0, 1):
                emit(p, "tensor", lambda t, sub=sub: t.matmul(
                    pb[2 + sub][0:64, :], ones64[64:65, 0:64],
                    r4b[64:65, sub * 512:(sub + 1) * 512], start=True, stop=True),
                    RBM, wait=[(RC, t_rc)])
            t_rbm = RBM.n
            def rb_ev(sc):
                sc.activation(out=rb[:, 0:512], in_=pb[2][0:64, :], func=AF.Copy)
                return sc.activation(out=rb[:, 512:1024], in_=pb[3][0:64, :], func=AF.Copy)
            emit(p, "scalar", rb_ev, RBE, wait=[(RBM, t_rbm)])
            t_rbe = RBE.n
            def ao_ev(v, hh=hh):
                i = v.tensor_mul(aos[0:64, :], pb[4][0:64, :], rb[:, 0:512])
                p.chain_bar("vector", v, i)
                i = v.tensor_mul(aos[64:128, :], pb[5][0:64, :], rb[:, 512:1024])
                return i
            emit(p, "vector", ao_ev, AOE, wait=[(RBE, t_rbe)])
            t_aoe = AOE.n
            t_x2l = emit_dma(p, "sync", tmp32[:, 0, :].rearrange("p (a t) -> p a t", a=1),
                             x2_fm[:, q0:q0 + 512].rearrange("(c p) t -> p c t", p=128)
                             [:, hh:hh + 1, :], X2L,
                             wait=[(OD, od_hist[qi]), (XQ, XQ.n)])
            def x3_f(v, hh=hh, t_x2l=t_x2l):
                v.wait_ge(X2L.h, t_x2l)
                return v.tensor_add(x3c[:, hh, :], tmp32[:, 0, :], aos[:])
            emit(p, "vector", x3_f, XQ, wait=[(AOE, t_aoe)])
        t_x3 = XQ.n
        t_od = emit_dma(p, "gpsimd", y[:, q0:q0 + 512].rearrange("(c p) t -> p c t", p=128),
                        x3c[:], OD, wait=[(XQ, t_x3)])
        od_hist.append(t_od)
    emit(p, "gpsimd", lambda g: g.nop().then_inc(p.sem("fin").h, 1), None, wait=[(OD, OD.n)])
    return p.build()



def build_l1_ssm(S_=S):
    p = Prog("ssm")
    nc = p.nc
    NTI = S_ // 512
    EC = 4

    x4_fm = nc.dram_tensor("x4_fm", [D, S_], F32, kind="ExternalInput").ap()
    inw = nc.dram_tensor("inw", [D, 512], F32, kind="ExternalInput").ap()
    skw = nc.dram_tensor("skw", [D, 512], F32, kind="ExternalInput").ap()
    wrow1 = nc.dram_tensor("wrow1", [512, 64], F32, kind="ExternalInput").ap()
    wrowBC = nc.dram_tensor("wrowBC", [512, 32], F32, kind="ExternalInput").ap()
    wd2sT = nc.dram_tensor("wd2sT", [64, 512], F32, kind="ExternalInput").ap()
    convw = nc.dram_tensor("convw", [128, 16], F32, kind="ExternalInput").ap()
    convb = nc.dram_tensor("convb", [128, 4], F32, kind="ExternalInput").ap()
    wd2b = nc.dram_tensor("wd2b", [128, 4], F32, kind="ExternalInput").ap()
    alog = nc.dram_tensor("alog", [128, 64], F32, kind="ExternalInput").ap()
    wdcol = nc.dram_tensor("wdcol", [128, 4], F32, kind="ExternalInput").ap()
    outw = nc.dram_tensor("outw", [512, D], F32, kind="ExternalInput").ap()
    nwcol = nc.dram_tensor("nwcol", [128, 8], F32, kind="ExternalInput").ap()
    nwrow = nc.dram_tensor("nwrow", [1, D], F32, kind="ExternalInput").ap()
    y = nc.dram_tensor("y", [D, S_], F32, kind="ExternalOutput").ap()

    xcd = nc.dram_tensor("xcd", [512, S_], BF16, kind="Internal")
    skd = nc.dram_tensor("skd", [512, S_], BF16, kind="Internal")
    dtbc_my = nc.dram_tensor("dtbc_my", [96, S_], F32, kind="Internal")
    dtbc = nc.dram_tensor("dtbc", [96, S_], F32, kind="Internal")
    po_my = nc.dram_tensor("po_my", [D, S_], F32, kind="Internal")
    po = nc.dram_tensor("po", [D, S_], F32, kind="Internal")

    xt = p.sb("xt", [128, 8, 512], F32)
    xn_bf = p.sb("xn_bf", [128, 8, 512], BF16)
    xsq = p.sb("xsq", [128, 512], F32)
    tmp32 = p.sb("tmp32", [128, 2, 512], F32)
    rows = p.sb("rows", [1, 7 * 512], F32)
    consts = p.sb("consts", [128, 4], F32)
    ones_cf = p.sb("ones_cf", [128, 1], F32)
    ones_f = p.sb("ones_f", [1, 128], F32)
    nw_sb = p.sb("nw_sb", [1, D], F32)
    inw_bf = p.sb("inw_bf", [128, 8, 512], BF16)
    skw_bf = p.sb("skw_bf", [128, 8, 512], BF16)
    w1_bf = p.sb("w1_bf", [128, 4, 64], BF16)
    wbc_bf = p.sb("wbc_bf", [128, 4, 32], BF16)
    wd2_bf = p.sb("wd2_bf", [64, 4, 128], BF16)
    outw_bf = p.sb("outw_bf", [128, 4 * 8 * 128], BF16)
    cw = p.sb("cw", [128, 16], F32)
    cb = p.sb("cb", [128, 4], F32)
    d2b = p.sb("d2b", [128, 4], F32)
    acol = p.sb("acol", [128, 64], F32)
    wdc = p.sb("wdc", [128, 4], F32)
    nwc = p.sb("nwc", [128, 8], F32)
    xit = p.sb("xit", [128, 4, 516], F32)
    xct = p.sb("xct", [128, 4, 512], BF16)
    dtbf = p.sb("dtbf", [64, S_], BF16)
    bcb = p.sb("bcb", [128, 32, 512], F32)
    delta = p.sb("delta", [128, 512], F32)
    dtx = p.sb("dtx", [128, 512], F32)
    a_t = p.sb("a_t", [128, 512], F32)
    u_t = p.sb("u_t", [128, 512], F32)
    h_t = p.sb("h_t", [128, 512], F32)
    yac = p.sb("yac", [128, 512], F32)
    sp1 = p.sb("sp1", [128, 512], F32)
    hcol = p.sb("hcol", [128, 64], F32)
    xcs = p.sb("xcs", [128, 512], BF16)
    sks = p.sb("sks", [128, 512], BF16)
    y3b = p.sb("y3b", [128, 4, 512], BF16)
    outev = p.sb("outev", [128, 8, 512], F32)
    pb = [p.ps(f"pb{i}", [128, 512], F32) for i in range(8)]

    LD = p.sem("ld"); LDC = p.sem("ldc"); LX = p.sem("lx"); CAST = p.sem("cast")
    STM = p.sem("stm"); SQ = p.sem("sq"); ROW = p.sem("row"); ROWA = p.sem("rowa")
    NRM = p.sem("nrm"); XQ = p.sem("xq"); PRJ = p.sem("prj"); PEV = p.sem("pev")
    PSK = p.sem("psk")
    CNV = p.sem("cnv"); CNE = p.sem("cne"); DBC = p.sem("dbc"); DBE = p.sem("dbe")
    CC = p.sem("cc"); DTL = p.sem("dtl")
    BCL = [p.sem("bcl0"), p.sem("bcl1")]
    XSL = [p.sem("xsl0"), p.sem("xsl1")]
    DLM = p.sem("dlm"); DLE = p.sem("dle"); SCN = p.sem("scn"); GPU = p.sem("gpu")
    OPM = p.sem("opm"); OPE = p.sem("ope"); YY = p.sem("yy")
    WR = p.sem("wr")
    POL = p.sem("pol")
    OD = p.sem("od")

    r_mu = rows[:, 0:512]; r_var = rows[:, 512:1024]; r_rstd = rows[:, 1024:1536]
    r_t = rows[:, 1536:2048]; r_ms = rows[:, 2048:2560]; r_s = rows[:, 2560:3072]
    r_t2 = rows[:, 3072:3584]

    emit_dma(p, "sync", cw[:], convw, LDC)
    emit_dma(p, "sync", cb[:], convb, LDC)
    emit_dma(p, "sync", d2b[:], wd2b, LDC)
    emit_dma(p, "sync", acol[:], alog, LDC)
    emit_dma(p, "sync", wdc[:], wdcol, LDC)
    emit_dma(p, "sync", nwc[:], nwcol, LDC)
    emit_dma(p, "sync", nw_sb[:], nwrow, LDC)
    t_ldc = LDC.n
    def init_c(v):
        v.memset(consts[:, 0:1], 1e-5)
        v.memset(consts[:, 1:2], 1.1920929e-07)
        v.memset(consts[:, 2:3], 1.0)
        v.memset(ones_cf[:], 1.0)
        v.memset(ones_f[:], 1.0)
        v.memset(hcol[:], 0.0)
        i = v.memset(xit[:], 0.0)
        return i
    emit(p, "vector", init_c, CAST, wait=[(LDC, t_ldc)])
    t_init = CAST.n
    emit(p, "scalar", lambda sc: sc.activation(out=acol[:], in_=acol[:], func=AF.Exp),
         ROWA, wait=[(CAST, t_init)])
    emit(p, "vector", lambda v: v.tensor_single_scalar(out=acol[:], in_=acol[:],
                                                       scalar=-1.0, op=ALU.mult),
         CAST, wait=[(ROWA, ROWA.n)])
    def wload(dram_ap, stage_view, dst_view):
        t_w = emit_dma(p, "sync", stage_view, dram_ap, LD, wait=[(CAST, CAST.n)])
        emit(p, "vector", lambda v: v.tensor_copy(dst_view, stage_view), CAST,
             wait=[(LD, t_w)])
    wload(inw.rearrange("(c p) q -> p c q", p=128), xt[:], inw_bf[:])
    wload(skw.rearrange("(c p) q -> p c q", p=128), xt[:], skw_bf[:])
    wload(wrow1.rearrange("(c p) q -> p c q", p=128), xt[:, 0:4, 0:64], w1_bf[:])
    wload(wrowBC.rearrange("(c p) q -> p c q", p=128), xt[:, 0:4, 0:32], wbc_bf[:])
    wload(wd2sT.rearrange("r (c q) -> r c q", c=4), xt[0:64, 0:4, 0:128], wd2_bf[:])
    wload(outw.rearrange("(ec p) f -> p ec f", p=128),
          xt[:].rearrange("p c q -> p (c q)").rearrange("p (ec f) -> p ec f", ec=4),
          outw_bf[:].rearrange("p (ec f) -> p ec f", ec=4))
    t_wc = CAST.n

    hist = {"xq": [0], "prj": [0], "pev": [0], "cnv": [0]}
    for ti in range(NTI):
        t0 = ti * 512
        w = [(CAST, t_wc)] if ti == 0 else [(XQ, hist["xq"][ti]), (CNV, hist["cnv"][ti])]
        t_x = emit_dma(p, "sync", xt[:],
                       x4_fm[:, t0:t0 + 512].rearrange("(c p) t -> p c t", p=128), LX, wait=w)
        def ssum(t):
            for c in range(8):
                last = t.matmul(pb[0][0:1, :], ones_cf[:], xt[:, c, :],
                                start=(c == 0), stop=(c == 7))
            return last
        emit(p, "tensor", ssum, STM, wait=[(LX, t_x), (ROW, ROW.n)])
        for c in range(8):
            w2 = [(LX, t_x), (WR, WR.n)] if c == 0 else [(STM, STM.n)]
            emit(p, "scalar", lambda sc, c=c: sc.activation(out=xsq[:], in_=xt[:, c, :],
                                                            func=AF.Square), SQ, wait=w2)
            emit(p, "tensor", lambda t, c=c: t.matmul(pb[1][0:1, :], ones_cf[:], xsq[:],
                                                      start=(c == 0), stop=(c == 7)),
                 STM, wait=[(SQ, SQ.n)])
        t_stm = STM.n
        def row_a(v):
            v.tensor_single_scalar(out=r_mu, in_=pb[0][0:1, :], scalar=1.0 / D, op=ALU.mult)
            i = v.tensor_single_scalar(out=r_var, in_=pb[1][0:1, :], scalar=1.0 / D, op=ALU.mult)
            p.chain_bar("vector", v, i)
            i = v.tensor_mul(r_t, r_mu, r_mu)
            p.chain_bar("vector", v, i)
            return v.tensor_sub(r_var, r_var, r_t)
        emit(p, "vector", row_a, ROW, wait=[(STM, t_stm)])
        emit(p, "scalar", lambda sc: sc.activation(out=r_t, in_=r_var, func=AF.Sqrt,
                                                   bias=consts[0:1, 0:1]),
             ROWA, wait=[(ROW, ROW.n)])
        def row_c(v):
            i = v.reciprocal(r_rstd, r_t)
            p.chain_bar("vector", v, i)
            i = v.tensor_mul(r_t2, r_rstd, r_rstd)
            p.chain_bar("vector", v, i)
            return v.tensor_mul(r_t2, r_t2, r_var)
        emit(p, "vector", row_c, ROW, wait=[(ROWA, ROWA.n)])
        emit(p, "scalar", lambda sc: sc.activation(out=r_t2, in_=r_t2, func=AF.Sqrt,
                                                   bias=consts[0:1, 1:2]),
             ROWA, wait=[(ROW, ROW.n)])
        def row_e(v):
            i = v.reciprocal(r_t2, r_t2)
            p.chain_bar("vector", v, i)
            i = v.tensor_mul(r_s, r_rstd, r_t2)
            p.chain_bar("vector", v, i)
            return v.tensor_mul(r_ms, r_mu, r_s)
        emit(p, "vector", row_e, ROW, wait=[(ROWA, ROWA.n)])
        t_rowe = ROW.n
        emit(p, "tensor", lambda t: t.matmul(pb[2][:], ones_f[:], r_s, start=True, stop=True),
             NRM, wait=[(ROW, t_rowe), (XQ, XQ.n)])
        for c in range(8):
            bank = 3 + (c % 2)
            w3 = [(NRM, NRM.n)] if c < 2 else [(XQ, XQ.n - 1)]
            emit(p, "tensor", lambda t, c=c, bank=bank: t.matmul(
                pb[bank][:], nw_sb[:, c * 128:(c + 1) * 128], r_ms, start=True, stop=True),
                NRM, wait=w3)
            t_r1 = NRM.n
            def xn_c(v, c=c, bank=bank):
                i = v.tensor_mul(tmp32[:, c % 2, :], xt[:, c, :], pb[2][:])
                p.chain_bar("vector", v, i)
                return v.scalar_tensor_tensor(out=xn_bf[:, c, :], in0=tmp32[:, c % 2, :],
                                              scalar=nwc[:, c:c + 1], in1=pb[bank][:],
                                              op0=ALU.mult, op1=ALU.subtract)
            emit(p, "vector", xn_c, XQ, wait=[(NRM, t_r1)])
        t_xn = XQ.n
        hist["xq"].append(t_xn)
        for ec in range(4):
            bank = 4 + ec % 2
            w4 = [(XQ, t_xn)]
            if PEV.n >= 2:
                w4.append((PEV, PEV.n - 1))
            def inmm(t, ec=ec, bank=bank):
                for c in range(8):
                    last = t.matmul(pb[bank][:], inw_bf[:, c, ec * 128:(ec + 1) * 128],
                                    xn_bf[:, c, :], start=(c == 0), stop=(c == 7))
                return last
            emit(p, "tensor", inmm, PRJ, wait=w4)
            t_pi = PRJ.n
            def inev(v, ec=ec, bank=bank):
                i = v.tensor_copy(xit[:, ec, 0:4], xit[:, ec, 512:516])
                p.chain_bar("vector", v, i)
                return v.tensor_copy(xit[:, ec, 4:516], pb[bank][:])
            emit(p, "vector", inev, PEV, wait=[(PRJ, t_pi), (CNV, CNV.n)])
            bank2 = 6 + ec % 2
            def skmm(t, ec=ec, bank2=bank2):
                for c in range(8):
                    last = t.matmul(pb[bank2][:], skw_bf[:, c, ec * 128:(ec + 1) * 128],
                                    xn_bf[:, c, :], start=(c == 0), stop=(c == 7))
                return last
            w5 = [(PRJ, PRJ.n)]
            if PSK.n >= 2:
                w5.append((PSK, PSK.n - 1))
            emit(p, "tensor", skmm, PRJ, wait=w5)
            t_ps = PRJ.n
            emit(p, "scalar", lambda sc, ec=ec, bank2=bank2: sc.activation(
                out=y3b[:, ec, :], in_=pb[bank2][:], func=AF.Silu), PSK,
                wait=[(PRJ, t_ps), (WR, WR.n)])
        t_pev = PEV.n
        t_psk = PSK.n
        hist["prj"].append(PRJ.n)
        emit_dma(p, "sync", skd.ap()[:, t0:t0 + 512].rearrange("(c p) t -> p c t", p=128),
                 y3b[:], WR, wait=[(PSK, t_psk)])
        for ec in range(4):
            def conv_f(v, ec=ec):
                i = v.tensor_scalar(out=tmp32[:, 0, :], in0=xit[:, ec, 4:516],
                                    scalar1=cw[:, ec * 4 + 3:ec * 4 + 4],
                                    scalar2=cb[:, ec:ec + 1], op0=ALU.mult, op1=ALU.add)
                p.chain_bar("vector", v, i)
                for k in range(1, 4):
                    i = v.scalar_tensor_tensor(out=tmp32[:, 0, :],
                                               in0=xit[:, ec, 4 - k:516 - k],
                                               scalar=cw[:, ec * 4 + 3 - k:ec * 4 + 4 - k],
                                               in1=tmp32[:, 0, :], op0=ALU.mult, op1=ALU.add)
                    if k < 3:
                        p.chain_bar("vector", v, i)
                return i
            emit(p, "vector", conv_f, CNV, wait=[(PEV, t_pev), (CNE, CNE.n)])
            t_cv = CNV.n
            emit(p, "scalar", lambda sc, ec=ec: sc.activation(
                out=xct[:, ec, :], in_=tmp32[:, 0, :], func=AF.Silu), CNE,
                wait=[(CNV, t_cv)])
        t_cne = CNE.n
        hist["cnv"].append(CNV.n)
        emit_dma(p, "sync", xcd.ap()[:, t0:t0 + 512].rearrange("(c p) t -> p c t", p=128),
                 xct[:], WR, wait=[(CNE, t_cne)])
        def dtmm(t):
            for ec in range(4):
                t.matmul(pb[0][0:64, :], w1_bf[:, ec, :], xct[:, ec, :],
                         start=(ec == 0), stop=(ec == 3))
            for ec in range(4):
                last = t.matmul(pb[1][0:32, :], wbc_bf[:, ec, :], xct[:, ec, :],
                                start=(ec == 0), stop=(ec == 3))
            return last
        emit(p, "tensor", dtmm, DBC, wait=[(CNE, t_cne), (DBE, DBE.n)])
        t_dm = DBC.n
        def dtev(v):
            i = v.tensor_copy(tmp32[:, 0, 0:512], pb[0][0:64, :].rearrange("a b -> a b"))
            return i
        def dtev2(v):
            i = v.tensor_copy(xsq[0:64, :], pb[0][0:64, :])
            p.chain_bar("vector", v, i)
            return v.tensor_copy(xsq[64:96, :] if False else tmp32[0:32, 1, :], pb[1][0:32, :])
        emit(p, "vector", dtev2, DBE, wait=[(DBC, t_dm), (WR, WR.n)])
        t_de = DBE.n
        emit_dma(p, "sync", dtbc_my.ap()[0:64, t0:t0 + 512], xsq[0:64, :], WR,
                 wait=[(DBE, t_de)])
        emit_dma(p, "sync", dtbc_my.ap()[64:96, t0:t0 + 512], tmp32[0:32, 1, :], WR,
                 wait=[(DBE, t_de)])
    t_pa = WR.n

    emit(p, "gpsimd", lambda g: g.collective_compute(
        "AllReduce", ALU.add, replica_groups=[[0, 1, 2, 3], [4, 5, 6, 7]],
        ins=[dtbc_my.ap().opt()], outs=[dtbc.ap().opt()]), CC, wait=[(WR, t_pa)])
    t_cc1 = CC.n
    PIECE = min(4096, S_)
    for j in range(ceil_div(S_, PIECE)):
        t_dt = emit_dma(p, "sync",
                        xt[0:64, :].rearrange("r c q -> r (c q)")[:, 0:PIECE]
                        .rearrange("r (c q) -> r c q", q=512),
                        dtbc.ap()[0:64, j * PIECE:(j + 1) * PIECE]
                        .rearrange("r (c q) -> r c q", q=512), DTL,
                        wait=[(CC, t_cc1), (CAST, CAST.n)])
        emit(p, "vector", lambda v, j=j: v.tensor_copy(
            dtbf[:, j * PIECE:(j + 1) * PIECE].rearrange("r (c q) -> r c q", q=512),
            xt[0:64, :].rearrange("r c q -> r (c q)")[:, 0:PIECE]
            .rearrange("r (c q) -> r c q", q=512)),
            CAST, wait=[(DTL, t_dt)])
    t_dtb = CAST.n

    ndve = 0
    for tc in range(NTI):
        t0 = tc * 512
        par = tc % 2
        t_bc = emit_dma(p, "sync", bcb[:],
                        dtbc.ap()[64:96, t0:t0 + 512].unsqueeze(0)
                        .broadcast_to([128, 32, 512]), BCL[par],
                        wait=[(GPU, GPU.n), (SCN, SCN.n)])
        for grp in range(4):
            w6 = [(CAST, t_dtb), (DLE, DLE.n)]
            emit(p, "tensor", lambda t, grp=grp, t0=t0: t.matmul(
                pb[6][:], wd2_bf[:, grp, :], dtbf[:, t0:t0 + 512], start=True, stop=True),
                DLM, wait=w6)
            t_dlm = DLM.n
            def spf(sc, grp=grp):
                i = sc.activation(out=sp1[:], in_=pb[6][:], func=AF.Abs,
                                  bias=d2b[:, grp:grp + 1])
                p.chain_bar("scalar", sc, i)
                i = sc.activation(out=sp1[:], in_=sp1[:], func=AF.Exp, scale=-1.0)
                p.chain_bar("scalar", sc, i)
                i = sc.activation(out=sp1[:], in_=sp1[:], func=AF.Ln, bias=consts[:, 2:3])
                p.chain_bar("scalar", sc, i)
                return sc.activation(out=delta[:], in_=pb[6][:], func=AF.Relu,
                                     bias=d2b[:, grp:grp + 1])
            emit(p, "scalar", spf, DLE, wait=[(DLM, t_dlm), (SCN, SCN.n)])
            t_sp = DLE.n
            t_xc = emit_dma(p, "sync", xcs[:],
                            xcd.ap()[grp * 128:(grp + 1) * 128, t0:t0 + 512], XSL[0],
                            wait=[(GPU, GPU.n), (SCN, SCN.n), (YY, YY.n)])
            t_sk = emit_dma(p, "sync", sks[:],
                            skd.ap()[grp * 128:(grp + 1) * 128, t0:t0 + 512], XSL[1],
                            wait=[(YY, YY.n)])
            def dfin(v, t_sp=t_sp, t_xc=t_xc):
                v.wait_ge(DLE.h, t_sp)
                v.wait_ge(XSL[0].h, t_xc)
                i = v.tensor_add(delta[:], delta[:], sp1[:])
                p.chain_bar("vector", v, i)
                return v.tensor_mul(dtx[:], delta[:], xcs[:])
            emit(p, "vector", dfin, SCN)
            t_df = SCN.n
            for n in range(16):
                col = grp * 16 + n
                emit(p, "scalar", lambda sc, col=col: sc.activation(
                    out=a_t[:], in_=delta[:], func=AF.Exp, scale=acol[:, col:col + 1]),
                    DLE, wait=[(SCN, SCN.n), (GPU, GPU.n)])
                t_a = DLE.n
                def umul(g, n=n, t_df=t_df, t_bc=t_bc, par=par):
                    g.wait_ge(SCN.h, t_df)
                    g.wait_ge(BCL[par].h, t_bc)
                    g.wait_ge(SCN.h, SCN.n)
                    return g.tensor_mul(u_t[:], dtx[:], bcb[:, n, :])
                emit(p, "gpsimd", lambda g, n=n, t_df=t_df, t_bc=t_bc, par=par,
                     w_scn=SCN.n: (
                    g.wait_ge(SCN.h, t_df),
                    g.wait_ge(BCL[par].h, t_bc),
                    g.wait_ge(SCN.h, w_scn),
                    g.tensor_mul(u_t[:], dtx[:], bcb[:, n, :]))[-1], GPU)
                t_u = GPU.n
                def scn_f(v, col=col, n=n, t_a=t_a, t_u=t_u):
                    v.wait_ge(DLE.h, t_a)
                    v.wait_ge(GPU.h, t_u)
                    i = v.tensor_tensor_scan(out=h_t[:], data0=a_t[:], data1=u_t[:],
                                             initial=hcol[:, col:col + 1],
                                             op0=ALU.mult, op1=ALU.add)
                    p.chain_bar("vector", v, i)
                    i = v.tensor_copy(hcol[:, col:col + 1], h_t[:, 511:512])
                    p.chain_bar("vector", v, i)
                    if n == 0:
                        i = v.tensor_mul(yac[:], h_t[:], bcb[:, 16 + n, :])
                    else:
                        i = v.tensor_mul(tmp32[:, 0, :], h_t[:], bcb[:, 16 + n, :])
                        p.chain_bar("vector", v, i)
                        i = v.tensor_add(yac[:], yac[:], tmp32[:, 0, :])
                    return i
                emit(p, "vector", scn_f, SCN)
            def yfin(v, grp=grp, t_sk=t_sk):
                v.wait_ge(XSL[1].h, t_sk)
                i = v.scalar_tensor_tensor(out=yac[:], in0=xcs[:], scalar=wdc[:, grp:grp + 1],
                                           in1=yac[:], op0=ALU.mult, op1=ALU.add)
                p.chain_bar("vector", v, i)
                return v.tensor_mul(y3b[:, grp, :], yac[:], sks[:])
            emit(p, "vector", yfin, YY, wait=[(OPM, OPM.n), (SCN, SCN.n)])
        t_yy = YY.n
        for dc in range(8):
            bank = 4 + dc % 2
            w7 = [(YY, t_yy)]
            if dc >= 2:
                w7.append((OPE, OPE.n - 1))
            def opmm(t, dc=dc, bank=bank):
                for grp in range(4):
                    last = t.matmul(pb[bank][:],
                                    outw_bf[:, grp * 1024 + dc * 128: grp * 1024 + dc * 128 + 128],
                                    y3b[:, grp, :], start=(grp == 0), stop=(grp == 3))
                return last
            emit(p, "tensor", opmm, OPM, wait=w7)
            t_om = OPM.n
            emit(p, "scalar", lambda sc, dc=dc, bank=bank: sc.activation(
                out=outev[:, dc, :], in_=pb[bank][:], func=AF.Copy), OPE,
                wait=[(OPM, t_om), (WR, WR.n)])
        t_oe = OPE.n
        emit_dma(p, "sync", po_my.ap()[:, t0:t0 + 512].rearrange("(c p) t -> p c t", p=128),
                 outev[:], WR, wait=[(OPE, t_oe)])
    t_scan_done = WR.n

    emit(p, "gpsimd", lambda g: g.collective_compute(
        "AllReduce", ALU.add, replica_groups=[[0, 1, 2, 3], [4, 5, 6, 7]],
        ins=[po_my.ap().opt()], outs=[po.ap().opt()]), CC, wait=[(WR, t_scan_done)])
    t_cc2 = CC.n

    for ti in range(NTI):
        t0 = ti * 512
        t_x = emit_dma(p, "sync", xt[:],
                       x4_fm[:, t0:t0 + 512].rearrange("(c p) t -> p c t", p=128), LX,
                       wait=[(CC, t_cc2), (XQ, XQ.n), (STM, STM.n), (OD, OD.n)])
        t_p = emit_dma(p, "sync", inw_bf[:].rearrange("p c q -> p (c q)").bitcast(F32).rearrange("p (c q) -> p c q", q=512),
                       po.ap()[:, t0:t0 + 512].rearrange("(c p) t -> p c t", p=128)[:, 0:4, :],
                       POL, wait=[(XQ, XQ.n)])
        t_p2 = emit_dma(p, "sync", skw_bf[:].rearrange("p c q -> p (c q)").bitcast(F32).rearrange("p (c q) -> p c q", q=512),
                        po.ap()[:, t0:t0 + 512].rearrange("(c p) t -> p c t", p=128)[:, 4:8, :],
                        POL)
        def ssum2(t):
            for c in range(8):
                last = t.matmul(pb[0][0:1, :], ones_cf[:], xt[:, c, :],
                                start=(c == 0), stop=(c == 7))
            return last
        emit(p, "tensor", ssum2, STM, wait=[(LX, LX.n), (ROW, ROW.n)])
        for c in range(8):
            w2 = [(LX, LX.n)] if c == 0 else [(STM, STM.n)]
            emit(p, "scalar", lambda sc, c=c: sc.activation(out=xsq[:], in_=xt[:, c, :],
                                                            func=AF.Square), SQ, wait=w2)
            emit(p, "tensor", lambda t, c=c: t.matmul(pb[1][0:1, :], ones_cf[:], xsq[:],
                                                      start=(c == 0), stop=(c == 7)),
                 STM, wait=[(SQ, SQ.n)])
        t_stm = STM.n
        def row_a2(v):
            v.tensor_single_scalar(out=r_mu, in_=pb[0][0:1, :], scalar=1.0 / D, op=ALU.mult)
            i = v.tensor_single_scalar(out=r_var, in_=pb[1][0:1, :], scalar=1.0 / D, op=ALU.mult)
            p.chain_bar("vector", v, i)
            i = v.tensor_mul(r_t, r_mu, r_mu)
            p.chain_bar("vector", v, i)
            return v.tensor_sub(r_var, r_var, r_t)
        emit(p, "vector", row_a2, ROW, wait=[(STM, t_stm)])
        emit(p, "scalar", lambda sc: sc.activation(out=r_t, in_=r_var, func=AF.Sqrt,
                                                   bias=consts[0:1, 0:1]),
             ROWA, wait=[(ROW, ROW.n)])
        def row_c2(v):
            i = v.reciprocal(r_rstd, r_t)
            p.chain_bar("vector", v, i)
            i = v.tensor_mul(r_ms, r_mu, r_rstd)
            p.chain_bar("vector", v, i)
            return v.tensor_single_scalar(out=r_s, in_=r_rstd, scalar=1.0, op=ALU.add)
        emit(p, "vector", row_c2, ROW, wait=[(ROWA, ROWA.n)])
        t_row2 = ROW.n
        emit(p, "tensor", lambda t: t.matmul(pb[2][:], ones_f[:], r_s, start=True, stop=True),
             NRM, wait=[(ROW, t_row2), (XQ, XQ.n)])
        emit(p, "tensor", lambda t: t.matmul(pb[3][:], ones_f[:], r_ms, start=True, stop=True),
             NRM)
        t_nrm = NRM.n
        def x5f(v, t_p=t_p, t_p2=t_p2):
            v.wait_ge(POL.h, t_p2)
            for c in range(8):
                i = v.tensor_mul(tmp32[:, c % 2, :], xt[:, c, :], pb[2][:])
                p.chain_bar("vector", v, i)
                i = v.tensor_sub(tmp32[:, c % 2, :], tmp32[:, c % 2, :], pb[3][:])
                p.chain_bar("vector", v, i)
                if c < 4:
                    src_po = inw_bf[:].rearrange("p c q -> p (c q)").bitcast(F32).rearrange("p (cc q) -> p cc q", q=512)[:, c, :]
                else:
                    src_po = skw_bf[:].rearrange("p c q -> p (c q)").bitcast(F32).rearrange("p (cc q) -> p cc q", q=512)[:, c - 4, :]
                i = v.tensor_add(xt[:, c, :], tmp32[:, c % 2, :], src_po)
                if c < 7:
                    p.chain_bar("vector", v, i)
            return i
        emit(p, "vector", x5f, XQ, wait=[(NRM, t_nrm)])
        t_x5 = XQ.n
        emit_dma(p, "gpsimd", y[:, t0:t0 + 512].rearrange("(c p) t -> p c t", p=128),
                 xt[:], OD, wait=[(XQ, t_x5)])
    emit(p, "gpsimd", lambda g: g.nop().then_inc(p.sem("fin").h, 1), None, wait=[(OD, OD.n)])
    return p.build()



def build_l1_ffn(TOK=2048):
    p = Prog("l1ffn")
    nc = p.nc
    NTI = TOK // 512

    x_fm = nc.dram_tensor("x_fm", [D, TOK], F32, kind="ExternalInput").ap()
    w1T = nc.dram_tensor("w1T", [D, 4096], F32, kind="ExternalInput").ap()
    w2T = nc.dram_tensor("w2T", [4096, D], F32, kind="ExternalInput").ap()
    b1col = nc.dram_tensor("b1col", [128, 32], F32, kind="ExternalInput").ap()
    b2col = nc.dram_tensor("b2col", [128, 8], F32, kind="ExternalInput").ap()
    y = nc.dram_tensor("y", [D, TOK], F32, kind="ExternalOutput").ap()
    w1b = nc.dram_tensor("w1b", [D, 4096], BF16, kind="Internal")
    w2b = nc.dram_tensor("w2b", [4096, D], BF16, kind="Internal")

    xt = p.sb("xt", [128, 8, 512], F32)
    x5t = p.sb("x5t", [128, 8, 512], F32)
    xfb = p.sb("xfb", [128, 8, 512], BF16)
    xsq = p.sb("xsq", [128, 512], F32)
    tmp32 = p.sb("tmp32", [128, 2, 512], F32)
    rows = p.sb("rows", [1, 6 * 512], F32)
    consts = p.sb("consts", [128, 4], F32)
    ones_cf = p.sb("ones_cf", [128, 1], F32)
    ones_f = p.sb("ones_f", [1, 128], F32)
    wst = p.sb("wst", [128, 4096], F32)
    wtb = p.sb("wtb", [128, 4096], BF16)
    absr = p.sb("absr", [128, 1], F32)
    acc1 = p.sb("acc1", [128, 1], F32)
    thr = p.sb("thr", [128, 1], F32)
    scl = p.sb("scl", [1, 2], F32)
    w1s = [p.sb(f"w1s{i}", [128, 8, 128], BF16) for i in range(2)]
    w2s = [p.sb(f"w2s{i}", [128, 32, 128], BF16) for i in range(2)]
    hbf = p.sb("hbf", [128, 32, 512], BF16)
    b1c = p.sb("b1c", [128, 32], F32)
    b2c = p.sb("b2c", [128, 8], F32)
    pb = [p.ps(f"pb{i}", [128, 512], F32) for i in range(8)]

    LD = p.sem("ld"); LDC = p.sem("ldc"); LX = p.sem("lx"); CAST = p.sem("cast")
    STM = p.sem("stm"); SQ = p.sem("sq"); ROW = p.sem("row"); ROWA = p.sem("rowa")
    NRM = p.sem("nrm"); XQ = p.sem("xq"); TRN = p.sem("trn"); TPE = p.sem("tpe")
    WW = p.sem("ww"); FFM = p.sem("ffm"); GEL = p.sem("gel"); FF2 = p.sem("ff2")
    FFE = p.sem("ffe"); OD = p.sem("od")
    LWA = [p.sem("lwa0"), p.sem("lwa1")]; LWB = [p.sem("lwb0"), p.sem("lwb1")]
    WT = [p.sem("wt0"), p.sem("wt1")]

    r_mu = rows[:, 0:512]; r_var = rows[:, 512:1024]; r_rstd = rows[:, 1024:1536]
    r_t = rows[:, 1536:2048]; r_ms = rows[:, 2048:2560]

    emit_dma(p, "sync", b1c[:], b1col, LDC)
    emit_dma(p, "sync", b2c[:], b2col, LDC)
    t_ldc = LDC.n
    def init_c(v):
        v.memset(consts[:, 0:1], 1e-5)
        v.memset(ones_cf[:], 1.0)
        i = v.memset(ones_f[:], 1.0)
        p.chain_bar("vector", v, i)
        return v.memset(acc1[:], 0.0)
    emit(p, "vector", init_c, CAST, wait=[(LDC, t_ldc)])
    t_init = CAST.n

    def ternarize(wT, wout, nrows, ncols):
        ntile = nrows // 128
        emit(p, "vector", lambda v: v.memset(acc1[:], 0.0), TRN,
             wait=[(TRN, TRN.n), (CAST, CAST.n), (TPE, TPE.n)])
        for j in range(ntile):
            t_w = emit_dma(p, "sync", wst[:, 0:ncols], wT[j * 128:(j + 1) * 128, :],
                           WT[j % 2], wait=[(TRN, TRN.n)])
            def absf(v, t_w=t_w, jp=j % 2, ncols=ncols):
                v.wait_ge(WT[jp].h, t_w)
                i = v.tensor_reduce(out=absr[:], in_=wst[:, 0:ncols], axis=AX.X, op=ALU.add,
                                    apply_absolute_value=True)
                p.chain_bar("vector", v, i)
                return v.tensor_add(acc1[:], acc1[:], absr[:])
            emit(p, "vector", absf, TRN)
        t_abs = TRN.n
        emit(p, "tensor", lambda t: t.matmul(pb[0][0:1, 0:1], acc1[:], ones_cf[:],
                                             start=True, stop=True), TPE,
             wait=[(TRN, t_abs)])
        t_tm = TPE.n
        def sclf(v, nrows=nrows, ncols=ncols):
            i = v.tensor_single_scalar(out=scl[:, 0:1], in_=pb[0][0:1, 0:1],
                                       scalar=1.0 / (nrows * ncols), op=ALU.mult)
            p.chain_bar("vector", v, i)
            i = v.tensor_scalar_max(scl[:, 0:1], scl[:, 0:1], 1e-8)
            p.chain_bar("vector", v, i)
            return v.tensor_single_scalar(out=scl[:, 1:2], in_=scl[:, 0:1], scalar=0.5,
                                          op=ALU.mult)
        emit(p, "vector", sclf, TRN, wait=[(TPE, t_tm)])
        emit(p, "tensor", lambda t: t.matmul(pb[0][:, 0:1], ones_f[:], scl[:, 1:2],
                                             start=True, stop=True), TPE,
             wait=[(TRN, TRN.n)])
        emit(p, "scalar", lambda sc: sc.activation(out=thr[:], in_=pb[0][:, 0:1],
                                                   func=AF.Copy), TRN, wait=[(TPE, TPE.n)])
        t_thr = TRN.n
        for j in range(ntile):
            t_w = emit_dma(p, "sync", wst[:, 0:ncols], wT[j * 128:(j + 1) * 128, :],
                           WT[j % 2], wait=[(TRN, TRN.n), (WW, WW.n)])
            def trnf3(v, t_w=t_w, jp=j % 2, ncols=ncols):
                v.wait_ge(WT[jp].h, t_w)
                i = v.tensor_single_scalar(out=tmp32[:, 0, 0:1], in_=thr[:, 0:1],
                                           scalar=-1.0, op=ALU.mult)
                p.chain_bar("vector", v, i)
                i = v.tensor_single_scalar(out=wtb[:, 0:ncols], in_=wst[:, 0:ncols],
                                           scalar=thr[:, 0:1], op=ALU.is_gt)
                p.chain_bar("vector", v, i)
                i = v.tensor_single_scalar(out=wst[:, 0:ncols], in_=wst[:, 0:ncols],
                                           scalar=tmp32[:, 0, 0:1], op=ALU.is_lt)
                p.chain_bar("vector", v, i)
                return v.tensor_sub(wtb[:, 0:ncols], wtb[:, 0:ncols], wst[:, 0:ncols])
            emit(p, "vector", trnf3, TRN)
            t_t = TRN.n
            emit_dma(p, "sync", wout[j * 128:(j + 1) * 128, :], wtb[:, 0:ncols], WW,
                     wait=[(TRN, t_t)])
    ternarize(w1T, w1b.ap(), D, 4096)
    ternarize(w2T, w2b.ap(), 4096, D)
    t_tern = WW.n

    for g in range(NTI):
        t0 = g * 512
        w = [(CAST, t_init)] if g == 0 else [(XQ, XQ.n), (FFE, FFE.n)]
        t_x = emit_dma(p, "sync", xt[:],
                       x_fm[:, t0:t0 + 512].rearrange("(c p) t -> p c t", p=128), LX, wait=w)
        def ssum(t):
            for c in range(8):
                last = t.matmul(pb[0][0:1, :], ones_cf[:], xt[:, c, :],
                                start=(c == 0), stop=(c == 7))
            return last
        emit(p, "tensor", ssum, STM, wait=[(LX, t_x), (ROW, ROW.n)])
        for c in range(8):
            w2_ = [(LX, t_x)] if c == 0 else [(STM, STM.n)]
            emit(p, "scalar", lambda sc, c=c: sc.activation(out=xsq[:], in_=xt[:, c, :],
                                                            func=AF.Square), SQ, wait=w2_)
            emit(p, "tensor", lambda t, c=c: t.matmul(pb[1][0:1, :], ones_cf[:], xsq[:],
                                                      start=(c == 0), stop=(c == 7)),
                 STM, wait=[(SQ, SQ.n)])
        t_stm = STM.n
        def row_a(v):
            v.tensor_single_scalar(out=r_mu, in_=pb[0][0:1, :], scalar=1.0 / D, op=ALU.mult)
            i = v.tensor_single_scalar(out=r_var, in_=pb[1][0:1, :], scalar=1.0 / D, op=ALU.mult)
            p.chain_bar("vector", v, i)
            i = v.tensor_mul(r_t, r_mu, r_mu)
            p.chain_bar("vector", v, i)
            return v.tensor_sub(r_var, r_var, r_t)
        emit(p, "vector", row_a, ROW, wait=[(STM, t_stm)])
        emit(p, "scalar", lambda sc: sc.activation(out=r_t, in_=r_var, func=AF.Sqrt,
                                                   bias=consts[0:1, 0:1]),
             ROWA, wait=[(ROW, ROW.n)])
        def row_c(v):
            i = v.reciprocal(r_rstd, r_t)
            p.chain_bar("vector", v, i)
            return v.tensor_mul(r_ms, r_mu, r_rstd)
        emit(p, "vector", row_c, ROW, wait=[(ROWA, ROWA.n)])
        t_rowe = ROW.n
        emit(p, "tensor", lambda t: t.matmul(pb[2][:], ones_f[:], r_rstd, start=True,
                                             stop=True), NRM, wait=[(ROW, t_rowe), (XQ, XQ.n)])
        emit(p, "tensor", lambda t: t.matmul(pb[3][:], ones_f[:], r_ms, start=True,
                                             stop=True), NRM)
        t_nrm = NRM.n
        def xn_f(v):
            for c in range(8):
                i = v.tensor_mul(tmp32[:, c % 2, :], xt[:, c, :], pb[2][:])
                p.chain_bar("vector", v, i)
                i = v.tensor_sub(xfb[:, c, :], tmp32[:, c % 2, :], pb[3][:])
                if c < 7:
                    p.chain_bar("vector", v, i)
            return i
        emit(p, "vector", xn_f, XQ, wait=[(NRM, t_nrm)])
        t_xn = XQ.n
        def rescopy(v):
            for c in range(8):
                i = v.tensor_copy(x5t[:, c, :], xt[:, c, :])
            return i
        emit(p, "vector", rescopy, XQ)
        t_res = XQ.n
        for f in range(32):
            wb = f % 2
            t_w1 = emit_dma(p, "sync", w1s[wb][:],
                            w1b.ap().rearrange("(c p) q -> p c q", p=128)
                            [:, :, f * 128:(f + 1) * 128], LWA[wb],
                            wait=[(WW, t_tern), (FFM, max(0, FFM.n - 1))])
            bank = 4 + (f % 2)
            def fc1_mm(t, f=f, wb=wb, bank=bank, t_w1=t_w1):
                t.wait_ge(LWA[wb].h, t_w1)
                for c in range(8):
                    last = t.matmul(pb[bank][:], w1s[wb][:, c, :], xfb[:, c, :],
                                    start=(c == 0), stop=(c == 7))
                return last
            w9 = [(XQ, t_xn)]
            if f >= 2:
                w9.append((GEL, GEL.n - 1))
            emit(p, "tensor", fc1_mm, FFM, wait=w9)
            t_f1 = FFM.n
            emit(p, "scalar", lambda sc, f=f, bank=bank: sc.activation(
                out=hbf[:, f, :], in_=pb[bank][:], func=AF.Gelu,
                bias=b1c[:, f:f + 1]), GEL, wait=[(FFM, t_f1)])
        t_gel = GEL.n
        for dc in range(8):
            wb = dc % 2
            t_w2 = emit_dma(p, "sync", w2s[wb][:],
                            w2b.ap().rearrange("(hc p) q -> p hc q", p=128)
                            [:, :, dc * 128:(dc + 1) * 128], LWB[wb],
                            wait=[(WW, t_tern), (FF2, max(0, FF2.n - 1))])
            bank = 6 + (dc % 2)
            def fc2_mm(t, dc=dc, wb=wb, bank=bank, t_w2=t_w2):
                t.wait_ge(LWB[wb].h, t_w2)
                for hc in range(32):
                    last = t.matmul(pb[bank][:], w2s[wb][:, hc, :], hbf[:, hc, :],
                                    start=(hc == 0), stop=(hc == 31))
                return last
            w10 = [(GEL, t_gel)]
            if dc >= 2:
                w10.append((FFE, FFE.n - 1))
            emit(p, "tensor", fc2_mm, FF2, wait=w10)
            t_f2 = FF2.n
            def yev(v, dc=dc, bank=bank):
                i = v.tensor_scalar(out=tmp32[:, dc % 2, :], in0=pb[bank][:],
                                    scalar1=b2c[:, dc:dc + 1], scalar2=None, op0=ALU.add)
                p.chain_bar("vector", v, i)
                return v.tensor_add(x5t[:, dc, :], x5t[:, dc, :], tmp32[:, dc % 2, :])
            wy = [(FF2, t_f2), (XQ, t_res)]
            if FFE.n >= 1:
                wy.append((FFE, FFE.n - 1))
            emit(p, "vector", yev, FFE, wait=wy)
        t_ffe = FFE.n
        emit_dma(p, "gpsimd", y[:, t0:t0 + 512].rearrange("(c p) t -> p c t", p=128),
                 x5t[:], OD, wait=[(FFE, t_ffe)])
    emit(p, "gpsimd", lambda g_: g_.nop().then_inc(p.sem("fin").h, 1), None,
         wait=[(OD, OD.n)])
    return p.build()



def build_l1_dn(TOK=2048, S_=S, dbg_nch=None, dbg_stage=99, dbg_cb=4):
    p = Prog("l1dn")
    nc = p.nc
    HB = 64
    TH = TOK + HB
    CH = 128
    NCH = S_ // CH
    CPR = TOK // CH
    NRK = S_ // TOK
    FT = 256
    NFT = TOK // FT
    FIROFF = {3: 1, 7: 4, 15: 11, 31: 26}

    xh_fm = nc.dram_tensor("xh_fm", [D, TH], F32, kind="ExternalInput").ap()
    x6_fm = nc.dram_tensor("x6_fm", [D, TOK], F32, kind="ExternalInput").ap()
    wq = nc.dram_tensor("wq", [D, D], F32, kind="ExternalInput").ap()
    wk = nc.dram_tensor("wk", [D, D], F32, kind="ExternalInput").ap()
    wv = nc.dram_tensor("wv", [D, D], F32, kind="ExternalInput").ap()
    wb4 = nc.dram_tensor("wb4", [D, 4], F32, kind="ExternalInput").ap()
    cw3 = nc.dram_tensor("cw3", [128, 3 * 32], F32, kind="ExternalInput").ap()
    firw = nc.dram_tensor("firw", [128, 8 * 57], F32, kind="ExternalInput").ap()
    gw1h = nc.dram_tensor("gw1h", [D, 2048], F32, kind="ExternalInput").ap()
    gw1s = nc.dram_tensor("gw1s", [28, 2048], F32, kind="ExternalInput").ap()
    gb1c = nc.dram_tensor("gb1c", [128, 16], F32, kind="ExternalInput").ap()
    gw2 = nc.dram_tensor("gw2", [2048, 28], F32, kind="ExternalInput").ap()
    gb2r = nc.dram_tensor("gb2r", [28, 1], F32, kind="ExternalInput").ap()
    nwc2 = nc.dram_tensor("nwc2", [128, 2], F32, kind="ExternalInput").ap()
    wo = nc.dram_tensor("wo", [D, D], F32, kind="ExternalInput").ap()
    masks4 = nc.dram_tensor("masks4", [128, 5 * 128], BF16, kind="ExternalInput").ap()
    scol28 = nc.dram_tensor("scol28", [128, 28 * 28], BF16, kind="ExternalInput").ap()
    gselc = nc.dram_tensor("gselc", [28, 4], F32, kind="ExternalInput").ap()
    ridx_d = nc.dram_tensor("ridx", [128, (S_ // CH) * 7], U32,
                            kind="ExternalInput").ap()
    didx_d = nc.dram_tensor("didx", [128, (TOK // FT) * 8], U32,
                            kind="ExternalInput").ap()
    y = nc.dram_tensor("y", [D, TOK], F32, kind="ExternalOutput").ap()

    qkv_h = [nc.dram_tensor(f"qkvh{j}", [769, TOK], BF16, kind="Internal")
             for j in range(4)]
    qkvh_all = nc.dram_tensor("qkvh_all", [4 * NRK * 769, TOK], BF16, kind="Internal")
    do_my = nc.dram_tensor("do_my", [256, S_], BF16, kind="Internal")
    do_all = nc.dram_tensor("do_all", [NRK * 256, S_], BF16, kind="Internal")
    wob = nc.dram_tensor("wob", [D, D], BF16, kind="Internal")
    wts_d = nc.dram_tensor("wts_d", [32, 2 * FT], F32, kind="Internal")
    sol3_d = nc.dram_tensor("sol3_d", [32, 512], BF16, kind="Internal")

    xt = p.sb("xt", [128, 8, 512], F32)
    xn_bf = p.sb("xn_bf", [128, 8, TH], BF16)
    vc = p.sb("vc", [128, 8, TH], BF16)
    xsq = p.sb("xsq", [128, 512], F32)
    tmp32 = p.sb("tmp32", [128, 2, 512], F32)
    rows = p.sb("rows", [1, 2560], F32)
    consts = p.sb("consts", [128, 4], F32)
    ones_cf = p.sb("ones_cf", [128, 1], F32)
    ones_cb = p.sb("ones_cb", [128, 1], BF16)
    ones_f = p.sb("ones_f", [1, 128], F32)
    wqk_bf = p.sb("wqk_bf", [128, 8, 2048], BF16)
    wv_bf = p.sb("wv_bf", [128, 8, 1024], BF16)
    wb_bf = p.sb("wb_bf", [128, 8, 4], BF16)
    cw_s = p.sb("cw_s", [128, 3, 32], F32)
    firw_s = p.sb("firw_s", [128, 8 * 57], F32)
    msk_s = p.sb("msk_s", [128, 5, 128], BF16)
    scol_s = p.sb("scol_s", [128, 28 * 28], BF16)
    gsel_s = p.sb("gsel_s", [28, 4], F32)
    gb1c_s = p.sb("gb1c_s", [128, 16], F32)
    gb2_s = p.sb("gb2_s", [28, 1], F32)
    nw_s = p.sb("nw_s", [128, 2], F32)
    gw1sb = p.sb("gw1sb", [28, 2048], BF16)
    gw2b = p.sb("gw2b", [128, 16, 28], BF16)
    ridx = p.sb("ridx_s", [128, NCH * 7], U32)
    didx = p.sb("didx_s", [128, NFT * 8], U32)
    pjt = p.sb("pjt", [128, 8, 516], F32)
    cvt = p.sb("cvt", [128, 8, 512], BF16)
    tails = p.sb("tails", [128, 3, 8, 4], F32)
    bt4 = p.sb("bt4", [4, 512], BF16)
    wbc = p.sb("wbc", [128, 4, FT], F32)

    pjtF = pjt[:].rearrange("p a b -> p (a b)")
    pjtB = pjtF.bitcast(BF16)
    qn = pjtB[:, 0:512].rearrange("p (a g t) -> p a g t", a=2, g=2)
    kn = pjtB[:, 512:1024].rearrange("p (a g t) -> p a g t", a=2, g=2)
    kbb = pjtB[:, 1024:1536].rearrange("p (a g t) -> p a g t", a=2, g=2)
    vbb = pjtB[:, 1536:2048].rearrange("p (a g t) -> p a g t", a=2, g=2)
    xyb = pjtB[:, 2048:3840].rearrange("p (a t) -> p a t", a=14)
    S_ab = [pjtB[:, 3840:4352], pjtB[:, 4352:4864]]
    uadj = pjtB[:, 4864:5120]
    wfm = pjtB[:, 5120:5376].rearrange("p (g t) -> p g t", g=2)
    ktm = pjtB[:, 5376:5632].rearrange("p (g t) -> p g t", g=2)
    sbfm = pjtB[:, 5632:6144].rearrange("p (g t) -> p g t", g=2)
    ofm_o = pjtB[:, 6144:6400].rearrange("p (g t) -> p g t", g=2)
    otm = pjtB[:, 6400:6656]
    btb = pjtB[:, 6656:6912].rearrange("p (a t) -> p a t", a=2)
    s_f32 = pjtF[:, 3456:3968]
    firb = pjtB[:, 0:8192].rearrange("p (s c t) -> p s c t", s=4, c=8)
    cvtF = cvt[:].rearrange("p a b -> p (a b)")
    cvtF32 = cvtF.bitcast(F32)
    scrq = cvtF[:, 0:512].rearrange("p (a t) -> p a t", a=4)
    scrA = cvtF[:, 0:256]
    scrB = cvtF[:, 256:512]
    zt = cvtF32[0:28, 256:512]
    wtile = cvtF32[0:28, 512:768]
    sw4 = cvtF32[0:4, 768:1024]
    st28b = cvtF[0:28, 1664:1920]
    onb = cvtF[:, 2048:4096].rearrange("p (c t) -> p c t", c=8)
    wvF = wv_bf[:].rearrange("p a b -> p (a b)")
    hm_bf = wvF[:, 0:4096].rearrange("p (c t) -> p c t", c=16)
    dt_bf = wvF[:, 4096:6144].rearrange("p (c t) -> p c t", c=8)
    wos = wvF[:, 6144:8192].rearrange("p (a c t) -> p a c t", a=2, c=8)

    pb = [p.ps(f"pb{i}", [128, 512], F32) for i in range(6)]
    pbb = [p.ps(f"pbb{i}", [128, 1024], BF16) for i in range(2)]

    LDC = p.sem("ldc"); LD = p.sem("ld"); CAST = p.sem("cast"); LX = p.sem("lx")
    STM = p.sem("stm"); SQ = p.sem("sq"); ROW = p.sem("row"); ROWA = p.sem("rowa")
    NRM = p.sem("nrm"); XQ = p.sem("xq"); PRJ = p.sem("prj"); PEV = p.sem("pev")
    CNV = p.sem("cnv"); CNE = p.sem("cne"); WRQ = p.sem("wrq"); CC = p.sem("cc")
    GQ = p.sem("gq"); GC = p.sem("gc"); GF = p.sem("gf"); BQ = p.sem("bq")
    DPE = p.sem("dpe"); DAC = p.sem("dac"); DVD = p.sem("dvd"); OD = p.sem("od")
    DS3 = p.sem("ds3")
    FPE = p.sem("fpe"); FAC = p.sem("fac"); FVD = p.sem("fvd"); FGP = p.sem("fgp")
    WBC = p.sem("wbc"); WOS = p.sem("wos")

    r_mu = rows[:, 0:512]; r_var = rows[:, 512:1024]; r_rstd = rows[:, 1024:1536]
    r_t = rows[:, 1536:2048]; r_ms = rows[:, 2048:2560]

    for (dst, src) in ((cw_s[:].rearrange("p a b -> p (a b)"), cw3),
                       (firw_s[:], firw), (msk_s[:].rearrange("p a b -> p (a b)"), masks4),
                       (scol_s[:], scol28), (gsel_s[:], gselc), (gb1c_s[:], gb1c),
                       (gb2_s[:], gb2r), (nw_s[:], nwc2), (ridx[:], ridx_d),
                       (didx[:], didx_d)):
        emit_dma(p, "sync", dst, src, LDC)
    t_ldc = LDC.n
    def init_c(v):
        v.memset(consts[:, 0:1], 1e-5)
        v.memset(consts[:, 1:2], 1.0)
        v.memset(consts[:, 2:3], 1e-6)
        v.memset(ones_cf[:], 1.0)
        v.memset(ones_cb[:], 1.0)
        v.memset(ones_f[:], 1.0)
        i = v.memset(tails[:].rearrange("p a b c -> p (a b c)"), 0.0)
        p.chain_bar("vector", v, i)
        return v.memset(pjtF, 0.0)
    emit(p, "vector", init_c, CAST, wait=[(LDC, t_ldc)])
    def wload(dram, dstsl, ncol):
        for jh in range(ceil_div(ncol, 512)):
            w0 = [(CAST, CAST.n)]
            t_w = emit_dma(p, "sync", xt[:, :, 0:512],
                           dram.rearrange("(c p) q -> p c q", p=128)
                           [:, :, jh * 512:(jh + 1) * 512], LD, wait=w0)
            emit(p, "vector", lambda v, dstsl=dstsl, jh=jh: v.tensor_copy(
                dstsl[:, :, jh * 512:(jh + 1) * 512], xt[:]), CAST, wait=[(LD, t_w)])
    wload(wq, wqk_bf[:, :, 0:1024], 1024)
    wload(wk, wqk_bf[:, :, 1024:2048], 1024)
    wload(wv, wv_bf[:], 1024)
    t_w4 = emit_dma(p, "sync", xt[:, :, 0:4], wb4.rearrange("(c p) q -> p c q", p=128),
                    LD, wait=[(CAST, CAST.n)])
    emit(p, "vector", lambda v: v.tensor_copy(wb_bf[:], xt[:, :, 0:4]), CAST,
         wait=[(LD, t_w4)])
    t_wc = CAST.n

    NTI = TOK // 512
    tile_ranges = [(0, HB)] + [(HB + i * 512, HB + (i + 1) * 512) for i in range(NTI)]
    KINDS = ("q", "k", "v")
    t_conv_prev = 0
    t_pev_all_prev = 0
    t_wrq_beta_prev = 0
    t_xn_prev = 0
    t_row_prev = 0
    for ti, (lo_, hi_) in enumerate(tile_ranges):
        TW = hi_ - lo_
        w0 = [(CAST, t_wc)] if ti == 0 else [(XQ, t_xn_prev), (STM, STM.n)]
        t_x = emit_dma(p, "sync", xt[:, :, 0:TW],
                       xh_fm[:, lo_:hi_].rearrange("(c p) t -> p c t", p=128), LX, wait=w0)
        def ssum(t, TW=TW):
            for c in range(8):
                last = t.matmul(pb[0][0:1, 0:TW], ones_cf[:], xt[:, c, 0:TW],
                                start=(c == 0), stop=(c == 7))
            return last
        emit(p, "tensor", ssum, STM, wait=[(LX, t_x), (ROW, t_row_prev)])
        for c in range(8):
            w2 = [(LX, t_x)] if c == 0 else []
            if c >= 1:
                w2.append((STM, STM.n))
            emit(p, "scalar", lambda sc, c=c, TW=TW: sc.activation(
                out=xsq[:, 0:TW], in_=xt[:, c, 0:TW], func=AF.Square), SQ, wait=w2)
            emit(p, "tensor", lambda t, c=c, TW=TW: t.matmul(
                pb[1][0:1, 0:TW], ones_cf[:], xsq[:, 0:TW], start=(c == 0), stop=(c == 7)),
                 STM, wait=[(SQ, SQ.n)])
        t_stm = STM.n
        def row_a(v, TW=TW):
            v.tensor_single_scalar(out=r_mu[:, 0:TW], in_=pb[0][0:1, 0:TW],
                                   scalar=1.0 / D, op=ALU.mult)
            i = v.tensor_single_scalar(out=r_var[:, 0:TW], in_=pb[1][0:1, 0:TW],
                                       scalar=1.0 / D, op=ALU.mult)
            p.chain_bar("vector", v, i)
            i = v.tensor_mul(r_t[:, 0:TW], r_mu[:, 0:TW], r_mu[:, 0:TW])
            p.chain_bar("vector", v, i)
            return v.tensor_sub(r_var[:, 0:TW], r_var[:, 0:TW], r_t[:, 0:TW])
        emit(p, "vector", row_a, ROW, wait=[(STM, t_stm)])
        emit(p, "scalar", lambda sc, TW=TW: sc.activation(
            out=r_t[:, 0:TW], in_=r_var[:, 0:TW], func=AF.Sqrt,
            bias=consts[0:1, 0:1]), ROWA, wait=[(ROW, ROW.n)])
        def row_c(v, TW=TW):
            i = v.reciprocal(r_rstd[:, 0:TW], r_t[:, 0:TW])
            p.chain_bar("vector", v, i)
            return v.tensor_mul(r_ms[:, 0:TW], r_mu[:, 0:TW], r_rstd[:, 0:TW])
        emit(p, "vector", row_c, ROW, wait=[(ROWA, ROWA.n)])
        t_rowe = ROW.n
        t_row_prev = t_rowe
        wnrm = [(ROW, t_rowe)]
        if ti > 0:
            wnrm.append((PEV, t_pev_all_prev))
            wnrm.append((XQ, t_xn_prev))
        emit(p, "tensor", lambda t, TW=TW: t.matmul(pb[2][:, 0:TW], ones_f[:],
                                                    r_rstd[:, 0:TW], start=True, stop=True),
             NRM, wait=wnrm)
        emit(p, "tensor", lambda t, TW=TW: t.matmul(pb[3][:, 0:TW], ones_f[:],
                                                    r_ms[:, 0:TW], start=True, stop=True),
             NRM)
        t_nrm = NRM.n
        def xn_f(v, lo_=lo_, TW=TW):
            for c in range(8):
                i = v.tensor_mul(tmp32[:, c % 2, 0:TW], xt[:, c, 0:TW], pb[2][:, 0:TW])
                p.chain_bar("vector", v, i)
                i = v.tensor_sub(xn_bf[:, c, lo_:lo_ + TW], tmp32[:, c % 2, 0:TW],
                                 pb[3][:, 0:TW])
                if c < 7:
                    p.chain_bar("vector", v, i)
            return i
        emit(p, "vector", xn_f, XQ, wait=[(NRM, t_nrm)])
        t_xn = XQ.n
        t_xn_prev = t_xn
        TWprev = tile_ranges[ti - 1][1] - tile_ranges[ti - 1][0] if ti > 0 else 0
        for kx, kind in enumerate(KINDS):
            t_pev_k = []
            for dc in range(8):
                bank = 3 + dc % 2
                w3 = [(XQ, t_xn)]
                if PEV.n >= 2:
                    w3.append((PEV, PEV.n - 1))
                if dc == 0 and kx == 0:
                    w3.append((XQ, t_xn))
                def pmm(t, dc=dc, bank=bank, kx=kx, lo_=lo_, TW=TW):
                    for c in range(8):
                        if kx < 2:
                            lh = wqk_bf[:, c, kx * 1024 + dc * 128:kx * 1024 + (dc + 1) * 128]
                        else:
                            lh = wv_bf[:, c, dc * 128:(dc + 1) * 128]
                        last = t.matmul(pb[bank][:, 0:TW], lh, xn_bf[:, c, lo_:lo_ + TW],
                                        start=(c == 0), stop=(c == 7))
                    return last
                emit(p, "tensor", pmm, PRJ, wait=w3)
                t_pm = PRJ.n
                def pev_f(v, dc=dc, bank=bank, kx=kx, TW=TW, ti=ti, TWprev=TWprev):
                    if ti > 0:
                        i = v.tensor_copy(pjt[:, dc, 0:4], tails[:, kx, dc, :])
                        p.chain_bar("vector", v, i)
                    i = v.tensor_copy(pjt[:, dc, 4:4 + TW], pb[bank][:, 0:TW])
                    p.chain_bar("vector", v, i)
                    return v.tensor_copy(tails[:, kx, dc, :], pjt[:, dc, TW:TW + 4])
                emit(p, "vector", pev_f, PEV, wait=[(PRJ, t_pm), (CNV, t_conv_prev)])
                t_pev_k.append(PEV.n)
            for dc in range(8):
                def conv_f(v, dc=dc, kx=kx, TW=TW):
                    i = v.tensor_scalar(out=tmp32[:, 0, 0:TW], in0=pjt[:, dc, 4:4 + TW],
                                        scalar1=cw_s[:, kx, dc * 4 + 3:dc * 4 + 4],
                                        scalar2=None, op0=ALU.mult)
                    for k in range(1, 4):
                        p.chain_bar("vector", v, i)
                        i = v.scalar_tensor_tensor(
                            out=tmp32[:, 0, 0:TW], in0=pjt[:, dc, 4 - k:4 + TW - k],
                            scalar=cw_s[:, kx, dc * 4 + 3 - k:dc * 4 + 4 - k],
                            in1=tmp32[:, 0, 0:TW], op0=ALU.mult, op1=ALU.add)
                    return i
                wcv = [(PEV, t_pev_k[dc]), (CNE, CNE.n)]
                emit(p, "vector", conv_f, CNV, wait=wcv)
                t_cv = CNV.n
                if kind == "v":
                    emit(p, "scalar", lambda sc, dc=dc, lo_=lo_, TW=TW: sc.activation(
                        out=vc[:, dc, lo_:lo_ + TW], in_=tmp32[:, 0, 0:TW], func=AF.Silu),
                        CNE, wait=[(CNV, t_cv)])
                else:
                    emit(p, "scalar", lambda sc, dc=dc, TW=TW: sc.activation(
                        out=cvt[:, dc, 0:TW], in_=tmp32[:, 0, 0:TW], func=AF.Silu),
                        CNE, wait=[(CNV, t_cv), (WRQ, WRQ.n)])
            t_conv_prev = CNV.n
            t_cne = CNE.n
            if kind != "v" and ti > 0:
                for g in range(2):
                    for hh in range(4):
                        emit_dma(p, "sync",
                                 qkv_h[hh].ap()[(kx * 2 + g) * 128:(kx * 2 + g + 1) * 128,
                                                lo_ - HB:hi_ - HB],
                                 cvt[:, hh * 2 + g, 0:TW], WRQ, wait=[(CNE, t_cne)])
        t_pev_all_prev = PEV.n
        if ti > 0:
            def bmm(t, lo_=lo_, TW=TW):
                for c in range(8):
                    last = t.matmul(pb[5][0:4, 0:TW], wb_bf[:, c, :],
                                    xn_bf[:, c, lo_:lo_ + TW], start=(c == 0),
                                    stop=(c == 7))
                return last
            emit(p, "tensor", bmm, PRJ, wait=[(XQ, t_xn), (BQ, t_wrq_beta_prev)])
            t_bm = PRJ.n
            emit(p, "scalar", lambda sc, TW=TW: sc.activation(
                out=bt4[:, 0:TW], in_=pb[5][0:4, 0:TW], func=AF.Sigmoid),
                 CNE, wait=[(PRJ, t_bm)])
            t_be = CNE.n
            for hh in range(4):
                emit_dma(p, "sync", qkv_h[hh].ap()[768:769, lo_ - HB:hi_ - HB],
                         bt4[hh:hh + 1, 0:TW], BQ, wait=[(CNE, t_be)])
            t_wrq_beta_prev = BQ.n
    t_cne_all = CNE.n
    for g in range(2):
        for hh in range(4):
            for jp in range(ceil_div(TOK, 1024)):
                emit_dma(p, "sync",
                         qkv_h[hh].ap()[(4 + g) * 128:(5 + g) * 128,
                                        jp * 1024:min((jp + 1) * 1024, TOK)],
                         vc[:, hh * 2 + g, HB + jp * 1024:HB + min((jp + 1) * 1024, TOK)],
                         WRQ, wait=[(CNE, t_cne_all)])
    t_pa = WRQ.n
    for j in range(4):
        emit(p, "gpsimd", lambda g, j=j: g.collective_compute(
            "AllGather", ALU.bypass,
            replica_groups=[[0, 1, 2, 3], [4, 5, 6, 7]],
            ins=[qkv_h[j].ap().opt()],
            outs=[qkvh_all.ap()[j * NRK * 769:(j + 1) * NRK * 769, :].opt()]),
             CC, wait=([(WRQ, t_pa), (BQ, BQ.n)] if j == 0 else []))
    t_cc1 = CC.n

    t_prjA = PRJ.n
    t_xqA = XQ.n
    def wob_cast(g):
        g.dma_start(out=wob.ap(), in_=wo).then_inc(GC.h, 16)
    GC.inc(16)
    p.add("gpsimd", wob_cast)
    t_wob = GC.n
    for jh in range(4):
        t_l = emit_dma(p, "sync", xt[:, :, 0:512],
                       gw1h.rearrange("(c p) q -> p c q", p=128)
                       [:, :, jh * 512:(jh + 1) * 512], LD,
                       wait=[(XQ, t_xqA), (PRJ, t_prjA)] if jh == 0 else
                            [(CAST, CAST.n)])
        emit(p, "vector", lambda v, jh=jh: v.tensor_copy(
            wqk_bf[:, :, jh * 512:(jh + 1) * 512], xt[:]), CAST, wait=[(LD, t_l)])
    xt28 = xt[:].rearrange("p c q -> p (c q)")[0:28, 0:2048]
    for jp in range(4):
        t_l = emit_dma(p, "sync", xt28[:, jp * 512:(jp + 1) * 512],
                       gw1s[:, jp * 512:(jp + 1) * 512], LD,
                       wait=[(CAST, CAST.n)] if jp == 0 else [(LD, LD.n)])
    emit(p, "vector", lambda v: v.tensor_copy(gw1sb[:], xt28), CAST, wait=[(LD, t_l)])
    tmpg = tmp32[:].rearrange("p c q -> p (c q)")[:, 0:448].rearrange(
        "p (c j) -> p c j", c=16)
    t_l = emit_dma(p, "sync", tmpg, gw2.rearrange("(c p) j -> p c j", p=128), LD,
                   wait=[(CNV, CNV.n), (XQ, t_xqA), (LD, LD.n)])
    emit(p, "vector", lambda v: v.tensor_copy(gw2b[:], tmpg), CAST, wait=[(LD, t_l)])
    t_fwc = CAST.n

    def dinit(v):
        i1 = v.memset(s_f32, 0.0)
        p.chain_bar("vector", v, i1)
        return v.memset(sbfm.rearrange("p g t -> p (g t)"), 0.0)
    emit(p, "vector", dinit, CAST, wait=[(CNV, CNV.n), (PEV, PEV.n)])
    t_dinit = CAST.n
    rq = rows[:, 0:128]; rk = rows[:, 128:256]
    rqi = rows[:, 256:384]; rki = rows[:, 384:512]
    Z = dict(prep=0, mask=0, rhsev=0, sqev=0, applylast=0, solev=0, wkev=0,
             wsmm=0, uadj=0, atb=0, outmm=0, dsmm=0, sadd=0, sbfmcp=0, outev=0,
             ofmtr=0, ofmev=0, od=0, colsum=0, sqrt=0, bcast=0, sq4=0, gath=0)
    prev = dict(Z)
    prev2 = dict(Z)
    for i in range(NCH if dbg_nch is None else dbg_nch):
        cur = dict(Z)
        par = i % 2
        rl = i // CPR
        c0 = (i % CPR) * CH
        tau = i * CH
        def gath(g, i=i, par=par, p2pe=prev2["outmm"],
                 p2vd=prev2["prep"], gq0=GQ.n):
            if i == 0:
                g.wait_ge(CC.h, t_cc1)
            else:
                g.wait_ge(GQ.h, gq0)
            if i >= 2:
                g.wait_ge(DPE.h, p2pe)
                g.wait_ge(DVD.h, p2vd)
            dsts = [qn, qn, kn, kn, vbb, vbb]
            qv = qkvh_all.ap().rearrange("(r) (cb t) -> (r cb) t", cb=CPR)
            for j6 in range(6):
                g.indirect_dma_start(
                    out=dsts[j6][:, par, j6 % 2, :], out_offset=None,
                    in_=qv,
                    in_offset=IndirectOffsetOnAxis(
                        ap=ridx[:, i * 7 + j6:i * 7 + j6 + 1], axis=0)
                ).then_inc(GQ.h, 16)
            g.indirect_dma_start(
                out=btb[:, par, :], out_offset=None,
                in_=qv,
                in_offset=IndirectOffsetOnAxis(
                    ap=ridx[:, i * 7 + 6:i * 7 + 7], axis=0)).then_inc(GQ.h, 16)
        GQ.inc(7 * 16)
        p.add("gpsimd", gath)
        cur["gath"] = GQ.n
        srcs = [qn[:, par, 0, :], qn[:, par, 1, :], kn[:, par, 0, :], kn[:, par, 1, :]]
        for j in range(4):
            w = [(GQ, cur["gath"])] if j == 0 else []
            if i > 0 and j == 0:
                w.append((DPE, prev["colsum"]))
            emit(p, "scalar", lambda sc, j=j, src=srcs[j]: sc.activation(
                out=scrq[:, j, :], in_=src, func=AF.Square), DAC, wait=w)
        cur["sq4"] = DAC.n
        def csum(t):
            t.matmul(pb[4][0:1, 256:384], ones_cb[:], scrq[:, 0, :], start=True,
                     stop=False)
            t.matmul(pb[4][0:1, 256:384], ones_cb[:], scrq[:, 1, :], start=False,
                     stop=True)
            t.matmul(pb[4][0:1, 384:512], ones_cb[:], scrq[:, 2, :], start=True,
                     stop=False)
            return t.matmul(pb[4][0:1, 384:512], ones_cb[:], scrq[:, 3, :],
                            start=False, stop=True)
        emit(p, "tensor", csum, DPE,
             wait=[(DAC, cur["sq4"])] + ([(DAC, prev["sqrt"])] if i > 0 else []))
        cur["colsum"] = DPE.n
        def sqr2(sc):
            sc.activation(out=rq, in_=pb[4][0:1, 256:384], func=AF.Sqrt,
                          bias=consts[0:1, 2:3])
            return sc.activation(out=rk, in_=pb[4][0:1, 384:512], func=AF.Sqrt,
                                 bias=consts[0:1, 2:3])
        emit(p, "scalar", sqr2, DAC, wait=[(DPE, cur["colsum"])])
        cur["sqrt"] = DAC.n
        def rcp2(v):
            i1 = v.reciprocal(rqi, rq)
            p.chain_bar("vector", v, i1)
            return v.reciprocal(rki, rk)
        emit(p, "vector", rcp2, DVD,
             wait=[(DAC, cur["sqrt"])] + ([(DPE, prev["bcast"])] if i > 0 else []))
        t_rcp = DVD.n
        def bc2(t):
            t.matmul(pb[1][:, 256:384], ones_f[:], rqi[0:1, :], start=True, stop=True)
            return t.matmul(pb[1][:, 384:512], ones_f[:], rki[0:1, :], start=True,
                            stop=True)
        emit(p, "tensor", bc2, DPE,
             wait=[(DVD, t_rcp)] + ([(DVD, prev["prep"])] if i > 0 else []))
        cur["bcast"] = DPE.n
        def preptt(v, par=par):
            for g in range(2):
                i1 = v.tensor_mul(qn[:, par, g, :], qn[:, par, g, :], pb[1][:, 256:384])
                p.chain_bar("vector", v, i1)
            for g in range(2):
                i1 = v.tensor_mul(kn[:, par, g, :], kn[:, par, g, :], pb[1][:, 384:512])
                p.chain_bar("vector", v, i1)
            for g in range(2):
                i1 = v.tensor_mul(kbb[:, par, g, :], kn[:, par, g, :], btb[:, par, :])
                p.chain_bar("vector", v, i1)
            i1 = v.tensor_mul(vbb[:, par, 0, :], vbb[:, par, 0, :], btb[:, par, :])
            p.chain_bar("vector", v, i1)
            return v.tensor_mul(vbb[:, par, 1, :], vbb[:, par, 1, :], btb[:, par, :])
        emit(p, "vector", preptt, DVD, wait=[(DPE, cur["bcast"])])
        cur["prep"] = DVD.n
        def amm(t, par=par):
            t.matmul(pb[0][:, 0:128], kbb[:, par, 0, :], kn[:, par, 0, :],
                     start=True, stop=False)
            t.matmul(pb[0][:, 0:128], kbb[:, par, 1, :], kn[:, par, 1, :],
                     start=False, stop=True)
            t.matmul(pb[0][:, 128:256], kn[:, par, 0, :], kbb[:, par, 0, :],
                     start=True, stop=False)
            return t.matmul(pb[0][:, 128:256], kn[:, par, 1, :], kbb[:, par, 1, :],
                            start=False, stop=True)
        emit(p, "tensor", amm, DPE, wait=[(DVD, cur["prep"])])
        cur["amm"] = DPE.n
        if dbg_stage < 2:
            prev2 = prev; prev = cur
            continue
        def mks(v):
            i1 = v.tensor_mul(xyb[:, 0, :], pb[0][:, 0:128], msk_s[:, 3, :])
            p.chain_bar("vector", v, i1)
            i1 = v.tensor_mul(xyb[:, 1, :], pb[0][:, 128:256], msk_s[:, 4, :])
            p.chain_bar("vector", v, i1)
            i1 = v.tensor_mul(xyb[:, 2, :], pb[0][:, 128:256], msk_s[:, 1, :])
            p.chain_bar("vector", v, i1)
            return v.tensor_sub(xyb[:, 3, :], msk_s[:, 0, :], xyb[:, 1, :])
        emit(p, "vector", mks, DVD, wait=[(DPE, cur["amm"])])
        cur["mask"] = DVD.n
        def rtr(t, par=par):
            for g in range(2):
                t.transpose(pbb[0][:, g * 128:(g + 1) * 128], vbb[:, par, g, :],
                            msk_s[:, 0, :])
            for g in range(2):
                i1 = t.transpose(pbb[0][:, 256 + g * 128:256 + (g + 1) * 128],
                                 kbb[:, par, g, :], msk_s[:, 0, :])
            return i1
        wrt = [(DVD, cur["prep"])]
        if i > 0:
            wrt += [(DAC, prev["rhsev"]), (DPE, prev["applylast"]), (DS3, 32 * i)]
        emit(p, "tensor", rtr, DPE, wait=wrt)
        cur["rhstr"] = DPE.n
        emit(p, "scalar", lambda sc: sc.activation(
            out=S_ab[0], in_=pbb[0][:, 0:512], func=AF.Copy, bias=0.0), DAC,
             wait=[(DPE, cur["rhstr"])])
        cur["rhsev"] = DAC.n
        rs_, rd_ = 3, 6
        xs_, xd_ = 0, 4
        ys_, yd_ = 1, 5
        for lv in range(4):
            wsq = [(DVD, cur["mask"])] if lv == 0 else [(DAC, DAC.n)]
            if lv == 0 and i > 0:
                wsq.append((DAC, prev["sqev"]))
            def sqmm(t, lv=lv, xs_=xs_, ys_=ys_):
                i1 = t.matmul(pb[1][:, 0:128], xyb[:, ys_, :], xyb[:, xs_, :],
                              start=True, stop=True)
                if lv < 3:
                    i1 = t.matmul(pb[1][:, 128:256], xyb[:, xs_, :], xyb[:, ys_, :],
                                  start=True, stop=True)
                return i1
            emit(p, "tensor", sqmm, DPE, wait=wsq)
            t_sq = DPE.n
            def sqev(sc, lv=lv, xd_=xd_, yd_=yd_):
                i1 = sc.activation(out=xyb[:, xd_, :], in_=pb[1][:, 0:128],
                                   func=AF.Copy, bias=0.0)
                if lv < 3:
                    i1 = sc.activation(out=xyb[:, yd_, :], in_=pb[1][:, 128:256],
                                       func=AF.Copy, bias=0.0)
                return i1
            emit(p, "scalar", sqev, DAC, wait=[(DPE, t_sq)])
            t_se = DAC.n
            def rmmf(t, rs_=rs_, xd_=xd_):
                t.matmul(pb[1][:, 256:384], msk_s[:, 0, :], xyb[:, rs_, :],
                         start=True, stop=False)
                return t.matmul(pb[1][:, 256:384], xyb[:, xd_, :], xyb[:, rs_, :],
                                start=False, stop=True)
            emit(p, "tensor", rmmf, DPE, wait=[(DAC, t_se)])
            t_rm2 = DPE.n
            emit(p, "scalar", lambda sc, rd_=rd_: sc.activation(
                out=xyb[:, rd_, :], in_=pb[1][:, 256:384], func=AF.Copy, bias=0.0),
                 DAC, wait=[(DPE, t_rm2)])
            rs_, rd_ = rd_, rs_
            xs_, xd_ = xd_, xs_
            ys_, yd_ = yd_, ys_
        cur["sqev"] = DAC.n
        wap = [(DAC, cur["rhsev"]), (DAC, cur["sqev"])]
        if i > 0:
            wap.append((DAC, prev["solev"]))
        if i == 0:
            wap.append((CAST, t_wc))
        if dbg_stage < 3:
            prev2 = prev; prev = cur
            continue
        def rel1(t):
            return t.transpose(pbb[0][0:64, 512:576], xyb[64:128, 3, 64:128],
                               msk_s[64:128, 0, 64:128])
        emit(p, "tensor", rel1, DPE, wait=wap)
        t_r1 = DPE.n
        emit(p, "scalar", lambda sc: sc.activation(
            out=xyb[0:64, 12, 0:64], in_=pbb[0][0:64, 512:576], func=AF.Copy,
            bias=0.0), DAC, wait=[(DPE, t_r1)])
        t_r1e = DAC.n
        def rel2(t):
            return t.transpose(pbb[0][0:32, 576:608], xyb[32:64, 12, 32:64],
                               msk_s[32:64, 0, 32:64])
        emit(p, "tensor", rel2, DPE, wait=[(DAC, t_r1e)])
        t_r2 = DPE.n
        emit(p, "scalar", lambda sc: sc.activation(
            out=xyb[0:32, 13, 0:32], in_=pbb[0][0:32, 576:608], func=AF.Copy,
            bias=0.0), DAC, wait=[(DPE, t_r2)])
        t_r2e = DAC.n
        if dbg_stage < 3.7:
            prev2 = prev; prev = cur
            continue
        t_sv = 0
        for cb in range(min(int(round(dbg_cb)), 4)):
            pbase = 0 if cb == 3 else cb * 32
            if cb >= 1:
                stg_lo = pbase
                t_ae = 0
                for gi in range(cb):
                    def accg2(t, cb=cb, pbase=pbase, gi=gi, stg_lo=stg_lo):
                        if gi == 0:
                            lh1 = msk_s[:, 0, cb * 32:(cb + 1) * 32]
                        else:
                            lh1 = msk_s[:, 0, stg_lo:stg_lo + 32]
                        t.matmul(pb[2][pbase:pbase + 32, 0:512], lh1, S_ab[0],
                                 start=True, stop=False)
                        c2 = gi
                        return t.matmul(pb[2][pbase:pbase + 32, 0:512],
                                        xyb[c2 * 32:(c2 + 1) * 32, 2,
                                            cb * 32:(cb + 1) * 32],
                                        S_ab[1][c2 * 32:(c2 + 1) * 32, :],
                                        start=False, stop=True)
                    wag = [(DAC, t_sv)] if gi == 0 else [(DAC, t_ae)]
                    if cb == 1 and gi == 0:
                        wag += [(DAC, cur["rhsev"]), (DAC, cur["sqev"])]
                        if i > 0:
                            wag.append((DAC, prev["solev"]))
                    emit(p, "tensor", accg2, DPE, wait=wag)
                    t_ag = DPE.n
                    wae = [(DPE, t_ag)]
                    if cb == 1 and i > 0:
                        wae.append((DVD, prev["uadj"]))
                    last_g = (gi == cb - 1)
                    if last_g and cb < 3:
                        dst_ae = S_ab[1][cb * 32:(cb + 1) * 32, :]
                    else:
                        dst_ae = S_ab[0][stg_lo:stg_lo + 32, :]
                    def accev(sc, pbase=pbase, dst_ae=dst_ae):
                        return sc.activation(out=dst_ae,
                                             in_=pb[2][pbase:pbase + 32, :],
                                             func=AF.Copy, bias=0.0)
                    emit(p, "scalar", accev, DAC, wait=wae)
                    t_ae = DAC.n
            if cb == 0:
                wta = [(DAC, cur["rhsev"]), (DAC, cur["sqev"])]
                if i > 0:
                    wta.append((DVD, prev["uadj"]))
            else:
                wta = [(DAC, t_ae)]
            if cb == 3:
                wta.append((DAC, t_r2e))
            def tap(t, cb=cb, pbase=pbase):
                if cb == 0:
                    rhsap = S_ab[0][0:32, :]
                    lh = xyb[0:32, 3, 0:32]
                elif cb == 3:
                    rhsap = S_ab[0][0:32, :]
                    lh = xyb[0:32, 13, 0:32]
                else:
                    rhsap = S_ab[1][cb * 32:(cb + 1) * 32, :]
                    lh = xyb[cb * 32:(cb + 1) * 32, 3, cb * 32:(cb + 1) * 32]
                return t.matmul(pb[3][pbase:pbase + 32, 0:512], lh, rhsap,
                                start=True, stop=True)
            emit(p, "tensor", tap, DPE, wait=wta)
            t_ta = DPE.n
            wse = [(DPE, t_ta)]
            if cb == 0 and i > 0:
                wse.append((DVD, prev["uadj"]))
            dst_se = S_ab[0][0:32, :] if cb == 3 else \
                S_ab[1][cb * 32:(cb + 1) * 32, :]
            def solevf(sc, pbase=pbase, dst_se=dst_se):
                return sc.activation(out=dst_se,
                                     in_=pb[3][pbase:pbase + 32, :],
                                     func=AF.Copy, bias=0.0)
            emit(p, "scalar", solevf, DAC, wait=wse)
            t_sv = DAC.n
            if cb == 3:
                t_d1 = emit_dma(p, "sync", sol3_d.ap(), S_ab[0][0:32, :], DS3,
                                wait=[(DAC, t_sv), (DS3, 32 * i)])
                emit_dma(p, "sync", S_ab[1][96:128, :], sol3_d.ap(), DS3,
                         wait=[(DS3, t_d1)])
        cur["applylast"] = DPE.n
        cur["solev"] = DAC.n
        if dbg_stage < 4:
            prev2 = prev; prev = cur
            continue
        def wktr(t, par=par):
            for g in range(2):
                t.transpose(pbb[1][:, g * 128:(g + 1) * 128],
                            S_ab[1][:, 256 + g * 128:256 + (g + 1) * 128],
                            msk_s[:, 0, :])
            for g in range(2):
                i1 = t.transpose(pbb[1][:, 256 + g * 128:256 + (g + 1) * 128],
                                 kn[:, par, g, :], msk_s[:, 0, :])
            return i1
        emit(p, "tensor", wktr, DPE,
             wait=[(DAC, cur["solev"])] + ([(DS3, 32 * (i + 1))] if dbg_cb >= 4 else []))
        cur["wktr"] = DPE.n
        def wkev(sc):
            for g in range(2):
                sc.activation(out=wfm[:, g, :], in_=pbb[1][:, g * 128:(g + 1) * 128],
                              func=AF.Copy, bias=0.0)
            for g in range(2):
                i1 = sc.activation(out=ktm[:, g, :],
                                   in_=pbb[1][:, 256 + g * 128:256 + (g + 1) * 128],
                                   func=AF.Copy, bias=0.0)
            return i1
        emit(p, "scalar", wkev, DAC, wait=[(DPE, cur["wktr"])])
        cur["wkev"] = DAC.n
        def wsm(t):
            t.matmul(pb[3][:, 0:256], wfm[:, 0, :], sbfm[:, 0, :], start=True,
                     stop=False)
            return t.matmul(pb[3][:, 0:256], wfm[:, 1, :], sbfm[:, 1, :],
                            start=False, stop=True)
        wws = [(DAC, cur["wkev"])]
        if i > 0:
            wws += [(DVD, prev["uadj"]), (DAC, prev["sbfmcp"])]
        emit(p, "tensor", wsm, DPE, wait=wws)
        cur["wsmm"] = DPE.n
        emit(p, "vector", lambda v: v.tensor_sub(uadj, S_ab[1][:, 0:256],
                                                 pb[3][:, 0:256]), DVD,
             wait=[(DPE, cur["wsmm"])])
        cur["uadj"] = DVD.n
        def atm(t, par=par):
            t.matmul(pb[0][:, 256:384], kn[:, par, 0, :], qn[:, par, 0, :],
                     start=True, stop=False)
            return t.matmul(pb[0][:, 256:384], kn[:, par, 1, :], qn[:, par, 1, :],
                            start=False, stop=True)
        wat = [(DVD, prev["atb"])] if i > 0 else []
        emit(p, "tensor", atm, DPE, wait=wat)
        cur["atmm"] = DPE.n
        emit(p, "vector", lambda v: v.tensor_mul(xyb[:, 7, :], pb[0][:, 256:384],
                                                 msk_s[:, 2, :]), DVD,
             wait=[(DPE, cur["atmm"])])
        cur["atb"] = DVD.n
        def omm(t, par=par):
            t.matmul(pb[4][:, 0:256], qn[:, par, 0, :], sbfm[:, 0, :], start=True,
                     stop=False)
            t.matmul(pb[4][:, 0:256], qn[:, par, 1, :], sbfm[:, 1, :], start=False,
                     stop=False)
            return t.matmul(pb[4][:, 0:256], xyb[:, 7, :], uadj, start=False,
                            stop=True)
        wom = [(DVD, cur["atb"]), (DVD, cur["uadj"])]
        if i > 0:
            wom.append((DAC, prev["outev"]))
        emit(p, "tensor", omm, DPE, wait=wom)
        cur["outmm"] = DPE.n
        def dsm(t):
            t.matmul(pb[5][:, 0:256], ktm[:, 0, :], uadj, start=True, stop=True)
            return t.matmul(pb[5][:, 256:512], ktm[:, 1, :], uadj, start=True,
                            stop=True)
        wds = [(DVD, cur["uadj"]), (DAC, cur["wkev"])]
        if i > 0:
            wds.append((DVD, prev["sadd"]))
        emit(p, "tensor", dsm, DPE, wait=wds)
        cur["dsmm"] = DPE.n
        def sadd(v):
            i1 = v.tensor_add(s_f32[:, 0:256], s_f32[:, 0:256], pb[5][:, 0:256])
            p.chain_bar("vector", v, i1)
            return v.tensor_add(s_f32[:, 256:512], s_f32[:, 256:512],
                                pb[5][:, 256:512])
        wsa = [(DPE, cur["dsmm"])]
        if i > 0:
            wsa.append((DAC, prev["sbfmcp"]))
        emit(p, "vector", sadd, DVD, wait=wsa)
        cur["sadd"] = DVD.n
        def sbcp(sc):
            sc.activation(out=sbfm[:, 0, :], in_=s_f32[:, 0:256], func=AF.Copy,
                          bias=0.0)
            return sc.activation(out=sbfm[:, 1, :], in_=s_f32[:, 256:512],
                                 func=AF.Copy, bias=0.0)
        emit(p, "scalar", sbcp, DAC, wait=[(DVD, cur["sadd"]), (DPE, cur["outmm"])])
        cur["sbfmcp"] = DAC.n
        emit(p, "scalar", lambda sc: sc.activation(out=otm, in_=pb[4][:, 0:256],
                                                   func=AF.Copy, bias=0.0), DAC,
             wait=[(DPE, cur["outmm"])])
        cur["outev"] = DAC.n
        def otr(t):
            t.transpose(pbb[1][:, 512 + 0 * 128:512 + 1 * 128], otm[:, 0:128],
                        msk_s[:, 0, :])
            return t.transpose(pbb[1][:, 512 + 1 * 128:512 + 2 * 128], otm[:, 128:256],
                               msk_s[:, 0, :])
        emit(p, "tensor", otr, DPE, wait=[(DAC, cur["outev"])])
        cur["ofmtr"] = DPE.n
        def ofev(sc):
            sc.activation(out=ofm_o[:, 0, :], in_=pbb[1][:, 512:640], func=AF.Copy,
                          bias=0.0)
            return sc.activation(out=ofm_o[:, 1, :], in_=pbb[1][:, 640:768],
                                 func=AF.Copy, bias=0.0)
        wof = [(DPE, cur["ofmtr"])]
        if i > 0:
            wof.append((OD, prev["od"]))
        emit(p, "scalar", ofev, DAC, wait=wof)
        cur["ofmev"] = DAC.n
        emit_dma(p, "sync",
                 do_my.ap().rearrange("(g p) t -> p g t", p=128)[:, :, tau:tau + CH],
                 ofm_o, OD, wait=[(DAC, cur["ofmev"])])
        cur["od"] = OD.n
        prev2 = prev
        prev = cur
    t_dexp = OD.n
    if dbg_nch is not None:
        lpar = (dbg_nch - 1) % 2
        def dmp(v, lpar=lpar):
            for g in range(2):
                i1 = v.tensor_copy(xt[:, g, 0:128], qn[:, lpar, g, :])
                p.chain_bar("vector", v, i1)
                i1 = v.tensor_copy(xt[:, 2 + g, 0:128], kn[:, lpar, g, :])
                p.chain_bar("vector", v, i1)
            i1 = v.tensor_copy(xt[:, 4, 0:128], xyb[:, 7, :])
            p.chain_bar("vector", v, i1)
            i1 = v.tensor_copy(xt[:, 5, 0:256], uadj)
            p.chain_bar("vector", v, i1)
            i1 = v.tensor_copy(xt[:, 7, 0:256], otm)
            p.chain_bar("vector", v, i1)
            i1 = v.tensor_copy(xt[:, 7, 256:512], sbfm.rearrange("p g t -> p (g t)")[:, 0:256])
            p.chain_bar("vector", v, i1)
            return v.tensor_copy(xt[:, 6, 0:512].rearrange("p (a b) -> p a b", a=2)
                                 .rearrange("p a b -> p (a b)"), S_ab[1])
        emit(p, "vector", dmp, CAST, wait=[(DAC, DAC.n), (DPE, DPE.n),
                                           (OD, OD.n), (DS3, DS3.n)])
        emit_dma(p, "sync", y[0:128, :].rearrange("p (a t) -> p a t", a=TOK // 512)
                 if False else y.rearrange("(c p) t -> p c t", p=128)[:, 0:8, 0:512],
                 xt[:, :, 0:512], OD, wait=[(CAST, CAST.n)])
        emit(p, "gpsimd", lambda g_: g_.nop().then_inc(p.sem("fin").h, 1), None,
             wait=[(OD, OD.n)])
        return p.build()
    def agdo(g):
        g.wait_ge(OD.h, t_dexp)
        g.collective_compute(
            "AllGather", ALU.bypass, replica_groups=[[0, 1, 2, 3], [4, 5, 6, 7]],
            ins=[do_my.ap().opt()], outs=[do_all.ap().opt()]).then_inc(CC.h, 1)
    CC.inc(1)
    p.add("gpsimd", agdo)
    t_cc2 = CC.n
    t_dDAC = DAC.n
    t_dDPE = DPE.n
    t_dDVD = DVD.n

    do_v = do_all.ap().rearrange("r (fb t) -> (r fb) t", fb=S_ // FT)
    FZ = dict(gf=0, stev=0, ztev=0, sw4ev=0, rmsev=0, cmbD=0, cmbG=0, onbD=0,
              onbG=0, yd=0, wbcn=0)
    fprev = dict(FZ)
    slot_used = [(None, 0)] * 4
    FSMAP = {3: 0, 7: 1, 15: 2, 31: 3}
    for ti in range(NFT):
        t0 = ti * FT
        sl = ti % 2
        fcur = dict(FZ)
        wx6 = [(CAST, t_fwc)] if ti == 0 else [(OD, fprev["yd"])]
        t_x6 = emit_dma(p, "sync", xt[:, :, 0:FT],
                        x6_fm[:, t0:t0 + FT].rearrange("(c p) t -> p c t", p=128),
                        LX, wait=wx6)
        def dga(g, ti=ti, t0=t0, pD=fprev["cmbD"], pG=fprev["cmbG"],
                pA=fprev["stev"]):
            if ti == 0:
                g.wait_ge(CC.h, t_cc2)
                g.wait_ge(PRJ.h, t_prjA)
            else:
                g.wait_ge(FVD.h, pD)
                g.wait_ge(FGP.h, pG)
                g.wait_ge(FAC.h, pA)
            for dc in range(8):
                g.indirect_dma_start(
                    out=dt_bf[:, dc, :], out_offset=None,
                    in_=do_v,
                    in_offset=IndirectOffsetOnAxis(
                        ap=didx[:, ti * 8 + dc:ti * 8 + dc + 1], axis=0)
                ).then_inc(GF.h, 16)
        GF.inc(8 * 16)
        p.add("gpsimd", dga)
        fcur["gf"] = GF.n
        t_fir = {}
        for s_ in (31, 15, 7, 3):
            fs = FSMAP[s_]
            eng = "vector"
            sem = FVD
            scr = tmp32[:, 0, 0:FT]
            wf = []
            if ti == 0:
                wf = [(CNE, t_cne_all), (CAST, t_fwc), (DAC, t_dDAC),
                      (DPE, t_dDPE), (DVD, t_dDVD), (OD, t_dexp)]
            else:
                wf = [(FGP, fprev["cmbG"])]
            for dc in range(8):
                def firf(v, s_=s_, fs=fs, dc=dc, t0=t0, scr=scr, eng=eng):
                    off = FIROFF[s_]
                    i1 = v.tensor_scalar(
                        out=scr, in0=vc[:, dc, HB + t0 - (s_ - 1):HB + t0 - (s_ - 1) + FT],
                        scalar1=firw_s[:, dc * 57 + off:dc * 57 + off + 1],
                        scalar2=None, op0=ALU.mult)
                    for j in range(1, s_):
                        p.chain_bar(eng, v, i1)
                        lag = s_ - 1 - j
                        i1 = v.scalar_tensor_tensor(
                            out=scr, in0=vc[:, dc, HB + t0 - lag:HB + t0 - lag + FT],
                            scalar=firw_s[:, dc * 57 + off + j:dc * 57 + off + j + 1],
                            in1=scr, op0=ALU.mult, op1=ALU.add)
                    p.chain_bar(eng, v, i1)
                    return v.tensor_copy(firb[:, fs, dc, :], scr)
                emit(p, eng, firf, sem,
                     wait=(wf + [(sem, sem.n)] if dc == 0 else [(sem, sem.n)]))
            t_fir[fs] = sem.n
        nmm = 0
        t_abs = {}
        for s_ in (0, 6, 5, 1, 2, 3, 4):
            for dc in range(8):
                scr = scrA if nmm % 2 == 0 else scrB
                wab = []
                if nmm >= 2:
                    wab.append((FPE, fprev_mm2[nmm - 2]))
                if s_ == 0 or s_ == 6:
                    src = vc[:, dc, HB + t0:HB + t0 + FT]
                elif s_ == 5:
                    src = dt_bf[:, dc, :]
                    if dc == 0:
                        wab.append((GF, fcur["gf"]))
                else:
                    src = firb[:, s_ - 1, dc, :]
                    if dc == 0:
                        wab.append((FVD, t_fir[s_ - 1]))
                if nmm == 0 and ti > 0:
                    wab += [(FAC, fprev["stev"]), (FAC, fprev["rmsev"])]
                if nmm == 0 and ti == 0:
                    wab.append((DPE, t_dDPE))
                def absf(sc, s_=s_, dc=dc, scr=scr, src=src):
                    if s_ == 0:
                        return sc.activation(out=scr, in_=src, func=AF.Abs,
                                             scale=firw_s[:, dc * 57:dc * 57 + 1])
                    return sc.activation(out=scr, in_=src, func=AF.Abs)
                emit(p, "scalar", absf, FAC, wait=wab)
                t_ab = FAC.n
                sidx = s_ * 4 + dc // 2
                def stmm(t, sidx=sidx, scr=scr, nmm=nmm):
                    return t.matmul(pb[0][0:28, 0:FT], scol_s[:, sidx * 28:(sidx + 1) * 28],
                                    scr, start=(nmm == 0), stop=(nmm == 55))
                emit(p, "tensor", stmm, FPE, wait=[(FAC, t_ab)])
                if nmm == 0:
                    fprev_mm2 = {}
                fprev_mm2[nmm] = FPE.n
                nmm += 1
        t_stmm = FPE.n
        emit(p, "scalar", lambda sc: sc.activation(
            out=st28b, in_=pb[0][0:28, 0:FT], func=AF.Copy, bias=0.0,
            scale=1.0 / 256.0), FAC, wait=[(FPE, t_stmm)])
        fcur["stev"] = FAC.n
        t_gel = {}
        for oc in range(16):
            bank = 3 + oc % 2
            wg = [(FAC, fcur["stev"])]
            if oc >= 2:
                wg.append((FAC, t_gel[oc - 2]))
            elif ti > 0:
                wg.append((FVD, fprev["yd_evD"]))
            else:
                wg.append((PRJ, t_prjA))
                wg.append((CAST, t_fwc))
            def gmm(t, oc=oc, bank=bank, t0=t0):
                for c in range(8):
                    t.matmul(pb[bank][:, 0:FT], wqk_bf[:, c, oc * 128:(oc + 1) * 128],
                             xn_bf[:, c, HB + t0:HB + t0 + FT],
                             start=(c == 0), stop=False)
                return t.matmul(pb[bank][:, 0:FT], gw1sb[:, oc * 128:(oc + 1) * 128],
                                st28b, start=False, stop=True)
            emit(p, "tensor", gmm, FPE, wait=wg)
            t_g = FPE.n
            emit(p, "scalar", lambda sc, oc=oc, bank=bank: sc.activation(
                out=hm_bf[:, oc, :], in_=pb[bank][:, 0:FT], func=AF.Gelu,
                bias=gb1c_s[:, oc:oc + 1]), FAC, wait=[(FPE, t_g)])
            t_gel[oc] = FAC.n
        wf2 = [(FAC, t_gel[15])]
        if ti > 0:
            wf2.append((FAC, fprev["ztev"]))
        def f2mm(t):
            for oc in range(16):
                last = t.matmul(pb[1][0:28, 0:FT], gw2b[:, oc, :], hm_bf[:, oc, :],
                                start=(oc == 0), stop=(oc == 15))
            return last
        emit(p, "tensor", f2mm, FPE, wait=wf2)
        t_f2 = FPE.n
        emit(p, "scalar", lambda sc: sc.activation(
            out=zt, in_=pb[1][0:28, 0:FT], func=AF.Identity, bias=gb2_s[0:28, 0:1]),
             FAC, wait=[(FPE, t_f2)])
        fcur["ztev"] = FAC.n
        zet = xsq[0:28, 256:512]
        def splus(sc):
            i1 = sc.activation(out=wtile, in_=zt, func=AF.Abs)
            p.chain_bar("scalar", sc, i1)
            i1 = sc.activation(out=zet, in_=wtile, func=AF.Exp, scale=-1.0)
            p.chain_bar("scalar", sc, i1)
            i1 = sc.activation(out=wtile, in_=zet, func=AF.Ln,
                               bias=consts[0:28, 1:2])
            p.chain_bar("scalar", sc, i1)
            return sc.activation(out=zet, in_=zt, func=AF.Relu)
        wsp = [(FAC, fcur["ztev"])]
        if ti > 0:
            wsp.append((WRQ, fprev["wtsd"]))
        emit(p, "scalar", splus, FAC, wait=wsp)
        t_sp = FAC.n
        def wtsf(v):
            i1 = v.tensor_add(wtile, wtile, zet)
            p.chain_bar("vector", v, i1)
            return v.tensor_single_scalar(out=wtile, in_=wtile, scalar=0.02,
                                          op=ALU.add)
        wwt = [(FAC, t_sp)]
        if ti > 0:
            wwt.append((WRQ, fprev["wtsd"]))
        emit(p, "vector", wtsf, FVD, wait=wwt)
        t_wts = FVD.n
        wsw = [(FVD, t_wts)]
        if ti > 0:
            wsw.append((FAC, fprev["sw4ev"]))
        emit(p, "tensor", lambda t: t.matmul(pb[2][0:4, 0:FT], gsel_s[:], wtile,
                                             start=True, stop=True), FPE, wait=wsw)
        t_swm = FPE.n
        emit(p, "scalar", lambda sc: sc.activation(
            out=sw4, in_=pb[2][0:4, 0:FT], func=AF.Copy, bias=0.0), FAC,
             wait=[(FPE, t_swm)])
        fcur["sw4ev"] = FAC.n
        wwd = [(FVD, t_wts), (FAC, fcur["sw4ev"]), (WBC, WBC.n)]
        emit_dma(p, "sync", wts_d.ap()[0:28, sl * FT:(sl + 1) * FT], wtile, WRQ,
                 wait=wwd)
        t_wtsd = emit_dma(p, "sync", wts_d.ap()[28:32, sl * FT:(sl + 1) * FT], sw4,
                          WRQ)
        fcur["wtsd"] = t_wtsd
        for h in range(4):
            scr2 = tmp32[:, 1, 256:512]
            t_s0 = 0
            for s_ in range(7):
                eng = "vector" if s_ == 0 else "gpsimd"
                sem = FVD if s_ == 0 else FGP
                slot = (h * 7 + s_) % 4
                wsl = [(WRQ, t_wtsd), (WBC, WBC.n)]
                if slot_used[slot][0] is not None:
                    wsl.append(slot_used[slot])
                emit_dma(p, "sync", wbc[:, slot, :],
                         wts_d.ap()[h * 7 + s_:h * 7 + s_ + 1,
                                    sl * FT:(sl + 1) * FT].broadcast_to([128, FT]),
                         WBC, wait=wsl)
                t_bc = WBC.n
                for dci in range(2):
                    dc = h * 2 + dci
                    wcm = ([(WBC, t_bc)] if dci == 0 else []) + [(sem, sem.n)]
                    if s_ == 0 and dci == 0:
                        wcm.append((LX, t_x6))
                        if ti > 0:
                            wcm.append((FVD, fprev["onbD"]))
                        if ti > 0:
                            wcm.append((FGP, fprev["onbG"]))
                    if s_ == 1 and dci == 0:
                        wcm.append((FVD, t_s0))
                    if s_ == 5 and dci == 0:
                        wcm.append((GF, fcur["gf"]))
                    if 1 <= s_ <= 4 and dci == 0:
                        wcm.append((FVD, t_fir[s_ - 1]))
                    oad = xt[:, dc, 256:512]
                    def cmbf(v, s_=s_, dc=dc, t0=t0, slot=slot, scr2=scr2, oad=oad,
                             eng=eng):
                        if s_ == 0:
                            return v.scalar_tensor_tensor(
                                out=oad, in0=vc[:, dc, HB + t0:HB + t0 + FT],
                                scalar=firw_s[:, dc * 57:dc * 57 + 1],
                                in1=wbc[:, slot, :], op0=ALU.mult, op1=ALU.mult)
                        if s_ == 6:
                            src = vc[:, dc, HB + t0:HB + t0 + FT]
                        elif s_ == 5:
                            src = dt_bf[:, dc, :]
                        else:
                            src = firb[:, s_ - 1, dc, :]
                        i1 = v.tensor_mul(scr2, src, wbc[:, slot, :])
                        p.chain_bar(eng, v, i1)
                        return v.tensor_add(oad, oad, scr2)
                    emit(p, eng, cmbf, sem, wait=wcm)
                slot_used[slot] = (sem, sem.n)
                if s_ == 0:
                    t_s0 = FVD.n
        fcur["cmbD"] = FVD.n
        fcur["cmbG"] = FGP.n
        for h in range(4):
            eng = "gpsimd"
            sem = FGP
            slot = (28 + h) % 4
            wsl = [(WRQ, t_wtsd), (WBC, WBC.n)]
            if slot_used[slot][0] is not None:
                wsl.append(slot_used[slot])
            emit_dma(p, "sync", wbc[:, slot, :],
                     wts_d.ap()[28 + h:29 + h,
                                sl * FT:(sl + 1) * FT].broadcast_to([128, FT]),
                     WBC, wait=wsl)
            t_bc = WBC.n
            emit(p, "vector", lambda v, slot=slot: v.reciprocal(
                wbc[:, slot, :], wbc[:, slot, :]), FVD, wait=[(WBC, t_bc)])
            t_rcp2 = FVD.n
            def dvf(v, h=h, slot=slot, eng=eng):
                i1 = None
                for dci in range(2):
                    if i1 is not None:
                        p.chain_bar(eng, v, i1)
                    dc = h * 2 + dci
                    i1 = v.tensor_mul(xt[:, dc, 256:512], xt[:, dc, 256:512],
                                      wbc[:, slot, :])
                return i1
            emit(p, eng, dvf, sem, wait=[(FVD, t_rcp2)])
            slot_used[slot] = (sem, sem.n)
        t_divD = FVD.n
        t_divG = FGP.n
        rbank = [(pb[5], 0), (pb[5], 256), (pb[0], 0), (pb[0], 256)]
        t_onb = {}
        for h in range(4):
            bnk, cof = rbank[h]
            wq1 = [(FGP, t_divG)]
            if h == 2:
                wq1.append((FAC, fcur["stev"]))
            for dci in range(2):
                dc = h * 2 + dci
                xss = xsq[:, 0:256] if dci == 0 else xsq[:, 256:512]
                wsq_ = list(wq1) if dci == 0 else []
                if h > 0 or dci > 0:
                    wsq_.append((FPE, FPE.n))
                def sqf(sc, dc=dc, xss=xss):
                    return sc.activation(out=xss, in_=xt[:, dc, 256:512],
                                         func=AF.Square)
                emit(p, "scalar", sqf, FAC, wait=wsq_)
                t_s2 = FAC.n
                def rmm(t, bnk=bnk, cof=cof, xss=xss, dci=dci):
                    return t.matmul(bnk[0:1, cof:cof + FT], ones_cf[:], xss,
                                    start=(dci == 0), stop=(dci == 1))
                emit(p, "tensor", rmm, FPE, wait=[(FAC, t_s2)])
            t_rm = FPE.n
            rr = rows[:, h * 256:(h + 1) * 256]
            rri = rows[:, 1024 + h * 256:1024 + (h + 1) * 256]
            def rsq(sc, bnk=bnk, cof=cof, rr=rr):
                return sc.activation(out=rr, in_=bnk[0:1, cof:cof + FT],
                                     func=AF.Sqrt, scale=1.0 / 256.0,
                                     bias=consts[0:1, 0:1])
            emit(p, "scalar", rsq, FAC, wait=[(FPE, t_rm)])
            fcur["rmsev"] = FAC.n
            t_rs = FAC.n
            emit(p, "vector", lambda v, rr=rr, rri=rri: v.reciprocal(rri, rr), FVD,
                 wait=[(FAC, t_rs)])
            t_rc = FVD.n
            wbcm = [(FVD, t_rc)]
            if h > 0:
                wbcm.append((FVD, t_onb[h - 1]))
            elif ti > 0:
                wbcm.append((FVD, fprev["onbD"]))
            def bcm(t, rri=rri):
                return t.matmul(pb[2][:, 256:512], ones_f[:], rri[0:1, :],
                                start=True, stop=True)
            emit(p, "tensor", bcm, FPE, wait=wbcm)
            t_bcm = FPE.n
            def onf(v, h=h):
                for dci in range(2):
                    dc = h * 2 + dci
                    i1 = v.scalar_tensor_tensor(
                        out=onb[:, dc, :], in0=xt[:, dc, 256:512],
                        scalar=nw_s[:, dc % 2:dc % 2 + 1], in1=pb[2][:, 256:512],
                        op0=ALU.mult, op1=ALU.mult)
                return i1
            won = [(FPE, t_bcm), (FGP, t_divG)]
            emit(p, "vector", onf, FVD, wait=won)
            t_onb[h] = FVD.n
        fcur["onbD"] = FVD.n
        fcur["onbG"] = FGP.n
        t_yev = {}
        for oc in range(8):
            wwo = [(GC, t_wob), (WOS, WOS.n)]
            if ti * 8 + oc >= 2:
                wwo.append((FPE, t_womm_hist[(ti * 8 + oc) - 2]))
            t_wl = emit_dma(
                p, "sync", wos[:, oc % 2, :, :],
                wob.ap().rearrange("(c p) o -> p c o", p=128)
                [:, :, oc * 128:(oc + 1) * 128], WOS, wait=wwo)
            bank = 3 + oc % 2
            wmo = [(WOS, t_wl), (FVD, fcur["onbD"])]
            if oc == 0:
                wmo.append((FGP, fcur["onbG"]))
            if ti * 8 + oc >= 2:
                wmo.append((FVD, t_yev_hist[(ti * 8 + oc) - 2]))
            def womm(t, oc=oc, bank=bank):
                for dc in range(8):
                    last = t.matmul(pb[bank][:, 0:FT], wos[:, oc % 2, dc, :],
                                    onb[:, dc, :], start=(dc == 0), stop=(dc == 7))
                return last
            emit(p, "tensor", womm, FPE, wait=wmo)
            if ti == 0 and oc == 0:
                t_womm_hist = {}
                t_yev_hist = {}
            t_womm_hist[ti * 8 + oc] = FPE.n
            t_mo = FPE.n
            def yev(v, oc=oc, bank=bank):
                return v.tensor_add(xt[:, oc, 0:FT], xt[:, oc, 0:FT],
                                    pb[bank][:, 0:FT])
            emit(p, "vector", yev, FVD, wait=[(FPE, t_mo), (LX, t_x6)])
            t_yev_hist[ti * 8 + oc] = FVD.n
        fcur["yd_evD"] = FVD.n
        t_yd = emit_dma(p, "sync", y[:, t0:t0 + FT].rearrange("(c p) t -> p c t", p=128),
                        xt[:, :, 0:FT], OD,
                        wait=[(FVD, fcur["yd_evD"])])
        fcur["yd"] = t_yd
        fprev = fcur
    emit(p, "gpsimd", lambda g_: g_.nop().then_inc(p.sem("fin").h, 1), None,
         wait=[(OD, OD.n)])
    return p.build()


def _host_tables_moe(KSEL=2048):
    esel = np.zeros((128, 8), np.float32)
    for e in range(8):
        esel[16 * e:16 * e + 16, e] = 1
    trilblk = np.zeros((128, 128), np.float32)
    for q in range(128):
        for pp in range(16 * (q // 16), q):
            trilblk[pp, q] = 1
    tril128 = np.zeros((128, 128), np.float32)
    for q in range(128):
        tril128[:q, q] = 1
    identF = np.eye(128, dtype=np.float32)
    identB = np.eye(128).astype(BF)
    revj = np.tile((128 - np.arange(128))[None, :], (128, 1)).astype(np.float32)
    geoff = (KSEL + (KSEL + 1) * (np.arange(128) // 16)).astype(np.float32).reshape(128, 1)
    return esel, trilblk, tril128, identF, identB, revj, geoff


def _moe_maps(xf_full, inp, parts=None):
    TOK = 2048
    esel, trilblk, tril128, identF, identB, revj, geoff = _host_tables_moe()
    gate_w = inp["L0_ffn_gate_w"]; ln_g = inp["L0_ffn_ln_gamma"]
    ln_b = inp["L0_ffn_ln_beta"]
    fc1s = inp["L0_ffn_fc1s"]; b1s = inp["L0_ffn_b1s"]
    fc2s = inp["L0_ffn_fc2s"]; b2s = inp["L0_ffn_b2s"]
    maps = []
    for cid in range(8):
        e = cid
        sl = slice(cid * TOK, (cid + 1) * TOK)
        m = {
            "x_fm": np.ascontiguousarray(xf_full[:, sl]),
            "gate_wT": np.ascontiguousarray(gate_w.T),
            "eselT": np.ascontiguousarray(esel.T),
            "esel": esel, "trilblk": trilblk, "tril128": tril128,
            "identF": identF, "identB": identB, "revj": revj,
            "gamcol": np.ascontiguousarray(ln_g.reshape(8, 128).T),
            "betcol": np.ascontiguousarray(ln_b.reshape(8, 128).T),
            "b1col": np.ascontiguousarray(b1s[e].reshape(32, 128).T),
            "b2col": np.ascontiguousarray(b2s[e].reshape(8, 128).T),
            "fc1T": np.ascontiguousarray(fc1s[e].T),
            "fc2T": np.ascontiguousarray(fc2s[e].T),
            "gamrow": ln_g[None, :],
            "ehrow": np.eye(8, dtype=np.float32)[e][None, :],
            "ecol": np.full((128, 1), cid, np.float32), "geoff": geoff,
        }
        if parts is not None:
            b = cid // 4
            psl = slice((cid % 4) * TOK, (cid % 4 + 1) * TOK)
            for i in range(4):
                m[f"part{i}"] = np.ascontiguousarray(parts[b * 4 + i][:, psl])
        maps.append(m)
    return maps


_BUILD_CACHE = {}


def _get(name, fn):
    if name not in _BUILD_CACHE:
        _BUILD_CACHE[name] = fn()
    return _BUILD_CACHE[name]


def kernel(**inp):
    x = np.asarray(inp["L0_attn_a_x"], np.float32)
    S_ = 8192
    TOK = 2048

    wq = inp["L0_attn_a_w_q"]; wk = inp["L0_attn_a_w_k"]
    wv = inp["L0_attn_a_w_v"]; wo = inp["L0_attn_a_w_o"]
    nw0 = inp["L0_attn_a_norm_w"]
    rf = rope_factor_np()
    rf2 = np.ascontiguousarray(np.concatenate([rf, rf], 1).T)
    dmask = causal_diag_masks_np()
    hsel = np.zeros((128, 2), np.float32); hsel[:64, 0] = 1; hsel[64:, 1] = 1
    nc = _get("l1", build_l0_attn_a)
    maps = []
    for cid in range(8):
        b, g = cid // 4, cid % 4
        hs = [g, g + 4, g + 8, g + 12]
        wk_g = wk[:, g * 64:(g + 1) * 64]
        maps.append({
            "x_fm": np.ascontiguousarray(x[b].T), "rf2": rf2,
            "wq": np.ascontiguousarray(np.concatenate(
                [wq[:, 64 * h:64 * h + 64] for h in hs], 1)),
            "wk2": np.ascontiguousarray(np.concatenate([wk_g, wk_g], 1)),
            "wv": np.ascontiguousarray(wv[:, g * 64:(g + 1) * 64]),
            "wo": np.ascontiguousarray(np.concatenate(
                [wo[64 * h:64 * h + 64, :] for h in hs], 1)),
            "nwrow": nw0[None, :],
            "nwcol": np.ascontiguousarray(nw0.reshape(8, 128).T),
            "dmask": dmask, "hsel": hsel,
        })
    res = run_bass_kernel_spmd(nc, maps, core_ids=list(range(8)))
    parts = [np.asarray(res.results[c]["y"]) for c in range(8)]

    xf_full = np.ascontiguousarray(x.reshape(NTOK, 1024).T)
    nc = _get("l2", lambda: build_l0_moe(NT=NTOK))
    maps = _moe_maps(xf_full, inp, parts=parts)
    res = run_bass_kernel_spmd(nc, maps, core_ids=list(range(8)))
    x2f = np.concatenate([np.asarray(res.results[c]["y"]) for c in range(8)], 1)

    Wq = inp["L0_attn_b_Wq"]; Wk = inp["L0_attn_b_Wk"]; Wv = inp["L0_attn_b_Wv"]
    bq = inp["L0_attn_b_bq"]; bk = inp["L0_attn_b_bk"]; bv = inp["L0_attn_b_bv"]
    HL = 256
    nc = _get("l3", lambda: build_l0_attn_b(TOK=TOK))
    maps = []
    for cid in range(8):
        base = cid * TOK
        b = cid // 4
        bs, be = b * S_, (b + 1) * S_
        xh = np.zeros((1024, TOK + 2 * HL), np.float32)
        lo = max(bs, base - HL); hi = min(be, base + TOK + HL)
        xh[:, lo - (base - HL):hi - (base - HL)] = x2f[:, lo:hi]
        NQT = TOK // 512
        bm = np.zeros((NQT * 8, 128, 512), BF)
        for qi in range(NQT):
            for kc in range(8):
                qg = base + qi * 512 + np.arange(512)[None, :]
                kg = base - HL + (qi * 4 + kc) * 128 + np.arange(128)[:, None]
                m = (np.abs(qg - kg) <= 256) & (kg >= bs) & (kg < be)
                bm[qi * 8 + kc] = m.astype(BF)
        maps.append({
            "xh_fm": xh,
            "x2_fm": np.ascontiguousarray(x2f[:, base:base + TOK]),
            "wqT": np.ascontiguousarray(Wq.T), "wkT": np.ascontiguousarray(Wk.T),
            "wvT": np.ascontiguousarray(Wv.T),
            "bqc": np.ascontiguousarray(bq.reshape(8, 128).T),
            "bkc": np.ascontiguousarray(bk.reshape(8, 128).T),
            "bvrow": bv[None, :], "bmask": bm,
        })
    res = run_bass_kernel_spmd(nc, maps, core_ids=list(range(8)))
    x3f = np.concatenate([np.asarray(res.results[c]["y"]) for c in range(8)], 1)

    nc = _get("l4", lambda: build_l0_moe(NT=NTOK, add_ln=False))
    maps = _moe_maps(x3f, inp)
    res = run_bass_kernel_spmd(nc, maps, core_ids=list(range(8)))
    x4 = np.stack([np.ascontiguousarray(np.concatenate(
        [np.asarray(res.results[b * 4 + i]["y"]) for i in range(4)], 1).T)
        for b in range(2)])

    nwa = inp["L1_attn_a_norm_w"]; skip_w = inp["L1_attn_a_skip_w"]
    in_w = inp["L1_attn_a_in_w"]
    conv_w = inp["L1_attn_a_conv_w"]; conv_b = inp["L1_attn_a_conv_b"]
    wd1 = inp["L1_attn_a_wd1"]; wd2 = inp["L1_attn_a_wd2"]
    wd2_b = inp["L1_attn_a_wd2_b"]
    wB = inp["L1_attn_a_wB"]; wC = inp["L1_attn_a_wC"]
    A_log = inp["L1_attn_a_A_log"]; W_D = inp["L1_attn_a_W_D"]
    out_w = inp["L1_attn_a_out_w"]
    nc = _get("l5", lambda: build_l1_ssm(S_=S_))
    maps = []
    for cid in range(8):
        b, qtr = cid // 4, cid % 4
        sh = slice(qtr * 512, qtr * 512 + 512)
        def col(v512):
            return np.ascontiguousarray(v512.reshape(4, 128).T)
        maps.append({
            "x4_fm": np.ascontiguousarray(x4[b].T),
            "inw": np.ascontiguousarray(in_w[sh].T),
            "skw": np.ascontiguousarray(skip_w[sh].T),
            "wrow1": np.ascontiguousarray(wd1[:, sh].T),
            "wrowBC": np.ascontiguousarray(np.concatenate([wB, wC], 0)[:, sh].T),
            "wd2sT": np.ascontiguousarray(wd2[sh].T),
            "convw": np.ascontiguousarray(
                conv_w[sh].reshape(4, 128, 4).transpose(1, 0, 2).reshape(128, 16)),
            "convb": col(conv_b[sh]), "wd2b": col(wd2_b[sh]),
            "alog": np.ascontiguousarray(
                A_log[sh].reshape(4, 128, 16).transpose(1, 0, 2).reshape(128, 64)),
            "wdcol": col(W_D[sh]),
            "outw": np.ascontiguousarray(out_w[:, sh].T),
            "nwcol": np.ascontiguousarray(nwa.reshape(8, 128).T),
            "nwrow": nwa[None, :],
        })
    res = run_bass_kernel_spmd(nc, maps, core_ids=list(range(8)))
    x5f = np.concatenate([np.asarray(res.results[0]["y"]),
                          np.asarray(res.results[4]["y"])], 1)

    W1 = inp["L1_ffn_W1"]; b1 = inp["L1_ffn_b1"]
    W2 = inp["L1_ffn_W2"]; b2 = inp["L1_ffn_b2"]
    nc = _get("l6", lambda: build_l1_ffn(TOK=TOK))
    ffn_maps = lambda xf: [{
        "x_fm": np.ascontiguousarray(xf[:, cid * TOK:(cid + 1) * TOK]),
        "w1T": np.ascontiguousarray(W1.T), "w2T": np.ascontiguousarray(W2.T),
        "b1col": np.ascontiguousarray(b1.reshape(32, 128).T),
        "b2col": np.ascontiguousarray(b2.reshape(8, 128).T),
    } for cid in range(8)]
    res = run_bass_kernel_spmd(nc, ffn_maps(x5f), core_ids=list(range(8)))
    x6f = np.concatenate([np.asarray(res.results[c]["y"]) for c in range(8)], 1)

    g7 = lambda n: inp["L1_attn_b_" + n]
    w_q7, w_k7, w_v7, w_b7 = g7("w_q"), g7("w_k"), g7("w_v"), g7("w_b")
    firs_w = {1: inp["L1_attn_b_fir1"], 3: inp["L1_attn_b_fir3"],
              7: inp["L1_attn_b_fir7"], 15: inp["L1_attn_b_fir15"],
              31: inp["L1_attn_b_fir31"]}
    NRK = S_ // TOK
    CPR = TOK // 128
    NCH7 = S_ // 128
    FT = 256
    NFT = TOK // FT
    FIROFF = {1: 0, 3: 1, 7: 4, 15: 11, 31: 26}
    firw_np = np.zeros((128, 8 * 57), np.float32)
    for dc in range(8):
        for ks, off in FIROFF.items():
            for j in range(ks):
                firw_np[:, dc * 57 + off + j] = firs_w[ks][dc * 128:(dc + 1) * 128, j]
    cw3_np = np.zeros((128, 96), np.float32)
    for kx, cw in enumerate((g7("conv_q_w"), g7("conv_k_w"), g7("conv_v_w"))):
        for dc in range(8):
            cw3_np[:, kx * 32 + dc * 4:kx * 32 + dc * 4 + 4] = cw[dc * 128:(dc + 1) * 128]
    r_ = np.arange(128)
    ident = np.eye(128)
    bdm = np.zeros((128, 128))
    for bb in range(4):
        bdm[bb * 32:(bb + 1) * 32, bb * 32:(bb + 1) * 32] = 1
    masks4 = np.concatenate([ident, -np.triu(np.ones((128, 128)), 1),
                             np.triu(np.ones((128, 128)), 0),
                             np.tril(np.ones((128, 128)), -1) * bdm,
                             np.triu(np.ones((128, 128)), 1) * bdm], 1).astype(BF)
    scol = np.zeros((128, 28 * 28), np.float32)
    for vv in range(28):
        scol[:, vv * 28 + vv] = 1.0
    scol = scol.astype(BF)
    gsel = np.zeros((28, 4), np.float32)
    for rr in range(28):
        gsel[rr, rr // 7] = 1.0
    gate_w1 = g7("gate_w1")
    nc = _get("l7", lambda: build_l1_dn(TOK=TOK, S_=S_))
    maps = []
    for cid in range(8):
        b, tb = cid // 4, cid % 4
        base = b * S_ + tb * TOK
        xh = np.zeros((D, TOK + 64), np.float32)
        lo = tb * TOK - 64
        src = x6f[:, max(b * S_ + lo, b * S_):base + TOK]
        xh[:, (TOK + 64) - src.shape[1]:] = src
        h = cid % 4
        ridx = np.zeros((128, NCH7 * 7), np.uint32)
        for i in range(NCH7):
            rl, cb = i // CPR, i % CPR
            for j in range(6):
                ridx[:, i * 7 + j] = (((h * NRK + rl) * 769 + j * 128 + r_) * CPR + cb)
            ridx[:, i * 7 + 6] = (((h * NRK + rl) * 769 + 768) * CPR + cb)
        didx = np.zeros((128, NFT * 8), np.uint32)
        for ti in range(NFT):
            for dc in range(8):
                didx[:, ti * 8 + dc] = (((dc // 2) * 256 + (dc % 2) * 128 + r_)
                                        * (S_ // FT) + tb * NFT + ti)
        maps.append({
            "xh_fm": xh,
            "x6_fm": np.ascontiguousarray(x6f[:, base:base + TOK]),
            "wq": w_q7, "wk": w_k7, "wv": w_v7, "wb4": w_b7, "cw3": cw3_np,
            "firw": firw_np, "gw1h": np.ascontiguousarray(gate_w1[0:D]),
            "gw1s": np.ascontiguousarray(gate_w1[D:]),
            "gb1c": np.ascontiguousarray(g7("gate_b1").reshape(16, 128).T),
            "gw2": g7("gate_w2"),
            "gb2r": np.ascontiguousarray(g7("gate_b2")[:, None]),
            "nwc2": np.ascontiguousarray(g7("norm_w").reshape(2, 128).T),
            "wo": g7("w_o"), "masks4": masks4, "scol28": scol, "gselc": gsel,
            "ridx": ridx, "didx": didx,
        })
    res = run_bass_kernel_spmd(nc, maps, core_ids=list(range(8)))
    x7f = np.concatenate([np.asarray(res.results[c]["y"]) for c in range(8)], 1)

    nc = _get("l6", lambda: build_l1_ffn(TOK=TOK))
    res = run_bass_kernel_spmd(nc, ffn_maps(x7f), core_ids=list(range(8)))
    outf = np.concatenate([np.asarray(res.results[c]["y"]) for c in range(8)], 1)
    return np.ascontiguousarray(outf.T).reshape(2, S_, 1024).astype(np.float32)
